# version 05
# speedup vs baseline: 1.2531x; 1.0310x over previous
import numpy as np
import ml_dtypes
import concourse.bass as bass
import concourse.mybir as mybir
from concourse.bass_utils import run_bass_kernel_spmd

F32 = mybir.dt.float32
BF16 = mybir.dt.bfloat16
AF = mybir.ActivationFunctionType
ALU = mybir.AluOpType
AX = mybir.AxisListType
BF16NP = ml_dtypes.bfloat16


class Cfg:
    def __init__(self, S=8192, phases=None):
        self.R = 8
        self.B = 2
        self.S = S
        self.D = 1024
        self.H = 16
        self.dh = 64
        self.DM = 512
        self.DI = 1024
        self.DS = 16
        self.DTR = 32
        self.NL = 4
        self.CK = 4
        self.pos = S // self.R
        self.TL = self.B * self.pos
        self.T = self.B * S
        self.TCH = min(1024, self.pos)
        self.NT_LOC = self.TL // 128
        self.MB = 3 + S
        self.phases = phases if phases is not None else [
            "p0", "attn_a", "ffn1", "mamba", "ffn2", "out"]

    def mcol0(self, b):
        return b * self.MB + 3


def head_perm(cfg):
    p = []
    for half in (0, 1):
        for hp in (0, 1):
            for i in range(32):
                p.append(hp * 64 + half * 32 + i)
    return np.array(p)


def build_aux(cfg):
    S, dh = cfg.S, cfg.dh
    aux = {}
    aux["ident"] = np.eye(128, dtype=BF16NP)
    half = dh // 2
    theta = 1.0 / (1000000.0 ** (2 * np.arange(half) / dh))
    ang = np.arange(S)[None, :] * theta[:, None]
    cos, sin = np.cos(ang).astype(np.float32), np.sin(ang).astype(np.float32)
    ropeK = np.concatenate([np.tile(cos, (2, 1)), np.tile(sin, (2, 1))], 0)
    aux["ropeK"] = ropeK.astype(BF16NP)
    aux["ropeQ"] = (ropeK * 0.125).astype(BF16NP)
    hmp = np.zeros((128, 2), np.float32)
    perm = head_perm(cfg)
    for j in range(128):
        hmp[j, 0 if perm[j] < 64 else 1] = 1.0
    aux["headmask_pi"] = hmp.astype(BF16NP)
    hm = np.zeros((128, 2), np.float32)
    hm[:64, 0] = 1.0
    hm[64:, 1] = 1.0
    aux["headmask"] = hm.astype(BF16NP)
    TCH = cfg.TCH
    u = np.arange(-(TCH - 128), TCH)
    aux["band"] = np.where(u[None, :] >= np.arange(128)[:, None],
                           0.0, -1e30).astype(BF16NP)
    aux["ones_1x128_f32"] = np.ones((1, 128), np.float32)
    o4 = np.zeros((128, 128), np.float32)
    o4[0::32, :] = 1.0
    aux["ones4"] = o4
    aux["ones_1x64_f32"] = np.ones((1, 64), np.float32)
    aux["ones_col_f32"] = np.ones((128, 1), np.float32)
    aux["ones_1x128_bf"] = np.ones((1, 128), BF16NP)
    aux["ones_col_bf"] = np.ones((128, 1), BF16NP)
    aux["cbias"] = np.tile(np.array([[1e-5, 1e-6, 1.0, 0.0]], np.float32),
                           (128, 1))
    aux["ones_dram_bf"] = np.ones((S,), BF16NP)
    return aux


def shard_inputs(cfg, inp):
    aux = build_aux(cfg)
    perm = head_perm(cfg)
    x = np.asarray(inp["L0_attn_a_x"])
    maps = []
    for c in range(cfg.R):
        m = dict(aux)
        sl = slice(cfg.pos * c, cfg.pos * (c + 1))
        m["x_in"] = np.ascontiguousarray(
            np.concatenate([x[0, sl], x[1, sl]], 0)).astype(np.float32)
        lcols = np.arange(2 * c * 64, 2 * c * 64 + 128)
        g = np.asarray(inp["L0_attn_a_norm_scale"])
        m["wqa"] = np.ascontiguousarray(
            (np.asarray(inp["L0_attn_a_Wq"]) * g[:, None])[:, lcols[perm]])
        m["wka"] = np.ascontiguousarray(
            (np.asarray(inp["L0_attn_a_Wk"]) * g[:, None])[:, lcols[perm]])
        m["wva"] = np.ascontiguousarray(
            (np.asarray(inp["L0_attn_a_Wv"]) * g[:, None])[:, lcols])
        m["woa"] = np.ascontiguousarray(np.asarray(inp["L0_attn_a_Wo"])[lcols, :])
        m["wqf"] = np.ascontiguousarray(np.asarray(inp["L0_ffn_Wq"])[:, lcols])
        m["wkf"] = np.ascontiguousarray(np.asarray(inp["L0_ffn_Wk"])[:, lcols])
        m["wvf"] = np.ascontiguousarray(np.asarray(inp["L0_ffn_Wv"])[:, lcols])
        m["wof"] = np.ascontiguousarray(np.asarray(inp["L0_ffn_Wo"])[lcols, :])
        m["wff"] = np.ascontiguousarray(np.asarray(inp["L0_ffn_Wff"]))
        dsl = slice(128 * c, 128 * (c + 1))
        m["lin1_w"] = np.ascontiguousarray(np.asarray(inp["L0_attn_b_lin1_w"]))
        m["lin1_b"] = np.asarray(inp["L0_attn_b_lin1_b"]).reshape(1, -1)
        m["lin2_w"] = np.ascontiguousarray(np.asarray(inp["L0_attn_b_lin2_w"]))
        m["lin2_b"] = np.asarray(inp["L0_attn_b_lin2_b"]).reshape(1, -1)
        m["norm_w"] = np.ascontiguousarray(np.asarray(inp["L0_attn_b_norm_w"]))
        iw = np.asarray(inp["L0_attn_b_in_w"])
        m["inw"] = np.ascontiguousarray(
            np.concatenate([iw[:, :, dsl],
                            iw[:, :, cfg.DI + 128 * c: cfg.DI + 128 * (c + 1)]], -1))
        m["convw"] = np.ascontiguousarray(np.asarray(inp["L0_attn_b_conv_w"])[:, dsl, :])
        m["convb"] = np.ascontiguousarray(np.asarray(inp["L0_attn_b_conv_b"])[:, dsl])
        m["xprojw"] = np.ascontiguousarray(np.asarray(inp["L0_attn_b_xproj_w"])[:, dsl, :])
        m["dtw"] = np.ascontiguousarray(np.asarray(inp["L0_attn_b_dt_w"])[:, :, dsl])
        m["dtb"] = np.ascontiguousarray(np.asarray(inp["L0_attn_b_dt_b"])[:, dsl])
        m["alog"] = np.ascontiguousarray(np.asarray(inp["L0_attn_b_A_log"])[:, dsl, :])
        m["dpv"] = np.ascontiguousarray(np.asarray(inp["L0_attn_b_Dp"])[:, dsl])
        m["outw"] = np.ascontiguousarray(np.asarray(inp["L0_attn_b_out_w"])[:, dsl, :])
        maps.append(m)
    return maps



ENGS = ("tensor", "vector", "scalar", "gpsimd", "sync")

GRAN = {"ar0": 1024, "ar1": 1024, "ar2": 1024, "ar3": 1024,
        "E": 512, "F": 512, "G": 32, "W0": 128, "W1": 128,
        "psA": 512, "psB": 512, "psC": 512, "psT": 1024}


def keys(name, lo=None, hi=None):
    if lo is None:
        return ((name, 0),)
    gw = GRAN.get(name, 1 << 30)
    return tuple((name, g) for g in range(lo // gw, (hi + gw - 1) // gw))


class _Op:
    __slots__ = ("idx", "eng", "fn", "deps", "kind", "marked", "inc_no",
                 "waits", "clock", "ccsem", "tag")

    def __init__(self, idx, eng, fn, kind, tag):
        self.idx, self.eng, self.fn, self.kind, self.tag = idx, eng, fn, kind, tag
        self.deps = []
        self.marked = False
        self.inc_no = None
        self.waits = None
        self.clock = None
        self.ccsem = None


class Prog:
    def __init__(self):
        self.ops = []
        self.last_w = {}
        self.readers = {}
        self.n_cc = 0

    def op(self, eng, fn, reads=(), writes=(), kind="c", tag=""):
        o = _Op(len(self.ops), eng, fn, kind, tag)
        dep = {}
        for k in reads:
            w = self.last_w.get(k)
            if w is not None:
                dep[w.idx] = w
        for k in writes:
            w = self.last_w.get(k)
            if w is not None:
                dep[w.idx] = w
            for rd in self.readers.get(k, ()):
                dep[rd.idx] = rd
        o.deps = [d for d in dep.values()
                  if not (d.eng == "tensor" and eng == "tensor"
                          and d.kind == "c" and kind == "c")]
        for k in reads:
            self.readers.setdefault(k, []).append(o)
        for k in writes:
            self.last_w[k] = o
            self.readers[k] = []
        if kind == "cc":
            o.ccsem = self.n_cc
            self.n_cc += 1
        self.ops.append(o)
        return o

    def finalize_and_emit(self, block, sems):
        for o in self.ops:
            if o.kind == "d":
                o.marked = True
            for d in o.deps:
                d.marked = True
        counters = {}
        for o in self.ops:
            if not o.marked:
                continue
            if o.kind == "cc":
                o.inc_no = 1
            else:
                key = (o.eng, o.kind)
                counters[key] = counters.get(key, 0) + 1
                o.inc_no = counters[key]

        def semkey(o):
            return ("cc", o.ccsem) if o.kind == "cc" else (o.eng, o.kind)

        def val(o):
            return o.inc_no * (16 if o.kind == "d" else 1)

        last_on_eng = {}
        for o in self.ops:
            prev = last_on_eng.get(o.eng)
            base = dict(prev.clock) if prev is not None else {}
            need = {}
            for d in o.deps:
                sk, v = semkey(d), val(d)
                if need.get(sk, -1) < v:
                    need[sk] = v
                for sk2, v2 in d.clock.items():
                    if base.get(sk2, -1) < v2:
                        base[sk2] = v2
            o.waits = {sk: v for sk, v in need.items() if base.get(sk, 0) < v}
            clk = base
            for sk, v in need.items():
                if clk.get(sk, 0) < v:
                    clk[sk] = v
            o.clock = clk
            last_on_eng[o.eng] = o

        by_eng = {e: [] for e in ENGS}
        for o in self.ops:
            by_eng[o.eng].append(o)

        def emit_stream(eng_name):
            ops = by_eng[eng_name]

            def go(eng):
                for o in ops:
                    for sk in sorted(o.waits, key=str):
                        eng.wait_ge(sems[sk], o.waits[sk])
                    inst = o.fn(eng)
                    if o.kind == "cc":
                        inst.then_inc(sems[("cc", o.ccsem)])
                    elif o.marked:
                        if o.kind == "d":
                            inst.then_inc(sems[(o.eng, "d")], 16)
                        else:
                            inst.then_inc(sems[(o.eng, "c")], 1)
            return go

        for e in ENGS:
            if by_eng[e]:
                getattr(block, e)(emit_stream(e))


def ceil_div(a, b):
    return (a + b - 1) // b


def as_col(ap1d):
    return ap1d.rearrange("(a b) -> a b", b=1)


def as_row(ap1d):
    return ap1d.rearrange("(a b) -> a b", a=1)



class K:
    def __init__(self, cfg):
        self.cfg = cfg
        self.nc = bass.Bass("TRN2", target_bir_lowering=False, debug=False)
        self.sb = {}
        self.sbuf_bytes = 0

    def dtn(self, name, shape, dtype, kind="Internal"):
        if kind == "Internal":
            return self.nc.dram_tensor(name, list(shape), dtype)
        return self.nc.dram_tensor(name, list(shape), dtype, kind=kind)

    def st(self, name, shape, dtype):
        t = self.ctx.enter_context(self.nc.sbuf_tensor(name, list(shape), dtype))
        self.sb[name] = t
        self.sbuf_bytes += int(np.prod(shape[1:])) * mybir.dt.size(dtype)
        return t

    def pt(self, name, shape, dtype):
        t = self.ctx.enter_context(self.nc.psum_tensor(name, list(shape), dtype))
        self.sb[name] = t
        return t

    def mm(self, out, lhsT, rhs, start, stop, reads, writes, tag="mm", tp=None):
        self.p.op("tensor",
                  lambda e, o=out, l=lhsT, r=rhs, s1=start, s2=stop, t=tp:
                  e.matmul(o, l, r, start=s1, stop=s2, tile_position=t),
                  reads, writes, tag=tag)

    def trans(self, out, in_, reads, writes, tag="tr"):
        ident = self.sb["c_ident"][:]
        self.p.op("tensor",
                  lambda e, o=out, i=in_, d=ident: e.transpose(o, i, d),
                  tuple(reads) + keys("c_ident"), writes, tag=tag)

    def cp(self, eng, dst, src, reads, writes, tag="cp"):
        if eng == "scalar":
            self.p.op("scalar", lambda e, d=dst, s=src: e.copy(d, s),
                      reads, writes, tag=tag)
        else:
            self.p.op("vector", lambda e, d=dst, s=src: e.tensor_copy(d, s),
                      reads, writes, tag=tag)

    def act(self, dst, src, func, reads, writes, bias=None, scale=None,
            accum=None, tag="act"):
        kw = {}
        if bias is not None:
            kw["bias"] = bias
        if scale is not None:
            kw["scale"] = scale
        if accum is not None:
            kw["accum_out"] = accum
        self.p.op("scalar",
                  lambda e, d=dst, s=src, f=func, kw=kw: e.activation(d, s, f, **kw),
                  reads, writes, tag=tag)

    def tt(self, op_alu, dst, a, b, reads, writes, tag="tt", eng="vector"):
        self.p.op(eng, lambda e, d=dst, x=a, y=b, o=op_alu:
                  e.tensor_tensor(out=d, in0=x, in1=y, op=o), reads, writes, tag=tag)

    def ts(self, dst, a, s1, op0, reads, writes, s2=None, op1=None, tag="ts"):
        def f(e, d=dst, x=a, v1=s1, v2=s2, o0=op0, o1=op1):
            if o1 is None:
                return e.tensor_scalar(out=d, in0=x, scalar1=v1, scalar2=None, op0=o0)
            return e.tensor_scalar(out=d, in0=x, scalar1=v1, scalar2=v2,
                                   op0=o0, op1=o1)
        self.p.op("vector", f, reads, writes, tag=tag)

    def stt(self, dst, in0, scalar, in1, op0, op1, reads, writes, tag="stt"):
        self.p.op("vector", lambda e, d=dst, a=in0, s=scalar, b=in1, o0=op0, o1=op1:
                  e.scalar_tensor_tensor(out=d, in0=a, scalar=s, in1=b, op0=o0, op1=o1),
                  reads, writes, tag=tag)

    def scan_op(self, dst, d0, d1, init, reads, writes, tag="scan"):
        self.p.op("vector", lambda e, d=dst, a=d0, b=d1, i=init:
                  e.tensor_tensor_scan(out=d, data0=a, data1=b, initial=i,
                                       op0=ALU.mult, op1=ALU.add),
                  reads, writes, tag=tag)

    def red(self, dst, src, op_alu, reads, writes, tag="red"):
        self.p.op("vector", lambda e, d=dst, s=src, o=op_alu:
                  e.tensor_reduce(out=d, in_=s, axis=AX.X, op=o), reads, writes, tag=tag)

    def recip(self, dst, src, reads, writes, tag="recip"):
        self.p.op("vector", lambda e, d=dst, s=src: e.reciprocal(d, s),
                  reads, writes, tag=tag)

    def dma(self, out_ap, in_ap, reads, writes, eng="sync", tag="dma",
            transpose=False):
        if transpose:
            self.p.op(eng, lambda e, o=out_ap, i=in_ap:
                      e.dma_start(out=o, in_=i, transpose=True),
                      reads, writes, kind="d", tag=tag)
        else:
            self.p.op(eng, lambda e, o=out_ap, i=in_ap: e.dma_start(out=o, in_=i),
                      reads, writes, kind="d", tag=tag)

    def coll(self, kind, op_alu, in_ap, out_ap, reads, writes, tag="cc"):
        groups = [list(range(self.cfg.R))]
        self.p.op("gpsimd", lambda e, k=kind, o=op_alu, i=in_ap, u=out_ap, g=groups:
                  e.collective_compute(k, o, replica_groups=g, ins=[i], outs=[u]),
                  reads, writes, kind="cc", tag=tag)

    def memset(self, ap, v, reads, writes, tag="memset", eng="gpsimd"):
        self.p.op(eng, lambda e, a=ap, vv=v: e.memset(a, vv), reads, writes, tag=tag)


    def dma_wfm(self, dst_slice, src_ap, mcols, reads, writes, tag):
        self.dma(dst_slice.rearrange("p (c m) -> p c m", m=mcols),
                 src_ap.rearrange("(c p) m -> p c m", p=128),
                 reads=reads, writes=writes, tag=tag)

    def build(self):
        import contextlib
        cfg = self.cfg
        nc = self.nc
        self.ctx = contextlib.ExitStack()
        S, B, D, TL, T = cfg.S, cfg.B, cfg.D, cfg.TL, cfg.T

        ein = {}

        def EIN(name, shape, dtype=F32):
            ein[name] = self.dtn(name, shape, dtype, kind="ExternalInput").ap()

        EIN("x_in", [TL, D])
        for nm, shp in [("wqa", [D, 128]), ("wka", [D, 128]), ("wva", [D, 128]),
                        ("woa", [128, D]), ("wqf", [D, 128]), ("wkf", [D, 128]),
                        ("wvf", [D, 128]), ("wof", [128, D]), ("wff", [D, D]),
                        ("lin1_w", [D, cfg.DM]), ("lin1_b", [1, cfg.DM]),
                        ("lin2_w", [cfg.DM, D]), ("lin2_b", [1, D]),
                        ("norm_w", [cfg.NL, cfg.DM]), ("inw", [cfg.NL, cfg.DM, 256]),
                        ("convw", [cfg.NL, 128, cfg.CK]), ("convb", [cfg.NL, 128]),
                        ("xprojw", [cfg.NL, 128, 64]), ("dtw", [cfg.NL, cfg.DTR, 128]),
                        ("dtb", [cfg.NL, 128]), ("alog", [cfg.NL, 128, cfg.DS]),
                        ("dpv", [cfg.NL, 128]), ("outw", [cfg.NL, 128, cfg.DM])]:
            EIN(nm, shp)
        for nm, shp, dt_ in [("ident", [128, 128], BF16), ("ropeK", [128, S], BF16),
                             ("ropeQ", [128, S], BF16), ("headmask_pi", [128, 2], BF16),
                             ("headmask", [128, 2], BF16),
                             ("band", [128, 2 * cfg.TCH - 128], BF16),
                             ("ones_1x128_f32", [1, 128], F32),
                             ("ones4", [128, 128], F32),
                             ("ones_1x64_f32", [1, 64], F32),
                             ("ones_col_f32", [128, 1], F32),
                             ("ones_1x128_bf", [1, 128], BF16),
                             ("ones_col_bf", [128, 1], BF16),
                             ("cbias", [128, 4], F32),
                             ("ones_dram_bf", [S], BF16)]:
            EIN(nm, shp, dt_)
        self.ein = ein
        out_t = self.dtn("out", [TL, D], F32, kind="ExternalOutput").ap()

        wbf = {}
        for nm, shp in [("wqa", [D, 128]), ("wka", [D, 128]), ("wva", [D, 128]),
                        ("woa", [128, D]), ("wqf", [D, 128]), ("wkf", [D, 128]),
                        ("wvf", [D, 128]), ("wof", [128, D]), ("wff", [D, D]),
                        ("lin1_w", [D, cfg.DM]), ("lin2_w", [cfg.DM, D])]:
            wbf[nm] = self.dtn("wbf_" + nm, shp, BF16).ap()
        for l in range(cfg.NL):
            wbf[f"inw{l}"] = self.dtn(f"wbf_inw{l}", [cfg.DM, 256], BF16).ap()
            wbf[f"xprojw{l}"] = self.dtn(f"wbf_xprojw{l}", [128, 64], BF16).ap()
            wbf[f"dtw{l}"] = self.dtn(f"wbf_dtw{l}", [cfg.DTR, 128], BF16).ap()
            wbf[f"outw{l}"] = self.dtn(f"wbf_outw{l}", [128, cfg.DM], BF16).ap()
        self.wbf = wbf

        dd = {}
        dd["ag_xn_in"] = self.dtn("ag_xn_in", [D, TL], BF16).ap()
        dd["ag_xn_out"] = self.dtn("ag_xn_out", [cfg.R * D, TL], BF16).ap()
        dd["ag_hr_in"] = self.dtn("ag_hr_in", [cfg.DM, TL], BF16).ap()
        dd["ag_hr_out"] = self.dtn("ag_hr_out", [cfg.R * cfg.DM, TL], BF16).ap()
        dd["zpart"] = self.dtn("zpart", [T, D], F32).ap()
        dd["zred"] = self.dtn("zred", [TL, D], F32).ap()
        dd["zpartM"] = self.dtn("zpartM", [T, cfg.DM], F32).ap()
        dd["zredM"] = self.dtn("zredM", [TL, cfg.DM], F32).ap()
        dd["dblpart"] = self.dtn("dblpart", [64, T], F32).ap()
        dd["dblfull"] = self.dtn("dblfull", [64, T], F32).ap()
        dd["QA"] = self.dtn("QA", [4, 65, S], BF16).ap()
        dd["KA"] = self.dtn("KA", [4, 65, S], BF16).ap()
        dd["VA"] = self.dtn("VA", [4, S, 64], BF16).ap()
        dd["norms_a"] = self.dtn("norms_a", [4, 2, S], F32).ap()
        dd["QF"] = self.dtn("QF", [B, 128, S], BF16).ap()
        dd["KF"] = self.dtn("KF", [B, 128, S], BF16).ap()
        dd["VF"] = self.dtn("VF", [B, S, 128], BF16).ap()
        dd["norms_f"] = self.dtn("norms_f", [B, 2, 2, S], F32).ap()
        dd["negMk"] = self.dtn("negMk", [B, 2, S], BF16).ap()
        dd["CONTR"] = self.dtn("CONTR", [B, 128, S], BF16).ap()
        dd["ZSP"] = self.dtn("ZSP", [128, T], BF16).ap()
        dd["USP"] = self.dtn("USP", [128, T], BF16).ap()
        dd["x_in"] = ein["x_in"]
        dd["xcur"] = self.dtn("xcur", [TL, D], F32).ap()
        dd["out"] = out_t
        self.d = dd

        AW = 16448
        for i in range(4):
            self.st(f"ar{i}", [128, AW], BF16)
        self.st("E", [128, 8192], F32)
        self.st("F", [128, 4096], F32)
        self.st("G", [128, 2304], F32)
        self.st("W0", [128, 4096], BF16)
        self.st("W1", [128, 2048], BF16)
        self.st("c_ident", [128, 128], BF16)
        self.st("c_hmpi", [128, 2], BF16)
        self.st("c_hm", [128, 2], BF16)
        self.st("c_band", [128, 2 * cfg.TCH - 128], BF16)
        self.st("c_o1x128f", [1, 128], F32)
        self.st("c_o4", [128, 128], F32)
        self.st("c_o1x64f", [1, 64], F32)
        self.st("c_ocolf", [128, 1], F32)
        self.st("c_o1x128b", [1, 128], BF16)
        self.st("c_ocolb", [128, 1], BF16)
        self.st("c_cb", [128, 4], F32)
        self.pt("psA", [128, 2048], F32)
        self.pt("psB", [128, 1024], F32)
        self.pt("psC", [128, 512], F32)
        self.pt("psT", [128, 1024], BF16)

        self.p = Prog()
        ph = cfg.phases
        if "p0" in ph:
            self.phase_p0()
        if "attn_a" in ph:
            self.phase_ln_ag(extra_rms=True, tag="ln1", do_ag=True, fm_ar="ar0")
            self.phase_attn_a_proj()
            self.phase_attn_a_attn()
            self.phase_out_rs(("W0", 3072), "woa")
            self.phase_residual("woa")
        for p_ in ph:
            if p_.startswith("ffn1"):
                stop = p_.split(":")[1] if ":" in p_ else None
                self.phase_ffn(0, stop=stop)
        if "mamba" in ph:
            self.phase_mamba()
        if "ffn2" in ph:
            self.phase_ffn(1)
        if "out" in ph:
            self.phase_out()

        sems = {}
        for e in ENGS:
            for knd in ("c", "d"):
                sems[(e, knd)] = self.ctx.enter_context(nc.semaphore(f"s_{e}_{knd}"))
        for i in range(self.p.n_cc):
            sems[("cc", i)] = self.ctx.enter_context(nc.semaphore(f"s_cc_{i}"))
        with nc.Block() as block:
            self.p.finalize_and_emit(block, sems)
        self.ctx.close()
        return nc

    def phase_p0(self):
        cfg, sb, ein = self.cfg, self.sb, self.ein
        for nm, dst in [("ident", "c_ident"), ("headmask_pi", "c_hmpi"),
                        ("headmask", "c_hm"), ("band", "c_band"),
                        ("ones_1x128_f32", "c_o1x128f"), ("ones4", "c_o4"),
                        ("ones_1x64_f32", "c_o1x64f"),
                        ("ones_col_f32", "c_ocolf"), ("ones_1x128_bf", "c_o1x128b"),
                        ("ones_col_bf", "c_ocolb"), ("cbias", "c_cb")]:
            self.dma(sb[dst][:], ein[nm], reads=(), writes=keys(dst), tag="cload")
        self.dma(self.d["xcur"], self.d["x_in"], reads=(), writes=keys("xcur"),
                 tag="xload")
        F, G, ar0 = sb["F"], sb["G"], sb["ar0"]

        def cast(ap_in, ap_out, rows, cols, okey, scale_rows=None):
            for i in range(ceil_div(rows, 128)):
                r0, rn = i * 128, min(128, rows - i * 128)
                for j in range(ceil_div(cols, 2048)):
                    c0, cn = j * 2048, min(2048, cols - j * 2048)
                    pp = (i + j) % 2
                    kf = keys("F", pp * 2048, pp * 2048 + cn)
                    ka = keys("ar0", pp * 2048, pp * 2048 + cn)
                    self.dma(F[0:rn, pp * 2048:pp * 2048 + cn],
                             ap_in[r0:r0 + rn, c0:c0 + cn],
                             reads=(), writes=kf, tag="wc_in")
                    if scale_rows is not None:
                        ks = keys("G", 2176 + pp, 2177 + pp)
                        self.dma(G[0:rn, 2176 + pp:2177 + pp],
                                 as_col(scale_rows[r0:r0 + rn]),
                                 reads=(), writes=ks, tag="wc_sc")
                        self.ts(F[0:rn, pp * 2048:pp * 2048 + cn],
                                F[0:rn, pp * 2048:pp * 2048 + cn],
                                G[0:rn, 2176 + pp:2177 + pp], ALU.mult,
                                reads=kf + ks, writes=kf, tag="wc_scale")
                    self.cp("vector", ar0[0:rn, pp * 2048:pp * 2048 + cn],
                            F[0:rn, pp * 2048:pp * 2048 + cn],
                            reads=kf, writes=ka, tag="wc_cast")
                    self.dma(ap_out[r0:r0 + rn, c0:c0 + cn],
                             ar0[0:rn, pp * 2048:pp * 2048 + cn],
                             reads=ka, writes=((okey, 0),), tag="wc_out")

        wb = self.wbf
        for nm, rows, cols in [("wqa", cfg.D, 128), ("wka", cfg.D, 128),
                               ("wva", cfg.D, 128), ("woa", 128, cfg.D),
                               ("wqf", cfg.D, 128), ("wkf", cfg.D, 128),
                               ("wvf", cfg.D, 128), ("wof", 128, cfg.D),
                               ("wff", cfg.D, cfg.D), ("lin1_w", cfg.D, cfg.DM),
                               ("lin2_w", cfg.DM, cfg.D)]:
            cast(ein[nm], wb[nm], rows, cols, "wbf_" + nm)
        for l in range(cfg.NL):
            cast(ein["inw"][l], wb[f"inw{l}"], cfg.DM, 256, f"wbf_inw{l}",
                 scale_rows=ein["norm_w"][l, :])
            cast(ein["xprojw"][l], wb[f"xprojw{l}"], 128, 64, f"wbf_xprojw{l}")
            cast(ein["dtw"][l], wb[f"dtw{l}"], cfg.DTR, 128, f"wbf_dtw{l}")
            cast(ein["outw"][l], wb[f"outw{l}"], 128, cfg.DM, f"wbf_outw{l}")

    def phase_ln_ag(self, extra_rms, tag, do_ag, fm_ar):
        cfg, sb = self.cfg, self.sb
        D, NT, TL = cfg.D, cfg.NT_LOC, cfg.TL
        F, G, W1, E = sb["F"], sb["G"], sb["W1"], sb["E"]
        arF = sb[fm_ar]
        psT = sb["psT"]
        nch = D // 128
        for i in range(NT):
            pp = i % 2
            xo = pp * 2048
            gs = pp * 16
            kx = keys("F", xo, xo + 1024)
            kxc = keys("F", xo + 1024, xo + 2048)
            kst = keys("G", gs, gs + 16)
            ke = keys("E", pp * 1024, pp * 1024 + 1024)
            self.dma(F[:, xo:xo + 1024], self.d["xcur"][128 * i:128 * (i + 1), :],
                     reads=keys("xcur"), writes=kx, tag=tag + "_in")
            self.red(G[:, gs:gs + 1], F[:, xo:xo + 1024], ALU.add,
                     reads=kx, writes=kst, tag=tag + "_sum")
            self.ts(G[:, gs + 1:gs + 2], G[:, gs:gs + 1], 1.0 / D, ALU.mult,
                    reads=kst, writes=kst, tag=tag + "_mean")
            self.ts(F[:, xo + 1024:xo + 2048], F[:, xo:xo + 1024],
                    G[:, gs + 1:gs + 2], ALU.subtract,
                    reads=kx + kst, writes=kxc, tag=tag + "_sub")
            self.act(E[:, pp * 1024:pp * 1024 + 1024], F[:, xo + 1024:xo + 2048],
                     AF.Square, reads=kxc, writes=ke, tag=tag + "_sq")
            self.red(G[:, gs + 2:gs + 3], E[:, pp * 1024:pp * 1024 + 1024],
                     ALU.add, reads=ke, writes=kst, tag=tag + "_var")
            self.act(G[:, gs + 3:gs + 4], G[:, gs + 2:gs + 3], AF.Sqrt,
                     reads=kst + keys("c_cb"), writes=kst, scale=1.0 / D,
                     bias=self.sb["c_cb"][:, 0:1], tag=tag + "_std")
            self.recip(G[:, gs + 4:gs + 5], G[:, gs + 3:gs + 4],
                       reads=kst, writes=kst, tag=tag + "_rstd")
            self.ts(F[:, xo:xo + 1024], F[:, xo + 1024:xo + 2048],
                    G[:, gs + 4:gs + 5], ALU.mult,
                    reads=kxc + kst, writes=kx, tag=tag + "_norm")
            if extra_rms:
                self.act(E[:, pp * 1024:pp * 1024 + 1024], F[:, xo:xo + 1024],
                         AF.Square, reads=kx, writes=ke, tag=tag + "_sq2")
                self.red(G[:, gs + 5:gs + 6], E[:, pp * 1024:pp * 1024 + 1024],
                         ALU.add, reads=ke, writes=kst, tag=tag + "_v2")
                self.act(G[:, gs + 6:gs + 7], G[:, gs + 5:gs + 6], AF.Sqrt,
                         reads=kst + keys("c_cb"), writes=kst, scale=1.0 / D,
                         bias=self.sb["c_cb"][:, 1:2], tag=tag + "_std2")
                self.recip(G[:, gs + 7:gs + 8], G[:, gs + 6:gs + 7],
                           reads=kst, writes=kst, tag=tag + "_rstd2")
                self.ts(F[:, xo:xo + 1024], F[:, xo:xo + 1024],
                        G[:, gs + 7:gs + 8], ALU.mult,
                        reads=kx + kst, writes=kx, tag=tag + "_norm2")
            kbf = keys("W1", pp * 1024, pp * 1024 + 1024)
            self.cp("vector", W1[:, pp * 1024:pp * 1024 + 1024], F[:, xo:xo + 1024],
                    reads=kx, writes=kbf, tag=tag + "_cast")
            for j in range(nch):
                kT = keys("psT", 128 * (j % 8), 128 * (j % 8) + 128)
                self.trans(psT[:, 128 * (j % 8):128 * (j % 8) + 128],
                           W1[:, pp * 1024 + 128 * j:pp * 1024 + 128 * (j + 1)],
                           reads=kbf, writes=kT, tag=tag + "_T")
                self.cp("vector",
                        arF[:, TL * j + 128 * i:TL * j + 128 * (i + 1)],
                        psT[:, 128 * (j % 8):128 * (j % 8) + 128],
                        reads=kT, writes=keys(fm_ar, TL * j + 128 * i,
                                              TL * j + 128 * (i + 1)),
                        tag=tag + "_Tev")
        if do_ag:
            for j in range(nch):
                self.dma(self.d["ag_xn_in"][128 * j:128 * (j + 1), :],
                         arF[:, TL * j:TL * j + TL],
                         reads=keys(fm_ar, TL * j, TL * j + TL),
                         writes=(("ag_xn_in", j),), tag=tag + "_agin")
            self.coll("AllGather", ALU.bypass, self.d["ag_xn_in"],
                      self.d["ag_xn_out"],
                      reads=tuple(("ag_xn_in", j) for j in range(nch)),
                      writes=(("ag_xn_out", 0),), tag=tag + "_AG")

    def phase_attn_a_proj(self):
        cfg, sb, d = self.cfg, self.sb, self.d
        S, D = cfg.S, cfg.D
        nch = D // 128
        W0, W1, G, F = sb["W0"], sb["W1"], sb["G"], sb["F"]
        psA, psB, psT = sb["psA"], sb["psB"], sb["psT"]
        CL = cfg.TL
        half = CL // 2
        for nm, off in [("wqa", 0), ("wka", 1024), ("wva", 2048), ("woa", 3072)]:
            if nm == "woa":
                self.dma(W0[:, off:off + 1024], self.wbf[nm],
                         reads=(("wbf_" + nm, 0),),
                         writes=keys("W0", off, off + 1024), tag="aw_load")
            else:
                self.dma_wfm(W0[:, off:off + 1024], self.wbf[nm], 128,
                             reads=(("wbf_" + nm, 0),),
                             writes=keys("W0", off, off + 1024), tag="aw_load")
        for u in range(4):
            self.dma(d["KA"][u, 64, :], self.ein["ones_dram_bf"],
                     reads=(), writes=(("KA", u),), tag="ka_ones")

        for r in range(cfg.R):
            pp = r % 2
            arn = f"ar{pp}"
            arin = sb[arn]
            kin = keys(arn, 0, nch * CL)
            self.dma(arin[:, 0:nch * CL].rearrange("p (c m) -> p c m", m=CL),
                     d["ag_xn_out"][128 * nch * r:128 * nch * (r + 1), :]
                     .rearrange("(c p) m -> p c m", p=128),
                     reads=(("ag_xn_out", 0),), writes=kin, tag="aproj_in")
            kRT = keys("ar3", 0, 2048)
            self.dma(sb["ar3"][:, 0:1024],
                     self.ein["ropeQ"][:, cfg.pos * r:cfg.pos * r + cfg.pos],
                     reads=(), writes=kRT, tag="rtq")
            self.dma(sb["ar3"][:, 1024:2048],
                     self.ein["ropeK"][:, cfg.pos * r:cfg.pos * r + cfg.pos],
                     reads=(), writes=kRT, tag="rtk")

            def project(woff):
                kw = keys("W0", woff, woff + 1024)
                kps = keys("psA", 0, CL)
                for j in range(nch):
                    for c_ in range(CL // 512):
                        self.mm(psA[:, 512 * c_:512 * (c_ + 1)],
                                W0[:, woff + 128 * j:woff + 128 * (j + 1)],
                                arin[:, CL * j + 512 * c_:CL * j + 512 * (c_ + 1)],
                                start=(j == 0), stop=(j == nch - 1),
                                reads=kin + kw, writes=kps, tag="aproj_mm")
                return kps

            def rope(kps, toff):
                tb = sb["ar3"]
                kd = keys("W1", 0, CL)
                for b in range(2):
                    cs = b * half
                    kt = keys("F", 0, 2 * half)
                    self.tt(ALU.mult, F[0:64, 0:half], psA[0:64, cs:cs + half],
                            tb[0:64, toff:toff + half], reads=kps + kRT,
                            writes=kt, tag="rope1")
                    self.tt(ALU.mult, F[0:64, half:2 * half],
                            psA[64:128, cs:cs + half],
                            tb[64:128, toff:toff + half], reads=kps + kRT,
                            writes=kt, tag="rope2")
                    self.tt(ALU.subtract, W1[0:64, cs:cs + half], F[0:64, 0:half],
                            F[0:64, half:2 * half], reads=kt, writes=kd,
                            tag="rope3")
                    self.tt(ALU.mult, F[0:64, 0:half], psA[0:64, cs:cs + half],
                            tb[64:128, toff:toff + half], reads=kps + kRT,
                            writes=kt, tag="rope4")
                    self.tt(ALU.mult, F[0:64, half:2 * half],
                            psA[64:128, cs:cs + half],
                            tb[0:64, toff:toff + half], reads=kps + kRT,
                            writes=kt, tag="rope5")
                    self.tt(ALU.add, W1[64:128, cs:cs + half], F[0:64, 0:half],
                            F[0:64, half:2 * half], reads=kt, writes=kd,
                            tag="rope6")
                return kd

            def norms_store(kd, qk, dst_dram, dst_name):
                ksq = keys("ar2", 0, CL)
                self.act(sb["ar2"][:, 0:CL], W1[:, 0:CL], AF.Square,
                         reads=kd, writes=ksq, tag="an_sq")
                kn = keys("psA", 0, CL)
                for c_ in range(CL // 512):
                    self.mm(psA[0:2, 512 * c_:512 * (c_ + 1)], sb["c_hmpi"][:],
                            sb["ar2"][:, 512 * c_:512 * (c_ + 1)],
                            start=True, stop=True, reads=ksq + keys("c_hmpi"),
                            writes=kn, tag="an_mm")
                kev = keys("F", 2048, 2048 + CL)
                self.cp("vector", F[0:2, 2048:2048 + CL], psA[0:2, 0:CL],
                        reads=kn, writes=kev, tag="an_ev")
                for hp in range(2):
                    for b in range(2):
                        uu = b * 2 + hp
                        self.dma(d["norms_a"][uu, qk, cfg.pos * r:cfg.pos * r + half],
                                 F[hp:hp + 1, 2048 + b * half:2048 + (b + 1) * half],
                                 reads=kev, writes=(("norms_a", uu, qk),),
                                 tag="an_out")
                        for piece, r0 in ((0, 32 * hp), (1, 64 + 32 * hp)):
                            self.dma(dst_dram[uu, 32 * piece:32 * piece + 32,
                                              cfg.pos * r: cfg.pos * r + half],
                                     W1[r0:r0 + 32, b * half:(b + 1) * half],
                                     reads=kd, writes=((dst_name, uu),),
                                     tag="aqk_out")

            kq = project(0)
            kdq = rope(kq, 0)
            norms_store(kdq, 0, d["QA"], "QA")
            kk = project(1024)
            kdk = rope(kk, 1024)
            norms_store(kdk, 1, d["KA"], "KA")
            kv = project(2048)
            kvs = keys("ar2", 4096, 4096 + CL)
            self.cp("scalar", sb["ar2"][:, 4096:4096 + CL], psA[:, 0:CL],
                    reads=kv, writes=kvs, tag="av_ev")
            ntile = CL // 128
            for g in range(ceil_div(ntile, 8)):
                kT = keys("psT", 0, 1024)
                for tl in range(min(8, ntile - 8 * g)):
                    i = 8 * g + tl
                    self.trans(psT[:, 128 * tl:128 * (tl + 1)],
                               sb["ar2"][:, 4096 + 128 * i:4096 + 128 * (i + 1)],
                               reads=kvs, writes=kT, tag="av_T")
                eb = 8192 + 1024 * (g % 2)
                kTe = keys("ar2", eb, eb + 1024)
                self.cp("vector", sb["ar2"][:, eb:eb + 1024], psT[:, 0:1024],
                        reads=kT, writes=kTe, tag="av_Tev")
                for tl in range(min(8, ntile - 8 * g)):
                    i = 8 * g + tl
                    b = i // (ntile // 2)
                    row0 = cfg.pos * r + 128 * (i % (ntile // 2))
                    for hp in range(2):
                        uu = b * 2 + hp
                        self.dma(d["VA"][uu, row0:row0 + 128, :],
                                 sb["ar2"][:, eb + 128 * tl + 64 * hp:
                                           eb + 128 * tl + 64 * (hp + 1)],
                                 reads=kTe, writes=(("VA", uu),), tag="av_out")
        for u in range(4):
            ku = keys("G", 2048, 2080)
            ng = max(1, S // 2048)
            w = min(2048, S)
            for gg in range(ng):
                kg = keys("G", 0, w)
                self.dma(G[0:1, 0:w], d["norms_a"][u, 1, w * gg:w * (gg + 1)],
                         reads=(("norms_a", u, 1),), writes=kg, tag="mk_in")
                self.red(G[0:1, 2048 + gg:2049 + gg], G[0:1, 0:w], ALU.max,
                         reads=kg, writes=ku, tag="mk_red")
            self.red(G[0:1, 2080:2081], G[0:1, 2048:2048 + ng], ALU.max,
                     reads=ku, writes=keys("G", 2080, 2081), tag="mk_red2")
            self.act(G[0:1, 2081:2082], G[0:1, 2080:2081], AF.Sqrt,
                     reads=keys("G", 2080, 2081), writes=keys("G", 2081, 2082),
                     tag="mk_sqrt")
            for r in range(S // 1024):
                km = keys("G", 0, 1024)
                km2 = keys("G", 1024, 2048)
                self.dma(G[0:1, 0:1024],
                         d["norms_a"][u, 0, 1024 * r:1024 * (r + 1)],
                         reads=(("norms_a", u, 0),), writes=km, tag="mr_in")
                self.act(G[0:1, 1024:2048], G[0:1, 0:1024], AF.Sqrt,
                         reads=km, writes=km2, tag="mr_sqrt")
                self.ts(G[0:1, 0:1024], G[0:1, 1024:2048],
                        G[0:1, 2081:2082], ALU.mult,
                        reads=km2 + keys("G", 2081, 2082), writes=km,
                        s2=-1.0, op1=ALU.mult, tag="mr_mul")
                self.cp("vector", W1[0:1, 0:1024], G[0:1, 0:1024],
                        reads=km, writes=keys("W1", 0, 1024), tag="mr_cast")
                self.dma(d["QA"][u, 64, 1024 * r:1024 * (r + 1)],
                         W1[0:1, 0:1024], reads=keys("W1", 0, 1024),
                         writes=(("QA", u),), tag="mr_out")

    def phase_attn_a_attn(self):
        cfg, sb, d = self.cfg, self.sb, self.d
        S, TCH = cfg.S, cfg.TCH
        ar0, ar1, ar2 = sb["ar0"], sb["ar1"], sb["ar2"]
        psA, psB, G, E = sb["psA"], sb["psB"], sb["G"], sb["E"]
        nstile = S // 128
        nchunk = S // TCH
        spt = TCH // 128
        VOFF, POFF, COFF = 0, 4160, 10240
        for u in range(4):
            b, hp = u // 2, u % 2
            kq = keys("ar0", 0, S)
            kk = keys("ar1", 0, S)
            kv = keys("ar2", VOFF, VOFF + 65 * nstile)
            self.dma(ar0[0:65, 0:S], d["QA"][u], reads=(("QA", u),),
                     writes=kq, tag="aa_lq")
            self.dma(ar1[0:65, 0:S], d["KA"][u], reads=(("KA", u),),
                     writes=kk, tag="aa_lk")
            self.dma(ar2[:, VOFF:VOFF + 65 * nstile].rearrange(
                         "p (i c) -> p i c", c=65)[:, :, 0:64],
                     d["VA"][u].rearrange("(i p) c -> p i c", p=128),
                     reads=(("VA", u),), writes=kv, tag="aa_lv")
            self.memset(ar2[:, VOFF:VOFF + 65 * nstile].rearrange(
                            "p (i c) -> p i c", c=65)[:, :, 64:65], 1.0,
                        reads=(), writes=kv, tag="aa_vone", eng="vector")
            psC = sb["psC"]
            psTf = sb["psT"][:].bitcast(F32)
            for r in range(nchunk):
                nj = spt * (r + 1)
                kout = keys("psB", 0, TCH)

                def aa_pv(j):
                    po = 1024 * (j % 3)
                    kP = keys("ar2", POFF + po, POFF + po + 1024)
                    for c_ in range(TCH // 512):
                        self.mm(psB[0:65, 512 * c_:512 * (c_ + 1)],
                                ar2[:, VOFF + 65 * j:VOFF + 65 * (j + 1)],
                                ar2[:, POFF + po + 512 * c_:
                                    POFF + po + 512 * (c_ + 1)],
                                start=(j == 0), stop=(j == nj - 1),
                                reads=kP + kv, writes=kout, tag="aa_pv")

                for j in range(nj):
                    po = 1024 * (j % 3)
                    r3 = j % 3
                    if r3 < 2:
                        halves = [(psA[:, 1024 * r3:1024 * r3 + 512],
                                   keys("psA", 1024 * r3, 1024 * r3 + 512)),
                                  (psA[:, 1024 * r3 + 512:1024 * (r3 + 1)],
                                   keys("psA", 1024 * r3 + 512,
                                        1024 * (r3 + 1)))]
                    else:
                        halves = [(psC[:, 0:512], keys("psC", 0, 512)),
                                  (psTf[:, 0:512], keys("psT", 0, 1024))]
                    ksc = halves[0][1] + halves[1][1]
                    for c_ in range(TCH // 512):
                        self.mm(halves[c_][0],
                                ar1[0:65, 128 * j:128 * (j + 1)],
                                ar0[0:65, TCH * r + 512 * c_:
                                    TCH * r + 512 * (c_ + 1)],
                                start=True, stop=True,
                                reads=kq + kk, writes=halves[c_][1],
                                tag="aa_mm1")
                    o = j - spt * r
                    if r3 < 2:
                        if o >= 0:
                            stc = (TCH - 128) - 128 * o
                            self.tt(ALU.add, psA[:, 1024 * r3:1024 * (r3 + 1)],
                                    psA[:, 1024 * r3:1024 * (r3 + 1)],
                                    sb["c_band"][:, stc:stc + TCH],
                                    reads=ksc + keys("c_band"),
                                    writes=ksc, tag="aa_band")
                        kP = keys("ar2", POFF + po, POFF + po + 1024)
                        self.act(ar2[:, POFF + po:POFF + po + 1024],
                                 psA[:, 1024 * r3:1024 * (r3 + 1)], AF.Exp,
                                 reads=ksc, writes=kP, tag="aa_exp")
                    else:
                        for c_ in range(TCH // 512):
                            if o >= 0:
                                stc = (TCH - 128) - 128 * o + 512 * c_
                                self.tt(ALU.add, halves[c_][0], halves[c_][0],
                                        sb["c_band"][:, stc:stc + 512],
                                        reads=halves[c_][1] + keys("c_band"),
                                        writes=halves[c_][1], tag="aa_band")
                            kP = keys("ar2", POFF + po + 512 * c_,
                                      POFF + po + 512 * (c_ + 1))
                            self.act(ar2[:, POFF + po + 512 * c_:
                                         POFF + po + 512 * (c_ + 1)],
                                     halves[c_][0], AF.Exp,
                                     reads=halves[c_][1], writes=kP,
                                     tag="aa_exp")
                    if j >= 1:
                        aa_pv(j - 1)
                if nj >= 1:
                    aa_pv(nj - 1)
                kev = keys("E", 0, TCH)
                self.cp("scalar", E[0:64, 0:TCH], psB[0:64, 0:TCH],
                        reads=kout, writes=kev, tag="aa_ev")
                kz = keys("G", 0, TCH)
                self.recip(G[0:1, 0:TCH], psB[64:65, 0:TCH],
                           reads=kout, writes=kz, tag="aa_recz")
                for c_ in range(TCH // 512):
                    self.mm(psB[0:64, 512 * c_:512 * (c_ + 1)],
                            sb["c_o1x64f"][:], G[0:1, 512 * c_:512 * (c_ + 1)],
                            start=True, stop=True,
                            reads=kz + keys("c_o1x64f"), writes=kout,
                            tag="aa_zrep")
                kcs = keys("ar2", COFF, COFF + TCH)
                self.tt(ALU.mult, ar2[0:64, COFF:COFF + TCH],
                        E[0:64, 0:TCH], psB[0:64, 0:TCH],
                        reads=kev + kout, writes=kcs, tag="aa_nrm")
                self.dma(d["CONTR"][b, 64 * hp:64 * hp + 64, TCH * r:TCH * (r + 1)],
                         ar2[0:64, COFF:COFF + TCH],
                         reads=kcs, writes=(("CONTR", b, hp),), tag="aa_co")

    def phase_out_rs(self, wspec, tag, zp="zpart", zr="zred", ncols=1024):
        cfg, sb, d = self.cfg, self.sb, self.d
        S = cfg.S
        wname, woff = wspec
        wa = sb[wname]
        psB, F = sb["psB"], sb["F"]
        kw = keys(wname, woff, woff + ncols)
        for b in range(cfg.B):
            for strip in range(S // 1024):
                sp = strip % 2
                kst = keys("W1", 1024 * sp, 1024 * (sp + 1))
                self.dma(sb["W1"][:, 1024 * sp:1024 * (sp + 1)],
                         d["CONTR"][b][:, 1024 * strip:1024 * (strip + 1)],
                         reads=(("CONTR", b, 0), ("CONTR", b, 1)),
                         writes=kst, tag=tag + "_ci")
                for ti in range(8):
                    i = 8 * strip + ti
                    kz = keys("psB", 0, ncols)
                    for c_ in range(ncols // 512):
                        self.mm(psB[:, 512 * c_:512 * (c_ + 1)],
                                sb["W1"][:, 1024 * sp + 128 * ti:
                                         1024 * sp + 128 * (ti + 1)],
                                wa[:, woff + 512 * c_:woff + 512 * (c_ + 1)],
                                start=True, stop=True, reads=kst + kw,
                                writes=kz, tag=tag + "_mm")
                    pp = i % 2
                    kf = keys("F", 2048 * pp, 2048 * pp + ncols)
                    self.cp("scalar" if i % 2 == 0 else "vector",
                            F[:, 2048 * pp:2048 * pp + ncols], psB[:, 0:ncols],
                            reads=kz, writes=kf, tag=tag + "_ev")
                    r, pi = (128 * i) // cfg.pos, (128 * i) % cfg.pos
                    row0 = r * cfg.TL + b * cfg.pos + pi
                    self.dma(d[zp][row0:row0 + 128, 0:ncols],
                             F[:, 2048 * pp:2048 * pp + ncols],
                             reads=kf, writes=((zp, row0 // 128),),
                             tag=tag + "_zo")
        self.coll("ReduceScatter", ALU.add, d[zp], d[zr],
                  reads=tuple((zp, i) for i in range(cfg.T // 128)),
                  writes=((zr, 0),), tag=tag + "_RS")

    def phase_residual(self, tag, src="zred"):
        cfg, sb, d = self.cfg, self.sb, self.d
        F = sb["F"]
        for i in range(cfg.NT_LOC):
            pp = i % 2
            ka = keys("F", 2048 * pp, 2048 * pp + 1024)
            kb = keys("F", 2048 * pp + 1024, 2048 * pp + 2048)
            self.dma(F[:, 2048 * pp:2048 * pp + 1024],
                     d[src][128 * i:128 * (i + 1), 0:1024],
                     reads=((src, 0),), writes=ka, tag=tag + "_rin")
            self.dma(F[:, 2048 * pp + 1024:2048 * pp + 2048],
                     d["xcur"][128 * i:128 * (i + 1), :],
                     reads=keys("xcur"), writes=kb, tag=tag + "_xin")
            self.tt(ALU.add, F[:, 2048 * pp:2048 * pp + 1024],
                    F[:, 2048 * pp:2048 * pp + 1024],
                    F[:, 2048 * pp + 1024:2048 * pp + 2048],
                    reads=ka + kb, writes=ka, tag=tag + "_add")
            self.dma(d["xcur"][128 * i:128 * (i + 1), :],
                     F[:, 2048 * pp:2048 * pp + 1024],
                     reads=ka, writes=keys("xcur"), tag=tag + "_xout")

    def phase_ffn(self, inst, stop=None):
        cfg, sb, d = self.cfg, self.sb, self.d
        S, D = cfg.S, cfg.D
        tg = f"f{inst}"
        nch = D // 128
        CL = cfg.TL
        half = CL // 2
        W0, W1, G, F, E = sb["W0"], sb["W1"], sb["G"], sb["F"], sb["E"]
        psA, psB, psT = sb["psA"], sb["psB"], sb["psT"]
        self.phase_ln_ag(extra_rms=False, tag=tg + "ln", do_ag=True, fm_ar="ar0")
        for nm, off in [("wqf", 0), ("wkf", 1024), ("wvf", 2048), ("wof", 3072)]:
            if nm == "wof":
                self.dma(W0[:, off:off + 1024], self.wbf[nm],
                         reads=(("wbf_" + nm, 0),),
                         writes=keys("W0", off, off + 1024), tag=tg + "w")
            else:
                self.dma_wfm(W0[:, off:off + 1024], self.wbf[nm], 128,
                             reads=(("wbf_" + nm, 0),),
                             writes=keys("W0", off, off + 1024), tag=tg + "w")
        for r in range(cfg.R):
            pp = r % 2
            arn = f"ar{pp}"
            arin = sb[arn]
            kin = keys(arn, 0, nch * CL)
            self.dma(arin[:, 0:nch * CL].rearrange("p (c m) -> p c m", m=CL),
                     d["ag_xn_out"][128 * nch * r:128 * nch * (r + 1), :]
                     .rearrange("(c p) m -> p c m", p=128),
                     reads=(("ag_xn_out", 0),), writes=kin, tag=tg + "pin")

            def project(woff):
                kw = keys("W0", woff, woff + 1024)
                kps = keys("psA", 0, CL)
                for j in range(nch):
                    for c_ in range(CL // 512):
                        self.mm(psA[:, 512 * c_:512 * (c_ + 1)],
                                W0[:, woff + 128 * j:woff + 128 * (j + 1)],
                                arin[:, CL * j + 512 * c_:CL * j + 512 * (c_ + 1)],
                                start=(j == 0), stop=(j == nch - 1),
                                reads=kin + kw, writes=kps, tag=tg + "pmm")
                return kps

            def qk_store(kev, qk, dstn):
                for b in range(2):
                    self.dma(d[dstn][b][:, cfg.pos * r:cfg.pos * r + half],
                             W1[:, b * half:(b + 1) * half],
                             reads=kev, writes=((dstn, b),), tag=tg + "qko")
                ksq = keys("ar2", 0, CL)
                self.act(sb["ar2"][:, 0:CL], W1[:, 0:CL], AF.Square,
                         reads=kev, writes=ksq, tag=tg + "nsq")
                kn = keys("psA", 0, CL)
                for c_ in range(CL // 512):
                    self.mm(psA[0:2, 512 * c_:512 * (c_ + 1)], sb["c_hm"][:],
                            sb["ar2"][:, 512 * c_:512 * (c_ + 1)],
                            start=True, stop=True, reads=ksq + keys("c_hm"),
                            writes=kn, tag=tg + "nmm")
                kev2 = keys("F", 2048, 2048 + CL)
                self.cp("vector", F[0:2, 2048:2048 + CL], psA[0:2, 0:CL],
                        reads=kn, writes=kev2, tag=tg + "nev")
                for hp in range(2):
                    for b in range(2):
                        self.dma(d["norms_f"][b, qk, hp,
                                              cfg.pos * r:cfg.pos * r + half],
                                 F[hp:hp + 1, 2048 + b * half:2048 + (b + 1) * half],
                                 reads=kev2, writes=(("norms_f", qk),),
                                 tag=tg + "no")

            kq = project(0)
            kevq = keys("W1", 0, CL)
            self.act(W1[:, 0:CL], psA[:, 0:CL], AF.Copy, scale=1.0 / 32.0,
                     reads=kq, writes=kevq, tag=tg + "qev")
            qk_store(kevq, 0, "QF")
            kk = project(1024)
            kevk = keys("W1", 0, CL)
            self.cp("scalar", W1[:, 0:CL], psA[:, 0:CL], reads=kk,
                    writes=kevk, tag=tg + "kev")
            qk_store(kevk, 1, "KF")
            kv = project(2048)
            kvs = keys("ar2", 4096, 4096 + CL)
            self.cp("scalar", sb["ar2"][:, 4096:4096 + CL], psA[:, 0:CL],
                    reads=kv, writes=kvs, tag=tg + "vev")
            ntile = CL // 128
            for g in range(ceil_div(ntile, 8)):
                kT = keys("psT", 0, 1024)
                for tl in range(min(8, ntile - 8 * g)):
                    i = 8 * g + tl
                    self.trans(psT[:, 128 * tl:128 * (tl + 1)],
                               sb["ar2"][:, 4096 + 128 * i:4096 + 128 * (i + 1)],
                               reads=kvs, writes=kT, tag=tg + "vT")
                eb = 8192 + 1024 * (g % 2)
                kTe = keys("ar2", eb, eb + 1024)
                self.cp("vector", sb["ar2"][:, eb:eb + 1024], psT[:, 0:1024],
                        reads=kT, writes=kTe, tag=tg + "vTe")
                for tl in range(min(8, ntile - 8 * g)):
                    i = 8 * g + tl
                    b = i // (ntile // 2)
                    row0 = cfg.pos * r + 128 * (i % (ntile // 2))
                    self.dma(d["VF"][b, row0:row0 + 128, :],
                             sb["ar2"][:, eb + 128 * tl:eb + 128 * (tl + 1)],
                             reads=kTe, writes=(("VF", b),), tag=tg + "vo")
        if stop == "proj":
            return
        for b in range(2):
            for hp in range(2):
                ku = keys("G", 2080, 2082)
                ng = max(1, S // 2048)
                w = min(2048, S)
                for gg in range(ng):
                    kg = keys("G", 0, w)
                    self.dma(G[0:1, 0:w],
                             d["norms_f"][b, 0, hp, w * gg:w * (gg + 1)],
                             reads=(("norms_f", 0),), writes=kg, tag=tg + "mqi")
                    self.red(G[0:1, 2048 + gg:2049 + gg], G[0:1, 0:w], ALU.max,
                             reads=kg, writes=keys("G", 2048, 2048 + ng),
                             tag=tg + "mqr")
                self.red(G[0:1, 2080:2081], G[0:1, 2048:2048 + ng], ALU.max,
                         reads=keys("G", 2048, 2048 + ng), writes=ku,
                         tag=tg + "mqr2")
                self.act(G[0:1, 2081:2082], G[0:1, 2080:2081], AF.Sqrt,
                         reads=ku, writes=ku, tag=tg + "mqs")
                for r in range(S // 1024):
                    km = keys("G", 0, 1024)
                    km2 = keys("G", 1024, 2048)
                    self.dma(G[0:1, 0:1024],
                             d["norms_f"][b, 1, hp, 1024 * r:1024 * (r + 1)],
                             reads=(("norms_f", 1),), writes=km, tag=tg + "mki")
                    self.act(G[0:1, 1024:2048], G[0:1, 0:1024], AF.Sqrt,
                             reads=km, writes=km2, tag=tg + "mks")
                    self.ts(G[0:1, 0:1024], G[0:1, 1024:2048],
                            G[0:1, 2081:2082], ALU.mult, reads=km2 + ku,
                            writes=km, s2=-1.0, op1=ALU.mult, tag=tg + "mkm")
                    kmb = keys("W1", 0, 1024)
                    self.cp("vector", W1[0:1, 0:1024], G[0:1, 0:1024],
                            reads=km, writes=kmb, tag=tg + "mkc")
                    self.dma(d["negMk"][b, hp, 1024 * r:1024 * (r + 1)],
                             W1[0:1, 0:1024], reads=kmb,
                             writes=(("negMk", b, hp),), tag=tg + "mko")
        if stop == "negmk":
            return
        ar0, ar1, ar2, ar3 = sb["ar0"], sb["ar1"], sb["ar2"], sb["ar3"]
        nkt = S // 128
        psC = sb["psC"]
        psTf = sb["psT"][:].bitcast(F32)
        PREG = {(0, 0): ("ar0", 0), (1, 0): ("ar0", 8192),
                (2, 0): ("ar3", 0), (0, 1): ("ar1", 0),
                (1, 1): ("ar1", 8192), (2, 1): ("ar3", 8192)}
        for b in range(cfg.B):
            kQ = keys("ar2", 0, S)
            self.dma(ar2[:, 0:S], d["QF"][b], reads=(("QF", b),), writes=kQ,
                     tag=tg + "alq")
            kM = keys("G", 1024, 1024 + 2 * nkt)
            kMs = keys("W1", 1536, 1536 + 2 * nkt)
            for hp in range(2):
                self.dma(W1[:, 1536 + nkt * hp:1536 + nkt * (hp + 1)],
                         d["negMk"][b, hp].rearrange("(a q) -> a q", q=128),
                         reads=(("negMk", b, hp),), writes=kMs,
                         tag=tg + "aM", transpose=True)
            self.cp("vector", G[:, 1024:1024 + 2 * nkt],
                    W1[:, 1536:1536 + 2 * nkt], reads=kMs, writes=kM,
                    tag=tg + "aMc")
            kacc = keys("E", 0, S)
            self.memset(E[:, 0:S], 0.0, reads=(), writes=kacc, tag=tg + "az")

            def scores_block(j):
                pp = j % 2
                kK = keys("W1", 128 * pp, 128 * (pp + 1))
                self.dma(W1[:, 128 * pp:128 * (pp + 1)],
                         d["KF"][b][:, 128 * j:128 * (j + 1)],
                         reads=(("KF", b),), writes=kK, tag=tg + "aki")
                kVt = keys("W1", 1664 + 128 * pp, 1792 + 128 * pp)
                self.dma(W1[:, 1664 + 128 * pp:1792 + 128 * pp],
                         d["VF"][b][128 * j:128 * (j + 1), :],
                         reads=(("VF", b),), writes=kVt, tag=tg + "avt")
                pn0, pb0 = PREG[(j % 3, 0)]
                pn1, pb1 = PREG[(j % 3, 1)]
                for s in range(16):
                    hp = s % 2
                    q0 = 1024 * (s // 2)
                    r3 = s % 3
                    Pn, PB = (pn0, pb0) if hp == 0 else (pn1, pb1)
                    Par = sb[Pn]
                    if r3 < 2:
                        rps, ra = psA, 1024 * r3
                        ksa = keys("psA", ra, ra + 1024)
                    else:
                        rps, ra = psB, 0
                        ksa = keys("psB", 0, 1024)
                    for c_ in range(2):
                        self.mm(rps[:, ra + 512 * c_:ra + 512 * (c_ + 1)],
                                W1[64 * hp:64 * (hp + 1),
                                   128 * pp:128 * pp + 128],
                                ar2[64 * hp:64 * (hp + 1),
                                    q0 + 512 * c_:q0 + 512 * (c_ + 1)],
                                start=True, stop=True,
                                reads=kK + kQ, writes=ksa, tag=tg + "amm")
                    self.act(Par[:, PB + q0:PB + q0 + 1024],
                             rps[:, ra:ra + 1024], AF.Exp,
                             bias=G[:, 1024 + nkt * hp + j:
                                    1025 + nkt * hp + j],
                             reads=ksa + kM,
                             writes=keys(Pn, PB + q0, PB + q0 + 1024),
                             tag=tg + "aexp")
                vs = 256 + 128 * (j % 3)
                for hp in range(2):
                    Pn, PB = (pn0, pb0) if hp == 0 else (pn1, pb1)
                    Par = sb[Pn]
                    kP = keys(Pn, PB, PB + 8192)
                    kzz = keys("G", 2136 + 2 * hp, 2138 + 2 * hp)
                    self.red(G[:, 2136 + 2 * hp:2137 + 2 * hp],
                             Par[:, PB:PB + 8192], ALU.add,
                             reads=kP, writes=kzz, tag=tg + "azs")
                    self.recip(G[:, 2137 + 2 * hp:2138 + 2 * hp],
                               G[:, 2136 + 2 * hp:2137 + 2 * hp],
                               reads=kzz, writes=kzz, tag=tg + "azr")
                    self.ts(W1[:, vs + 64 * hp:vs + 64 * (hp + 1)],
                            W1[:, 1664 + 128 * pp + 64 * hp:
                               1664 + 128 * pp + 64 * (hp + 1)],
                            G[:, 2137 + 2 * hp:2138 + 2 * hp], ALU.mult,
                            reads=kVt + kzz,
                            writes=keys("W1", vs + 64 * hp, vs + 64 * (hp + 1)),
                            tag=tg + "avv")

            def pv_block2(ja, jb):
                for s in range(16):
                    q0 = 512 * s
                    if s % 2 == 0:
                        ps, kps = psC, keys("psC", 0, 512)
                    else:
                        ps, kps = psTf, keys("psT", 0, 1024)
                    for i_, j in enumerate((ja, jb)):
                        vs = 256 + 128 * (j % 3)
                        for hp in range(2):
                            Pn, PB = PREG[(j % 3, hp)]
                            Par = sb[Pn]
                            self.mm(ps[64 * hp:64 * (hp + 1), 0:512],
                                    W1[:, vs + 64 * hp:vs + 64 * (hp + 1)],
                                    Par[:, PB + q0:PB + q0 + 512],
                                    start=(i_ == 0), stop=(i_ == 1),
                                    reads=keys(Pn, PB + q0, PB + q0 + 512) +
                                    keys("W1", vs + 64 * hp,
                                         vs + 64 * (hp + 1)),
                                    writes=kps, tag=tg + "apv")
                    ke = keys("E", q0, q0 + 512)
                    self.tt(ALU.add, E[:, q0:q0 + 512], E[:, q0:q0 + 512],
                            ps[:, 0:512], reads=ke + kps, writes=ke,
                            tag=tg + "aac")

            pend = None
            for j in range(nkt):
                scores_block(j)
                if pend is not None:
                    pv_block2(*pend)
                    pend = None
                if j % 2 == 1:
                    pend = (j - 1, j)
            if pend is not None:
                pv_block2(*pend)
            for g in range(S // 2048):
                kcs = keys("ar2", 8192 + 2048 * (g % 2), 8192 + 2048 * (g % 2) + 2048)
                self.cp("vector", ar2[:, 8192 + 2048 * (g % 2):
                                      8192 + 2048 * (g % 2) + 2048],
                        E[:, 2048 * g:2048 * (g + 1)],
                        reads=kacc, writes=kcs, tag=tg + "acast")
                self.dma(d["CONTR"][b][:, 2048 * g:2048 * (g + 1)],
                         ar2[:, 8192 + 2048 * (g % 2):
                             8192 + 2048 * (g % 2) + 2048],
                         reads=kcs, writes=(("CONTR", b, 0), ("CONTR", b, 1)),
                         tag=tg + "aco")
        if stop == "attn":
            return
        self.phase_out_rs(("W0", 3072), tg + "wo")
        if stop == "rs":
            return
        for i in range(cfg.NT_LOC):
            pp = i % 2
            kf = keys("F", 2048 * pp, 2048 * pp + 1024)
            self.dma(F[:, 2048 * pp:2048 * pp + 1024],
                     d["zred"][128 * i:128 * (i + 1), :],
                     reads=(("zred", 0),), writes=kf, tag=tg + "zi")
            kw1 = keys("W1", 1024 * pp, 1024 * (pp + 1))
            self.cp("vector", W1[:, 1024 * pp:1024 * (pp + 1)],
                    F[:, 2048 * pp:2048 * pp + 1024], reads=kf, writes=kw1,
                    tag=tg + "zc")
            for j in range(8):
                kT = keys("psT", 128 * (j % 8), 128 * (j % 8) + 128)
                self.trans(psT[:, 128 * (j % 8):128 * (j % 8) + 128],
                           W1[:, 1024 * pp + 128 * j:1024 * pp + 128 * (j + 1)],
                           reads=kw1, writes=kT, tag=tg + "zT")
                self.cp("vector",
                        sb["ar0"][:, CL * j + 128 * i:CL * j + 128 * (i + 1)],
                        psT[:, 128 * (j % 8):128 * (j % 8) + 128],
                        reads=kT, writes=keys("ar0", CL * j + 128 * i,
                                              CL * j + 128 * (i + 1)),
                        tag=tg + "zTe")
        self.dma_wfm(sb["ar1"][:, 0:8 * 1024], self.wbf["wff"], 1024,
                     reads=(("wbf_wff", 0),), writes=keys("ar1", 0, 8192),
                     tag=tg + "wffl")
        kzfm = keys("ar0", 0, 8 * CL)
        kwf = keys("ar1", 0, 8192)
        if stop == "zfm":
            return
        nth = CL // 1024
        for th in range(nth):
            tc = 1024 * th
            for i in range(8):
                kps = keys("psA", 0, 1024)
                for kc in range(8):
                    for c_ in range(2):
                        self.mm(psA[:, 512 * c_:512 * (c_ + 1)],
                                sb["ar1"][:, 1024 * kc + 128 * i:
                                          1024 * kc + 128 * (i + 1)],
                                sb["ar0"][:, CL * kc + tc + 512 * c_:
                                          CL * kc + tc + 512 * (c_ + 1)],
                                start=(kc == 0), stop=(kc == 7),
                                reads=kwf + kzfm, writes=kps, tag=tg + "fmm")
                kw = keys("ar2", 2048 * i + 1024 * (th % 2),
                          2048 * i + 1024 * (th % 2) + 1024)
                if stop != "zn3":
                    self.cp("vector",
                            sb["ar2"][:, 2048 * i + 1024 * (th % 2):
                                      2048 * i + 1024 * (th % 2) + 1024],
                            psA[:, 0:1024], reads=kps, writes=kw, tag=tg + "fev")
                ksq = keys("F", 0, 1024)
                if stop not in ("zn3", "zn4"):
                    wsl = sb["ar2"][:, 2048 * i + 1024 * (th % 2):
                                    2048 * i + 1024 * (th % 2) + 1024]
                    self.tt(ALU.mult, F[:, 0:1024], wsl, wsl,
                            reads=kw, writes=ksq, tag=tg + "fsq")
                if stop not in ("zn1", "zn3", "zn4"):
                    for c_ in range(2):
                        self.mm(psB[0:1, 512 * c_:512 * (c_ + 1)],
                                sb["c_ocolf"][:], F[:, 512 * c_:512 * (c_ + 1)],
                                start=(i == 0) or stop == "zn2",
                                stop=(i == 7) or stop == "zn2",
                                reads=ksq + keys("c_ocolf"),
                                writes=keys("psB", 0, 1024), tag=tg + "fns")
            if stop in ("zmm", "zn1", "zn2", "zn3", "zn4"):
                continue
            krs0 = keys("G", 0, 1024)
            self.cp("vector", G[0:1, 0:1024], psB[0:1, 0:1024],
                    reads=keys("psB", 0, 1024), writes=krs0, tag=tg + "frse")
            krs1 = keys("G", 1024, 2048)
            self.act(G[0:1, 1024:2048], G[0:1, 0:1024], AF.Sqrt,
                     reads=krs0, writes=krs1, tag=tg + "frs")
            self.ts(G[0:1, 0:1024], G[0:1, 1024:2048], 1e-12, ALU.max,
                    reads=krs1, writes=krs0, tag=tg + "fmx")
            krr = keys("G", 1024, 2048)
            self.recip(G[0:1, 1024:2048], G[0:1, 0:1024], reads=krs0,
                       writes=krr, tag=tg + "frr")
            kbr = keys("psB", 0, 1024)
            for c_ in range(2):
                self.mm(psB[:, 512 * c_:512 * (c_ + 1)], sb["c_o1x128f"][:],
                        G[0:1, 1024 + 512 * c_:1024 + 512 * (c_ + 1)],
                        start=True, stop=True,
                        reads=krr + keys("c_o1x128f"), writes=kbr,
                        tag=tg + "fbc")
            if stop == "zbc":
                continue
            for i in range(8):
                co = 2048 * i + 1024 * (th % 2)
                kw = keys("ar2", co, co + 1024)
                kg1 = keys("ar3", 1024 * i, 1024 * (i + 1))
                self.tt(ALU.mult, sb["ar3"][:, 1024 * i:1024 * (i + 1)],
                        sb["ar2"][:, co:co + 1024], psB[:, 0:1024],
                        reads=kw + kbr, writes=kg1, tag=tg + "fnm")
                self.act(sb["ar2"][:, co:co + 1024],
                         sb["ar3"][:, 1024 * i:1024 * (i + 1)],
                         AF.Gelu, reads=kg1, writes=kw, tag=tg + "fgel")
            if stop == "zgel":
                continue
            for tt_ in range(8):
                kT = keys("psT", 0, 1024)
                for i in range(8):
                    co = 2048 * i + 1024 * (th % 2)
                    self.trans(psT[:, 128 * i:128 * (i + 1)],
                               sb["ar2"][:, co + 128 * tt_:co + 128 * (tt_ + 1)],
                               reads=keys("ar2", co, co + 1024),
                               writes=kT, tag=tg + "fT")
                it = 8 * th + tt_
                pp = tt_ % 2
                kxf = keys("F", 2048 * pp, 2048 * pp + 1024)
                self.dma(F[:, 2048 * pp:2048 * pp + 1024],
                         d["xcur"][128 * it:128 * (it + 1), :],
                         reads=keys("xcur"), writes=kxf, tag=tg + "fxi")
                self.tt(ALU.add, F[:, 2048 * pp:2048 * pp + 1024],
                        F[:, 2048 * pp:2048 * pp + 1024], psT[:, 0:1024],
                        reads=kxf + kT, writes=kxf, tag=tg + "fadd")
                self.dma(d["xcur"][128 * it:128 * (it + 1), :],
                         F[:, 2048 * pp:2048 * pp + 1024],
                         reads=kxf, writes=keys("xcur"), tag=tg + "fxo")

    def phase_mamba(self):
        cfg, sb, d = self.cfg, self.sb, self.d
        S, DM, T = cfg.S, cfg.DM, cfg.T
        MB = cfg.MB
        W0, W1, G, F, E = sb["W0"], sb["W1"], sb["G"], sb["F"], sb["E"]
        psA, psB, psT = sb["psA"], sb["psB"], sb["psT"]
        ar0, ar1, ar2, ar3 = sb["ar0"], sb["ar1"], sb["ar2"], sb["ar3"]
        NT = cfg.NT_LOC
        CL = cfg.TL
        nct = S // 2048
        ngl = T // 2048

        def pcol(g):
            b = g // nct
            return cfg.mcol0(b) + 2048 * (g % nct)

        def scol(g):
            return 2048 * g

        self.phase_ln_ag(extra_rms=False, tag="mln", do_ag=False, fm_ar="ar0")
        self.dma_wfm(ar1[:, 0:8 * 512], self.wbf["lin1_w"], 512,
                     reads=(("wbf_lin1_w", 0),), writes=keys("ar1", 0, 4096),
                     tag="mb_l1w")
        kb1 = keys("G", 0, 512)
        self.dma(G[0:1, 0:DM], self.ein["lin1_b"], reads=(), writes=kb1,
                 tag="mb_l1b")
        kb1b = keys("W1", 1536, 2048)
        self.cp("vector", W1[0:1, 1536:1536 + DM], G[0:1, 0:DM],
                reads=kb1, writes=kb1b, tag="mb_l1bc")
        for i in range(NT):
            kps = keys("psB", 0, 512)
            for k in range(8):
                self.mm(psB[:, 0:512],
                        ar0[:, CL * k + 128 * i:CL * k + 128 * (i + 1)],
                        ar1[:, 512 * k:512 * (k + 1)], start=(k == 0), stop=False,
                        reads=keys("ar0", CL * k + 128 * i,
                                   CL * k + 128 * (i + 1)) +
                        keys("ar1", 512 * k, 512 * (k + 1)),
                        writes=kps, tag="mb_l1mm")
            self.mm(psB[:, 0:512], sb["c_o1x128b"][:], W1[0:1, 1536:1536 + DM],
                    start=False, stop=True,
                    reads=kb1b + keys("c_o1x128b"), writes=kps, tag="mb_l1bias")
            ke = keys("E", 512 * i, 512 * (i + 1))
            self.cp("scalar" if i % 2 else "vector", E[:, 512 * i:512 * (i + 1)],
                    psB[:, 0:512], reads=kps, writes=ke, tag="mb_l1ev")

        for l in range(cfg.NL):
            lt = f"mb{l}"
            kcw = keys("G", 2144, 2176)
            self.dma(G[:, 2144:2144 + cfg.CK], self.ein["convw"][l],
                     reads=(), writes=kcw, tag=lt + "cw")
            self.dma(G[:, 2148:2149], as_col(self.ein["convb"][l, :]),
                     reads=(), writes=kcw, tag=lt + "cb")
            self.dma(G[:, 2149:2150], as_col(self.ein["dtb"][l, :]),
                     reads=(), writes=kcw, tag=lt + "db")
            self.dma(G[:, 2150:2151], as_col(self.ein["dpv"][l, :]),
                     reads=(), writes=kcw, tag=lt + "dp")
            kA = keys("G", 2176, 2208)
            self.dma(G[:, 2176:2176 + cfg.DS], self.ein["alog"][l],
                     reads=(), writes=kA, tag=lt + "al")
            self.act(G[:, 2176:2176 + cfg.DS], G[:, 2176:2176 + cfg.DS],
                     AF.Exp, reads=kA, writes=kA, tag=lt + "aexp")
            self.ts(G[:, 2176:2176 + cfg.DS], G[:, 2176:2176 + cfg.DS],
                    -1.0, ALU.mult, reads=kA, writes=kA, tag=lt + "aneg")
            for i in range(NT):
                pp = i % 2
                gs = pp * 16
                kst = keys("G", gs, gs + 16)
                ke = keys("E", 512 * i, 512 * (i + 1))
                ksq = keys("F", 2048 * pp, 2048 * pp + 512)
                self.act(F[:, 2048 * pp:2048 * pp + 512],
                         E[:, 512 * i:512 * (i + 1)], AF.Square,
                         reads=ke, writes=ksq, tag=lt + "rsq")
                self.red(G[:, gs:gs + 1], F[:, 2048 * pp:2048 * pp + 512],
                         ALU.add, reads=ksq, writes=kst, tag=lt + "rs")
                self.act(G[:, gs + 1:gs + 2], G[:, gs:gs + 1], AF.Sqrt,
                         reads=kst + keys("c_cb"), writes=kst, scale=1.0 / DM,
                         bias=self.sb["c_cb"][:, 0:1], tag=lt + "rsd")
                self.recip(G[:, gs + 2:gs + 3], G[:, gs + 1:gs + 2],
                           reads=kst, writes=kst, tag=lt + "rrc")
                khr = keys("F", 2048 * pp + 512, 2048 * pp + 1024)
                self.ts(F[:, 2048 * pp + 512:2048 * pp + 1024],
                        E[:, 512 * i:512 * (i + 1)], G[:, gs + 2:gs + 3],
                        ALU.mult, reads=ke + kst, writes=khr, tag=lt + "rmul")
                kbf = keys("W1", 512 * pp, 512 * (pp + 1))
                self.cp("vector", W1[:, 512 * pp:512 * (pp + 1)],
                        F[:, 2048 * pp + 512:2048 * pp + 1024],
                        reads=khr, writes=kbf, tag=lt + "rcast")
                for j in range(DM // 128):
                    kT = keys("psT", 128 * j, 128 * (j + 1))
                    self.trans(psT[:, 128 * j:128 * (j + 1)],
                               W1[:, 512 * pp + 128 * j:512 * pp + 128 * (j + 1)],
                               reads=kbf, writes=kT, tag=lt + "rT")
                    self.cp("vector",
                            ar0[:, CL * j + 128 * i:CL * j + 128 * (i + 1)],
                            psT[:, 128 * j:128 * (j + 1)], reads=kT,
                            writes=keys("ar0", CL * j + 128 * i,
                                        CL * j + 128 * (i + 1)),
                            tag=lt + "rTe")
            for j in range(DM // 128):
                self.dma(d["ag_hr_in"][128 * j:128 * (j + 1), :],
                         ar0[:, CL * j:CL * j + CL],
                         reads=keys("ar0", CL * j, CL * j + CL),
                         writes=(("ag_hr_in", j),), tag=lt + "agi")
            self.coll("AllGather", ALU.bypass, d["ag_hr_in"], d["ag_hr_out"],
                      reads=tuple(("ag_hr_in", j) for j in range(DM // 128)),
                      writes=(("ag_hr_out", 0),), tag=lt + "AG")
            self.dma_wfm(W0[:, 0:1024], self.wbf[f"inw{l}"], 256,
                         reads=((f"wbf_inw{l}", 0),), writes=keys("W0", 0, 1024),
                         tag=lt + "iw")
            for r in range(cfg.R):
                kin = keys("ar0", 4 * CL, 8 * CL)
                self.dma(ar0[:, 4 * CL:8 * CL].rearrange(
                             "p (c m) -> p c m", m=CL),
                         d["ag_hr_out"][512 * r:512 * (r + 1), :]
                         .rearrange("(c p) m -> p c m", p=128),
                         reads=(("ag_hr_out", 0),), writes=kin, tag=lt + "xzi")
                for m2 in range(2):
                    kps = keys("psA", 0, 2048)
                    for k in range(4):
                        for c_ in range(CL // 512):
                            self.mm(psA[:, 512 * c_:512 * (c_ + 1)],
                                    W0[:, 256 * k + 128 * m2:
                                       256 * k + 128 * (m2 + 1)],
                                    ar0[:, 4 * CL + CL * k + 512 * c_:
                                        4 * CL + CL * k + 512 * (c_ + 1)],
                                    start=(k == 0), stop=(k == 3),
                                    reads=kin + keys("W0", 0, 1024),
                                    writes=kps, tag=lt + "xzmm")
                    if m2 == 0:
                        for b in range(2):
                            cc = cfg.mcol0(b) + cfg.pos * r
                            self.cp("vector", ar2[:, cc:cc + cfg.pos],
                                    psA[:, b * cfg.pos:(b + 1) * cfg.pos],
                                    reads=kps, writes=keys("ar2", cc, cc + cfg.pos),
                                    tag=lt + "xpev")
                    else:
                        kzs = keys("ar1", 4096, 4096 + 2048)
                        self.cp("vector", ar1[:, 4096:4096 + 2048],
                                psA[:, 0:2048], reads=kps, writes=kzs,
                                tag=lt + "zev")
                        for b in range(2):
                            self.dma(d["ZSP"][:, b * S + cfg.pos * r:
                                              b * S + cfg.pos * r + cfg.pos],
                                     ar1[:, 4096 + b * cfg.pos:
                                         4096 + (b + 1) * cfg.pos],
                                     reads=kzs, writes=(("ZSP", 0),),
                                     tag=lt + "zsp")
            for b in range(2):
                self.memset(ar2[:, b * MB:b * MB + 3], 0.0, reads=(),
                            writes=keys("ar2", b * MB, b * MB + 3),
                            tag=lt + "pad", eng="vector")
            for b in range(2):
                c0 = cfg.mcol0(b)
                for ch in range(nct):
                    kacc = keys("F", 0, 2048)
                    src0 = c0 + 2048 * ch - 3
                    self.ts(F[:, 0:2048], ar2[:, src0:src0 + 2048],
                            G[:, 2144:2145], ALU.mult,
                            reads=keys("ar2", src0, src0 + 2048) + kcw,
                            writes=kacc, tag=lt + "cv0")
                    for k in range(1, cfg.CK):
                        self.stt(F[:, 0:2048], ar2[:, src0 + k:src0 + k + 2048],
                                 G[:, 2144 + k:2145 + k], F[:, 0:2048],
                                 ALU.mult, ALU.add,
                                 reads=keys("ar2", src0 + k, src0 + k + 2048) +
                                 kcw + kacc, writes=kacc, tag=lt + "cvk")
                    ku = keys("ar1", 6144, 8192)
                    self.act(ar1[:, 6144:6144 + 2048], F[:, 0:2048], AF.Silu,
                             bias=G[:, 2148:2149], reads=kacc + kcw, writes=ku,
                             tag=lt + "cvs")
                    g = b * nct + ch
                    self.dma(d["USP"][:, scol(g):scol(g) + 2048],
                             ar1[:, 6144:6144 + 2048], reads=ku,
                             writes=(("USP", g),), tag=lt + "usp")
            self.dma(W0[:, 1024:1088], self.wbf[f"xprojw{l}"],
                     reads=((f"wbf_xprojw{l}", 0),),
                     writes=keys("W0", 1024, 1088), tag=lt + "xpw")
            for g in range(ngl):
                kuc = keys("ar1", 8192, 10240)
                self.dma(ar1[:, 8192:8192 + 2048],
                         d["USP"][:, scol(g):scol(g) + 2048],
                         reads=(("USP", g),), writes=kuc, tag=lt + "xui")
                for c_ in range(4):
                    kps = keys("psB", 512 * (c_ % 2), 512 * (c_ % 2) + 512)
                    self.mm(psB[0:64, 512 * (c_ % 2):512 * (c_ % 2) + 512],
                            W0[:, 1024:1088],
                            ar1[:, 8192 + 512 * c_:8192 + 512 * (c_ + 1)],
                            start=True, stop=True,
                            reads=kuc + keys("W0", 1024, 1088), writes=kps,
                            tag=lt + "xpm")
                    kf = keys("F", 2048 + 512 * (c_ % 2),
                              2048 + 512 * (c_ % 2) + 512)
                    self.cp("vector", F[0:64, 2048 + 512 * (c_ % 2):
                                        2048 + 512 * (c_ % 2) + 512],
                            psB[0:64, 512 * (c_ % 2):512 * (c_ % 2) + 512],
                            reads=kps, writes=kf, tag=lt + "xpe")
                    self.dma(d["dblpart"][0:64, 2048 * g + 512 * c_:
                                          2048 * g + 512 * (c_ + 1)],
                             F[0:64, 2048 + 512 * (c_ % 2):
                               2048 + 512 * (c_ % 2) + 512],
                             reads=kf, writes=(("dblpart", g),), tag=lt + "xpo")
            self.coll("AllReduce", ALU.add, d["dblpart"], d["dblfull"],
                      reads=tuple(("dblpart", g) for g in range(ngl)),
                      writes=(("dblfull", 0),), tag=lt + "AR")
            self.dma(W0[0:cfg.DTR, 1088:1216], self.wbf[f"dtw{l}"],
                     reads=((f"wbf_dtw{l}", 0),), writes=keys("W0", 1088, 1216),
                     tag=lt + "dtw")
            for g in range(ngl):
                pc = pcol(g)
                kdg = keys("G", 0, 2048)
                self.dma(G[0:cfg.DTR, 0:2048],
                         d["dblfull"][0:cfg.DTR, 2048 * g:2048 * (g + 1)],
                         reads=(("dblfull", 0),), writes=kdg, tag=lt + "dti")
                kdb = keys("W1", 0, 2048)
                self.cp("vector", W1[0:cfg.DTR, 0:2048], G[0:cfg.DTR, 0:2048],
                        reads=kdg, writes=kdb, tag=lt + "dtc")
                kps = keys("psA", 0, 2048)
                for c_ in range(4):
                    self.mm(psA[:, 512 * c_:512 * (c_ + 1)],
                            W0[0:cfg.DTR, 1088:1216],
                            W1[0:cfg.DTR, 512 * c_:512 * (c_ + 1)],
                            start=True, stop=True,
                            reads=kdb + keys("W0", 1088, 1216), writes=kps,
                            tag=lt + "dtm")
                kx = keys("F", 0, 2048)
                self.ts(F[:, 0:2048], psA[:, 0:2048], G[:, 2149:2150],
                        ALU.add, reads=kps + kcw, writes=kx, tag=lt + "dt0")
                ka = keys("F", 2048, 4096)
                self.act(F[:, 2048:4096], F[:, 0:2048], AF.Abs,
                         reads=kx, writes=ka, tag=lt + "dt1")
                kge = keys("G", 0, 2048)
                self.act(G[:, 0:2048], F[:, 2048:4096], AF.Exp, scale=-1.0,
                         reads=ka, writes=kge, tag=lt + "dt2")
                self.act(F[:, 2048:4096], G[:, 0:2048], AF.Ln,
                         bias=self.sb["c_cb"][:, 2:3],
                         reads=kge + keys("c_cb"), writes=ka, tag=lt + "dt3")
                self.stt(ar3[:, pc:pc + 2048], F[:, 0:2048],
                         self.sb["c_cb"][:, 3:4],
                         F[:, 2048:4096], ALU.max, ALU.add,
                         reads=kx + ka + keys("c_cb"),
                         writes=keys("ar3", pc, pc + 2048), tag=lt + "dt4")
            for g in range(ngl):
                pc = pcol(g)
                kuc = keys("ar2", S, S + 2048)
                self.dma(ar2[:, S:S + 2048],
                         d["USP"][:, scol(g):scol(g) + 2048],
                         reads=(("USP", g),), writes=kuc, tag=lt + "dui")
                self.tt(ALU.mult, ar1[:, pc:pc + 2048], ar3[:, pc:pc + 2048],
                        ar2[:, S:S + 2048],
                        reads=keys("ar3", pc, pc + 2048) + kuc,
                        writes=keys("ar1", pc, pc + 2048), tag=lt + "du")
            for b in range(2):
                c0 = cfg.mcol0(b)
                for n in range(cfg.DS):
                    kan = keys("ar2", 0, S)
                    self.act(ar2[:, 0:S], ar3[:, c0:c0 + S], AF.Exp,
                             scale=G[:, 2176 + n:2177 + n],
                             reads=keys("ar3", c0, c0 + S) + kA,
                             writes=kan, tag=lt + "sa")
                    kcy = keys("G", 2208, 2240)
                    self.memset(G[:, 2208:2209], 0.0, reads=(), writes=kcy,
                                tag=lt + "scy", eng="vector")
                    for c in range(nct):
                        kbr = keys("G", 0, 2048)
                        self.dma(G[0:1, 0:2048],
                                 as_row(d["dblfull"][cfg.DTR + n,
                                                     b * S + 2048 * c:
                                                     b * S + 2048 * (c + 1)]),
                                 reads=(("dblfull", 0),), writes=kbr,
                                 tag=lt + "sB")
                        kbc = keys("W0", 1280, 3328)
                        self.cp("vector", W0[0:1, 1280:3328], G[0:1, 0:2048],
                                reads=kbr, writes=kbc, tag=lt + "sBc")
                        kbrp = keys("psA", 0, 2048)
                        for c_ in range(4):
                            self.mm(psA[:, 512 * c_:512 * (c_ + 1)],
                                    sb["c_o1x128b"][:],
                                    W0[0:1, 1280 + 512 * c_:
                                       1280 + 512 * (c_ + 1)],
                                    start=True, stop=True,
                                    reads=kbc + keys("c_o1x128b"),
                                    writes=kbrp, tag=lt + "sBr")
                        kw = keys("ar2", S, S + 2048)
                        self.tt(ALU.mult, ar2[:, S:S + 2048],
                                ar1[:, c0 + 2048 * c:c0 + 2048 * (c + 1)],
                                psA[:, 0:2048],
                                reads=keys("ar1", c0 + 2048 * c,
                                           c0 + 2048 * (c + 1)) + kbrp,
                                writes=kw, tag=lt + "sw")
                        kh = keys("ar2", S + 2048, S + 4096)
                        self.scan_op(ar2[:, S + 2048:S + 4096],
                                     ar2[:, 2048 * c:2048 * (c + 1)],
                                     ar2[:, S:S + 2048], G[:, 2208:2209],
                                     reads=kan + kw + kcy, writes=kh,
                                     tag=lt + "scan")
                        self.cp("vector", G[:, 2208:2209],
                                ar2[:, S + 4095:S + 4096], reads=kh,
                                writes=kcy, tag=lt + "scar")
                        kcr = keys("F", 0, 2048)
                        self.dma(F[0:1, 0:2048],
                                 as_row(d["dblfull"][cfg.DTR + cfg.DS + n,
                                                     b * S + 2048 * c:
                                                     b * S + 2048 * (c + 1)]),
                                 reads=(("dblfull", 0),), writes=kcr,
                                 tag=lt + "sC")
                        kcc = keys("W1", 0, 2048)
                        self.cp("vector", W1[0:1, 0:2048], F[0:1, 0:2048],
                                reads=kcr, writes=kcc, tag=lt + "sCc")
                        kcrp = keys("psA", 0, 2048)
                        for c_ in range(4):
                            self.mm(psA[:, 512 * c_:512 * (c_ + 1)],
                                    sb["c_o1x128b"][:],
                                    W1[0:1, 512 * c_:512 * (c_ + 1)],
                                    start=True, stop=True,
                                    reads=kcc + keys("c_o1x128b"),
                                    writes=kcrp, tag=lt + "sCr")
                        kpr = keys("ar2", S + 4096, S + 6144)
                        self.tt(ALU.mult, ar2[:, S + 4096:S + 6144],
                                ar2[:, S + 2048:S + 4096], psA[:, 0:2048],
                                reads=kh + kcrp, writes=kpr, tag=lt + "sp")
                        ky = keys("ar0", c0 + 2048 * c, c0 + 2048 * (c + 1))
                        if n == 0:
                            self.cp("vector",
                                    ar0[:, c0 + 2048 * c:c0 + 2048 * (c + 1)],
                                    ar2[:, S + 4096:S + 6144], reads=kpr,
                                    writes=ky, tag=lt + "sy0")
                        else:
                            self.tt(ALU.add,
                                    ar0[:, c0 + 2048 * c:c0 + 2048 * (c + 1)],
                                    ar0[:, c0 + 2048 * c:c0 + 2048 * (c + 1)],
                                    ar2[:, S + 4096:S + 6144],
                                    reads=ky + kpr, writes=ky, tag=lt + "sy")
            for g in range(ngl):
                pc = pcol(g)
                kuc = keys("ar2", S, S + 2048)
                self.dma(ar2[:, S:S + 2048],
                         d["USP"][:, scol(g):scol(g) + 2048],
                         reads=(("USP", g),), writes=kuc, tag=lt + "yui")
                kzc = keys("ar2", S + 2048, S + 4096)
                self.dma(ar2[:, S + 2048:S + 4096],
                         d["ZSP"][:, scol(g):scol(g) + 2048],
                         reads=(("ZSP", 0),), writes=kzc, tag=lt + "yzi")
                k1 = keys("ar2", S + 4096, S + 6144)
                self.stt(ar2[:, S + 4096:S + 6144], ar2[:, S:S + 2048],
                         G[:, 2150:2151], ar0[:, pc:pc + 2048],
                         ALU.mult, ALU.add,
                         reads=kuc + keys("ar0", pc, pc + 2048) + kcw,
                         writes=k1, tag=lt + "yd")
                k2 = keys("ar2", S + 6144, S + 8192)
                self.act(ar2[:, S + 6144:S + 8192], ar2[:, S + 2048:S + 4096],
                         AF.Silu, reads=kzc, writes=k2, tag=lt + "ysz")
                self.tt(ALU.mult, ar0[:, pc:pc + 2048], ar2[:, S + 4096:S + 6144],
                        ar2[:, S + 6144:S + 8192], reads=k1 + k2,
                        writes=keys("ar0", pc, pc + 2048), tag=lt + "yg")
            self.dma(W1[:, 0:DM], self.wbf[f"outw{l}"],
                     reads=((f"wbf_outw{l}", 0),), writes=keys("W1", 0, DM),
                     tag=lt + "oww")
            for b in range(2):
                c0 = cfg.mcol0(b)
                for i in range(S // 128):
                    kps = keys("psB", 0, 512)
                    self.mm(psB[:, 0:512],
                            ar0[:, c0 + 128 * i:c0 + 128 * (i + 1)],
                            W1[:, 0:DM], start=True, stop=True,
                            reads=keys("ar0", c0 + 128 * i, c0 + 128 * (i + 1)) +
                            keys("W1", 0, DM), writes=kps, tag=lt + "owm")
                    pp = i % 2
                    kf = keys("F", 2048 * pp, 2048 * pp + 512)
                    self.cp("scalar" if i % 2 == 0 else "vector",
                            F[:, 2048 * pp:2048 * pp + 512], psB[:, 0:512],
                            reads=kps, writes=kf, tag=lt + "owe")
                    r, pi = (128 * i) // cfg.pos, (128 * i) % cfg.pos
                    row0 = r * cfg.TL + b * cfg.pos + pi
                    self.dma(d["zpartM"][row0:row0 + 128, :],
                             F[:, 2048 * pp:2048 * pp + 512],
                             reads=kf, writes=(("zpartM", row0 // 128),),
                             tag=lt + "owo")
            self.coll("ReduceScatter", ALU.add, d["zpartM"], d["zredM"],
                      reads=tuple(("zpartM", i) for i in range(T // 128)),
                      writes=(("zredM", 0),), tag=lt + "RS")
            for i in range(NT):
                pp = i % 2
                kf = keys("F", 2048 * pp, 2048 * pp + 512)
                self.dma(F[:, 2048 * pp:2048 * pp + 512],
                         d["zredM"][128 * i:128 * (i + 1), :],
                         reads=(("zredM", 0),), writes=kf, tag=lt + "hri")
                self.tt(ALU.add, E[:, 512 * i:512 * (i + 1)],
                        E[:, 512 * i:512 * (i + 1)],
                        F[:, 2048 * pp:2048 * pp + 512],
                        reads=keys("E", 512 * i, 512 * (i + 1)) + kf,
                        writes=keys("E", 512 * i, 512 * (i + 1)), tag=lt + "hra")
        self.dma_wfm(ar0[:, 0:4096], self.wbf["lin2_w"], 1024,
                     reads=(("wbf_lin2_w", 0),), writes=keys("ar0", 0, 4096),
                     tag="mb_l2w")
        kb2 = keys("G", 0, 1024)
        self.dma(G[0:1, 0:1024], self.ein["lin2_b"], reads=(), writes=kb2,
                 tag="mb_l2b")
        kb2b = keys("W1", 1024, 2048)
        self.cp("vector", W1[0:1, 1024:2048], G[0:1, 0:1024],
                reads=kb2, writes=kb2b, tag="mb_l2bc")
        for i in range(cfg.NT_LOC):
            khb = keys("W1", 0, 512)
            self.cp("vector", W1[:, 0:512], E[:, 512 * i:512 * (i + 1)],
                    reads=keys("E", 512 * i, 512 * (i + 1)), writes=khb,
                    tag="mb_hc")
            khf = keys("W1", 512, 1024)
            for j in range(4):
                kT = keys("psT", 128 * j, 128 * (j + 1))
                self.trans(psT[:, 128 * j:128 * (j + 1)],
                           W1[:, 128 * j:128 * (j + 1)], reads=khb, writes=kT,
                           tag="mb_hT")
                self.cp("vector", W1[:, 512 + 128 * j:512 + 128 * (j + 1)],
                        psT[:, 128 * j:128 * (j + 1)], reads=kT, writes=khf,
                        tag="mb_hTe")
            kps = keys("psA", 0, 1024)
            for c_ in range(2):
                for k in range(4):
                    self.mm(psA[:, 512 * c_:512 * (c_ + 1)],
                            W1[:, 512 + 128 * k:512 + 128 * (k + 1)],
                            ar0[:, 1024 * k + 512 * c_:1024 * k + 512 * (c_ + 1)],
                            start=(k == 0), stop=False,
                            reads=khf + keys("ar0", 1024 * k + 512 * c_,
                                             1024 * k + 512 * (c_ + 1)),
                            writes=kps, tag="mb_l2m")
                self.mm(psA[:, 512 * c_:512 * (c_ + 1)], sb["c_o1x128b"][:],
                        W1[0:1, 1024 + 512 * c_:1024 + 512 * (c_ + 1)],
                        start=False, stop=True,
                        reads=kb2b + keys("c_o1x128b"), writes=kps,
                        tag="mb_l2bias")
            p3 = psA[:, 0:1024].rearrange("p (g k) -> p g k", k=32)
            km = keys("G", 1024, 1056)
            self.red(G[:, 1024:1056], p3, ALU.max, reads=kps, writes=km,
                     tag="mb_smx")
            g3 = G[:, 1024:1056].rearrange("p (g o) -> p g o", o=1)
            f3a = F[:, 0:1024].rearrange("p (g k) -> p g k", k=32)
            f3b = F[:, 1024:2048].rearrange("p (g k) -> p g k", k=32)
            ksb = keys("F", 0, 1024)
            p3b, g3b = bass.broadcast_tensor_aps(p3, g3)
            self.p.op("vector", lambda e, d_=f3a, a=p3b, bb=g3b:
                      e.tensor_tensor(out=d_, in0=a, in1=bb, op=ALU.subtract),
                      kps + km, ksb, tag="mb_sms")
            kex = keys("F", 1024, 2048)
            self.act(F[:, 1024:2048], F[:, 0:1024], AF.Exp, reads=ksb,
                     writes=kex, tag="mb_sme")
            ks2 = keys("G", 1056, 1088)
            self.red(G[:, 1056:1088], f3b, ALU.add, reads=kex, writes=ks2,
                     tag="mb_smsum")
            kr2 = keys("G", 1088, 1120)
            self.recip(G[:, 1088:1120], G[:, 1056:1088], reads=ks2, writes=kr2,
                       tag="mb_smr")
            g3r = G[:, 1088:1120].rearrange("p (g o) -> p g o", o=1)
            f3c, g3rb = bass.broadcast_tensor_aps(f3b, g3r)
            self.p.op("vector", lambda e, d_=f3a, a=f3c, bb=g3rb:
                      e.tensor_tensor(out=d_, in0=a, in1=bb, op=ALU.mult),
                      kex + kr2, ksb, tag="mb_smm")
            kxf = keys("F", 2048, 3072)
            self.dma(F[:, 2048:3072], d["xcur"][128 * i:128 * (i + 1), :],
                     reads=keys("xcur"), writes=kxf, tag="mb_xi")
            self.tt(ALU.add, F[:, 2048:3072], F[:, 2048:3072], F[:, 0:1024],
                    reads=kxf + ksb, writes=kxf, tag="mb_xadd")
            self.dma(d["xcur"][128 * i:128 * (i + 1), :], F[:, 2048:3072],
                     reads=kxf, writes=keys("xcur"), tag="mb_xo")

    def phase_out(self):
        self.dma(self.d["out"], self.d["xcur"], reads=keys("xcur"),
                 writes=(("out", 0),), tag="final_out")
        self.p.op("sync", lambda e: e.nop(), reads=(("out", 0),), writes=(),
                  tag="final_wait")


def kernel(**inputs):
    cfg = Cfg()
    maps = shard_inputs(cfg, inputs)
    k = K(cfg)
    nc = k.build()
    res = run_bass_kernel_spmd(nc, maps, core_ids=list(range(cfg.R)))
    outs = [r["out"] for r in res.results]
    full = np.zeros((cfg.B, cfg.S, cfg.D), np.float32)
    for c in range(cfg.R):
        sl = slice(cfg.pos * c, cfg.pos * (c + 1))
        full[0, sl] = outs[c][:cfg.pos]
        full[1, sl] = outs[c][cfg.pos:]
    return full


# version 06
# speedup vs baseline: 1.2714x; 1.0146x over previous
import numpy as np
import ml_dtypes
import concourse.bass as bass
import concourse.mybir as mybir
from concourse.bass_utils import run_bass_kernel_spmd

F32 = mybir.dt.float32
BF16 = mybir.dt.bfloat16
AF = mybir.ActivationFunctionType
ALU = mybir.AluOpType
AX = mybir.AxisListType
BF16NP = ml_dtypes.bfloat16


class Cfg:
    def __init__(self, S=8192, phases=None):
        self.R = 8
        self.B = 2
        self.S = S
        self.D = 1024
        self.H = 16
        self.dh = 64
        self.DM = 512
        self.DI = 1024
        self.DS = 16
        self.DTR = 32
        self.NL = 4
        self.CK = 4
        self.pos = S // self.R
        self.TL = self.B * self.pos
        self.T = self.B * S
        self.TCH = min(1024, self.pos)
        self.NT_LOC = self.TL // 128
        self.MB = 3 + S
        self.phases = phases if phases is not None else [
            "p0", "attn_a", "ffn1", "mamba", "ffn2", "out"]

    def mcol0(self, b):
        return b * self.MB + 3


def head_perm(cfg):
    p = []
    for half in (0, 1):
        for hp in (0, 1):
            for i in range(32):
                p.append(hp * 64 + half * 32 + i)
    return np.array(p)


def build_aux(cfg):
    S, dh = cfg.S, cfg.dh
    aux = {}
    aux["ident"] = np.eye(128, dtype=BF16NP)
    half = dh // 2
    theta = 1.0 / (1000000.0 ** (2 * np.arange(half) / dh))
    ang = np.arange(S)[None, :] * theta[:, None]
    cos, sin = np.cos(ang).astype(np.float32), np.sin(ang).astype(np.float32)
    ropeK = np.concatenate([np.tile(cos, (2, 1)), np.tile(sin, (2, 1))], 0)
    aux["ropeK"] = ropeK.astype(BF16NP)
    aux["ropeQ"] = (ropeK * 0.125).astype(BF16NP)
    hmp = np.zeros((128, 2), np.float32)
    perm = head_perm(cfg)
    for j in range(128):
        hmp[j, 0 if perm[j] < 64 else 1] = 1.0
    aux["headmask_pi"] = hmp.astype(BF16NP)
    hm = np.zeros((128, 2), np.float32)
    hm[:64, 0] = 1.0
    hm[64:, 1] = 1.0
    aux["headmask"] = hm.astype(BF16NP)
    TCH = cfg.TCH
    u = np.arange(-(TCH - 128), TCH)
    aux["band"] = np.where(u[None, :] >= np.arange(128)[:, None],
                           0.0, -1e30).astype(BF16NP)
    aux["ones_1x128_f32"] = np.ones((1, 128), np.float32)
    o4 = np.zeros((128, 128), np.float32)
    o4[0::32, :] = 1.0
    aux["ones4"] = o4
    aux["ones_1x64_f32"] = np.ones((1, 64), np.float32)
    aux["ones_col_f32"] = np.ones((128, 1), np.float32)
    aux["ones_1x128_bf"] = np.ones((1, 128), BF16NP)
    aux["ones_col_bf"] = np.ones((128, 1), BF16NP)
    aux["cbias"] = np.tile(np.array([[1e-5, 1e-6, 1.0, 0.0]], np.float32),
                           (128, 1))
    aux["ones_dram_bf"] = np.ones((S,), BF16NP)
    return aux


def shard_inputs(cfg, inp):
    aux = build_aux(cfg)
    perm = head_perm(cfg)
    x = np.asarray(inp["L0_attn_a_x"])
    maps = []
    for c in range(cfg.R):
        m = dict(aux)
        sl = slice(cfg.pos * c, cfg.pos * (c + 1))
        m["x_in"] = np.ascontiguousarray(
            np.concatenate([x[0, sl], x[1, sl]], 0)).astype(np.float32)
        lcols = np.arange(2 * c * 64, 2 * c * 64 + 128)
        g = np.asarray(inp["L0_attn_a_norm_scale"])
        m["wqa"] = np.ascontiguousarray(
            (np.asarray(inp["L0_attn_a_Wq"]) * g[:, None])[:, lcols[perm]])
        m["wka"] = np.ascontiguousarray(
            (np.asarray(inp["L0_attn_a_Wk"]) * g[:, None])[:, lcols[perm]])
        m["wva"] = np.ascontiguousarray(
            (np.asarray(inp["L0_attn_a_Wv"]) * g[:, None])[:, lcols])
        m["woa"] = np.ascontiguousarray(np.asarray(inp["L0_attn_a_Wo"])[lcols, :])
        m["wqf"] = np.ascontiguousarray(np.asarray(inp["L0_ffn_Wq"])[:, lcols])
        m["wkf"] = np.ascontiguousarray(np.asarray(inp["L0_ffn_Wk"])[:, lcols])
        m["wvf"] = np.ascontiguousarray(np.asarray(inp["L0_ffn_Wv"])[:, lcols])
        m["wof"] = np.ascontiguousarray(np.asarray(inp["L0_ffn_Wo"])[lcols, :])
        m["wff"] = np.ascontiguousarray(np.asarray(inp["L0_ffn_Wff"]))
        dsl = slice(128 * c, 128 * (c + 1))
        m["lin1_w"] = np.ascontiguousarray(np.asarray(inp["L0_attn_b_lin1_w"]))
        m["lin1_b"] = np.asarray(inp["L0_attn_b_lin1_b"]).reshape(1, -1)
        m["lin2_w"] = np.ascontiguousarray(np.asarray(inp["L0_attn_b_lin2_w"]))
        m["lin2_b"] = np.asarray(inp["L0_attn_b_lin2_b"]).reshape(1, -1)
        m["norm_w"] = np.ascontiguousarray(np.asarray(inp["L0_attn_b_norm_w"]))
        iw = np.asarray(inp["L0_attn_b_in_w"])
        m["inw"] = np.ascontiguousarray(
            np.concatenate([iw[:, :, dsl],
                            iw[:, :, cfg.DI + 128 * c: cfg.DI + 128 * (c + 1)]], -1))
        m["convw"] = np.ascontiguousarray(np.asarray(inp["L0_attn_b_conv_w"])[:, dsl, :])
        m["convb"] = np.ascontiguousarray(np.asarray(inp["L0_attn_b_conv_b"])[:, dsl])
        m["xprojw"] = np.ascontiguousarray(np.asarray(inp["L0_attn_b_xproj_w"])[:, dsl, :])
        m["dtw"] = np.ascontiguousarray(np.asarray(inp["L0_attn_b_dt_w"])[:, :, dsl])
        m["dtb"] = np.ascontiguousarray(np.asarray(inp["L0_attn_b_dt_b"])[:, dsl])
        m["alog"] = np.ascontiguousarray(np.asarray(inp["L0_attn_b_A_log"])[:, dsl, :])
        m["dpv"] = np.ascontiguousarray(np.asarray(inp["L0_attn_b_Dp"])[:, dsl])
        m["outw"] = np.ascontiguousarray(np.asarray(inp["L0_attn_b_out_w"])[:, dsl, :])
        maps.append(m)
    return maps



ENGS = ("tensor", "vector", "scalar", "gpsimd", "sync")

GRAN = {"ar0": 1024, "ar1": 1024, "ar2": 1024, "ar3": 1024,
        "E": 512, "F": 512, "G": 32, "W0": 128, "W1": 128,
        "psA": 512, "psB": 512, "psC": 512, "psT": 1024}


def keys(name, lo=None, hi=None):
    if lo is None:
        return ((name, 0),)
    gw = GRAN.get(name, 1 << 30)
    return tuple((name, g) for g in range(lo // gw, (hi + gw - 1) // gw))


class _Op:
    __slots__ = ("idx", "eng", "fn", "deps", "kind", "marked", "inc_no",
                 "waits", "clock", "ccsem", "tag")

    def __init__(self, idx, eng, fn, kind, tag):
        self.idx, self.eng, self.fn, self.kind, self.tag = idx, eng, fn, kind, tag
        self.deps = []
        self.marked = False
        self.inc_no = None
        self.waits = None
        self.clock = None
        self.ccsem = None


class Prog:
    def __init__(self):
        self.ops = []
        self.last_w = {}
        self.readers = {}
        self.n_cc = 0

    def op(self, eng, fn, reads=(), writes=(), kind="c", tag=""):
        o = _Op(len(self.ops), eng, fn, kind, tag)
        dep = {}
        for k in reads:
            w = self.last_w.get(k)
            if w is not None:
                dep[w.idx] = w
        for k in writes:
            w = self.last_w.get(k)
            if w is not None:
                dep[w.idx] = w
            for rd in self.readers.get(k, ()):
                dep[rd.idx] = rd
        o.deps = [d for d in dep.values()
                  if not (d.eng == "tensor" and eng == "tensor"
                          and d.kind == "c" and kind == "c")]
        for k in reads:
            self.readers.setdefault(k, []).append(o)
        for k in writes:
            self.last_w[k] = o
            self.readers[k] = []
        if kind == "cc":
            o.ccsem = self.n_cc
            self.n_cc += 1
        self.ops.append(o)
        return o

    def finalize_and_emit(self, block, sems):
        for o in self.ops:
            if o.kind == "d":
                o.marked = True
            for d in o.deps:
                d.marked = True
        counters = {}
        for o in self.ops:
            if not o.marked:
                continue
            if o.kind == "cc":
                o.inc_no = 1
            else:
                key = (o.eng, o.kind)
                counters[key] = counters.get(key, 0) + 1
                o.inc_no = counters[key]

        def semkey(o):
            return ("cc", o.ccsem) if o.kind == "cc" else (o.eng, o.kind)

        def val(o):
            return o.inc_no * (16 if o.kind == "d" else 1)

        last_on_eng = {}
        for o in self.ops:
            prev = last_on_eng.get(o.eng)
            base = dict(prev.clock) if prev is not None else {}
            need = {}
            for d in o.deps:
                sk, v = semkey(d), val(d)
                if need.get(sk, -1) < v:
                    need[sk] = v
                for sk2, v2 in d.clock.items():
                    if base.get(sk2, -1) < v2:
                        base[sk2] = v2
            o.waits = {sk: v for sk, v in need.items() if base.get(sk, 0) < v}
            clk = base
            for sk, v in need.items():
                if clk.get(sk, 0) < v:
                    clk[sk] = v
            o.clock = clk
            last_on_eng[o.eng] = o

        by_eng = {e: [] for e in ENGS}
        for o in self.ops:
            by_eng[o.eng].append(o)

        def emit_stream(eng_name):
            ops = by_eng[eng_name]

            def go(eng):
                for o in ops:
                    for sk in sorted(o.waits, key=str):
                        eng.wait_ge(sems[sk], o.waits[sk])
                    inst = o.fn(eng)
                    if o.kind == "cc":
                        inst.then_inc(sems[("cc", o.ccsem)])
                    elif o.marked:
                        if o.kind == "d":
                            inst.then_inc(sems[(o.eng, "d")], 16)
                        else:
                            inst.then_inc(sems[(o.eng, "c")], 1)
            return go

        for e in ENGS:
            if by_eng[e]:
                getattr(block, e)(emit_stream(e))


def ceil_div(a, b):
    return (a + b - 1) // b


def as_col(ap1d):
    return ap1d.rearrange("(a b) -> a b", b=1)


def as_row(ap1d):
    return ap1d.rearrange("(a b) -> a b", a=1)



class K:
    def __init__(self, cfg):
        self.cfg = cfg
        self.nc = bass.Bass("TRN2", target_bir_lowering=False, debug=False)
        self.sb = {}
        self.sbuf_bytes = 0

    def dtn(self, name, shape, dtype, kind="Internal", shared=False):
        if shared:
            return self.nc.dram_tensor(name, list(shape), dtype,
                                       addr_space="Shared")
        if kind == "Internal":
            return self.nc.dram_tensor(name, list(shape), dtype)
        return self.nc.dram_tensor(name, list(shape), dtype, kind=kind)

    def st(self, name, shape, dtype):
        t = self.ctx.enter_context(self.nc.sbuf_tensor(name, list(shape), dtype))
        self.sb[name] = t
        self.sbuf_bytes += int(np.prod(shape[1:])) * mybir.dt.size(dtype)
        return t

    def pt(self, name, shape, dtype):
        t = self.ctx.enter_context(self.nc.psum_tensor(name, list(shape), dtype))
        self.sb[name] = t
        return t

    def mm(self, out, lhsT, rhs, start, stop, reads, writes, tag="mm", tp=None):
        self.p.op("tensor",
                  lambda e, o=out, l=lhsT, r=rhs, s1=start, s2=stop, t=tp:
                  e.matmul(o, l, r, start=s1, stop=s2, tile_position=t),
                  reads, writes, tag=tag)

    def trans(self, out, in_, reads, writes, tag="tr"):
        ident = self.sb["c_ident"][:]
        self.p.op("tensor",
                  lambda e, o=out, i=in_, d=ident: e.transpose(o, i, d),
                  tuple(reads) + keys("c_ident"), writes, tag=tag)

    def cp(self, eng, dst, src, reads, writes, tag="cp"):
        if eng == "scalar":
            self.p.op("scalar", lambda e, d=dst, s=src: e.copy(d, s),
                      reads, writes, tag=tag)
        else:
            self.p.op("vector", lambda e, d=dst, s=src: e.tensor_copy(d, s),
                      reads, writes, tag=tag)

    def act(self, dst, src, func, reads, writes, bias=None, scale=None,
            accum=None, tag="act"):
        kw = {}
        if bias is not None:
            kw["bias"] = bias
        if scale is not None:
            kw["scale"] = scale
        if accum is not None:
            kw["accum_out"] = accum
        self.p.op("scalar",
                  lambda e, d=dst, s=src, f=func, kw=kw: e.activation(d, s, f, **kw),
                  reads, writes, tag=tag)

    def tt(self, op_alu, dst, a, b, reads, writes, tag="tt", eng="vector"):
        self.p.op(eng, lambda e, d=dst, x=a, y=b, o=op_alu:
                  e.tensor_tensor(out=d, in0=x, in1=y, op=o), reads, writes, tag=tag)

    def ts(self, dst, a, s1, op0, reads, writes, s2=None, op1=None, tag="ts"):
        def f(e, d=dst, x=a, v1=s1, v2=s2, o0=op0, o1=op1):
            if o1 is None:
                return e.tensor_scalar(out=d, in0=x, scalar1=v1, scalar2=None, op0=o0)
            return e.tensor_scalar(out=d, in0=x, scalar1=v1, scalar2=v2,
                                   op0=o0, op1=o1)
        self.p.op("vector", f, reads, writes, tag=tag)

    def stt(self, dst, in0, scalar, in1, op0, op1, reads, writes, tag="stt"):
        self.p.op("vector", lambda e, d=dst, a=in0, s=scalar, b=in1, o0=op0, o1=op1:
                  e.scalar_tensor_tensor(out=d, in0=a, scalar=s, in1=b, op0=o0, op1=o1),
                  reads, writes, tag=tag)

    def scan_op(self, dst, d0, d1, init, reads, writes, tag="scan"):
        self.p.op("vector", lambda e, d=dst, a=d0, b=d1, i=init:
                  e.tensor_tensor_scan(out=d, data0=a, data1=b, initial=i,
                                       op0=ALU.mult, op1=ALU.add),
                  reads, writes, tag=tag)

    def red(self, dst, src, op_alu, reads, writes, tag="red"):
        self.p.op("vector", lambda e, d=dst, s=src, o=op_alu:
                  e.tensor_reduce(out=d, in_=s, axis=AX.X, op=o), reads, writes, tag=tag)

    def recip(self, dst, src, reads, writes, tag="recip"):
        self.p.op("vector", lambda e, d=dst, s=src: e.reciprocal(d, s),
                  reads, writes, tag=tag)

    def dma(self, out_ap, in_ap, reads, writes, eng="sync", tag="dma",
            transpose=False):
        if transpose:
            self.p.op(eng, lambda e, o=out_ap, i=in_ap:
                      e.dma_start(out=o, in_=i, transpose=True),
                      reads, writes, kind="d", tag=tag)
        else:
            self.p.op(eng, lambda e, o=out_ap, i=in_ap: e.dma_start(out=o, in_=i),
                      reads, writes, kind="d", tag=tag)

    def coll(self, kind, op_alu, in_ap, out_ap, reads, writes, tag="cc"):
        groups = [list(range(self.cfg.R))]
        self.p.op("gpsimd", lambda e, k=kind, o=op_alu, i=in_ap, u=out_ap, g=groups:
                  e.collective_compute(k, o, replica_groups=g, ins=[i], outs=[u]),
                  reads, writes, kind="cc", tag=tag)

    def memset(self, ap, v, reads, writes, tag="memset", eng="gpsimd"):
        self.p.op(eng, lambda e, a=ap, vv=v: e.memset(a, vv), reads, writes, tag=tag)


    def dma_wfm(self, dst_slice, src_ap, mcols, reads, writes, tag):
        self.dma(dst_slice.rearrange("p (c m) -> p c m", m=mcols),
                 src_ap.rearrange("(c p) m -> p c m", p=128),
                 reads=reads, writes=writes, tag=tag)

    def build(self):
        import contextlib
        cfg = self.cfg
        nc = self.nc
        self.ctx = contextlib.ExitStack()
        S, B, D, TL, T = cfg.S, cfg.B, cfg.D, cfg.TL, cfg.T

        ein = {}

        def EIN(name, shape, dtype=F32):
            ein[name] = self.dtn(name, shape, dtype, kind="ExternalInput").ap()

        EIN("x_in", [TL, D])
        for nm, shp in [("wqa", [D, 128]), ("wka", [D, 128]), ("wva", [D, 128]),
                        ("woa", [128, D]), ("wqf", [D, 128]), ("wkf", [D, 128]),
                        ("wvf", [D, 128]), ("wof", [128, D]), ("wff", [D, D]),
                        ("lin1_w", [D, cfg.DM]), ("lin1_b", [1, cfg.DM]),
                        ("lin2_w", [cfg.DM, D]), ("lin2_b", [1, D]),
                        ("norm_w", [cfg.NL, cfg.DM]), ("inw", [cfg.NL, cfg.DM, 256]),
                        ("convw", [cfg.NL, 128, cfg.CK]), ("convb", [cfg.NL, 128]),
                        ("xprojw", [cfg.NL, 128, 64]), ("dtw", [cfg.NL, cfg.DTR, 128]),
                        ("dtb", [cfg.NL, 128]), ("alog", [cfg.NL, 128, cfg.DS]),
                        ("dpv", [cfg.NL, 128]), ("outw", [cfg.NL, 128, cfg.DM])]:
            EIN(nm, shp)
        for nm, shp, dt_ in [("ident", [128, 128], BF16), ("ropeK", [128, S], BF16),
                             ("ropeQ", [128, S], BF16), ("headmask_pi", [128, 2], BF16),
                             ("headmask", [128, 2], BF16),
                             ("band", [128, 2 * cfg.TCH - 128], BF16),
                             ("ones_1x128_f32", [1, 128], F32),
                             ("ones4", [128, 128], F32),
                             ("ones_1x64_f32", [1, 64], F32),
                             ("ones_col_f32", [128, 1], F32),
                             ("ones_1x128_bf", [1, 128], BF16),
                             ("ones_col_bf", [128, 1], BF16),
                             ("cbias", [128, 4], F32),
                             ("ones_dram_bf", [S], BF16)]:
            EIN(nm, shp, dt_)
        self.ein = ein
        out_t = self.dtn("out", [TL, D], F32, kind="ExternalOutput").ap()

        wbf = {}
        for nm, shp in [("wqa", [D, 128]), ("wka", [D, 128]), ("wva", [D, 128]),
                        ("woa", [128, D]), ("wqf", [D, 128]), ("wkf", [D, 128]),
                        ("wvf", [D, 128]), ("wof", [128, D]), ("wff", [D, D]),
                        ("lin1_w", [D, cfg.DM]), ("lin2_w", [cfg.DM, D])]:
            wbf[nm] = self.dtn("wbf_" + nm, shp, BF16).ap()
        for l in range(cfg.NL):
            wbf[f"inw{l}"] = self.dtn(f"wbf_inw{l}", [cfg.DM, 256], BF16).ap()
            wbf[f"xprojw{l}"] = self.dtn(f"wbf_xprojw{l}", [128, 64], BF16).ap()
            wbf[f"dtw{l}"] = self.dtn(f"wbf_dtw{l}", [cfg.DTR, 128], BF16).ap()
            wbf[f"outw{l}"] = self.dtn(f"wbf_outw{l}", [128, cfg.DM], BF16).ap()
        self.wbf = wbf

        dd = {}
        dd["ag_xn_in"] = self.dtn("ag_xn_in", [D, TL], BF16).ap()
        dd["ag_xn_out"] = self.dtn("ag_xn_out", [cfg.R * D, TL], BF16, shared=True).ap()
        dd["ag_hr_in"] = self.dtn("ag_hr_in", [cfg.DM, TL], BF16).ap()
        dd["ag_hr_out"] = self.dtn("ag_hr_out", [cfg.R * cfg.DM, TL], BF16, shared=True).ap()
        dd["zpart"] = self.dtn("zpart", [T, D], F32).ap()
        dd["zred"] = self.dtn("zred", [TL, D], F32).ap()
        dd["zpartM"] = self.dtn("zpartM", [T, cfg.DM], F32).ap()
        dd["zredM"] = self.dtn("zredM", [TL, cfg.DM], F32).ap()
        dd["dblpart"] = self.dtn("dblpart", [64, T], F32).ap()
        dd["dblfull"] = self.dtn("dblfull", [64, T], F32, shared=True).ap()
        dd["QA"] = self.dtn("QA", [4, 65, S], BF16).ap()
        dd["KA"] = self.dtn("KA", [4, 65, S], BF16).ap()
        dd["VA"] = self.dtn("VA", [4, S, 64], BF16).ap()
        dd["norms_a"] = self.dtn("norms_a", [4, 2, S], F32).ap()
        dd["QF"] = self.dtn("QF", [B, 128, S], BF16).ap()
        dd["KF"] = self.dtn("KF", [B, 128, S], BF16).ap()
        dd["VF"] = self.dtn("VF", [B, S, 128], BF16).ap()
        dd["norms_f"] = self.dtn("norms_f", [B, 2, 2, S], F32).ap()
        dd["negMk"] = self.dtn("negMk", [B, 2, S], BF16).ap()
        dd["CONTR"] = self.dtn("CONTR", [B, 128, S], BF16).ap()
        dd["ZSP"] = self.dtn("ZSP", [128, T], BF16).ap()
        dd["USP"] = self.dtn("USP", [128, T], BF16).ap()
        dd["x_in"] = ein["x_in"]
        dd["xcur"] = self.dtn("xcur", [TL, D], F32).ap()
        dd["out"] = out_t
        self.d = dd

        AW = 16448
        for i in range(4):
            self.st(f"ar{i}", [128, AW], BF16)
        self.st("E", [128, 8192], F32)
        self.st("F", [128, 4096], F32)
        self.st("G", [128, 2304], F32)
        self.st("W0", [128, 4096], BF16)
        self.st("W1", [128, 2048], BF16)
        self.st("c_ident", [128, 128], BF16)
        self.st("c_hmpi", [128, 2], BF16)
        self.st("c_hm", [128, 2], BF16)
        self.st("c_band", [128, 2 * cfg.TCH - 128], BF16)
        self.st("c_o1x128f", [1, 128], F32)
        self.st("c_o4", [128, 128], F32)
        self.st("c_o1x64f", [1, 64], F32)
        self.st("c_ocolf", [128, 1], F32)
        self.st("c_o1x128b", [1, 128], BF16)
        self.st("c_ocolb", [128, 1], BF16)
        self.st("c_cb", [128, 4], F32)
        self.pt("psA", [128, 2048], F32)
        self.pt("psB", [128, 1024], F32)
        self.pt("psC", [128, 512], F32)
        self.pt("psT", [128, 1024], BF16)

        self.p = Prog()
        ph = cfg.phases
        if "p0" in ph:
            self.phase_p0()
        if "attn_a" in ph:
            self.phase_ln_ag(extra_rms=True, tag="ln1", do_ag=True, fm_ar="ar0")
            self.phase_attn_a_proj()
            self.phase_attn_a_attn()
            self.phase_out_rs(("W0", 3072), "woa")
            self.phase_residual("woa")
        for p_ in ph:
            if p_.startswith("ffn1"):
                stop = p_.split(":")[1] if ":" in p_ else None
                self.phase_ffn(0, stop=stop)
        if "mamba" in ph:
            self.phase_mamba()
        if "ffn2" in ph:
            self.phase_ffn(1)
        if "out" in ph:
            self.phase_out()

        sems = {}
        for e in ENGS:
            for knd in ("c", "d"):
                sems[(e, knd)] = self.ctx.enter_context(nc.semaphore(f"s_{e}_{knd}"))
        for i in range(self.p.n_cc):
            sems[("cc", i)] = self.ctx.enter_context(nc.semaphore(f"s_cc_{i}"))
        with nc.Block() as block:
            self.p.finalize_and_emit(block, sems)
        self.ctx.close()
        return nc

    def phase_p0(self):
        cfg, sb, ein = self.cfg, self.sb, self.ein
        for nm, dst in [("ident", "c_ident"), ("headmask_pi", "c_hmpi"),
                        ("headmask", "c_hm"), ("band", "c_band"),
                        ("ones_1x128_f32", "c_o1x128f"), ("ones4", "c_o4"),
                        ("ones_1x64_f32", "c_o1x64f"),
                        ("ones_col_f32", "c_ocolf"), ("ones_1x128_bf", "c_o1x128b"),
                        ("ones_col_bf", "c_ocolb"), ("cbias", "c_cb")]:
            self.dma(sb[dst][:], ein[nm], reads=(), writes=keys(dst), tag="cload")
        self.dma(self.d["xcur"], self.d["x_in"], reads=(), writes=keys("xcur"),
                 tag="xload")
        F, G, ar0 = sb["F"], sb["G"], sb["ar0"]

        def cast(ap_in, ap_out, rows, cols, okey, scale_rows=None):
            for i in range(ceil_div(rows, 128)):
                r0, rn = i * 128, min(128, rows - i * 128)
                for j in range(ceil_div(cols, 2048)):
                    c0, cn = j * 2048, min(2048, cols - j * 2048)
                    pp = (i + j) % 2
                    kf = keys("F", pp * 2048, pp * 2048 + cn)
                    ka = keys("ar0", pp * 2048, pp * 2048 + cn)
                    self.dma(F[0:rn, pp * 2048:pp * 2048 + cn],
                             ap_in[r0:r0 + rn, c0:c0 + cn],
                             reads=(), writes=kf, tag="wc_in")
                    if scale_rows is not None:
                        ks = keys("G", 2176 + pp, 2177 + pp)
                        self.dma(G[0:rn, 2176 + pp:2177 + pp],
                                 as_col(scale_rows[r0:r0 + rn]),
                                 reads=(), writes=ks, tag="wc_sc")
                        self.ts(F[0:rn, pp * 2048:pp * 2048 + cn],
                                F[0:rn, pp * 2048:pp * 2048 + cn],
                                G[0:rn, 2176 + pp:2177 + pp], ALU.mult,
                                reads=kf + ks, writes=kf, tag="wc_scale")
                    self.cp("vector", ar0[0:rn, pp * 2048:pp * 2048 + cn],
                            F[0:rn, pp * 2048:pp * 2048 + cn],
                            reads=kf, writes=ka, tag="wc_cast")
                    self.dma(ap_out[r0:r0 + rn, c0:c0 + cn],
                             ar0[0:rn, pp * 2048:pp * 2048 + cn],
                             reads=ka, writes=((okey, 0),), tag="wc_out")

        wb = self.wbf
        for nm, rows, cols in [("wqa", cfg.D, 128), ("wka", cfg.D, 128),
                               ("wva", cfg.D, 128), ("woa", 128, cfg.D),
                               ("wqf", cfg.D, 128), ("wkf", cfg.D, 128),
                               ("wvf", cfg.D, 128), ("wof", 128, cfg.D),
                               ("wff", cfg.D, cfg.D), ("lin1_w", cfg.D, cfg.DM),
                               ("lin2_w", cfg.DM, cfg.D)]:
            cast(ein[nm], wb[nm], rows, cols, "wbf_" + nm)
        for l in range(cfg.NL):
            cast(ein["inw"][l], wb[f"inw{l}"], cfg.DM, 256, f"wbf_inw{l}",
                 scale_rows=ein["norm_w"][l, :])
            cast(ein["xprojw"][l], wb[f"xprojw{l}"], 128, 64, f"wbf_xprojw{l}")
            cast(ein["dtw"][l], wb[f"dtw{l}"], cfg.DTR, 128, f"wbf_dtw{l}")
            cast(ein["outw"][l], wb[f"outw{l}"], 128, cfg.DM, f"wbf_outw{l}")

    def phase_ln_ag(self, extra_rms, tag, do_ag, fm_ar):
        cfg, sb = self.cfg, self.sb
        D, NT, TL = cfg.D, cfg.NT_LOC, cfg.TL
        F, G, W1, E = sb["F"], sb["G"], sb["W1"], sb["E"]
        arF = sb[fm_ar]
        psT = sb["psT"]
        nch = D // 128
        for i in range(NT):
            pp = i % 2
            xo = pp * 2048
            gs = pp * 16
            kx = keys("F", xo, xo + 1024)
            kxc = keys("F", xo + 1024, xo + 2048)
            kst = keys("G", gs, gs + 16)
            ke = keys("E", pp * 1024, pp * 1024 + 1024)
            self.dma(F[:, xo:xo + 1024], self.d["xcur"][128 * i:128 * (i + 1), :],
                     reads=keys("xcur"), writes=kx, tag=tag + "_in")
            self.red(G[:, gs:gs + 1], F[:, xo:xo + 1024], ALU.add,
                     reads=kx, writes=kst, tag=tag + "_sum")
            self.ts(G[:, gs + 1:gs + 2], G[:, gs:gs + 1], 1.0 / D, ALU.mult,
                    reads=kst, writes=kst, tag=tag + "_mean")
            self.ts(F[:, xo + 1024:xo + 2048], F[:, xo:xo + 1024],
                    G[:, gs + 1:gs + 2], ALU.subtract,
                    reads=kx + kst, writes=kxc, tag=tag + "_sub")
            self.act(E[:, pp * 1024:pp * 1024 + 1024], F[:, xo + 1024:xo + 2048],
                     AF.Square, reads=kxc, writes=ke, tag=tag + "_sq")
            self.red(G[:, gs + 2:gs + 3], E[:, pp * 1024:pp * 1024 + 1024],
                     ALU.add, reads=ke, writes=kst, tag=tag + "_var")
            self.act(G[:, gs + 3:gs + 4], G[:, gs + 2:gs + 3], AF.Sqrt,
                     reads=kst + keys("c_cb"), writes=kst, scale=1.0 / D,
                     bias=self.sb["c_cb"][:, 0:1], tag=tag + "_std")
            self.recip(G[:, gs + 4:gs + 5], G[:, gs + 3:gs + 4],
                       reads=kst, writes=kst, tag=tag + "_rstd")
            self.ts(F[:, xo:xo + 1024], F[:, xo + 1024:xo + 2048],
                    G[:, gs + 4:gs + 5], ALU.mult,
                    reads=kxc + kst, writes=kx, tag=tag + "_norm")
            if extra_rms:
                self.act(E[:, pp * 1024:pp * 1024 + 1024], F[:, xo:xo + 1024],
                         AF.Square, reads=kx, writes=ke, tag=tag + "_sq2")
                self.red(G[:, gs + 5:gs + 6], E[:, pp * 1024:pp * 1024 + 1024],
                         ALU.add, reads=ke, writes=kst, tag=tag + "_v2")
                self.act(G[:, gs + 6:gs + 7], G[:, gs + 5:gs + 6], AF.Sqrt,
                         reads=kst + keys("c_cb"), writes=kst, scale=1.0 / D,
                         bias=self.sb["c_cb"][:, 1:2], tag=tag + "_std2")
                self.recip(G[:, gs + 7:gs + 8], G[:, gs + 6:gs + 7],
                           reads=kst, writes=kst, tag=tag + "_rstd2")
                self.ts(F[:, xo:xo + 1024], F[:, xo:xo + 1024],
                        G[:, gs + 7:gs + 8], ALU.mult,
                        reads=kx + kst, writes=kx, tag=tag + "_norm2")
            kbf = keys("W1", pp * 1024, pp * 1024 + 1024)
            self.cp("vector", W1[:, pp * 1024:pp * 1024 + 1024], F[:, xo:xo + 1024],
                    reads=kx, writes=kbf, tag=tag + "_cast")
            for j in range(nch):
                kT = keys("psT", 128 * (j % 8), 128 * (j % 8) + 128)
                self.trans(psT[:, 128 * (j % 8):128 * (j % 8) + 128],
                           W1[:, pp * 1024 + 128 * j:pp * 1024 + 128 * (j + 1)],
                           reads=kbf, writes=kT, tag=tag + "_T")
                self.cp("vector",
                        arF[:, TL * j + 128 * i:TL * j + 128 * (i + 1)],
                        psT[:, 128 * (j % 8):128 * (j % 8) + 128],
                        reads=kT, writes=keys(fm_ar, TL * j + 128 * i,
                                              TL * j + 128 * (i + 1)),
                        tag=tag + "_Tev")
        if do_ag:
            for j in range(nch):
                self.dma(self.d["ag_xn_in"][128 * j:128 * (j + 1), :],
                         arF[:, TL * j:TL * j + TL],
                         reads=keys(fm_ar, TL * j, TL * j + TL),
                         writes=(("ag_xn_in", j),), tag=tag + "_agin")
            self.coll("AllGather", ALU.bypass, self.d["ag_xn_in"],
                      self.d["ag_xn_out"],
                      reads=tuple(("ag_xn_in", j) for j in range(nch)),
                      writes=(("ag_xn_out", 0),), tag=tag + "_AG")

    def phase_attn_a_proj(self):
        cfg, sb, d = self.cfg, self.sb, self.d
        S, D = cfg.S, cfg.D
        nch = D // 128
        W0, W1, G, F = sb["W0"], sb["W1"], sb["G"], sb["F"]
        psA, psB, psT = sb["psA"], sb["psB"], sb["psT"]
        CL = cfg.TL
        half = CL // 2
        for nm, off in [("wqa", 0), ("wka", 1024), ("wva", 2048), ("woa", 3072)]:
            if nm == "woa":
                self.dma(W0[:, off:off + 1024], self.wbf[nm],
                         reads=(("wbf_" + nm, 0),),
                         writes=keys("W0", off, off + 1024), tag="aw_load")
            else:
                self.dma_wfm(W0[:, off:off + 1024], self.wbf[nm], 128,
                             reads=(("wbf_" + nm, 0),),
                             writes=keys("W0", off, off + 1024), tag="aw_load")
        for u in range(4):
            self.dma(d["KA"][u, 64, :], self.ein["ones_dram_bf"],
                     reads=(), writes=(("KA", u),), tag="ka_ones")

        for r in range(cfg.R):
            pp = r % 2
            arn = f"ar{pp}"
            arin = sb[arn]
            kin = keys(arn, 0, nch * CL)
            self.dma(arin[:, 0:nch * CL].rearrange("p (c m) -> p c m", m=CL),
                     d["ag_xn_out"][128 * nch * r:128 * nch * (r + 1), :]
                     .rearrange("(c p) m -> p c m", p=128),
                     reads=(("ag_xn_out", 0),), writes=kin, tag="aproj_in")
            kRT = keys("ar3", 0, 2048)
            self.dma(sb["ar3"][:, 0:1024],
                     self.ein["ropeQ"][:, cfg.pos * r:cfg.pos * r + cfg.pos],
                     reads=(), writes=kRT, tag="rtq")
            self.dma(sb["ar3"][:, 1024:2048],
                     self.ein["ropeK"][:, cfg.pos * r:cfg.pos * r + cfg.pos],
                     reads=(), writes=kRT, tag="rtk")

            def project(woff):
                kw = keys("W0", woff, woff + 1024)
                kps = keys("psA", 0, CL)
                for j in range(nch):
                    for c_ in range(CL // 512):
                        self.mm(psA[:, 512 * c_:512 * (c_ + 1)],
                                W0[:, woff + 128 * j:woff + 128 * (j + 1)],
                                arin[:, CL * j + 512 * c_:CL * j + 512 * (c_ + 1)],
                                start=(j == 0), stop=(j == nch - 1),
                                reads=kin + kw, writes=kps, tag="aproj_mm")
                return kps

            def rope(kps, toff):
                tb = sb["ar3"]
                kd = keys("W1", 0, CL)
                for b in range(2):
                    cs = b * half
                    kt = keys("F", 0, 2 * half)
                    self.tt(ALU.mult, F[0:64, 0:half], psA[0:64, cs:cs + half],
                            tb[0:64, toff:toff + half], reads=kps + kRT,
                            writes=kt, tag="rope1")
                    self.tt(ALU.mult, F[0:64, half:2 * half],
                            psA[64:128, cs:cs + half],
                            tb[64:128, toff:toff + half], reads=kps + kRT,
                            writes=kt, tag="rope2")
                    self.tt(ALU.subtract, W1[0:64, cs:cs + half], F[0:64, 0:half],
                            F[0:64, half:2 * half], reads=kt, writes=kd,
                            tag="rope3")
                    self.tt(ALU.mult, F[0:64, 0:half], psA[0:64, cs:cs + half],
                            tb[64:128, toff:toff + half], reads=kps + kRT,
                            writes=kt, tag="rope4")
                    self.tt(ALU.mult, F[0:64, half:2 * half],
                            psA[64:128, cs:cs + half],
                            tb[0:64, toff:toff + half], reads=kps + kRT,
                            writes=kt, tag="rope5")
                    self.tt(ALU.add, W1[64:128, cs:cs + half], F[0:64, 0:half],
                            F[0:64, half:2 * half], reads=kt, writes=kd,
                            tag="rope6")
                return kd

            def norms_store(kd, qk, dst_dram, dst_name):
                ksq = keys("ar2", 0, CL)
                self.act(sb["ar2"][:, 0:CL], W1[:, 0:CL], AF.Square,
                         reads=kd, writes=ksq, tag="an_sq")
                kn = keys("psA", 0, CL)
                for c_ in range(CL // 512):
                    self.mm(psA[0:2, 512 * c_:512 * (c_ + 1)], sb["c_hmpi"][:],
                            sb["ar2"][:, 512 * c_:512 * (c_ + 1)],
                            start=True, stop=True, reads=ksq + keys("c_hmpi"),
                            writes=kn, tag="an_mm")
                kev = keys("F", 2048, 2048 + CL)
                self.cp("vector", F[0:2, 2048:2048 + CL], psA[0:2, 0:CL],
                        reads=kn, writes=kev, tag="an_ev")
                for hp in range(2):
                    for b in range(2):
                        uu = b * 2 + hp
                        self.dma(d["norms_a"][uu, qk, cfg.pos * r:cfg.pos * r + half],
                                 F[hp:hp + 1, 2048 + b * half:2048 + (b + 1) * half],
                                 reads=kev, writes=(("norms_a", uu, qk),),
                                 tag="an_out")
                        for piece, r0 in ((0, 32 * hp), (1, 64 + 32 * hp)):
                            self.dma(dst_dram[uu, 32 * piece:32 * piece + 32,
                                              cfg.pos * r: cfg.pos * r + half],
                                     W1[r0:r0 + 32, b * half:(b + 1) * half],
                                     reads=kd, writes=((dst_name, uu),),
                                     tag="aqk_out")

            kq = project(0)
            kdq = rope(kq, 0)
            norms_store(kdq, 0, d["QA"], "QA")
            kk = project(1024)
            kdk = rope(kk, 1024)
            norms_store(kdk, 1, d["KA"], "KA")
            kv = project(2048)
            kvs = keys("ar2", 4096, 4096 + CL)
            self.cp("scalar", sb["ar2"][:, 4096:4096 + CL], psA[:, 0:CL],
                    reads=kv, writes=kvs, tag="av_ev")
            ntile = CL // 128
            for g in range(ceil_div(ntile, 8)):
                kT = keys("psT", 0, 1024)
                for tl in range(min(8, ntile - 8 * g)):
                    i = 8 * g + tl
                    self.trans(psT[:, 128 * tl:128 * (tl + 1)],
                               sb["ar2"][:, 4096 + 128 * i:4096 + 128 * (i + 1)],
                               reads=kvs, writes=kT, tag="av_T")
                eb = 8192 + 1024 * (g % 2)
                kTe = keys("ar2", eb, eb + 1024)
                self.cp("vector", sb["ar2"][:, eb:eb + 1024], psT[:, 0:1024],
                        reads=kT, writes=kTe, tag="av_Tev")
                for tl in range(min(8, ntile - 8 * g)):
                    i = 8 * g + tl
                    b = i // (ntile // 2)
                    row0 = cfg.pos * r + 128 * (i % (ntile // 2))
                    for hp in range(2):
                        uu = b * 2 + hp
                        self.dma(d["VA"][uu, row0:row0 + 128, :],
                                 sb["ar2"][:, eb + 128 * tl + 64 * hp:
                                           eb + 128 * tl + 64 * (hp + 1)],
                                 reads=kTe, writes=(("VA", uu),), tag="av_out")
        for u in range(4):
            ku = keys("G", 2048, 2080)
            ng = max(1, S // 2048)
            w = min(2048, S)
            for gg in range(ng):
                kg = keys("G", 0, w)
                self.dma(G[0:1, 0:w], d["norms_a"][u, 1, w * gg:w * (gg + 1)],
                         reads=(("norms_a", u, 1),), writes=kg, tag="mk_in")
                self.red(G[0:1, 2048 + gg:2049 + gg], G[0:1, 0:w], ALU.max,
                         reads=kg, writes=ku, tag="mk_red")
            self.red(G[0:1, 2080:2081], G[0:1, 2048:2048 + ng], ALU.max,
                     reads=ku, writes=keys("G", 2080, 2081), tag="mk_red2")
            self.act(G[0:1, 2081:2082], G[0:1, 2080:2081], AF.Sqrt,
                     reads=keys("G", 2080, 2081), writes=keys("G", 2081, 2082),
                     tag="mk_sqrt")
            for r in range(S // 1024):
                km = keys("G", 0, 1024)
                km2 = keys("G", 1024, 2048)
                self.dma(G[0:1, 0:1024],
                         d["norms_a"][u, 0, 1024 * r:1024 * (r + 1)],
                         reads=(("norms_a", u, 0),), writes=km, tag="mr_in")
                self.act(G[0:1, 1024:2048], G[0:1, 0:1024], AF.Sqrt,
                         reads=km, writes=km2, tag="mr_sqrt")
                self.ts(G[0:1, 0:1024], G[0:1, 1024:2048],
                        G[0:1, 2081:2082], ALU.mult,
                        reads=km2 + keys("G", 2081, 2082), writes=km,
                        s2=-1.0, op1=ALU.mult, tag="mr_mul")
                self.cp("vector", W1[0:1, 0:1024], G[0:1, 0:1024],
                        reads=km, writes=keys("W1", 0, 1024), tag="mr_cast")
                self.dma(d["QA"][u, 64, 1024 * r:1024 * (r + 1)],
                         W1[0:1, 0:1024], reads=keys("W1", 0, 1024),
                         writes=(("QA", u),), tag="mr_out")

    def phase_attn_a_attn(self):
        cfg, sb, d = self.cfg, self.sb, self.d
        S, TCH = cfg.S, cfg.TCH
        ar0, ar1, ar2 = sb["ar0"], sb["ar1"], sb["ar2"]
        psA, psB, G, E = sb["psA"], sb["psB"], sb["G"], sb["E"]
        nstile = S // 128
        nchunk = S // TCH
        spt = TCH // 128
        VOFF, POFF, COFF = 0, 4160, 10240
        for u in range(4):
            b, hp = u // 2, u % 2
            kq = keys("ar0", 0, S)
            kk = keys("ar1", 0, S)
            kv = keys("ar2", VOFF, VOFF + 65 * nstile)
            self.dma(ar0[0:65, 0:S], d["QA"][u], reads=(("QA", u),),
                     writes=kq, tag="aa_lq")
            self.dma(ar1[0:65, 0:S], d["KA"][u], reads=(("KA", u),),
                     writes=kk, tag="aa_lk")
            self.dma(ar2[:, VOFF:VOFF + 65 * nstile].rearrange(
                         "p (i c) -> p i c", c=65)[:, :, 0:64],
                     d["VA"][u].rearrange("(i p) c -> p i c", p=128),
                     reads=(("VA", u),), writes=kv, tag="aa_lv")
            self.memset(ar2[:, VOFF:VOFF + 65 * nstile].rearrange(
                            "p (i c) -> p i c", c=65)[:, :, 64:65], 1.0,
                        reads=(), writes=kv, tag="aa_vone", eng="vector")
            psC = sb["psC"]
            psTf = sb["psT"][:].bitcast(F32)
            for r in range(nchunk):
                nj = spt * (r + 1)
                kout = keys("psB", 0, TCH)

                def aa_pv(j):
                    po = 1024 * (j % 3)
                    kP = keys("ar2", POFF + po, POFF + po + 1024)
                    for c_ in range(TCH // 512):
                        self.mm(psB[0:65, 512 * c_:512 * (c_ + 1)],
                                ar2[:, VOFF + 65 * j:VOFF + 65 * (j + 1)],
                                ar2[:, POFF + po + 512 * c_:
                                    POFF + po + 512 * (c_ + 1)],
                                start=(j == 0), stop=(j == nj - 1),
                                reads=kP + kv, writes=kout, tag="aa_pv")

                for j in range(nj):
                    po = 1024 * (j % 3)
                    r3 = j % 3
                    if r3 < 2:
                        halves = [(psA[:, 1024 * r3:1024 * r3 + 512],
                                   keys("psA", 1024 * r3, 1024 * r3 + 512)),
                                  (psA[:, 1024 * r3 + 512:1024 * (r3 + 1)],
                                   keys("psA", 1024 * r3 + 512,
                                        1024 * (r3 + 1)))]
                    else:
                        halves = [(psC[:, 0:512], keys("psC", 0, 512)),
                                  (psTf[:, 0:512], keys("psT", 0, 1024))]
                    ksc = halves[0][1] + halves[1][1]
                    for c_ in range(TCH // 512):
                        self.mm(halves[c_][0],
                                ar1[0:65, 128 * j:128 * (j + 1)],
                                ar0[0:65, TCH * r + 512 * c_:
                                    TCH * r + 512 * (c_ + 1)],
                                start=True, stop=True,
                                reads=kq + kk, writes=halves[c_][1],
                                tag="aa_mm1")
                    o = j - spt * r
                    if r3 < 2:
                        if o >= 0:
                            stc = (TCH - 128) - 128 * o
                            self.tt(ALU.add, psA[:, 1024 * r3:1024 * (r3 + 1)],
                                    psA[:, 1024 * r3:1024 * (r3 + 1)],
                                    sb["c_band"][:, stc:stc + TCH],
                                    reads=ksc + keys("c_band"),
                                    writes=ksc, tag="aa_band")
                        kP = keys("ar2", POFF + po, POFF + po + 1024)
                        self.act(ar2[:, POFF + po:POFF + po + 1024],
                                 psA[:, 1024 * r3:1024 * (r3 + 1)], AF.Exp,
                                 reads=ksc, writes=kP, tag="aa_exp")
                    else:
                        for c_ in range(TCH // 512):
                            if o >= 0:
                                stc = (TCH - 128) - 128 * o + 512 * c_
                                self.tt(ALU.add, halves[c_][0], halves[c_][0],
                                        sb["c_band"][:, stc:stc + 512],
                                        reads=halves[c_][1] + keys("c_band"),
                                        writes=halves[c_][1], tag="aa_band")
                            kP = keys("ar2", POFF + po + 512 * c_,
                                      POFF + po + 512 * (c_ + 1))
                            self.act(ar2[:, POFF + po + 512 * c_:
                                         POFF + po + 512 * (c_ + 1)],
                                     halves[c_][0], AF.Exp,
                                     reads=halves[c_][1], writes=kP,
                                     tag="aa_exp")
                    if j >= 1:
                        aa_pv(j - 1)
                if nj >= 1:
                    aa_pv(nj - 1)
                kev = keys("E", 0, TCH)
                self.cp("scalar", E[0:64, 0:TCH], psB[0:64, 0:TCH],
                        reads=kout, writes=kev, tag="aa_ev")
                kz = keys("G", 0, TCH)
                self.recip(G[0:1, 0:TCH], psB[64:65, 0:TCH],
                           reads=kout, writes=kz, tag="aa_recz")
                for c_ in range(TCH // 512):
                    self.mm(psB[0:64, 512 * c_:512 * (c_ + 1)],
                            sb["c_o1x64f"][:], G[0:1, 512 * c_:512 * (c_ + 1)],
                            start=True, stop=True,
                            reads=kz + keys("c_o1x64f"), writes=kout,
                            tag="aa_zrep")
                kcs = keys("ar2", COFF, COFF + TCH)
                self.tt(ALU.mult, ar2[0:64, COFF:COFF + TCH],
                        E[0:64, 0:TCH], psB[0:64, 0:TCH],
                        reads=kev + kout, writes=kcs, tag="aa_nrm")
                self.dma(d["CONTR"][b, 64 * hp:64 * hp + 64, TCH * r:TCH * (r + 1)],
                         ar2[0:64, COFF:COFF + TCH],
                         reads=kcs, writes=(("CONTR", b, hp),), tag="aa_co")

    def phase_out_rs(self, wspec, tag, zp="zpart", zr="zred", ncols=1024):
        cfg, sb, d = self.cfg, self.sb, self.d
        S = cfg.S
        wname, woff = wspec
        wa = sb[wname]
        psB, F = sb["psB"], sb["F"]
        kw = keys(wname, woff, woff + ncols)
        for b in range(cfg.B):
            for strip in range(S // 1024):
                sp = strip % 2
                kst = keys("W1", 1024 * sp, 1024 * (sp + 1))
                self.dma(sb["W1"][:, 1024 * sp:1024 * (sp + 1)],
                         d["CONTR"][b][:, 1024 * strip:1024 * (strip + 1)],
                         reads=(("CONTR", b, 0), ("CONTR", b, 1)),
                         writes=kst, tag=tag + "_ci")
                for ti in range(8):
                    i = 8 * strip + ti
                    kz = keys("psB", 0, ncols)
                    for c_ in range(ncols // 512):
                        self.mm(psB[:, 512 * c_:512 * (c_ + 1)],
                                sb["W1"][:, 1024 * sp + 128 * ti:
                                         1024 * sp + 128 * (ti + 1)],
                                wa[:, woff + 512 * c_:woff + 512 * (c_ + 1)],
                                start=True, stop=True, reads=kst + kw,
                                writes=kz, tag=tag + "_mm")
                    pp = i % 2
                    kf = keys("F", 2048 * pp, 2048 * pp + ncols)
                    self.cp("scalar" if i % 2 == 0 else "vector",
                            F[:, 2048 * pp:2048 * pp + ncols], psB[:, 0:ncols],
                            reads=kz, writes=kf, tag=tag + "_ev")
                    r, pi = (128 * i) // cfg.pos, (128 * i) % cfg.pos
                    row0 = r * cfg.TL + b * cfg.pos + pi
                    self.dma(d[zp][row0:row0 + 128, 0:ncols],
                             F[:, 2048 * pp:2048 * pp + ncols],
                             reads=kf, writes=((zp, row0 // 128),),
                             tag=tag + "_zo")
        self.coll("ReduceScatter", ALU.add, d[zp], d[zr],
                  reads=tuple((zp, i) for i in range(cfg.T // 128)),
                  writes=((zr, 0),), tag=tag + "_RS")

    def phase_residual(self, tag, src="zred"):
        cfg, sb, d = self.cfg, self.sb, self.d
        F = sb["F"]
        for i in range(cfg.NT_LOC):
            pp = i % 2
            ka = keys("F", 2048 * pp, 2048 * pp + 1024)
            kb = keys("F", 2048 * pp + 1024, 2048 * pp + 2048)
            self.dma(F[:, 2048 * pp:2048 * pp + 1024],
                     d[src][128 * i:128 * (i + 1), 0:1024],
                     reads=((src, 0),), writes=ka, tag=tag + "_rin")
            self.dma(F[:, 2048 * pp + 1024:2048 * pp + 2048],
                     d["xcur"][128 * i:128 * (i + 1), :],
                     reads=keys("xcur"), writes=kb, tag=tag + "_xin")
            self.tt(ALU.add, F[:, 2048 * pp:2048 * pp + 1024],
                    F[:, 2048 * pp:2048 * pp + 1024],
                    F[:, 2048 * pp + 1024:2048 * pp + 2048],
                    reads=ka + kb, writes=ka, tag=tag + "_add")
            self.dma(d["xcur"][128 * i:128 * (i + 1), :],
                     F[:, 2048 * pp:2048 * pp + 1024],
                     reads=ka, writes=keys("xcur"), tag=tag + "_xout")

    def phase_ffn(self, inst, stop=None):
        cfg, sb, d = self.cfg, self.sb, self.d
        S, D = cfg.S, cfg.D
        tg = f"f{inst}"
        nch = D // 128
        CL = cfg.TL
        half = CL // 2
        W0, W1, G, F, E = sb["W0"], sb["W1"], sb["G"], sb["F"], sb["E"]
        psA, psB, psT = sb["psA"], sb["psB"], sb["psT"]
        self.phase_ln_ag(extra_rms=False, tag=tg + "ln", do_ag=True, fm_ar="ar0")
        for nm, off in [("wqf", 0), ("wkf", 1024), ("wvf", 2048), ("wof", 3072)]:
            if nm == "wof":
                self.dma(W0[:, off:off + 1024], self.wbf[nm],
                         reads=(("wbf_" + nm, 0),),
                         writes=keys("W0", off, off + 1024), tag=tg + "w")
            else:
                self.dma_wfm(W0[:, off:off + 1024], self.wbf[nm], 128,
                             reads=(("wbf_" + nm, 0),),
                             writes=keys("W0", off, off + 1024), tag=tg + "w")
        for r in range(cfg.R):
            pp = r % 2
            arn = f"ar{pp}"
            arin = sb[arn]
            kin = keys(arn, 0, nch * CL)
            self.dma(arin[:, 0:nch * CL].rearrange("p (c m) -> p c m", m=CL),
                     d["ag_xn_out"][128 * nch * r:128 * nch * (r + 1), :]
                     .rearrange("(c p) m -> p c m", p=128),
                     reads=(("ag_xn_out", 0),), writes=kin, tag=tg + "pin")

            def project(woff):
                kw = keys("W0", woff, woff + 1024)
                kps = keys("psA", 0, CL)
                for j in range(nch):
                    for c_ in range(CL // 512):
                        self.mm(psA[:, 512 * c_:512 * (c_ + 1)],
                                W0[:, woff + 128 * j:woff + 128 * (j + 1)],
                                arin[:, CL * j + 512 * c_:CL * j + 512 * (c_ + 1)],
                                start=(j == 0), stop=(j == nch - 1),
                                reads=kin + kw, writes=kps, tag=tg + "pmm")
                return kps

            def qk_store(kev, qk, dstn):
                for b in range(2):
                    self.dma(d[dstn][b][:, cfg.pos * r:cfg.pos * r + half],
                             W1[:, b * half:(b + 1) * half],
                             reads=kev, writes=((dstn, b),), tag=tg + "qko")
                ksq = keys("ar2", 0, CL)
                self.act(sb["ar2"][:, 0:CL], W1[:, 0:CL], AF.Square,
                         reads=kev, writes=ksq, tag=tg + "nsq")
                kn = keys("psA", 0, CL)
                for c_ in range(CL // 512):
                    self.mm(psA[0:2, 512 * c_:512 * (c_ + 1)], sb["c_hm"][:],
                            sb["ar2"][:, 512 * c_:512 * (c_ + 1)],
                            start=True, stop=True, reads=ksq + keys("c_hm"),
                            writes=kn, tag=tg + "nmm")
                kev2 = keys("F", 2048, 2048 + CL)
                self.cp("vector", F[0:2, 2048:2048 + CL], psA[0:2, 0:CL],
                        reads=kn, writes=kev2, tag=tg + "nev")
                for hp in range(2):
                    for b in range(2):
                        self.dma(d["norms_f"][b, qk, hp,
                                              cfg.pos * r:cfg.pos * r + half],
                                 F[hp:hp + 1, 2048 + b * half:2048 + (b + 1) * half],
                                 reads=kev2, writes=(("norms_f", qk),),
                                 tag=tg + "no")

            kq = project(0)
            kevq = keys("W1", 0, CL)
            self.act(W1[:, 0:CL], psA[:, 0:CL], AF.Copy, scale=1.0 / 32.0,
                     reads=kq, writes=kevq, tag=tg + "qev")
            qk_store(kevq, 0, "QF")
            kk = project(1024)
            kevk = keys("W1", 0, CL)
            self.cp("scalar", W1[:, 0:CL], psA[:, 0:CL], reads=kk,
                    writes=kevk, tag=tg + "kev")
            qk_store(kevk, 1, "KF")
            kv = project(2048)
            kvs = keys("ar2", 4096, 4096 + CL)
            self.cp("scalar", sb["ar2"][:, 4096:4096 + CL], psA[:, 0:CL],
                    reads=kv, writes=kvs, tag=tg + "vev")
            ntile = CL // 128
            for g in range(ceil_div(ntile, 8)):
                kT = keys("psT", 0, 1024)
                for tl in range(min(8, ntile - 8 * g)):
                    i = 8 * g + tl
                    self.trans(psT[:, 128 * tl:128 * (tl + 1)],
                               sb["ar2"][:, 4096 + 128 * i:4096 + 128 * (i + 1)],
                               reads=kvs, writes=kT, tag=tg + "vT")
                eb = 8192 + 1024 * (g % 2)
                kTe = keys("ar2", eb, eb + 1024)
                self.cp("vector", sb["ar2"][:, eb:eb + 1024], psT[:, 0:1024],
                        reads=kT, writes=kTe, tag=tg + "vTe")
                for tl in range(min(8, ntile - 8 * g)):
                    i = 8 * g + tl
                    b = i // (ntile // 2)
                    row0 = cfg.pos * r + 128 * (i % (ntile // 2))
                    self.dma(d["VF"][b, row0:row0 + 128, :],
                             sb["ar2"][:, eb + 128 * tl:eb + 128 * (tl + 1)],
                             reads=kTe, writes=(("VF", b),), tag=tg + "vo")
        if stop == "proj":
            return
        for b in range(2):
            for hp in range(2):
                ku = keys("G", 2080, 2082)
                ng = max(1, S // 2048)
                w = min(2048, S)
                for gg in range(ng):
                    kg = keys("G", 0, w)
                    self.dma(G[0:1, 0:w],
                             d["norms_f"][b, 0, hp, w * gg:w * (gg + 1)],
                             reads=(("norms_f", 0),), writes=kg, tag=tg + "mqi")
                    self.red(G[0:1, 2048 + gg:2049 + gg], G[0:1, 0:w], ALU.max,
                             reads=kg, writes=keys("G", 2048, 2048 + ng),
                             tag=tg + "mqr")
                self.red(G[0:1, 2080:2081], G[0:1, 2048:2048 + ng], ALU.max,
                         reads=keys("G", 2048, 2048 + ng), writes=ku,
                         tag=tg + "mqr2")
                self.act(G[0:1, 2081:2082], G[0:1, 2080:2081], AF.Sqrt,
                         reads=ku, writes=ku, tag=tg + "mqs")
                for r in range(S // 1024):
                    km = keys("G", 0, 1024)
                    km2 = keys("G", 1024, 2048)
                    self.dma(G[0:1, 0:1024],
                             d["norms_f"][b, 1, hp, 1024 * r:1024 * (r + 1)],
                             reads=(("norms_f", 1),), writes=km, tag=tg + "mki")
                    self.act(G[0:1, 1024:2048], G[0:1, 0:1024], AF.Sqrt,
                             reads=km, writes=km2, tag=tg + "mks")
                    self.ts(G[0:1, 0:1024], G[0:1, 1024:2048],
                            G[0:1, 2081:2082], ALU.mult, reads=km2 + ku,
                            writes=km, s2=-1.0, op1=ALU.mult, tag=tg + "mkm")
                    kmb = keys("W1", 0, 1024)
                    self.cp("vector", W1[0:1, 0:1024], G[0:1, 0:1024],
                            reads=km, writes=kmb, tag=tg + "mkc")
                    self.dma(d["negMk"][b, hp, 1024 * r:1024 * (r + 1)],
                             W1[0:1, 0:1024], reads=kmb,
                             writes=(("negMk", b, hp),), tag=tg + "mko")
        if stop == "negmk":
            return
        ar0, ar1, ar2, ar3 = sb["ar0"], sb["ar1"], sb["ar2"], sb["ar3"]
        nkt = S // 128
        psC = sb["psC"]
        psTf = sb["psT"][:].bitcast(F32)
        PREG = {(0, 0): ("ar0", 0), (1, 0): ("ar0", 8192),
                (2, 0): ("ar3", 0), (0, 1): ("ar1", 0),
                (1, 1): ("ar1", 8192), (2, 1): ("ar3", 8192)}
        for b in range(cfg.B):
            kQ = keys("ar2", 0, S)
            self.dma(ar2[:, 0:S], d["QF"][b], reads=(("QF", b),), writes=kQ,
                     tag=tg + "alq")
            kM = keys("G", 1024, 1024 + 2 * nkt)
            kMs = keys("W1", 1536, 1536 + 2 * nkt)
            for hp in range(2):
                self.dma(W1[:, 1536 + nkt * hp:1536 + nkt * (hp + 1)],
                         d["negMk"][b, hp].rearrange("(a q) -> a q", q=128),
                         reads=(("negMk", b, hp),), writes=kMs,
                         tag=tg + "aM", transpose=True)
            self.cp("vector", G[:, 1024:1024 + 2 * nkt],
                    W1[:, 1536:1536 + 2 * nkt], reads=kMs, writes=kM,
                    tag=tg + "aMc")
            kacc = keys("E", 0, S)
            self.memset(E[:, 0:S], 0.0, reads=(), writes=kacc, tag=tg + "az")

            def scores_block(j):
                pp = j % 2
                kK = keys("W1", 128 * pp, 128 * (pp + 1))
                self.dma(W1[:, 128 * pp:128 * (pp + 1)],
                         d["KF"][b][:, 128 * j:128 * (j + 1)],
                         reads=(("KF", b),), writes=kK, tag=tg + "aki")
                kVt = keys("W1", 1664 + 128 * pp, 1792 + 128 * pp)
                self.dma(W1[:, 1664 + 128 * pp:1792 + 128 * pp],
                         d["VF"][b][128 * j:128 * (j + 1), :],
                         reads=(("VF", b),), writes=kVt, tag=tg + "avt")
                pn0, pb0 = PREG[(j % 3, 0)]
                pn1, pb1 = PREG[(j % 3, 1)]
                for s in range(16):
                    hp = s % 2
                    q0 = 1024 * (s // 2)
                    r3 = s % 3
                    Pn, PB = (pn0, pb0) if hp == 0 else (pn1, pb1)
                    Par = sb[Pn]
                    if r3 < 2:
                        rps, ra = psA, 1024 * r3
                        ksa = keys("psA", ra, ra + 1024)
                    else:
                        rps, ra = psB, 0
                        ksa = keys("psB", 0, 1024)
                    for c_ in range(2):
                        self.mm(rps[:, ra + 512 * c_:ra + 512 * (c_ + 1)],
                                W1[64 * hp:64 * (hp + 1),
                                   128 * pp:128 * pp + 128],
                                ar2[64 * hp:64 * (hp + 1),
                                    q0 + 512 * c_:q0 + 512 * (c_ + 1)],
                                start=True, stop=True,
                                reads=kK + kQ, writes=ksa, tag=tg + "amm")
                    self.act(Par[:, PB + q0:PB + q0 + 1024],
                             rps[:, ra:ra + 1024], AF.Exp,
                             bias=G[:, 1024 + nkt * hp + j:
                                    1025 + nkt * hp + j],
                             reads=ksa + kM,
                             writes=keys(Pn, PB + q0, PB + q0 + 1024),
                             tag=tg + "aexp")
                vs = 256 + 128 * (j % 3)
                for hp in range(2):
                    Pn, PB = (pn0, pb0) if hp == 0 else (pn1, pb1)
                    Par = sb[Pn]
                    kP = keys(Pn, PB, PB + 8192)
                    kzz = keys("G", 2136 + 2 * hp, 2138 + 2 * hp)
                    self.red(G[:, 2136 + 2 * hp:2137 + 2 * hp],
                             Par[:, PB:PB + 8192], ALU.add,
                             reads=kP, writes=kzz, tag=tg + "azs")
                    self.recip(G[:, 2137 + 2 * hp:2138 + 2 * hp],
                               G[:, 2136 + 2 * hp:2137 + 2 * hp],
                               reads=kzz, writes=kzz, tag=tg + "azr")
                    self.ts(W1[:, vs + 64 * hp:vs + 64 * (hp + 1)],
                            W1[:, 1664 + 128 * pp + 64 * hp:
                               1664 + 128 * pp + 64 * (hp + 1)],
                            G[:, 2137 + 2 * hp:2138 + 2 * hp], ALU.mult,
                            reads=kVt + kzz,
                            writes=keys("W1", vs + 64 * hp, vs + 64 * (hp + 1)),
                            tag=tg + "avv")

            def pv_block2(ja, jb):
                for s in range(16):
                    q0 = 512 * s
                    if s % 2 == 0:
                        ps, kps = psC, keys("psC", 0, 512)
                    else:
                        ps, kps = psTf, keys("psT", 0, 1024)
                    for i_, j in enumerate((ja, jb)):
                        vs = 256 + 128 * (j % 3)
                        for hp in range(2):
                            Pn, PB = PREG[(j % 3, hp)]
                            Par = sb[Pn]
                            self.mm(ps[64 * hp:64 * (hp + 1), 0:512],
                                    W1[:, vs + 64 * hp:vs + 64 * (hp + 1)],
                                    Par[:, PB + q0:PB + q0 + 512],
                                    start=(i_ == 0), stop=(i_ == 1),
                                    reads=keys(Pn, PB + q0, PB + q0 + 512) +
                                    keys("W1", vs + 64 * hp,
                                         vs + 64 * (hp + 1)),
                                    writes=kps, tag=tg + "apv")
                    ke = keys("E", q0, q0 + 512)
                    self.tt(ALU.add, E[:, q0:q0 + 512], E[:, q0:q0 + 512],
                            ps[:, 0:512], reads=ke + kps, writes=ke,
                            tag=tg + "aac")

            pend = None
            for j in range(nkt):
                scores_block(j)
                if pend is not None:
                    pv_block2(*pend)
                    pend = None
                if j % 2 == 1:
                    pend = (j - 1, j)
            if pend is not None:
                pv_block2(*pend)
            for g in range(S // 2048):
                kcs = keys("ar2", 8192 + 2048 * (g % 2), 8192 + 2048 * (g % 2) + 2048)
                self.cp("vector", ar2[:, 8192 + 2048 * (g % 2):
                                      8192 + 2048 * (g % 2) + 2048],
                        E[:, 2048 * g:2048 * (g + 1)],
                        reads=kacc, writes=kcs, tag=tg + "acast")
                self.dma(d["CONTR"][b][:, 2048 * g:2048 * (g + 1)],
                         ar2[:, 8192 + 2048 * (g % 2):
                             8192 + 2048 * (g % 2) + 2048],
                         reads=kcs, writes=(("CONTR", b, 0), ("CONTR", b, 1)),
                         tag=tg + "aco")
        if stop == "attn":
            return
        self.phase_out_rs(("W0", 3072), tg + "wo")
        if stop == "rs":
            return
        for i in range(cfg.NT_LOC):
            pp = i % 2
            kf = keys("F", 2048 * pp, 2048 * pp + 1024)
            self.dma(F[:, 2048 * pp:2048 * pp + 1024],
                     d["zred"][128 * i:128 * (i + 1), :],
                     reads=(("zred", 0),), writes=kf, tag=tg + "zi")
            kw1 = keys("W1", 1024 * pp, 1024 * (pp + 1))
            self.cp("vector", W1[:, 1024 * pp:1024 * (pp + 1)],
                    F[:, 2048 * pp:2048 * pp + 1024], reads=kf, writes=kw1,
                    tag=tg + "zc")
            for j in range(8):
                kT = keys("psT", 128 * (j % 8), 128 * (j % 8) + 128)
                self.trans(psT[:, 128 * (j % 8):128 * (j % 8) + 128],
                           W1[:, 1024 * pp + 128 * j:1024 * pp + 128 * (j + 1)],
                           reads=kw1, writes=kT, tag=tg + "zT")
                self.cp("vector",
                        sb["ar0"][:, CL * j + 128 * i:CL * j + 128 * (i + 1)],
                        psT[:, 128 * (j % 8):128 * (j % 8) + 128],
                        reads=kT, writes=keys("ar0", CL * j + 128 * i,
                                              CL * j + 128 * (i + 1)),
                        tag=tg + "zTe")
        self.dma_wfm(sb["ar1"][:, 0:8 * 1024], self.wbf["wff"], 1024,
                     reads=(("wbf_wff", 0),), writes=keys("ar1", 0, 8192),
                     tag=tg + "wffl")
        kzfm = keys("ar0", 0, 8 * CL)
        kwf = keys("ar1", 0, 8192)
        if stop == "zfm":
            return
        nth = CL // 1024
        for th in range(nth):
            tc = 1024 * th
            for i in range(8):
                kps = keys("psA", 0, 1024)
                for kc in range(8):
                    for c_ in range(2):
                        self.mm(psA[:, 512 * c_:512 * (c_ + 1)],
                                sb["ar1"][:, 1024 * kc + 128 * i:
                                          1024 * kc + 128 * (i + 1)],
                                sb["ar0"][:, CL * kc + tc + 512 * c_:
                                          CL * kc + tc + 512 * (c_ + 1)],
                                start=(kc == 0), stop=(kc == 7),
                                reads=kwf + kzfm, writes=kps, tag=tg + "fmm")
                kw = keys("ar2", 2048 * i + 1024 * (th % 2),
                          2048 * i + 1024 * (th % 2) + 1024)
                if stop != "zn3":
                    self.cp("vector",
                            sb["ar2"][:, 2048 * i + 1024 * (th % 2):
                                      2048 * i + 1024 * (th % 2) + 1024],
                            psA[:, 0:1024], reads=kps, writes=kw, tag=tg + "fev")
                ksq = keys("F", 0, 1024)
                if stop not in ("zn3", "zn4"):
                    wsl = sb["ar2"][:, 2048 * i + 1024 * (th % 2):
                                    2048 * i + 1024 * (th % 2) + 1024]
                    self.tt(ALU.mult, F[:, 0:1024], wsl, wsl,
                            reads=kw, writes=ksq, tag=tg + "fsq")
                if stop not in ("zn1", "zn3", "zn4"):
                    for c_ in range(2):
                        self.mm(psB[0:1, 512 * c_:512 * (c_ + 1)],
                                sb["c_ocolf"][:], F[:, 512 * c_:512 * (c_ + 1)],
                                start=(i == 0) or stop == "zn2",
                                stop=(i == 7) or stop == "zn2",
                                reads=ksq + keys("c_ocolf"),
                                writes=keys("psB", 0, 1024), tag=tg + "fns")
            if stop in ("zmm", "zn1", "zn2", "zn3", "zn4"):
                continue
            krs0 = keys("G", 0, 1024)
            self.cp("vector", G[0:1, 0:1024], psB[0:1, 0:1024],
                    reads=keys("psB", 0, 1024), writes=krs0, tag=tg + "frse")
            krs1 = keys("G", 1024, 2048)
            self.act(G[0:1, 1024:2048], G[0:1, 0:1024], AF.Sqrt,
                     reads=krs0, writes=krs1, tag=tg + "frs")
            self.ts(G[0:1, 0:1024], G[0:1, 1024:2048], 1e-12, ALU.max,
                    reads=krs1, writes=krs0, tag=tg + "fmx")
            krr = keys("G", 1024, 2048)
            self.recip(G[0:1, 1024:2048], G[0:1, 0:1024], reads=krs0,
                       writes=krr, tag=tg + "frr")
            kbr = keys("psB", 0, 1024)
            for c_ in range(2):
                self.mm(psB[:, 512 * c_:512 * (c_ + 1)], sb["c_o1x128f"][:],
                        G[0:1, 1024 + 512 * c_:1024 + 512 * (c_ + 1)],
                        start=True, stop=True,
                        reads=krr + keys("c_o1x128f"), writes=kbr,
                        tag=tg + "fbc")
            if stop == "zbc":
                continue
            for i in range(8):
                co = 2048 * i + 1024 * (th % 2)
                kw = keys("ar2", co, co + 1024)
                kg1 = keys("ar3", 1024 * i, 1024 * (i + 1))
                self.tt(ALU.mult, sb["ar3"][:, 1024 * i:1024 * (i + 1)],
                        sb["ar2"][:, co:co + 1024], psB[:, 0:1024],
                        reads=kw + kbr, writes=kg1, tag=tg + "fnm")
                self.act(sb["ar2"][:, co:co + 1024],
                         sb["ar3"][:, 1024 * i:1024 * (i + 1)],
                         AF.Gelu, reads=kg1, writes=kw, tag=tg + "fgel")
            if stop == "zgel":
                continue
            for tt_ in range(8):
                kT = keys("psT", 0, 1024)
                for i in range(8):
                    co = 2048 * i + 1024 * (th % 2)
                    self.trans(psT[:, 128 * i:128 * (i + 1)],
                               sb["ar2"][:, co + 128 * tt_:co + 128 * (tt_ + 1)],
                               reads=keys("ar2", co, co + 1024),
                               writes=kT, tag=tg + "fT")
                it = 8 * th + tt_
                pp = tt_ % 2
                kxf = keys("F", 2048 * pp, 2048 * pp + 1024)
                self.dma(F[:, 2048 * pp:2048 * pp + 1024],
                         d["xcur"][128 * it:128 * (it + 1), :],
                         reads=keys("xcur"), writes=kxf, tag=tg + "fxi")
                self.tt(ALU.add, F[:, 2048 * pp:2048 * pp + 1024],
                        F[:, 2048 * pp:2048 * pp + 1024], psT[:, 0:1024],
                        reads=kxf + kT, writes=kxf, tag=tg + "fadd")
                self.dma(d["xcur"][128 * it:128 * (it + 1), :],
                         F[:, 2048 * pp:2048 * pp + 1024],
                         reads=kxf, writes=keys("xcur"), tag=tg + "fxo")

    def phase_mamba(self):
        cfg, sb, d = self.cfg, self.sb, self.d
        S, DM, T = cfg.S, cfg.DM, cfg.T
        MB = cfg.MB
        W0, W1, G, F, E = sb["W0"], sb["W1"], sb["G"], sb["F"], sb["E"]
        psA, psB, psT = sb["psA"], sb["psB"], sb["psT"]
        ar0, ar1, ar2, ar3 = sb["ar0"], sb["ar1"], sb["ar2"], sb["ar3"]
        NT = cfg.NT_LOC
        CL = cfg.TL
        nct = S // 2048
        ngl = T // 2048

        def pcol(g):
            b = g // nct
            return cfg.mcol0(b) + 2048 * (g % nct)

        def scol(g):
            return 2048 * g

        self.phase_ln_ag(extra_rms=False, tag="mln", do_ag=False, fm_ar="ar0")
        self.dma_wfm(ar1[:, 0:8 * 512], self.wbf["lin1_w"], 512,
                     reads=(("wbf_lin1_w", 0),), writes=keys("ar1", 0, 4096),
                     tag="mb_l1w")
        kb1 = keys("G", 0, 512)
        self.dma(G[0:1, 0:DM], self.ein["lin1_b"], reads=(), writes=kb1,
                 tag="mb_l1b")
        kb1b = keys("W1", 1536, 2048)
        self.cp("vector", W1[0:1, 1536:1536 + DM], G[0:1, 0:DM],
                reads=kb1, writes=kb1b, tag="mb_l1bc")
        for i in range(NT):
            kps = keys("psB", 0, 512)
            for k in range(8):
                self.mm(psB[:, 0:512],
                        ar0[:, CL * k + 128 * i:CL * k + 128 * (i + 1)],
                        ar1[:, 512 * k:512 * (k + 1)], start=(k == 0), stop=False,
                        reads=keys("ar0", CL * k + 128 * i,
                                   CL * k + 128 * (i + 1)) +
                        keys("ar1", 512 * k, 512 * (k + 1)),
                        writes=kps, tag="mb_l1mm")
            self.mm(psB[:, 0:512], sb["c_o1x128b"][:], W1[0:1, 1536:1536 + DM],
                    start=False, stop=True,
                    reads=kb1b + keys("c_o1x128b"), writes=kps, tag="mb_l1bias")
            ke = keys("E", 512 * i, 512 * (i + 1))
            self.cp("scalar" if i % 2 else "vector", E[:, 512 * i:512 * (i + 1)],
                    psB[:, 0:512], reads=kps, writes=ke, tag="mb_l1ev")

        for l in range(cfg.NL):
            lt = f"mb{l}"
            kcw = keys("G", 2144, 2176)
            self.dma(G[:, 2144:2144 + cfg.CK], self.ein["convw"][l],
                     reads=(), writes=kcw, tag=lt + "cw")
            self.dma(G[:, 2148:2149], as_col(self.ein["convb"][l, :]),
                     reads=(), writes=kcw, tag=lt + "cb")
            self.dma(G[:, 2149:2150], as_col(self.ein["dtb"][l, :]),
                     reads=(), writes=kcw, tag=lt + "db")
            self.dma(G[:, 2150:2151], as_col(self.ein["dpv"][l, :]),
                     reads=(), writes=kcw, tag=lt + "dp")
            kA = keys("G", 2176, 2208)
            self.dma(G[:, 2176:2176 + cfg.DS], self.ein["alog"][l],
                     reads=(), writes=kA, tag=lt + "al")
            self.act(G[:, 2176:2176 + cfg.DS], G[:, 2176:2176 + cfg.DS],
                     AF.Exp, reads=kA, writes=kA, tag=lt + "aexp")
            self.ts(G[:, 2176:2176 + cfg.DS], G[:, 2176:2176 + cfg.DS],
                    -1.0, ALU.mult, reads=kA, writes=kA, tag=lt + "aneg")
            for i in range(NT):
                pp = i % 2
                gs = pp * 16
                kst = keys("G", gs, gs + 16)
                ke = keys("E", 512 * i, 512 * (i + 1))
                ksq = keys("F", 2048 * pp, 2048 * pp + 512)
                self.act(F[:, 2048 * pp:2048 * pp + 512],
                         E[:, 512 * i:512 * (i + 1)], AF.Square,
                         reads=ke, writes=ksq, tag=lt + "rsq")
                self.red(G[:, gs:gs + 1], F[:, 2048 * pp:2048 * pp + 512],
                         ALU.add, reads=ksq, writes=kst, tag=lt + "rs")
                self.act(G[:, gs + 1:gs + 2], G[:, gs:gs + 1], AF.Sqrt,
                         reads=kst + keys("c_cb"), writes=kst, scale=1.0 / DM,
                         bias=self.sb["c_cb"][:, 0:1], tag=lt + "rsd")
                self.recip(G[:, gs + 2:gs + 3], G[:, gs + 1:gs + 2],
                           reads=kst, writes=kst, tag=lt + "rrc")
                khr = keys("F", 2048 * pp + 512, 2048 * pp + 1024)
                self.ts(F[:, 2048 * pp + 512:2048 * pp + 1024],
                        E[:, 512 * i:512 * (i + 1)], G[:, gs + 2:gs + 3],
                        ALU.mult, reads=ke + kst, writes=khr, tag=lt + "rmul")
                kbf = keys("W1", 512 * pp, 512 * (pp + 1))
                self.cp("vector", W1[:, 512 * pp:512 * (pp + 1)],
                        F[:, 2048 * pp + 512:2048 * pp + 1024],
                        reads=khr, writes=kbf, tag=lt + "rcast")
                for j in range(DM // 128):
                    kT = keys("psT", 128 * j, 128 * (j + 1))
                    self.trans(psT[:, 128 * j:128 * (j + 1)],
                               W1[:, 512 * pp + 128 * j:512 * pp + 128 * (j + 1)],
                               reads=kbf, writes=kT, tag=lt + "rT")
                    self.cp("vector",
                            ar0[:, CL * j + 128 * i:CL * j + 128 * (i + 1)],
                            psT[:, 128 * j:128 * (j + 1)], reads=kT,
                            writes=keys("ar0", CL * j + 128 * i,
                                        CL * j + 128 * (i + 1)),
                            tag=lt + "rTe")
            for j in range(DM // 128):
                self.dma(d["ag_hr_in"][128 * j:128 * (j + 1), :],
                         ar0[:, CL * j:CL * j + CL],
                         reads=keys("ar0", CL * j, CL * j + CL),
                         writes=(("ag_hr_in", j),), tag=lt + "agi")
            self.coll("AllGather", ALU.bypass, d["ag_hr_in"], d["ag_hr_out"],
                      reads=tuple(("ag_hr_in", j) for j in range(DM // 128)),
                      writes=(("ag_hr_out", 0),), tag=lt + "AG")
            self.dma_wfm(W0[:, 0:1024], self.wbf[f"inw{l}"], 256,
                         reads=((f"wbf_inw{l}", 0),), writes=keys("W0", 0, 1024),
                         tag=lt + "iw")
            for r in range(cfg.R):
                kin = keys("ar0", 4 * CL, 8 * CL)
                self.dma(ar0[:, 4 * CL:8 * CL].rearrange(
                             "p (c m) -> p c m", m=CL),
                         d["ag_hr_out"][512 * r:512 * (r + 1), :]
                         .rearrange("(c p) m -> p c m", p=128),
                         reads=(("ag_hr_out", 0),), writes=kin, tag=lt + "xzi")
                for m2 in range(2):
                    kps = keys("psA", 0, 2048)
                    for k in range(4):
                        for c_ in range(CL // 512):
                            self.mm(psA[:, 512 * c_:512 * (c_ + 1)],
                                    W0[:, 256 * k + 128 * m2:
                                       256 * k + 128 * (m2 + 1)],
                                    ar0[:, 4 * CL + CL * k + 512 * c_:
                                        4 * CL + CL * k + 512 * (c_ + 1)],
                                    start=(k == 0), stop=(k == 3),
                                    reads=kin + keys("W0", 0, 1024),
                                    writes=kps, tag=lt + "xzmm")
                    if m2 == 0:
                        for b in range(2):
                            cc = cfg.mcol0(b) + cfg.pos * r
                            self.cp("vector", ar2[:, cc:cc + cfg.pos],
                                    psA[:, b * cfg.pos:(b + 1) * cfg.pos],
                                    reads=kps, writes=keys("ar2", cc, cc + cfg.pos),
                                    tag=lt + "xpev")
                    else:
                        kzs = keys("ar1", 4096, 4096 + 2048)
                        self.cp("vector", ar1[:, 4096:4096 + 2048],
                                psA[:, 0:2048], reads=kps, writes=kzs,
                                tag=lt + "zev")
                        for b in range(2):
                            self.dma(d["ZSP"][:, b * S + cfg.pos * r:
                                              b * S + cfg.pos * r + cfg.pos],
                                     ar1[:, 4096 + b * cfg.pos:
                                         4096 + (b + 1) * cfg.pos],
                                     reads=kzs, writes=(("ZSP", 0),),
                                     tag=lt + "zsp")
            for b in range(2):
                self.memset(ar2[:, b * MB:b * MB + 3], 0.0, reads=(),
                            writes=keys("ar2", b * MB, b * MB + 3),
                            tag=lt + "pad", eng="vector")
            for b in range(2):
                c0 = cfg.mcol0(b)
                for ch in range(nct):
                    kacc = keys("F", 0, 2048)
                    src0 = c0 + 2048 * ch - 3
                    self.ts(F[:, 0:2048], ar2[:, src0:src0 + 2048],
                            G[:, 2144:2145], ALU.mult,
                            reads=keys("ar2", src0, src0 + 2048) + kcw,
                            writes=kacc, tag=lt + "cv0")
                    for k in range(1, cfg.CK):
                        self.stt(F[:, 0:2048], ar2[:, src0 + k:src0 + k + 2048],
                                 G[:, 2144 + k:2145 + k], F[:, 0:2048],
                                 ALU.mult, ALU.add,
                                 reads=keys("ar2", src0 + k, src0 + k + 2048) +
                                 kcw + kacc, writes=kacc, tag=lt + "cvk")
                    ku = keys("ar1", 6144, 8192)
                    self.act(ar1[:, 6144:6144 + 2048], F[:, 0:2048], AF.Silu,
                             bias=G[:, 2148:2149], reads=kacc + kcw, writes=ku,
                             tag=lt + "cvs")
                    g = b * nct + ch
                    self.dma(d["USP"][:, scol(g):scol(g) + 2048],
                             ar1[:, 6144:6144 + 2048], reads=ku,
                             writes=(("USP", g),), tag=lt + "usp")
            self.dma(W0[:, 1024:1088], self.wbf[f"xprojw{l}"],
                     reads=((f"wbf_xprojw{l}", 0),),
                     writes=keys("W0", 1024, 1088), tag=lt + "xpw")
            for g in range(ngl):
                kuc = keys("ar1", 8192, 10240)
                self.dma(ar1[:, 8192:8192 + 2048],
                         d["USP"][:, scol(g):scol(g) + 2048],
                         reads=(("USP", g),), writes=kuc, tag=lt + "xui")
                for c_ in range(4):
                    kps = keys("psB", 512 * (c_ % 2), 512 * (c_ % 2) + 512)
                    self.mm(psB[0:64, 512 * (c_ % 2):512 * (c_ % 2) + 512],
                            W0[:, 1024:1088],
                            ar1[:, 8192 + 512 * c_:8192 + 512 * (c_ + 1)],
                            start=True, stop=True,
                            reads=kuc + keys("W0", 1024, 1088), writes=kps,
                            tag=lt + "xpm")
                    kf = keys("F", 2048 + 512 * (c_ % 2),
                              2048 + 512 * (c_ % 2) + 512)
                    self.cp("vector", F[0:64, 2048 + 512 * (c_ % 2):
                                        2048 + 512 * (c_ % 2) + 512],
                            psB[0:64, 512 * (c_ % 2):512 * (c_ % 2) + 512],
                            reads=kps, writes=kf, tag=lt + "xpe")
                    self.dma(d["dblpart"][0:64, 2048 * g + 512 * c_:
                                          2048 * g + 512 * (c_ + 1)],
                             F[0:64, 2048 + 512 * (c_ % 2):
                               2048 + 512 * (c_ % 2) + 512],
                             reads=kf, writes=(("dblpart", g),), tag=lt + "xpo")
            self.coll("AllReduce", ALU.add, d["dblpart"], d["dblfull"],
                      reads=tuple(("dblpart", g) for g in range(ngl)),
                      writes=(("dblfull", 0),), tag=lt + "AR")
            self.dma(W0[0:cfg.DTR, 1088:1216], self.wbf[f"dtw{l}"],
                     reads=((f"wbf_dtw{l}", 0),), writes=keys("W0", 1088, 1216),
                     tag=lt + "dtw")
            for g in range(ngl):
                pc = pcol(g)
                kdg = keys("G", 0, 2048)
                self.dma(G[0:cfg.DTR, 0:2048],
                         d["dblfull"][0:cfg.DTR, 2048 * g:2048 * (g + 1)],
                         reads=(("dblfull", 0),), writes=kdg, tag=lt + "dti")
                kdb = keys("W1", 0, 2048)
                self.cp("vector", W1[0:cfg.DTR, 0:2048], G[0:cfg.DTR, 0:2048],
                        reads=kdg, writes=kdb, tag=lt + "dtc")
                kps = keys("psA", 0, 2048)
                for c_ in range(4):
                    self.mm(psA[:, 512 * c_:512 * (c_ + 1)],
                            W0[0:cfg.DTR, 1088:1216],
                            W1[0:cfg.DTR, 512 * c_:512 * (c_ + 1)],
                            start=True, stop=True,
                            reads=kdb + keys("W0", 1088, 1216), writes=kps,
                            tag=lt + "dtm")
                kx = keys("F", 0, 2048)
                self.ts(F[:, 0:2048], psA[:, 0:2048], G[:, 2149:2150],
                        ALU.add, reads=kps + kcw, writes=kx, tag=lt + "dt0")
                ka = keys("F", 2048, 4096)
                self.act(F[:, 2048:4096], F[:, 0:2048], AF.Abs,
                         reads=kx, writes=ka, tag=lt + "dt1")
                kge = keys("G", 0, 2048)
                self.act(G[:, 0:2048], F[:, 2048:4096], AF.Exp, scale=-1.0,
                         reads=ka, writes=kge, tag=lt + "dt2")
                self.act(F[:, 2048:4096], G[:, 0:2048], AF.Ln,
                         bias=self.sb["c_cb"][:, 2:3],
                         reads=kge + keys("c_cb"), writes=ka, tag=lt + "dt3")
                self.stt(ar3[:, pc:pc + 2048], F[:, 0:2048],
                         self.sb["c_cb"][:, 3:4],
                         F[:, 2048:4096], ALU.max, ALU.add,
                         reads=kx + ka + keys("c_cb"),
                         writes=keys("ar3", pc, pc + 2048), tag=lt + "dt4")
            for g in range(ngl):
                pc = pcol(g)
                kuc = keys("ar2", S, S + 2048)
                self.dma(ar2[:, S:S + 2048],
                         d["USP"][:, scol(g):scol(g) + 2048],
                         reads=(("USP", g),), writes=kuc, tag=lt + "dui")
                self.tt(ALU.mult, ar1[:, pc:pc + 2048], ar3[:, pc:pc + 2048],
                        ar2[:, S:S + 2048],
                        reads=keys("ar3", pc, pc + 2048) + kuc,
                        writes=keys("ar1", pc, pc + 2048), tag=lt + "du")
            for b in range(2):
                c0 = cfg.mcol0(b)
                for n in range(cfg.DS):
                    kan = keys("ar2", 0, S)
                    self.act(ar2[:, 0:S], ar3[:, c0:c0 + S], AF.Exp,
                             scale=G[:, 2176 + n:2177 + n],
                             reads=keys("ar3", c0, c0 + S) + kA,
                             writes=kan, tag=lt + "sa")
                    kcy = keys("G", 2208, 2240)
                    self.memset(G[:, 2208:2209], 0.0, reads=(), writes=kcy,
                                tag=lt + "scy", eng="vector")
                    for c in range(nct):
                        kbr = keys("G", 0, 2048)
                        self.dma(G[0:1, 0:2048],
                                 as_row(d["dblfull"][cfg.DTR + n,
                                                     b * S + 2048 * c:
                                                     b * S + 2048 * (c + 1)]),
                                 reads=(("dblfull", 0),), writes=kbr,
                                 tag=lt + "sB")
                        kbc = keys("W0", 1280, 3328)
                        self.cp("vector", W0[0:1, 1280:3328], G[0:1, 0:2048],
                                reads=kbr, writes=kbc, tag=lt + "sBc")
                        kbrp = keys("psA", 0, 2048)
                        for c_ in range(4):
                            self.mm(psA[:, 512 * c_:512 * (c_ + 1)],
                                    sb["c_o1x128b"][:],
                                    W0[0:1, 1280 + 512 * c_:
                                       1280 + 512 * (c_ + 1)],
                                    start=True, stop=True,
                                    reads=kbc + keys("c_o1x128b"),
                                    writes=kbrp, tag=lt + "sBr")
                        kw = keys("ar2", S, S + 2048)
                        self.tt(ALU.mult, ar2[:, S:S + 2048],
                                ar1[:, c0 + 2048 * c:c0 + 2048 * (c + 1)],
                                psA[:, 0:2048],
                                reads=keys("ar1", c0 + 2048 * c,
                                           c0 + 2048 * (c + 1)) + kbrp,
                                writes=kw, tag=lt + "sw")
                        kh = keys("ar2", S + 2048, S + 4096)
                        self.scan_op(ar2[:, S + 2048:S + 4096],
                                     ar2[:, 2048 * c:2048 * (c + 1)],
                                     ar2[:, S:S + 2048], G[:, 2208:2209],
                                     reads=kan + kw + kcy, writes=kh,
                                     tag=lt + "scan")
                        self.cp("vector", G[:, 2208:2209],
                                ar2[:, S + 4095:S + 4096], reads=kh,
                                writes=kcy, tag=lt + "scar")
                        kcr = keys("F", 0, 2048)
                        self.dma(F[0:1, 0:2048],
                                 as_row(d["dblfull"][cfg.DTR + cfg.DS + n,
                                                     b * S + 2048 * c:
                                                     b * S + 2048 * (c + 1)]),
                                 reads=(("dblfull", 0),), writes=kcr,
                                 tag=lt + "sC")
                        kcc = keys("W1", 0, 2048)
                        self.cp("vector", W1[0:1, 0:2048], F[0:1, 0:2048],
                                reads=kcr, writes=kcc, tag=lt + "sCc")
                        kcrp = keys("psA", 0, 2048)
                        for c_ in range(4):
                            self.mm(psA[:, 512 * c_:512 * (c_ + 1)],
                                    sb["c_o1x128b"][:],
                                    W1[0:1, 512 * c_:512 * (c_ + 1)],
                                    start=True, stop=True,
                                    reads=kcc + keys("c_o1x128b"),
                                    writes=kcrp, tag=lt + "sCr")
                        kpr = keys("ar2", S + 4096, S + 6144)
                        self.tt(ALU.mult, ar2[:, S + 4096:S + 6144],
                                ar2[:, S + 2048:S + 4096], psA[:, 0:2048],
                                reads=kh + kcrp, writes=kpr, tag=lt + "sp")
                        ky = keys("ar0", c0 + 2048 * c, c0 + 2048 * (c + 1))
                        if n == 0:
                            self.cp("vector",
                                    ar0[:, c0 + 2048 * c:c0 + 2048 * (c + 1)],
                                    ar2[:, S + 4096:S + 6144], reads=kpr,
                                    writes=ky, tag=lt + "sy0")
                        else:
                            self.tt(ALU.add,
                                    ar0[:, c0 + 2048 * c:c0 + 2048 * (c + 1)],
                                    ar0[:, c0 + 2048 * c:c0 + 2048 * (c + 1)],
                                    ar2[:, S + 4096:S + 6144],
                                    reads=ky + kpr, writes=ky, tag=lt + "sy")
            for g in range(ngl):
                pc = pcol(g)
                kuc = keys("ar2", S, S + 2048)
                self.dma(ar2[:, S:S + 2048],
                         d["USP"][:, scol(g):scol(g) + 2048],
                         reads=(("USP", g),), writes=kuc, tag=lt + "yui")
                kzc = keys("ar2", S + 2048, S + 4096)
                self.dma(ar2[:, S + 2048:S + 4096],
                         d["ZSP"][:, scol(g):scol(g) + 2048],
                         reads=(("ZSP", 0),), writes=kzc, tag=lt + "yzi")
                k1 = keys("ar2", S + 4096, S + 6144)
                self.stt(ar2[:, S + 4096:S + 6144], ar2[:, S:S + 2048],
                         G[:, 2150:2151], ar0[:, pc:pc + 2048],
                         ALU.mult, ALU.add,
                         reads=kuc + keys("ar0", pc, pc + 2048) + kcw,
                         writes=k1, tag=lt + "yd")
                k2 = keys("ar2", S + 6144, S + 8192)
                self.act(ar2[:, S + 6144:S + 8192], ar2[:, S + 2048:S + 4096],
                         AF.Silu, reads=kzc, writes=k2, tag=lt + "ysz")
                self.tt(ALU.mult, ar0[:, pc:pc + 2048], ar2[:, S + 4096:S + 6144],
                        ar2[:, S + 6144:S + 8192], reads=k1 + k2,
                        writes=keys("ar0", pc, pc + 2048), tag=lt + "yg")
            self.dma(W1[:, 0:DM], self.wbf[f"outw{l}"],
                     reads=((f"wbf_outw{l}", 0),), writes=keys("W1", 0, DM),
                     tag=lt + "oww")
            for b in range(2):
                c0 = cfg.mcol0(b)
                for i in range(S // 128):
                    kps = keys("psB", 0, 512)
                    self.mm(psB[:, 0:512],
                            ar0[:, c0 + 128 * i:c0 + 128 * (i + 1)],
                            W1[:, 0:DM], start=True, stop=True,
                            reads=keys("ar0", c0 + 128 * i, c0 + 128 * (i + 1)) +
                            keys("W1", 0, DM), writes=kps, tag=lt + "owm")
                    pp = i % 2
                    kf = keys("F", 2048 * pp, 2048 * pp + 512)
                    self.cp("scalar" if i % 2 == 0 else "vector",
                            F[:, 2048 * pp:2048 * pp + 512], psB[:, 0:512],
                            reads=kps, writes=kf, tag=lt + "owe")
                    r, pi = (128 * i) // cfg.pos, (128 * i) % cfg.pos
                    row0 = r * cfg.TL + b * cfg.pos + pi
                    self.dma(d["zpartM"][row0:row0 + 128, :],
                             F[:, 2048 * pp:2048 * pp + 512],
                             reads=kf, writes=(("zpartM", row0 // 128),),
                             tag=lt + "owo")
            self.coll("ReduceScatter", ALU.add, d["zpartM"], d["zredM"],
                      reads=tuple(("zpartM", i) for i in range(T // 128)),
                      writes=(("zredM", 0),), tag=lt + "RS")
            for i in range(NT):
                pp = i % 2
                kf = keys("F", 2048 * pp, 2048 * pp + 512)
                self.dma(F[:, 2048 * pp:2048 * pp + 512],
                         d["zredM"][128 * i:128 * (i + 1), :],
                         reads=(("zredM", 0),), writes=kf, tag=lt + "hri")
                self.tt(ALU.add, E[:, 512 * i:512 * (i + 1)],
                        E[:, 512 * i:512 * (i + 1)],
                        F[:, 2048 * pp:2048 * pp + 512],
                        reads=keys("E", 512 * i, 512 * (i + 1)) + kf,
                        writes=keys("E", 512 * i, 512 * (i + 1)), tag=lt + "hra")
        self.dma_wfm(ar0[:, 0:4096], self.wbf["lin2_w"], 1024,
                     reads=(("wbf_lin2_w", 0),), writes=keys("ar0", 0, 4096),
                     tag="mb_l2w")
        kb2 = keys("G", 0, 1024)
        self.dma(G[0:1, 0:1024], self.ein["lin2_b"], reads=(), writes=kb2,
                 tag="mb_l2b")
        kb2b = keys("W1", 1024, 2048)
        self.cp("vector", W1[0:1, 1024:2048], G[0:1, 0:1024],
                reads=kb2, writes=kb2b, tag="mb_l2bc")
        for i in range(cfg.NT_LOC):
            khb = keys("W1", 0, 512)
            self.cp("vector", W1[:, 0:512], E[:, 512 * i:512 * (i + 1)],
                    reads=keys("E", 512 * i, 512 * (i + 1)), writes=khb,
                    tag="mb_hc")
            khf = keys("W1", 512, 1024)
            for j in range(4):
                kT = keys("psT", 128 * j, 128 * (j + 1))
                self.trans(psT[:, 128 * j:128 * (j + 1)],
                           W1[:, 128 * j:128 * (j + 1)], reads=khb, writes=kT,
                           tag="mb_hT")
                self.cp("vector", W1[:, 512 + 128 * j:512 + 128 * (j + 1)],
                        psT[:, 128 * j:128 * (j + 1)], reads=kT, writes=khf,
                        tag="mb_hTe")
            kps = keys("psA", 0, 1024)
            for c_ in range(2):
                for k in range(4):
                    self.mm(psA[:, 512 * c_:512 * (c_ + 1)],
                            W1[:, 512 + 128 * k:512 + 128 * (k + 1)],
                            ar0[:, 1024 * k + 512 * c_:1024 * k + 512 * (c_ + 1)],
                            start=(k == 0), stop=False,
                            reads=khf + keys("ar0", 1024 * k + 512 * c_,
                                             1024 * k + 512 * (c_ + 1)),
                            writes=kps, tag="mb_l2m")
                self.mm(psA[:, 512 * c_:512 * (c_ + 1)], sb["c_o1x128b"][:],
                        W1[0:1, 1024 + 512 * c_:1024 + 512 * (c_ + 1)],
                        start=False, stop=True,
                        reads=kb2b + keys("c_o1x128b"), writes=kps,
                        tag="mb_l2bias")
            p3 = psA[:, 0:1024].rearrange("p (g k) -> p g k", k=32)
            km = keys("G", 1024, 1056)
            self.red(G[:, 1024:1056], p3, ALU.max, reads=kps, writes=km,
                     tag="mb_smx")
            g3 = G[:, 1024:1056].rearrange("p (g o) -> p g o", o=1)
            f3a = F[:, 0:1024].rearrange("p (g k) -> p g k", k=32)
            f3b = F[:, 1024:2048].rearrange("p (g k) -> p g k", k=32)
            ksb = keys("F", 0, 1024)
            p3b, g3b = bass.broadcast_tensor_aps(p3, g3)
            self.p.op("vector", lambda e, d_=f3a, a=p3b, bb=g3b:
                      e.tensor_tensor(out=d_, in0=a, in1=bb, op=ALU.subtract),
                      kps + km, ksb, tag="mb_sms")
            kex = keys("F", 1024, 2048)
            self.act(F[:, 1024:2048], F[:, 0:1024], AF.Exp, reads=ksb,
                     writes=kex, tag="mb_sme")
            ks2 = keys("G", 1056, 1088)
            self.red(G[:, 1056:1088], f3b, ALU.add, reads=kex, writes=ks2,
                     tag="mb_smsum")
            kr2 = keys("G", 1088, 1120)
            self.recip(G[:, 1088:1120], G[:, 1056:1088], reads=ks2, writes=kr2,
                       tag="mb_smr")
            g3r = G[:, 1088:1120].rearrange("p (g o) -> p g o", o=1)
            f3c, g3rb = bass.broadcast_tensor_aps(f3b, g3r)
            self.p.op("vector", lambda e, d_=f3a, a=f3c, bb=g3rb:
                      e.tensor_tensor(out=d_, in0=a, in1=bb, op=ALU.mult),
                      kex + kr2, ksb, tag="mb_smm")
            kxf = keys("F", 2048, 3072)
            self.dma(F[:, 2048:3072], d["xcur"][128 * i:128 * (i + 1), :],
                     reads=keys("xcur"), writes=kxf, tag="mb_xi")
            self.tt(ALU.add, F[:, 2048:3072], F[:, 2048:3072], F[:, 0:1024],
                    reads=kxf + ksb, writes=kxf, tag="mb_xadd")
            self.dma(d["xcur"][128 * i:128 * (i + 1), :], F[:, 2048:3072],
                     reads=kxf, writes=keys("xcur"), tag="mb_xo")

    def phase_out(self):
        self.dma(self.d["out"], self.d["xcur"], reads=keys("xcur"),
                 writes=(("out", 0),), tag="final_out")
        self.p.op("sync", lambda e: e.nop(), reads=(("out", 0),), writes=(),
                  tag="final_wait")


def kernel(**inputs):
    cfg = Cfg()
    maps = shard_inputs(cfg, inputs)
    k = K(cfg)
    nc = k.build()
    res = run_bass_kernel_spmd(nc, maps, core_ids=list(range(cfg.R)))
    outs = [r["out"] for r in res.results]
    full = np.zeros((cfg.B, cfg.S, cfg.D), np.float32)
    for c in range(cfg.R):
        sl = slice(cfg.pos * c, cfg.pos * (c + 1))
        full[0, sl] = outs[c][:cfg.pos]
        full[1, sl] = outs[c][cfg.pos:]
    return full


# version 07
# speedup vs baseline: 1.3852x; 1.0896x over previous
import numpy as np
import ml_dtypes
import concourse.bass as bass
import concourse.mybir as mybir
from concourse.bass_utils import run_bass_kernel_spmd

F32 = mybir.dt.float32
BF16 = mybir.dt.bfloat16
I32 = mybir.dt.int32
from concourse.bass_core_rust import IndirectOffsetOnAxis
AF = mybir.ActivationFunctionType
ALU = mybir.AluOpType
AX = mybir.AxisListType
BF16NP = ml_dtypes.bfloat16


class Cfg:
    def __init__(self, S=8192, phases=None):
        self.R = 8
        self.B = 2
        self.S = S
        self.D = 1024
        self.H = 16
        self.dh = 64
        self.DM = 512
        self.DI = 1024
        self.DS = 16
        self.DTR = 32
        self.NL = 4
        self.CK = 4
        self.pos = S // self.R
        self.TL = self.B * self.pos
        self.T = self.B * S
        self.TCH = min(1024, self.pos)
        self.NT_LOC = self.TL // 128
        self.MB = 3 + S
        self.phases = phases if phases is not None else [
            "p0", "attn_a", "ffn1", "mamba", "ffn2", "out"]

    def mcol0(self, b):
        return b * self.MB + 3


def head_perm(cfg):
    p = []
    for half in (0, 1):
        for hp in (0, 1):
            for i in range(32):
                p.append(hp * 64 + half * 32 + i)
    return np.array(p)


def build_aux(cfg):
    S, dh = cfg.S, cfg.dh
    aux = {}
    aux["ident"] = np.eye(128, dtype=BF16NP)
    half = dh // 2
    theta = 1.0 / (1000000.0 ** (2 * np.arange(half) / dh))
    ang = np.arange(S)[None, :] * theta[:, None]
    cos, sin = np.cos(ang).astype(np.float32), np.sin(ang).astype(np.float32)
    ropeK = np.concatenate([np.tile(cos, (2, 1)), np.tile(sin, (2, 1))], 0)
    aux["ropeK"] = ropeK.astype(BF16NP)
    aux["ropeQ"] = (ropeK * 0.125).astype(BF16NP)
    hmp = np.zeros((128, 2), np.float32)
    perm = head_perm(cfg)
    for j in range(128):
        hmp[j, 0 if perm[j] < 64 else 1] = 1.0
    aux["headmask_pi"] = hmp.astype(BF16NP)
    hm = np.zeros((128, 2), np.float32)
    hm[:64, 0] = 1.0
    hm[64:, 1] = 1.0
    aux["headmask"] = hm.astype(BF16NP)
    TCH = cfg.TCH
    u = np.arange(-(TCH - 128), TCH)
    aux["band"] = np.where(u[None, :] >= np.arange(128)[:, None],
                           0.0, -1e30).astype(BF16NP)
    aux["ones_1x128_f32"] = np.ones((1, 128), np.float32)
    o4 = np.zeros((128, 128), np.float32)
    o4[0::32, :] = 1.0
    aux["ones4"] = o4
    aux["ones_1x64_f32"] = np.ones((1, 64), np.float32)
    aux["ones_col_f32"] = np.ones((128, 1), np.float32)
    aux["ones_1x128_bf"] = np.ones((1, 128), BF16NP)
    aux["ones_col_bf"] = np.ones((128, 1), BF16NP)
    aux["cbias"] = np.tile(np.array([[1e-5, 1e-6, 1.0, 0.0]], np.float32),
                           (128, 1))
    aux["ones_dram_bf"] = np.ones((S,), BF16NP)
    return aux


def shard_inputs(cfg, inp):
    aux = build_aux(cfg)
    perm = head_perm(cfg)
    x = np.asarray(inp["L0_attn_a_x"])
    maps = []
    for c in range(cfg.R):
        m = dict(aux)
        sl = slice(cfg.pos * c, cfg.pos * (c + 1))
        m["x_in"] = np.ascontiguousarray(
            np.concatenate([x[0, sl], x[1, sl]], 0)).astype(np.float32)
        lcols = np.arange(2 * c * 64, 2 * c * 64 + 128)
        g = np.asarray(inp["L0_attn_a_norm_scale"])
        m["wqa"] = np.ascontiguousarray(
            (np.asarray(inp["L0_attn_a_Wq"]) * g[:, None])[:, lcols[perm]])
        m["wka"] = np.ascontiguousarray(
            (np.asarray(inp["L0_attn_a_Wk"]) * g[:, None])[:, lcols[perm]])
        m["wva"] = np.ascontiguousarray(
            (np.asarray(inp["L0_attn_a_Wv"]) * g[:, None])[:, lcols])
        m["woa"] = np.ascontiguousarray(np.asarray(inp["L0_attn_a_Wo"]))
        m["wqf"] = np.ascontiguousarray(np.asarray(inp["L0_ffn_Wq"])[:, lcols])
        m["wkf"] = np.ascontiguousarray(np.asarray(inp["L0_ffn_Wk"])[:, lcols])
        m["wvf"] = np.ascontiguousarray(np.asarray(inp["L0_ffn_Wv"])[:, lcols])
        m["wof"] = np.ascontiguousarray(np.asarray(inp["L0_ffn_Wo"]))
        m["wff"] = np.ascontiguousarray(np.asarray(inp["L0_ffn_Wff"]))
        dsl = slice(128 * c, 128 * (c + 1))
        gidx = np.zeros((128, 24), np.int32)
        for src_ in range(8):
            for b_ in range(2):
                gidx[:, src_ * 2 + b_] = (src_ * 2048 + c * 256 + b_ * 128 +
                                          np.arange(128))
            gidx[:, 16 + src_] = src_ * 1024 + c * 128 + np.arange(128)
        m["gidx"] = gidx
        m["lin1_w"] = np.ascontiguousarray(np.asarray(inp["L0_attn_b_lin1_w"]))
        m["lin1_b"] = np.asarray(inp["L0_attn_b_lin1_b"]).reshape(1, -1)
        m["lin2_w"] = np.ascontiguousarray(np.asarray(inp["L0_attn_b_lin2_w"]))
        m["lin2_b"] = np.asarray(inp["L0_attn_b_lin2_b"]).reshape(1, -1)
        m["norm_w"] = np.ascontiguousarray(np.asarray(inp["L0_attn_b_norm_w"]))
        iw = np.asarray(inp["L0_attn_b_in_w"])
        m["inw"] = np.ascontiguousarray(
            np.concatenate([iw[:, :, dsl],
                            iw[:, :, cfg.DI + 128 * c: cfg.DI + 128 * (c + 1)]], -1))
        m["convw"] = np.ascontiguousarray(np.asarray(inp["L0_attn_b_conv_w"])[:, dsl, :])
        m["convb"] = np.ascontiguousarray(np.asarray(inp["L0_attn_b_conv_b"])[:, dsl])
        m["xprojw"] = np.ascontiguousarray(np.asarray(inp["L0_attn_b_xproj_w"])[:, dsl, :])
        m["dtw"] = np.ascontiguousarray(np.asarray(inp["L0_attn_b_dt_w"])[:, :, dsl])
        m["dtb"] = np.ascontiguousarray(np.asarray(inp["L0_attn_b_dt_b"])[:, dsl])
        m["alog"] = np.ascontiguousarray(np.asarray(inp["L0_attn_b_A_log"])[:, dsl, :])
        m["dpv"] = np.ascontiguousarray(np.asarray(inp["L0_attn_b_Dp"])[:, dsl])
        m["outw"] = np.ascontiguousarray(np.asarray(inp["L0_attn_b_out_w"]))
        maps.append(m)
    return maps



ENGS = ("tensor", "vector", "scalar", "gpsimd", "sync")

GRAN = {"ar0": 1024, "ar1": 1024, "ar2": 1024, "ar3": 1024,
        "E": 512, "F": 512, "G": 32, "W0": 128, "W1": 128,
        "psA": 512, "psB": 512, "psC": 512, "psT": 1024}


def keys(name, lo=None, hi=None):
    if lo is None:
        return ((name, 0),)
    gw = GRAN.get(name, 1 << 30)
    return tuple((name, g) for g in range(lo // gw, (hi + gw - 1) // gw))


class _Op:
    __slots__ = ("idx", "eng", "fn", "deps", "kind", "marked", "inc_no",
                 "waits", "clock", "ccsem", "tag")

    def __init__(self, idx, eng, fn, kind, tag):
        self.idx, self.eng, self.fn, self.kind, self.tag = idx, eng, fn, kind, tag
        self.deps = []
        self.marked = False
        self.inc_no = None
        self.waits = None
        self.clock = None
        self.ccsem = None


class Prog:
    def __init__(self):
        self.ops = []
        self.last_w = {}
        self.readers = {}
        self.n_cc = 0

    def op(self, eng, fn, reads=(), writes=(), kind="c", tag=""):
        o = _Op(len(self.ops), eng, fn, kind, tag)
        dep = {}
        for k in reads:
            w = self.last_w.get(k)
            if w is not None:
                dep[w.idx] = w
        for k in writes:
            w = self.last_w.get(k)
            if w is not None:
                dep[w.idx] = w
            for rd in self.readers.get(k, ()):
                dep[rd.idx] = rd
        o.deps = [d for d in dep.values()
                  if not (d.eng == "tensor" and eng == "tensor"
                          and d.kind == "c" and kind == "c")]
        for k in reads:
            self.readers.setdefault(k, []).append(o)
        for k in writes:
            self.last_w[k] = o
            self.readers[k] = []
        if kind == "cc":
            o.ccsem = self.n_cc
            self.n_cc += 1
        self.ops.append(o)
        return o

    def finalize_and_emit(self, block, sems):
        for o in self.ops:
            if o.kind == "d":
                o.marked = True
            for d in o.deps:
                d.marked = True
        counters = {}
        for o in self.ops:
            if not o.marked:
                continue
            if o.kind == "cc":
                o.inc_no = 1
            else:
                key = (o.eng, o.kind)
                counters[key] = counters.get(key, 0) + 1
                o.inc_no = counters[key]

        def semkey(o):
            return ("cc", o.ccsem) if o.kind == "cc" else (o.eng, o.kind)

        def val(o):
            return o.inc_no * (16 if o.kind == "d" else 1)

        last_on_eng = {}
        for o in self.ops:
            prev = last_on_eng.get(o.eng)
            base = dict(prev.clock) if prev is not None else {}
            need = {}
            for d in o.deps:
                sk, v = semkey(d), val(d)
                if need.get(sk, -1) < v:
                    need[sk] = v
                for sk2, v2 in d.clock.items():
                    if base.get(sk2, -1) < v2:
                        base[sk2] = v2
            o.waits = {sk: v for sk, v in need.items() if base.get(sk, 0) < v}
            clk = base
            for sk, v in need.items():
                if clk.get(sk, 0) < v:
                    clk[sk] = v
            o.clock = clk
            last_on_eng[o.eng] = o

        by_eng = {e: [] for e in ENGS}
        for o in self.ops:
            by_eng[o.eng].append(o)

        def emit_stream(eng_name):
            ops = by_eng[eng_name]

            def go(eng):
                for o in ops:
                    for sk in sorted(o.waits, key=str):
                        eng.wait_ge(sems[sk], o.waits[sk])
                    inst = o.fn(eng)
                    if o.kind == "cc":
                        inst.then_inc(sems[("cc", o.ccsem)])
                    elif o.marked:
                        if o.kind == "d":
                            inst.then_inc(sems[(o.eng, "d")], 16)
                        else:
                            inst.then_inc(sems[(o.eng, "c")], 1)
            return go

        for e in ENGS:
            if by_eng[e]:
                getattr(block, e)(emit_stream(e))


def ceil_div(a, b):
    return (a + b - 1) // b


def as_col(ap1d):
    return ap1d.rearrange("(a b) -> a b", b=1)


def as_row(ap1d):
    return ap1d.rearrange("(a b) -> a b", a=1)



class K:
    def __init__(self, cfg):
        self.cfg = cfg
        self.nc = bass.Bass("TRN2", target_bir_lowering=False, debug=False)
        self.sb = {}
        self.sbuf_bytes = 0

    def dtn(self, name, shape, dtype, kind="Internal", shared=False):
        if shared:
            return self.nc.dram_tensor(name, list(shape), dtype,
                                       addr_space="Shared")
        if kind == "Internal":
            return self.nc.dram_tensor(name, list(shape), dtype)
        return self.nc.dram_tensor(name, list(shape), dtype, kind=kind)

    def st(self, name, shape, dtype):
        t = self.ctx.enter_context(self.nc.sbuf_tensor(name, list(shape), dtype))
        self.sb[name] = t
        self.sbuf_bytes += int(np.prod(shape[1:])) * mybir.dt.size(dtype)
        return t

    def pt(self, name, shape, dtype):
        t = self.ctx.enter_context(self.nc.psum_tensor(name, list(shape), dtype))
        self.sb[name] = t
        return t

    def mm(self, out, lhsT, rhs, start, stop, reads, writes, tag="mm", tp=None):
        self.p.op("tensor",
                  lambda e, o=out, l=lhsT, r=rhs, s1=start, s2=stop, t=tp:
                  e.matmul(o, l, r, start=s1, stop=s2, tile_position=t),
                  reads, writes, tag=tag)

    def trans(self, out, in_, reads, writes, tag="tr"):
        ident = self.sb["c_ident"][:]
        self.p.op("tensor",
                  lambda e, o=out, i=in_, d=ident: e.transpose(o, i, d),
                  tuple(reads) + keys("c_ident"), writes, tag=tag)

    def cp(self, eng, dst, src, reads, writes, tag="cp"):
        if eng == "scalar":
            self.p.op("scalar", lambda e, d=dst, s=src: e.copy(d, s),
                      reads, writes, tag=tag)
        else:
            self.p.op("vector", lambda e, d=dst, s=src: e.tensor_copy(d, s),
                      reads, writes, tag=tag)

    def act(self, dst, src, func, reads, writes, bias=None, scale=None,
            accum=None, tag="act"):
        kw = {}
        if bias is not None:
            kw["bias"] = bias
        if scale is not None:
            kw["scale"] = scale
        if accum is not None:
            kw["accum_out"] = accum
        self.p.op("scalar",
                  lambda e, d=dst, s=src, f=func, kw=kw: e.activation(d, s, f, **kw),
                  reads, writes, tag=tag)

    def tt(self, op_alu, dst, a, b, reads, writes, tag="tt", eng="vector"):
        self.p.op(eng, lambda e, d=dst, x=a, y=b, o=op_alu:
                  e.tensor_tensor(out=d, in0=x, in1=y, op=o), reads, writes, tag=tag)

    def ts(self, dst, a, s1, op0, reads, writes, s2=None, op1=None, tag="ts"):
        def f(e, d=dst, x=a, v1=s1, v2=s2, o0=op0, o1=op1):
            if o1 is None:
                return e.tensor_scalar(out=d, in0=x, scalar1=v1, scalar2=None, op0=o0)
            return e.tensor_scalar(out=d, in0=x, scalar1=v1, scalar2=v2,
                                   op0=o0, op1=o1)
        self.p.op("vector", f, reads, writes, tag=tag)

    def stt(self, dst, in0, scalar, in1, op0, op1, reads, writes, tag="stt"):
        self.p.op("vector", lambda e, d=dst, a=in0, s=scalar, b=in1, o0=op0, o1=op1:
                  e.scalar_tensor_tensor(out=d, in0=a, scalar=s, in1=b, op0=o0, op1=o1),
                  reads, writes, tag=tag)

    def scan_op(self, dst, d0, d1, init, reads, writes, tag="scan"):
        self.p.op("vector", lambda e, d=dst, a=d0, b=d1, i=init:
                  e.tensor_tensor_scan(out=d, data0=a, data1=b, initial=i,
                                       op0=ALU.mult, op1=ALU.add),
                  reads, writes, tag=tag)

    def red(self, dst, src, op_alu, reads, writes, tag="red"):
        self.p.op("vector", lambda e, d=dst, s=src, o=op_alu:
                  e.tensor_reduce(out=d, in_=s, axis=AX.X, op=o), reads, writes, tag=tag)

    def recip(self, dst, src, reads, writes, tag="recip"):
        self.p.op("vector", lambda e, d=dst, s=src: e.reciprocal(d, s),
                  reads, writes, tag=tag)

    def dma(self, out_ap, in_ap, reads, writes, eng="sync", tag="dma",
            transpose=False):
        if transpose:
            self.p.op(eng, lambda e, o=out_ap, i=in_ap:
                      e.dma_start(out=o, in_=i, transpose=True),
                      reads, writes, kind="d", tag=tag)
        else:
            self.p.op(eng, lambda e, o=out_ap, i=in_ap: e.dma_start(out=o, in_=i),
                      reads, writes, kind="d", tag=tag)

    def coll(self, kind, op_alu, in_ap, out_ap, reads, writes, tag="cc"):
        groups = [list(range(self.cfg.R))]
        self.p.op("gpsimd", lambda e, k=kind, o=op_alu, i=in_ap, u=out_ap, g=groups:
                  e.collective_compute(k, o, replica_groups=g, ins=[i], outs=[u]),
                  reads, writes, kind="cc", tag=tag)

    def memset(self, ap, v, reads, writes, tag="memset", eng="gpsimd"):
        self.p.op(eng, lambda e, a=ap, vv=v: e.memset(a, vv), reads, writes, tag=tag)


    def dma_wfm(self, dst_slice, src_ap, mcols, reads, writes, tag):
        self.dma(dst_slice.rearrange("p (c m) -> p c m", m=mcols),
                 src_ap.rearrange("(c p) m -> p c m", p=128),
                 reads=reads, writes=writes, tag=tag)

    def build(self):
        import contextlib
        cfg = self.cfg
        nc = self.nc
        self.ctx = contextlib.ExitStack()
        S, B, D, TL, T = cfg.S, cfg.B, cfg.D, cfg.TL, cfg.T

        ein = {}

        def EIN(name, shape, dtype=F32):
            ein[name] = self.dtn(name, shape, dtype, kind="ExternalInput").ap()

        EIN("x_in", [TL, D])
        for nm, shp in [("wqa", [D, 128]), ("wka", [D, 128]), ("wva", [D, 128]),
                        ("woa", [D, D]), ("wqf", [D, 128]), ("wkf", [D, 128]),
                        ("wvf", [D, 128]), ("wof", [D, D]), ("wff", [D, D]),
                        ("lin1_w", [D, cfg.DM]), ("lin1_b", [1, cfg.DM]),
                        ("lin2_w", [cfg.DM, D]), ("lin2_b", [1, D]),
                        ("norm_w", [cfg.NL, cfg.DM]), ("inw", [cfg.NL, cfg.DM, 256]),
                        ("convw", [cfg.NL, 128, cfg.CK]), ("convb", [cfg.NL, 128]),
                        ("xprojw", [cfg.NL, 128, 64]), ("dtw", [cfg.NL, cfg.DTR, 128]),
                        ("dtb", [cfg.NL, 128]), ("alog", [cfg.NL, 128, cfg.DS]),
                        ("dpv", [cfg.NL, 128]), ("outw", [cfg.NL, cfg.DI, cfg.DM])]:
            EIN(nm, shp)
        for nm, shp, dt_ in [("ident", [128, 128], BF16), ("ropeK", [128, S], BF16),
                             ("ropeQ", [128, S], BF16), ("headmask_pi", [128, 2], BF16),
                             ("headmask", [128, 2], BF16),
                             ("band", [128, 2 * cfg.TCH - 128], BF16),
                             ("ones_1x128_f32", [1, 128], F32),
                             ("ones4", [128, 128], F32),
                             ("ones_1x64_f32", [1, 64], F32),
                             ("ones_col_f32", [128, 1], F32),
                             ("ones_1x128_bf", [1, 128], BF16),
                             ("ones_col_bf", [128, 1], BF16),
                             ("cbias", [128, 4], F32),
                             ("ones_dram_bf", [S], BF16),
                             ("gidx", [128, 24], I32)]:
            EIN(nm, shp, dt_)
        self.ein = ein
        out_t = self.dtn("out", [TL, D], F32, kind="ExternalOutput").ap()

        wbf = {}
        for nm, shp in [("wqa", [D, 128]), ("wka", [D, 128]), ("wva", [D, 128]),
                        ("woa", [D, D]), ("wqf", [D, 128]), ("wkf", [D, 128]),
                        ("wvf", [D, 128]), ("wof", [D, D]), ("wff", [D, D]),
                        ("lin1_w", [D, cfg.DM]), ("lin2_w", [cfg.DM, D])]:
            wbf[nm] = self.dtn("wbf_" + nm, shp, BF16).ap()
        for l in range(cfg.NL):
            wbf[f"inw{l}"] = self.dtn(f"wbf_inw{l}", [cfg.DM, 256], BF16).ap()
            wbf[f"xprojw{l}"] = self.dtn(f"wbf_xprojw{l}", [128, 64], BF16).ap()
            wbf[f"dtw{l}"] = self.dtn(f"wbf_dtw{l}", [cfg.DTR, 128], BF16).ap()
            wbf[f"outw{l}"] = self.dtn(f"wbf_outw{l}", [cfg.DI, cfg.DM], BF16).ap()
        self.wbf = wbf

        dd = {}
        dd["ag_xn_in"] = self.dtn("ag_xn_in", [D, TL], BF16).ap()
        dd["ag_xn_out"] = self.dtn("ag_xn_out", [cfg.R * D, TL], BF16, shared=True).ap()
        dd["ag_hr_in"] = self.dtn("ag_hr_in", [cfg.DM, TL], BF16).ap()
        dd["ag_hr_out"] = self.dtn("ag_hr_out", [cfg.R * cfg.DM, TL], BF16, shared=True).ap()
        dd["CONTRX"] = self.dtn("CONTRX", [cfg.R, 256, cfg.pos], BF16).ap()
        for u in range(3):
            dd[f"cxg{u}"] = self.dtn(f"cxg{u}", [8 * cfg.R, 256, cfg.pos],
                                     BF16, shared=True).ap()
        dd["YX"] = self.dtn("YX", [cfg.R, 128, TL], BF16).ap()
        for u in range(4):
            dd[f"yxg{u}"] = self.dtn(f"yxg{u}", [8 * cfg.R, 128, TL],
                                     BF16, shared=True).ap()
        dd["dblpart"] = self.dtn("dblpart", [64, T], F32).ap()
        dd["dblfull"] = self.dtn("dblfull", [64, T], F32, shared=True).ap()
        dd["QA"] = self.dtn("QA", [4, 65, S], BF16).ap()
        dd["KA"] = self.dtn("KA", [4, 65, S], BF16).ap()
        dd["VA"] = self.dtn("VA", [4, S, 64], BF16).ap()
        dd["norms_a"] = self.dtn("norms_a", [4, 2, S], F32).ap()
        dd["QF"] = self.dtn("QF", [B, 128, S], BF16).ap()
        dd["KF"] = self.dtn("KF", [B, 128, S], BF16).ap()
        dd["VF"] = self.dtn("VF", [B, S, 128], BF16).ap()
        dd["norms_f"] = self.dtn("norms_f", [B, 2, 2, S], F32).ap()
        dd["negMk"] = self.dtn("negMk", [B, 2, S], BF16).ap()
        dd["ZSP"] = self.dtn("ZSP", [128, T], BF16).ap()
        dd["USP"] = self.dtn("USP", [128, T], BF16).ap()
        dd["x_in"] = ein["x_in"]
        dd["xcur"] = self.dtn("xcur", [TL, D], F32).ap()
        dd["out"] = out_t
        self.d = dd

        AW = 16448
        for i in range(4):
            self.st(f"ar{i}", [128, AW], BF16)
        self.st("E", [128, 8192], F32)
        self.st("F", [128, 4096], F32)
        self.st("G", [128, 2304], F32)
        self.st("W0", [128, 4096], BF16)
        self.st("W1", [128, 2048], BF16)
        self.st("c_ident", [128, 128], BF16)
        self.st("c_hmpi", [128, 2], BF16)
        self.st("c_hm", [128, 2], BF16)
        self.st("c_band", [128, 2 * cfg.TCH - 128], BF16)
        self.st("c_o1x128f", [1, 128], F32)
        self.st("c_o4", [128, 128], F32)
        self.st("c_o1x64f", [1, 64], F32)
        self.st("c_ocolf", [128, 1], F32)
        self.st("c_o1x128b", [1, 128], BF16)
        self.st("c_ocolb", [128, 1], BF16)
        self.st("c_cb", [128, 4], F32)
        self.st("IDXS", [128, 24], I32)
        self.pt("psA", [128, 2048], F32)
        self.pt("psB", [128, 1024], F32)
        self.pt("psC", [128, 512], F32)
        self.pt("psT", [128, 1024], BF16)

        self.p = Prog()
        ph = cfg.phases
        if "p0" in ph:
            self.phase_p0()
        if "attn_a" in ph:
            self.phase_ln_ag(extra_rms=True, tag="ln1", do_ag=True, fm_ar="ar0")
            self.phase_attn_a_proj()
            self.phase_attn_a_attn()
            self.phase_wo_a2a("woa_out", "woa", "attna", "cxg0")
        for p_ in ph:
            if p_.startswith("ffn1"):
                stop = p_.split(":")[1] if ":" in p_ else None
                self.phase_ffn(0, stop=stop)
        if "mamba" in ph:
            self.phase_mamba()
        if "ffn2" in ph:
            self.phase_ffn(1)
        if "out" in ph:
            self.phase_out()

        sems = {}
        for e in ENGS:
            for knd in ("c", "d"):
                sems[(e, knd)] = self.ctx.enter_context(nc.semaphore(f"s_{e}_{knd}"))
        for i in range(self.p.n_cc):
            sems[("cc", i)] = self.ctx.enter_context(nc.semaphore(f"s_cc_{i}"))
        with nc.Block() as block:
            self.p.finalize_and_emit(block, sems)
        self.ctx.close()
        return nc

    def phase_p0(self):
        cfg, sb, ein = self.cfg, self.sb, self.ein
        for nm, dst in [("ident", "c_ident"), ("headmask_pi", "c_hmpi"),
                        ("headmask", "c_hm"), ("band", "c_band"),
                        ("ones_1x128_f32", "c_o1x128f"), ("ones4", "c_o4"),
                        ("ones_1x64_f32", "c_o1x64f"),
                        ("ones_col_f32", "c_ocolf"), ("ones_1x128_bf", "c_o1x128b"),
                        ("ones_col_bf", "c_ocolb"), ("cbias", "c_cb")]:
            self.dma(sb[dst][:], ein[nm], reads=(), writes=keys(dst), tag="cload")
        self.dma(self.d["xcur"], self.d["x_in"], reads=(), writes=keys("xcur"),
                 tag="xload")
        self.dma(sb["IDXS"][:], ein["gidx"], reads=(), writes=keys("IDXS"),
                 tag="gidxload")
        F, G, ar0 = sb["F"], sb["G"], sb["ar0"]

        def cast(ap_in, ap_out, rows, cols, okey, scale_rows=None):
            for i in range(ceil_div(rows, 128)):
                r0, rn = i * 128, min(128, rows - i * 128)
                for j in range(ceil_div(cols, 2048)):
                    c0, cn = j * 2048, min(2048, cols - j * 2048)
                    pp = (i + j) % 2
                    kf = keys("F", pp * 2048, pp * 2048 + cn)
                    ka = keys("ar0", pp * 2048, pp * 2048 + cn)
                    self.dma(F[0:rn, pp * 2048:pp * 2048 + cn],
                             ap_in[r0:r0 + rn, c0:c0 + cn],
                             reads=(), writes=kf, tag="wc_in")
                    if scale_rows is not None:
                        ks = keys("G", 2176 + pp, 2177 + pp)
                        self.dma(G[0:rn, 2176 + pp:2177 + pp],
                                 as_col(scale_rows[r0:r0 + rn]),
                                 reads=(), writes=ks, tag="wc_sc")
                        self.ts(F[0:rn, pp * 2048:pp * 2048 + cn],
                                F[0:rn, pp * 2048:pp * 2048 + cn],
                                G[0:rn, 2176 + pp:2177 + pp], ALU.mult,
                                reads=kf + ks, writes=kf, tag="wc_scale")
                    self.cp("vector", ar0[0:rn, pp * 2048:pp * 2048 + cn],
                            F[0:rn, pp * 2048:pp * 2048 + cn],
                            reads=kf, writes=ka, tag="wc_cast")
                    self.dma(ap_out[r0:r0 + rn, c0:c0 + cn],
                             ar0[0:rn, pp * 2048:pp * 2048 + cn],
                             reads=ka, writes=((okey, 0),), tag="wc_out")

        wb = self.wbf
        for nm, rows, cols in [("wqa", cfg.D, 128), ("wka", cfg.D, 128),
                               ("wva", cfg.D, 128), ("woa", cfg.D, cfg.D),
                               ("wqf", cfg.D, 128), ("wkf", cfg.D, 128),
                               ("wvf", cfg.D, 128), ("wof", cfg.D, cfg.D),
                               ("wff", cfg.D, cfg.D), ("lin1_w", cfg.D, cfg.DM),
                               ("lin2_w", cfg.DM, cfg.D)]:
            cast(ein[nm], wb[nm], rows, cols, "wbf_" + nm)
        for l in range(cfg.NL):
            cast(ein["inw"][l], wb[f"inw{l}"], cfg.DM, 256, f"wbf_inw{l}",
                 scale_rows=ein["norm_w"][l, :])
            cast(ein["xprojw"][l], wb[f"xprojw{l}"], 128, 64, f"wbf_xprojw{l}")
            cast(ein["dtw"][l], wb[f"dtw{l}"], cfg.DTR, 128, f"wbf_dtw{l}")
            cast(ein["outw"][l], wb[f"outw{l}"], cfg.DI, cfg.DM, f"wbf_outw{l}")

    def phase_ln_ag(self, extra_rms, tag, do_ag, fm_ar):
        cfg, sb = self.cfg, self.sb
        D, NT, TL = cfg.D, cfg.NT_LOC, cfg.TL
        F, G, W1, E = sb["F"], sb["G"], sb["W1"], sb["E"]
        arF = sb[fm_ar]
        psT = sb["psT"]
        nch = D // 128
        for i in range(NT):
            pp = i % 2
            xo = pp * 2048
            gs = pp * 16
            kx = keys("F", xo, xo + 1024)
            kxc = keys("F", xo + 1024, xo + 2048)
            kst = keys("G", gs, gs + 16)
            ke = keys("E", pp * 1024, pp * 1024 + 1024)
            self.dma(F[:, xo:xo + 1024], self.d["xcur"][128 * i:128 * (i + 1), :],
                     reads=keys("xcur"), writes=kx, tag=tag + "_in")
            self.red(G[:, gs:gs + 1], F[:, xo:xo + 1024], ALU.add,
                     reads=kx, writes=kst, tag=tag + "_sum")
            self.ts(G[:, gs + 1:gs + 2], G[:, gs:gs + 1], 1.0 / D, ALU.mult,
                    reads=kst, writes=kst, tag=tag + "_mean")
            self.ts(F[:, xo + 1024:xo + 2048], F[:, xo:xo + 1024],
                    G[:, gs + 1:gs + 2], ALU.subtract,
                    reads=kx + kst, writes=kxc, tag=tag + "_sub")
            self.act(E[:, pp * 1024:pp * 1024 + 1024], F[:, xo + 1024:xo + 2048],
                     AF.Square, reads=kxc, writes=ke, tag=tag + "_sq")
            self.red(G[:, gs + 2:gs + 3], E[:, pp * 1024:pp * 1024 + 1024],
                     ALU.add, reads=ke, writes=kst, tag=tag + "_var")
            self.act(G[:, gs + 3:gs + 4], G[:, gs + 2:gs + 3], AF.Sqrt,
                     reads=kst + keys("c_cb"), writes=kst, scale=1.0 / D,
                     bias=self.sb["c_cb"][:, 0:1], tag=tag + "_std")
            self.recip(G[:, gs + 4:gs + 5], G[:, gs + 3:gs + 4],
                       reads=kst, writes=kst, tag=tag + "_rstd")
            self.ts(F[:, xo:xo + 1024], F[:, xo + 1024:xo + 2048],
                    G[:, gs + 4:gs + 5], ALU.mult,
                    reads=kxc + kst, writes=kx, tag=tag + "_norm")
            if extra_rms:
                self.act(E[:, pp * 1024:pp * 1024 + 1024], F[:, xo:xo + 1024],
                         AF.Square, reads=kx, writes=ke, tag=tag + "_sq2")
                self.red(G[:, gs + 5:gs + 6], E[:, pp * 1024:pp * 1024 + 1024],
                         ALU.add, reads=ke, writes=kst, tag=tag + "_v2")
                self.act(G[:, gs + 6:gs + 7], G[:, gs + 5:gs + 6], AF.Sqrt,
                         reads=kst + keys("c_cb"), writes=kst, scale=1.0 / D,
                         bias=self.sb["c_cb"][:, 1:2], tag=tag + "_std2")
                self.recip(G[:, gs + 7:gs + 8], G[:, gs + 6:gs + 7],
                           reads=kst, writes=kst, tag=tag + "_rstd2")
                self.ts(F[:, xo:xo + 1024], F[:, xo:xo + 1024],
                        G[:, gs + 7:gs + 8], ALU.mult,
                        reads=kx + kst, writes=kx, tag=tag + "_norm2")
            kbf = keys("W1", pp * 1024, pp * 1024 + 1024)
            self.cp("vector", W1[:, pp * 1024:pp * 1024 + 1024], F[:, xo:xo + 1024],
                    reads=kx, writes=kbf, tag=tag + "_cast")
            for j in range(nch):
                kT = keys("psT", 128 * (j % 8), 128 * (j % 8) + 128)
                self.trans(psT[:, 128 * (j % 8):128 * (j % 8) + 128],
                           W1[:, pp * 1024 + 128 * j:pp * 1024 + 128 * (j + 1)],
                           reads=kbf, writes=kT, tag=tag + "_T")
                self.cp("vector",
                        arF[:, TL * j + 128 * i:TL * j + 128 * (i + 1)],
                        psT[:, 128 * (j % 8):128 * (j % 8) + 128],
                        reads=kT, writes=keys(fm_ar, TL * j + 128 * i,
                                              TL * j + 128 * (i + 1)),
                        tag=tag + "_Tev")
        if do_ag:
            for j in range(nch):
                self.dma(self.d["ag_xn_in"][128 * j:128 * (j + 1), :],
                         arF[:, TL * j:TL * j + TL],
                         reads=keys(fm_ar, TL * j, TL * j + TL),
                         writes=(("ag_xn_in", j),), tag=tag + "_agin")
            self.coll("AllGather", ALU.bypass, self.d["ag_xn_in"],
                      self.d["ag_xn_out"],
                      reads=tuple(("ag_xn_in", j) for j in range(nch)),
                      writes=(("ag_xn_out", 0),), tag=tag + "_AG")

    def phase_attn_a_proj(self):
        cfg, sb, d = self.cfg, self.sb, self.d
        S, D = cfg.S, cfg.D
        nch = D // 128
        W0, W1, G, F = sb["W0"], sb["W1"], sb["G"], sb["F"]
        psA, psB, psT = sb["psA"], sb["psB"], sb["psT"]
        CL = cfg.TL
        half = CL // 2
        for nm, off in [("wqa", 0), ("wka", 1024), ("wva", 2048)]:
            self.dma_wfm(W0[:, off:off + 1024], self.wbf[nm], 128,
                         reads=(("wbf_" + nm, 0),),
                         writes=keys("W0", off, off + 1024), tag="aw_load")
        for u in range(4):
            self.dma(d["KA"][u, 64, :], self.ein["ones_dram_bf"],
                     reads=(), writes=(("KA", u),), tag="ka_ones")

        for r in range(cfg.R):
            pp = r % 2
            arn = f"ar{pp}"
            arin = sb[arn]
            kin = keys(arn, 0, nch * CL)
            self.dma(arin[:, 0:nch * CL].rearrange("p (c m) -> p c m", m=CL),
                     d["ag_xn_out"][128 * nch * r:128 * nch * (r + 1), :]
                     .rearrange("(c p) m -> p c m", p=128),
                     reads=(("ag_xn_out", 0),), writes=kin, tag="aproj_in")
            kRT = keys("ar3", 0, 2048)
            self.dma(sb["ar3"][:, 0:1024],
                     self.ein["ropeQ"][:, cfg.pos * r:cfg.pos * r + cfg.pos],
                     reads=(), writes=kRT, tag="rtq")
            self.dma(sb["ar3"][:, 1024:2048],
                     self.ein["ropeK"][:, cfg.pos * r:cfg.pos * r + cfg.pos],
                     reads=(), writes=kRT, tag="rtk")

            def project(woff):
                kw = keys("W0", woff, woff + 1024)
                kps = keys("psA", 0, CL)
                for j in range(nch):
                    for c_ in range(CL // 512):
                        self.mm(psA[:, 512 * c_:512 * (c_ + 1)],
                                W0[:, woff + 128 * j:woff + 128 * (j + 1)],
                                arin[:, CL * j + 512 * c_:CL * j + 512 * (c_ + 1)],
                                start=(j == 0), stop=(j == nch - 1),
                                reads=kin + kw, writes=kps, tag="aproj_mm")
                return kps

            def rope(kps, toff):
                tb = sb["ar3"]
                kd = keys("W1", 0, CL)
                for b in range(2):
                    cs = b * half
                    kt = keys("F", 0, 2 * half)
                    self.tt(ALU.mult, F[0:64, 0:half], psA[0:64, cs:cs + half],
                            tb[0:64, toff:toff + half], reads=kps + kRT,
                            writes=kt, tag="rope1")
                    self.tt(ALU.mult, F[0:64, half:2 * half],
                            psA[64:128, cs:cs + half],
                            tb[64:128, toff:toff + half], reads=kps + kRT,
                            writes=kt, tag="rope2")
                    self.tt(ALU.subtract, W1[0:64, cs:cs + half], F[0:64, 0:half],
                            F[0:64, half:2 * half], reads=kt, writes=kd,
                            tag="rope3")
                    self.tt(ALU.mult, F[0:64, 0:half], psA[0:64, cs:cs + half],
                            tb[64:128, toff:toff + half], reads=kps + kRT,
                            writes=kt, tag="rope4")
                    self.tt(ALU.mult, F[0:64, half:2 * half],
                            psA[64:128, cs:cs + half],
                            tb[0:64, toff:toff + half], reads=kps + kRT,
                            writes=kt, tag="rope5")
                    self.tt(ALU.add, W1[64:128, cs:cs + half], F[0:64, 0:half],
                            F[0:64, half:2 * half], reads=kt, writes=kd,
                            tag="rope6")
                return kd

            def norms_store(kd, qk, dst_dram, dst_name):
                ksq = keys("ar2", 0, CL)
                self.act(sb["ar2"][:, 0:CL], W1[:, 0:CL], AF.Square,
                         reads=kd, writes=ksq, tag="an_sq")
                kn = keys("psA", 0, CL)
                for c_ in range(CL // 512):
                    self.mm(psA[0:2, 512 * c_:512 * (c_ + 1)], sb["c_hmpi"][:],
                            sb["ar2"][:, 512 * c_:512 * (c_ + 1)],
                            start=True, stop=True, reads=ksq + keys("c_hmpi"),
                            writes=kn, tag="an_mm")
                kev = keys("F", 2048, 2048 + CL)
                self.cp("vector", F[0:2, 2048:2048 + CL], psA[0:2, 0:CL],
                        reads=kn, writes=kev, tag="an_ev")
                for hp in range(2):
                    for b in range(2):
                        uu = b * 2 + hp
                        self.dma(d["norms_a"][uu, qk, cfg.pos * r:cfg.pos * r + half],
                                 F[hp:hp + 1, 2048 + b * half:2048 + (b + 1) * half],
                                 reads=kev, writes=(("norms_a", uu, qk),),
                                 tag="an_out")
                        for piece, r0 in ((0, 32 * hp), (1, 64 + 32 * hp)):
                            self.dma(dst_dram[uu, 32 * piece:32 * piece + 32,
                                              cfg.pos * r: cfg.pos * r + half],
                                     W1[r0:r0 + 32, b * half:(b + 1) * half],
                                     reads=kd, writes=((dst_name, uu),),
                                     tag="aqk_out")

            kq = project(0)
            kdq = rope(kq, 0)
            norms_store(kdq, 0, d["QA"], "QA")
            kk = project(1024)
            kdk = rope(kk, 1024)
            norms_store(kdk, 1, d["KA"], "KA")
            kv = project(2048)
            kvs = keys("ar2", 4096, 4096 + CL)
            self.cp("scalar", sb["ar2"][:, 4096:4096 + CL], psA[:, 0:CL],
                    reads=kv, writes=kvs, tag="av_ev")
            ntile = CL // 128
            for g in range(ceil_div(ntile, 8)):
                kT = keys("psT", 0, 1024)
                for tl in range(min(8, ntile - 8 * g)):
                    i = 8 * g + tl
                    self.trans(psT[:, 128 * tl:128 * (tl + 1)],
                               sb["ar2"][:, 4096 + 128 * i:4096 + 128 * (i + 1)],
                               reads=kvs, writes=kT, tag="av_T")
                eb = 8192 + 1024 * (g % 2)
                kTe = keys("ar2", eb, eb + 1024)
                self.cp("vector", sb["ar2"][:, eb:eb + 1024], psT[:, 0:1024],
                        reads=kT, writes=kTe, tag="av_Tev")
                for tl in range(min(8, ntile - 8 * g)):
                    i = 8 * g + tl
                    b = i // (ntile // 2)
                    row0 = cfg.pos * r + 128 * (i % (ntile // 2))
                    for hp in range(2):
                        uu = b * 2 + hp
                        self.dma(d["VA"][uu, row0:row0 + 128, :],
                                 sb["ar2"][:, eb + 128 * tl + 64 * hp:
                                           eb + 128 * tl + 64 * (hp + 1)],
                                 reads=kTe, writes=(("VA", uu),), tag="av_out")
        for u in range(4):
            ku = keys("G", 2048, 2080)
            ng = max(1, S // 2048)
            w = min(2048, S)
            for gg in range(ng):
                kg = keys("G", 0, w)
                self.dma(G[0:1, 0:w], d["norms_a"][u, 1, w * gg:w * (gg + 1)],
                         reads=(("norms_a", u, 1),), writes=kg, tag="mk_in")
                self.red(G[0:1, 2048 + gg:2049 + gg], G[0:1, 0:w], ALU.max,
                         reads=kg, writes=ku, tag="mk_red")
            self.red(G[0:1, 2080:2081], G[0:1, 2048:2048 + ng], ALU.max,
                     reads=ku, writes=keys("G", 2080, 2081), tag="mk_red2")
            self.act(G[0:1, 2081:2082], G[0:1, 2080:2081], AF.Sqrt,
                     reads=keys("G", 2080, 2081), writes=keys("G", 2081, 2082),
                     tag="mk_sqrt")
            for r in range(S // 1024):
                km = keys("G", 0, 1024)
                km2 = keys("G", 1024, 2048)
                self.dma(G[0:1, 0:1024],
                         d["norms_a"][u, 0, 1024 * r:1024 * (r + 1)],
                         reads=(("norms_a", u, 0),), writes=km, tag="mr_in")
                self.act(G[0:1, 1024:2048], G[0:1, 0:1024], AF.Sqrt,
                         reads=km, writes=km2, tag="mr_sqrt")
                self.ts(G[0:1, 0:1024], G[0:1, 1024:2048],
                        G[0:1, 2081:2082], ALU.mult,
                        reads=km2 + keys("G", 2081, 2082), writes=km,
                        s2=-1.0, op1=ALU.mult, tag="mr_mul")
                self.cp("vector", W1[0:1, 0:1024], G[0:1, 0:1024],
                        reads=km, writes=keys("W1", 0, 1024), tag="mr_cast")
                self.dma(d["QA"][u, 64, 1024 * r:1024 * (r + 1)],
                         W1[0:1, 0:1024], reads=keys("W1", 0, 1024),
                         writes=(("QA", u),), tag="mr_out")

    def phase_attn_a_attn(self):
        cfg, sb, d = self.cfg, self.sb, self.d
        S, TCH = cfg.S, cfg.TCH
        ar0, ar1, ar2 = sb["ar0"], sb["ar1"], sb["ar2"]
        psA, psB, G, E = sb["psA"], sb["psB"], sb["G"], sb["E"]
        nstile = S // 128
        nchunk = S // TCH
        spt = TCH // 128
        VOFF, POFF, COFF = 0, 4160, 10240
        for u in range(4):
            b, hp = u // 2, u % 2
            kq = keys("ar0", 0, S)
            kk = keys("ar1", 0, S)
            kv = keys("ar2", VOFF, VOFF + 65 * nstile)
            self.dma(ar0[0:65, 0:S], d["QA"][u], reads=(("QA", u),),
                     writes=kq, tag="aa_lq")
            self.dma(ar1[0:65, 0:S], d["KA"][u], reads=(("KA", u),),
                     writes=kk, tag="aa_lk")
            self.dma(ar2[:, VOFF:VOFF + 65 * nstile].rearrange(
                         "p (i c) -> p i c", c=65)[:, :, 0:64],
                     d["VA"][u].rearrange("(i p) c -> p i c", p=128),
                     reads=(("VA", u),), writes=kv, tag="aa_lv")
            self.memset(ar2[:, VOFF:VOFF + 65 * nstile].rearrange(
                            "p (i c) -> p i c", c=65)[:, :, 64:65], 1.0,
                        reads=(), writes=kv, tag="aa_vone", eng="vector")
            psC = sb["psC"]
            psTf = sb["psT"][:].bitcast(F32)
            for r in range(nchunk):
                nj = spt * (r + 1)
                kout = keys("psB", 0, TCH)

                def aa_pv(j):
                    po = 1024 * (j % 3)
                    kP = keys("ar2", POFF + po, POFF + po + 1024)
                    for c_ in range(TCH // 512):
                        self.mm(psB[0:65, 512 * c_:512 * (c_ + 1)],
                                ar2[:, VOFF + 65 * j:VOFF + 65 * (j + 1)],
                                ar2[:, POFF + po + 512 * c_:
                                    POFF + po + 512 * (c_ + 1)],
                                start=(j == 0), stop=(j == nj - 1),
                                reads=kP + kv, writes=kout, tag="aa_pv")

                for j in range(nj):
                    po = 1024 * (j % 3)
                    r3 = j % 3
                    if r3 < 2:
                        halves = [(psA[:, 1024 * r3:1024 * r3 + 512],
                                   keys("psA", 1024 * r3, 1024 * r3 + 512)),
                                  (psA[:, 1024 * r3 + 512:1024 * (r3 + 1)],
                                   keys("psA", 1024 * r3 + 512,
                                        1024 * (r3 + 1)))]
                    else:
                        halves = [(psC[:, 0:512], keys("psC", 0, 512)),
                                  (psTf[:, 0:512], keys("psT", 0, 1024))]
                    ksc = halves[0][1] + halves[1][1]
                    for c_ in range(TCH // 512):
                        self.mm(halves[c_][0],
                                ar1[0:65, 128 * j:128 * (j + 1)],
                                ar0[0:65, TCH * r + 512 * c_:
                                    TCH * r + 512 * (c_ + 1)],
                                start=True, stop=True,
                                reads=kq + kk, writes=halves[c_][1],
                                tag="aa_mm1")
                    o = j - spt * r
                    if r3 < 2:
                        if o >= 0:
                            stc = (TCH - 128) - 128 * o
                            self.tt(ALU.add, psA[:, 1024 * r3:1024 * (r3 + 1)],
                                    psA[:, 1024 * r3:1024 * (r3 + 1)],
                                    sb["c_band"][:, stc:stc + TCH],
                                    reads=ksc + keys("c_band"),
                                    writes=ksc, tag="aa_band")
                        kP = keys("ar2", POFF + po, POFF + po + 1024)
                        self.act(ar2[:, POFF + po:POFF + po + 1024],
                                 psA[:, 1024 * r3:1024 * (r3 + 1)], AF.Exp,
                                 reads=ksc, writes=kP, tag="aa_exp")
                    else:
                        for c_ in range(TCH // 512):
                            if o >= 0:
                                stc = (TCH - 128) - 128 * o + 512 * c_
                                self.tt(ALU.add, halves[c_][0], halves[c_][0],
                                        sb["c_band"][:, stc:stc + 512],
                                        reads=halves[c_][1] + keys("c_band"),
                                        writes=halves[c_][1], tag="aa_band")
                            kP = keys("ar2", POFF + po + 512 * c_,
                                      POFF + po + 512 * (c_ + 1))
                            self.act(ar2[:, POFF + po + 512 * c_:
                                         POFF + po + 512 * (c_ + 1)],
                                     halves[c_][0], AF.Exp,
                                     reads=halves[c_][1], writes=kP,
                                     tag="aa_exp")
                    if j >= 1:
                        aa_pv(j - 1)
                if nj >= 1:
                    aa_pv(nj - 1)
                kev = keys("E", 0, TCH)
                self.cp("scalar", E[0:64, 0:TCH], psB[0:64, 0:TCH],
                        reads=kout, writes=kev, tag="aa_ev")
                kz = keys("G", 0, TCH)
                self.recip(G[0:1, 0:TCH], psB[64:65, 0:TCH],
                           reads=kout, writes=kz, tag="aa_recz")
                for c_ in range(TCH // 512):
                    self.mm(psB[0:64, 512 * c_:512 * (c_ + 1)],
                            sb["c_o1x64f"][:], G[0:1, 512 * c_:512 * (c_ + 1)],
                            start=True, stop=True,
                            reads=kz + keys("c_o1x64f"), writes=kout,
                            tag="aa_zrep")
                kcs = keys("ar2", COFF, COFF + TCH)
                self.tt(ALU.mult, ar2[0:64, COFF:COFF + TCH],
                        E[0:64, 0:TCH], psB[0:64, 0:TCH],
                        reads=kev + kout, writes=kcs, tag="aa_nrm")
                self.dma(d["CONTRX"][r, 128 * b + 64 * hp:
                                     128 * b + 64 * hp + 64, :],
                         ar2[0:64, COFF:COFF + TCH],
                         reads=kcs, writes=(("CONTRX", r),), tag="aa_co")

    def phase_wo_a2a(self, tag, wname, mode, gbuf):
        cfg, sb, d = self.cfg, self.sb, self.d
        psA, psT, F, W1 = sb["psA"], sb["psT"], sb["F"], sb["W1"]
        ar0, ar1, ar3 = sb["ar0"], sb["ar1"], sb["ar3"]
        CL = cfg.TL
        self.coll("AllGather", ALU.bypass, d["CONTRX"], d[gbuf],
                  reads=tuple(("CONTRX", r) for r in range(cfg.R)),
                  writes=((gbuf, 0),), tag=tag + "_AG")
        rows = d[gbuf].rearrange("s q t -> (s q) t")
        kcy = keys("ar3", 0, 16384)
        for j16 in range(16):
            self.p.op("gpsimd",
                      lambda e, o=ar3[:, 1024 * j16:1024 * (j16 + 1)],
                      i=rows, x=sb["IDXS"][:, j16:j16 + 1]:
                      e.indirect_dma_start(
                          out=o, out_offset=None, in_=i,
                          in_offset=IndirectOffsetOnAxis(x, 0)),
                      ((gbuf, 0),) + keys("IDXS"),
                      keys("ar3", 1024 * j16, 1024 * (j16 + 1)),
                      kind="d", tag=tag + "_gth")
        kw = keys("ar1", 0, 8192)
        self.dma_wfm(ar1[:, 0:8192], self.wbf[wname], 1024,
                     reads=(("wbf_" + wname, 0),), writes=kw, tag=tag + "_wl")
        for b in range(cfg.B):
            for i in range(cfg.pos // 128):
                pp = i % 2
                kps = keys("psA", 1024 * pp, 1024 * (pp + 1))
                for src in range(cfg.R):
                    sl = (src * 2 + b) * 1024
                    for c_ in range(2):
                        self.mm(psA[:, 1024 * pp + 512 * c_:
                                    1024 * pp + 512 * (c_ + 1)],
                                ar3[:, sl + 128 * i:sl + 128 * (i + 1)],
                                ar1[:, 1024 * src + 512 * c_:
                                    1024 * src + 512 * (c_ + 1)],
                                start=(src == 0), stop=(src == cfg.R - 1),
                                reads=kcy + kw, writes=kps, tag=tag + "_mm")
                it = b * (cfg.pos // 128) + i
                if mode == "attna":
                    ka = keys("F", 2048 * pp, 2048 * pp + 1024)
                    kb = keys("F", 2048 * pp + 1024, 2048 * pp + 2048)
                    self.dma(F[:, 2048 * pp + 1024:2048 * pp + 2048],
                             d["xcur"][128 * it:128 * (it + 1), :],
                             reads=keys("xcur"), writes=kb, tag=tag + "_xin")
                    self.tt(ALU.add, F[:, 2048 * pp:2048 * pp + 1024],
                            psA[:, 1024 * pp:1024 * (pp + 1)],
                            F[:, 2048 * pp + 1024:2048 * pp + 2048],
                            reads=kps + kb, writes=ka, tag=tag + "_add")
                    self.dma(d["xcur"][128 * it:128 * (it + 1), :],
                             F[:, 2048 * pp:2048 * pp + 1024],
                             reads=ka, writes=keys("xcur"), tag=tag + "_xout")
                else:
                    kw1 = keys("W1", 1024 * pp, 1024 * (pp + 1))
                    self.cp("vector", W1[:, 1024 * pp:1024 * (pp + 1)],
                            psA[:, 1024 * pp:1024 * (pp + 1)],
                            reads=kps, writes=kw1, tag=tag + "_zc")
                    for j in range(8):
                        kT = keys("psT", 128 * (j % 8), 128 * (j % 8) + 128)
                        self.trans(psT[:, 128 * (j % 8):128 * (j % 8) + 128],
                                   W1[:, 1024 * pp + 128 * j:
                                      1024 * pp + 128 * (j + 1)],
                                   reads=kw1, writes=kT, tag=tag + "_zT")
                        self.cp("vector",
                                ar0[:, CL * j + 128 * it:CL * j + 128 * (it + 1)],
                                psT[:, 128 * (j % 8):128 * (j % 8) + 128],
                                reads=kT,
                                writes=keys("ar0", CL * j + 128 * it,
                                            CL * j + 128 * (it + 1)),
                                tag=tag + "_zTe")

    def phase_ffn(self, inst, stop=None):
        cfg, sb, d = self.cfg, self.sb, self.d
        S, D = cfg.S, cfg.D
        tg = f"f{inst}"
        nch = D // 128
        CL = cfg.TL
        half = CL // 2
        W0, W1, G, F, E = sb["W0"], sb["W1"], sb["G"], sb["F"], sb["E"]
        psA, psB, psT = sb["psA"], sb["psB"], sb["psT"]
        self.phase_ln_ag(extra_rms=False, tag=tg + "ln", do_ag=True, fm_ar="ar0")
        for nm, off in [("wqf", 0), ("wkf", 1024), ("wvf", 2048)]:
            self.dma_wfm(W0[:, off:off + 1024], self.wbf[nm], 128,
                         reads=(("wbf_" + nm, 0),),
                         writes=keys("W0", off, off + 1024), tag=tg + "w")
        for r in range(cfg.R):
            pp = r % 2
            arn = f"ar{pp}"
            arin = sb[arn]
            kin = keys(arn, 0, nch * CL)
            self.dma(arin[:, 0:nch * CL].rearrange("p (c m) -> p c m", m=CL),
                     d["ag_xn_out"][128 * nch * r:128 * nch * (r + 1), :]
                     .rearrange("(c p) m -> p c m", p=128),
                     reads=(("ag_xn_out", 0),), writes=kin, tag=tg + "pin")

            def project(woff):
                kw = keys("W0", woff, woff + 1024)
                kps = keys("psA", 0, CL)
                for j in range(nch):
                    for c_ in range(CL // 512):
                        self.mm(psA[:, 512 * c_:512 * (c_ + 1)],
                                W0[:, woff + 128 * j:woff + 128 * (j + 1)],
                                arin[:, CL * j + 512 * c_:CL * j + 512 * (c_ + 1)],
                                start=(j == 0), stop=(j == nch - 1),
                                reads=kin + kw, writes=kps, tag=tg + "pmm")
                return kps

            def qk_store(kev, qk, dstn):
                for b in range(2):
                    self.dma(d[dstn][b][:, cfg.pos * r:cfg.pos * r + half],
                             W1[:, b * half:(b + 1) * half],
                             reads=kev, writes=((dstn, b),), tag=tg + "qko")
                ksq = keys("ar2", 0, CL)
                self.act(sb["ar2"][:, 0:CL], W1[:, 0:CL], AF.Square,
                         reads=kev, writes=ksq, tag=tg + "nsq")
                kn = keys("psA", 0, CL)
                for c_ in range(CL // 512):
                    self.mm(psA[0:2, 512 * c_:512 * (c_ + 1)], sb["c_hm"][:],
                            sb["ar2"][:, 512 * c_:512 * (c_ + 1)],
                            start=True, stop=True, reads=ksq + keys("c_hm"),
                            writes=kn, tag=tg + "nmm")
                kev2 = keys("F", 2048, 2048 + CL)
                self.cp("vector", F[0:2, 2048:2048 + CL], psA[0:2, 0:CL],
                        reads=kn, writes=kev2, tag=tg + "nev")
                for hp in range(2):
                    for b in range(2):
                        self.dma(d["norms_f"][b, qk, hp,
                                              cfg.pos * r:cfg.pos * r + half],
                                 F[hp:hp + 1, 2048 + b * half:2048 + (b + 1) * half],
                                 reads=kev2, writes=(("norms_f", qk),),
                                 tag=tg + "no")

            kq = project(0)
            kevq = keys("W1", 0, CL)
            self.act(W1[:, 0:CL], psA[:, 0:CL], AF.Copy, scale=1.0 / 32.0,
                     reads=kq, writes=kevq, tag=tg + "qev")
            qk_store(kevq, 0, "QF")
            kk = project(1024)
            kevk = keys("W1", 0, CL)
            self.cp("scalar", W1[:, 0:CL], psA[:, 0:CL], reads=kk,
                    writes=kevk, tag=tg + "kev")
            qk_store(kevk, 1, "KF")
            kv = project(2048)
            kvs = keys("ar2", 4096, 4096 + CL)
            self.cp("scalar", sb["ar2"][:, 4096:4096 + CL], psA[:, 0:CL],
                    reads=kv, writes=kvs, tag=tg + "vev")
            ntile = CL // 128
            for g in range(ceil_div(ntile, 8)):
                kT = keys("psT", 0, 1024)
                for tl in range(min(8, ntile - 8 * g)):
                    i = 8 * g + tl
                    self.trans(psT[:, 128 * tl:128 * (tl + 1)],
                               sb["ar2"][:, 4096 + 128 * i:4096 + 128 * (i + 1)],
                               reads=kvs, writes=kT, tag=tg + "vT")
                eb = 8192 + 1024 * (g % 2)
                kTe = keys("ar2", eb, eb + 1024)
                self.cp("vector", sb["ar2"][:, eb:eb + 1024], psT[:, 0:1024],
                        reads=kT, writes=kTe, tag=tg + "vTe")
                for tl in range(min(8, ntile - 8 * g)):
                    i = 8 * g + tl
                    b = i // (ntile // 2)
                    row0 = cfg.pos * r + 128 * (i % (ntile // 2))
                    self.dma(d["VF"][b, row0:row0 + 128, :],
                             sb["ar2"][:, eb + 128 * tl:eb + 128 * (tl + 1)],
                             reads=kTe, writes=(("VF", b),), tag=tg + "vo")
        if stop == "proj":
            return
        for b in range(2):
            for hp in range(2):
                ku = keys("G", 2080, 2082)
                ng = max(1, S // 2048)
                w = min(2048, S)
                for gg in range(ng):
                    kg = keys("G", 0, w)
                    self.dma(G[0:1, 0:w],
                             d["norms_f"][b, 0, hp, w * gg:w * (gg + 1)],
                             reads=(("norms_f", 0),), writes=kg, tag=tg + "mqi")
                    self.red(G[0:1, 2048 + gg:2049 + gg], G[0:1, 0:w], ALU.max,
                             reads=kg, writes=keys("G", 2048, 2048 + ng),
                             tag=tg + "mqr")
                self.red(G[0:1, 2080:2081], G[0:1, 2048:2048 + ng], ALU.max,
                         reads=keys("G", 2048, 2048 + ng), writes=ku,
                         tag=tg + "mqr2")
                self.act(G[0:1, 2081:2082], G[0:1, 2080:2081], AF.Sqrt,
                         reads=ku, writes=ku, tag=tg + "mqs")
                for r in range(S // 1024):
                    km = keys("G", 0, 1024)
                    km2 = keys("G", 1024, 2048)
                    self.dma(G[0:1, 0:1024],
                             d["norms_f"][b, 1, hp, 1024 * r:1024 * (r + 1)],
                             reads=(("norms_f", 1),), writes=km, tag=tg + "mki")
                    self.act(G[0:1, 1024:2048], G[0:1, 0:1024], AF.Sqrt,
                             reads=km, writes=km2, tag=tg + "mks")
                    self.ts(G[0:1, 0:1024], G[0:1, 1024:2048],
                            G[0:1, 2081:2082], ALU.mult, reads=km2 + ku,
                            writes=km, s2=-1.0, op1=ALU.mult, tag=tg + "mkm")
                    kmb = keys("W1", 0, 1024)
                    self.cp("vector", W1[0:1, 0:1024], G[0:1, 0:1024],
                            reads=km, writes=kmb, tag=tg + "mkc")
                    self.dma(d["negMk"][b, hp, 1024 * r:1024 * (r + 1)],
                             W1[0:1, 0:1024], reads=kmb,
                             writes=(("negMk", b, hp),), tag=tg + "mko")
        if stop == "negmk":
            return
        ar0, ar1, ar2, ar3 = sb["ar0"], sb["ar1"], sb["ar2"], sb["ar3"]
        nkt = S // 128
        psC = sb["psC"]
        psTf = sb["psT"][:].bitcast(F32)
        PREG = {(0, 0): ("ar0", 0), (1, 0): ("ar0", 8192),
                (2, 0): ("ar3", 0), (0, 1): ("ar1", 0),
                (1, 1): ("ar1", 8192), (2, 1): ("ar3", 8192)}
        for b in range(cfg.B):
            kQ = keys("ar2", 0, S)
            self.dma(ar2[:, 0:S], d["QF"][b], reads=(("QF", b),), writes=kQ,
                     tag=tg + "alq")
            kM = keys("G", 1024, 1024 + 2 * nkt)
            kMs = keys("W1", 1536, 1536 + 2 * nkt)
            for hp in range(2):
                self.dma(W1[:, 1536 + nkt * hp:1536 + nkt * (hp + 1)],
                         d["negMk"][b, hp].rearrange("(a q) -> a q", q=128),
                         reads=(("negMk", b, hp),), writes=kMs,
                         tag=tg + "aM", transpose=True)
            self.cp("vector", G[:, 1024:1024 + 2 * nkt],
                    W1[:, 1536:1536 + 2 * nkt], reads=kMs, writes=kM,
                    tag=tg + "aMc")
            kacc = keys("E", 0, S)
            self.memset(E[:, 0:S], 0.0, reads=(), writes=kacc, tag=tg + "az")

            def scores_block(j):
                pp = j % 2
                kK = keys("W1", 128 * pp, 128 * (pp + 1))
                self.dma(W1[:, 128 * pp:128 * (pp + 1)],
                         d["KF"][b][:, 128 * j:128 * (j + 1)],
                         reads=(("KF", b),), writes=kK, tag=tg + "aki")
                kVt = keys("W1", 1664 + 128 * pp, 1792 + 128 * pp)
                self.dma(W1[:, 1664 + 128 * pp:1792 + 128 * pp],
                         d["VF"][b][128 * j:128 * (j + 1), :],
                         reads=(("VF", b),), writes=kVt, tag=tg + "avt")
                pn0, pb0 = PREG[(j % 3, 0)]
                pn1, pb1 = PREG[(j % 3, 1)]
                for s in range(16):
                    hp = s % 2
                    q0 = 1024 * (s // 2)
                    r3 = s % 3
                    Pn, PB = (pn0, pb0) if hp == 0 else (pn1, pb1)
                    Par = sb[Pn]
                    if r3 < 2:
                        rps, ra = psA, 1024 * r3
                        ksa = keys("psA", ra, ra + 1024)
                    else:
                        rps, ra = psB, 0
                        ksa = keys("psB", 0, 1024)
                    for c_ in range(2):
                        self.mm(rps[:, ra + 512 * c_:ra + 512 * (c_ + 1)],
                                W1[64 * hp:64 * (hp + 1),
                                   128 * pp:128 * pp + 128],
                                ar2[64 * hp:64 * (hp + 1),
                                    q0 + 512 * c_:q0 + 512 * (c_ + 1)],
                                start=True, stop=True,
                                reads=kK + kQ, writes=ksa, tag=tg + "amm")
                    self.act(Par[:, PB + q0:PB + q0 + 1024],
                             rps[:, ra:ra + 1024], AF.Exp,
                             bias=G[:, 1024 + nkt * hp + j:
                                    1025 + nkt * hp + j],
                             reads=ksa + kM,
                             writes=keys(Pn, PB + q0, PB + q0 + 1024),
                             tag=tg + "aexp")
                vs = 256 + 128 * (j % 3)
                for hp in range(2):
                    Pn, PB = (pn0, pb0) if hp == 0 else (pn1, pb1)
                    Par = sb[Pn]
                    kP = keys(Pn, PB, PB + 8192)
                    kzz = keys("G", 2136 + 2 * hp, 2138 + 2 * hp)
                    self.red(G[:, 2136 + 2 * hp:2137 + 2 * hp],
                             Par[:, PB:PB + 8192], ALU.add,
                             reads=kP, writes=kzz, tag=tg + "azs")
                    self.recip(G[:, 2137 + 2 * hp:2138 + 2 * hp],
                               G[:, 2136 + 2 * hp:2137 + 2 * hp],
                               reads=kzz, writes=kzz, tag=tg + "azr")
                    self.ts(W1[:, vs + 64 * hp:vs + 64 * (hp + 1)],
                            W1[:, 1664 + 128 * pp + 64 * hp:
                               1664 + 128 * pp + 64 * (hp + 1)],
                            G[:, 2137 + 2 * hp:2138 + 2 * hp], ALU.mult,
                            reads=kVt + kzz,
                            writes=keys("W1", vs + 64 * hp, vs + 64 * (hp + 1)),
                            tag=tg + "avv")

            def pv_block2(ja, jb):
                for s in range(16):
                    q0 = 512 * s
                    if s % 2 == 0:
                        ps, kps = psC, keys("psC", 0, 512)
                    else:
                        ps, kps = psTf, keys("psT", 0, 1024)
                    for i_, j in enumerate((ja, jb)):
                        vs = 256 + 128 * (j % 3)
                        for hp in range(2):
                            Pn, PB = PREG[(j % 3, hp)]
                            Par = sb[Pn]
                            self.mm(ps[64 * hp:64 * (hp + 1), 0:512],
                                    W1[:, vs + 64 * hp:vs + 64 * (hp + 1)],
                                    Par[:, PB + q0:PB + q0 + 512],
                                    start=(i_ == 0), stop=(i_ == 1),
                                    reads=keys(Pn, PB + q0, PB + q0 + 512) +
                                    keys("W1", vs + 64 * hp,
                                         vs + 64 * (hp + 1)),
                                    writes=kps, tag=tg + "apv")
                    ke = keys("E", q0, q0 + 512)
                    self.tt(ALU.add, E[:, q0:q0 + 512], E[:, q0:q0 + 512],
                            ps[:, 0:512], reads=ke + kps, writes=ke,
                            tag=tg + "aac")

            pend = None
            for j in range(nkt):
                scores_block(j)
                if pend is not None:
                    pv_block2(*pend)
                    pend = None
                if j % 2 == 1:
                    pend = (j - 1, j)
            if pend is not None:
                pv_block2(*pend)
            for g in range(S // 2048):
                kcs = keys("ar2", 8192 + 2048 * (g % 2), 8192 + 2048 * (g % 2) + 2048)
                self.cp("vector", ar2[:, 8192 + 2048 * (g % 2):
                                      8192 + 2048 * (g % 2) + 2048],
                        E[:, 2048 * g:2048 * (g + 1)],
                        reads=kacc, writes=kcs, tag=tg + "acast")
                self.dma(d["CONTRX"][2 * g:2 * g + 2, 128 * b:128 * (b + 1), :]
                         .rearrange("r p t -> p r t"),
                         ar2[:, 8192 + 2048 * (g % 2):
                             8192 + 2048 * (g % 2) + 2048]
                         .rearrange("p (r t) -> p r t", r=2),
                         reads=kcs, writes=(("CONTRX", 2 * g), ("CONTRX", 2 * g + 1)),
                         tag=tg + "aco")
        if stop == "attn":
            return
        self.phase_wo_a2a(tg + "wo", "wof", "ffn", f"cxg{1 + inst}")
        if stop == "rs":
            return
        self.dma_wfm(sb["ar1"][:, 0:8 * 1024], self.wbf["wff"], 1024,
                     reads=(("wbf_wff", 0),), writes=keys("ar1", 0, 8192),
                     tag=tg + "wffl")
        kzfm = keys("ar0", 0, 8 * CL)
        kwf = keys("ar1", 0, 8192)
        if stop == "zfm":
            return
        nth = CL // 1024
        for th in range(nth):
            tc = 1024 * th
            for i in range(8):
                kps = keys("psA", 0, 1024)
                for kc in range(8):
                    for c_ in range(2):
                        self.mm(psA[:, 512 * c_:512 * (c_ + 1)],
                                sb["ar1"][:, 1024 * kc + 128 * i:
                                          1024 * kc + 128 * (i + 1)],
                                sb["ar0"][:, CL * kc + tc + 512 * c_:
                                          CL * kc + tc + 512 * (c_ + 1)],
                                start=(kc == 0), stop=(kc == 7),
                                reads=kwf + kzfm, writes=kps, tag=tg + "fmm")
                kw = keys("ar2", 2048 * i + 1024 * (th % 2),
                          2048 * i + 1024 * (th % 2) + 1024)
                if stop != "zn3":
                    self.cp("vector",
                            sb["ar2"][:, 2048 * i + 1024 * (th % 2):
                                      2048 * i + 1024 * (th % 2) + 1024],
                            psA[:, 0:1024], reads=kps, writes=kw, tag=tg + "fev")
                ksq = keys("F", 0, 1024)
                if stop not in ("zn3", "zn4"):
                    wsl = sb["ar2"][:, 2048 * i + 1024 * (th % 2):
                                    2048 * i + 1024 * (th % 2) + 1024]
                    self.tt(ALU.mult, F[:, 0:1024], wsl, wsl,
                            reads=kw, writes=ksq, tag=tg + "fsq")
                if stop not in ("zn1", "zn3", "zn4"):
                    for c_ in range(2):
                        self.mm(psB[0:1, 512 * c_:512 * (c_ + 1)],
                                sb["c_ocolf"][:], F[:, 512 * c_:512 * (c_ + 1)],
                                start=(i == 0) or stop == "zn2",
                                stop=(i == 7) or stop == "zn2",
                                reads=ksq + keys("c_ocolf"),
                                writes=keys("psB", 0, 1024), tag=tg + "fns")
            if stop in ("zmm", "zn1", "zn2", "zn3", "zn4"):
                continue
            krs0 = keys("G", 0, 1024)
            self.cp("vector", G[0:1, 0:1024], psB[0:1, 0:1024],
                    reads=keys("psB", 0, 1024), writes=krs0, tag=tg + "frse")
            krs1 = keys("G", 1024, 2048)
            self.act(G[0:1, 1024:2048], G[0:1, 0:1024], AF.Sqrt,
                     reads=krs0, writes=krs1, tag=tg + "frs")
            self.ts(G[0:1, 0:1024], G[0:1, 1024:2048], 1e-12, ALU.max,
                    reads=krs1, writes=krs0, tag=tg + "fmx")
            krr = keys("G", 1024, 2048)
            self.recip(G[0:1, 1024:2048], G[0:1, 0:1024], reads=krs0,
                       writes=krr, tag=tg + "frr")
            kbr = keys("psB", 0, 1024)
            for c_ in range(2):
                self.mm(psB[:, 512 * c_:512 * (c_ + 1)], sb["c_o1x128f"][:],
                        G[0:1, 1024 + 512 * c_:1024 + 512 * (c_ + 1)],
                        start=True, stop=True,
                        reads=krr + keys("c_o1x128f"), writes=kbr,
                        tag=tg + "fbc")
            if stop == "zbc":
                continue
            for i in range(8):
                co = 2048 * i + 1024 * (th % 2)
                kw = keys("ar2", co, co + 1024)
                kg1 = keys("ar3", 1024 * i, 1024 * (i + 1))
                self.tt(ALU.mult, sb["ar3"][:, 1024 * i:1024 * (i + 1)],
                        sb["ar2"][:, co:co + 1024], psB[:, 0:1024],
                        reads=kw + kbr, writes=kg1, tag=tg + "fnm")
                self.act(sb["ar2"][:, co:co + 1024],
                         sb["ar3"][:, 1024 * i:1024 * (i + 1)],
                         AF.Gelu, reads=kg1, writes=kw, tag=tg + "fgel")
            if stop == "zgel":
                continue
            for tt_ in range(8):
                kT = keys("psT", 0, 1024)
                for i in range(8):
                    co = 2048 * i + 1024 * (th % 2)
                    self.trans(psT[:, 128 * i:128 * (i + 1)],
                               sb["ar2"][:, co + 128 * tt_:co + 128 * (tt_ + 1)],
                               reads=keys("ar2", co, co + 1024),
                               writes=kT, tag=tg + "fT")
                it = 8 * th + tt_
                pp = tt_ % 2
                kxf = keys("F", 2048 * pp, 2048 * pp + 1024)
                self.dma(F[:, 2048 * pp:2048 * pp + 1024],
                         d["xcur"][128 * it:128 * (it + 1), :],
                         reads=keys("xcur"), writes=kxf, tag=tg + "fxi")
                self.tt(ALU.add, F[:, 2048 * pp:2048 * pp + 1024],
                        F[:, 2048 * pp:2048 * pp + 1024], psT[:, 0:1024],
                        reads=kxf + kT, writes=kxf, tag=tg + "fadd")
                self.dma(d["xcur"][128 * it:128 * (it + 1), :],
                         F[:, 2048 * pp:2048 * pp + 1024],
                         reads=kxf, writes=keys("xcur"), tag=tg + "fxo")

    def phase_mamba(self):
        cfg, sb, d = self.cfg, self.sb, self.d
        S, DM, T = cfg.S, cfg.DM, cfg.T
        MB = cfg.MB
        W0, W1, G, F, E = sb["W0"], sb["W1"], sb["G"], sb["F"], sb["E"]
        psA, psB, psT = sb["psA"], sb["psB"], sb["psT"]
        ar0, ar1, ar2, ar3 = sb["ar0"], sb["ar1"], sb["ar2"], sb["ar3"]
        NT = cfg.NT_LOC
        CL = cfg.TL
        nct = S // 2048
        ngl = T // 2048

        def pcol(g):
            b = g // nct
            return cfg.mcol0(b) + 2048 * (g % nct)

        def scol(g):
            return 2048 * g

        self.phase_ln_ag(extra_rms=False, tag="mln", do_ag=False, fm_ar="ar0")
        self.dma_wfm(ar1[:, 0:8 * 512], self.wbf["lin1_w"], 512,
                     reads=(("wbf_lin1_w", 0),), writes=keys("ar1", 0, 4096),
                     tag="mb_l1w")
        kb1 = keys("G", 0, 512)
        self.dma(G[0:1, 0:DM], self.ein["lin1_b"], reads=(), writes=kb1,
                 tag="mb_l1b")
        kb1b = keys("W1", 1536, 2048)
        self.cp("vector", W1[0:1, 1536:1536 + DM], G[0:1, 0:DM],
                reads=kb1, writes=kb1b, tag="mb_l1bc")
        for i in range(NT):
            kps = keys("psB", 0, 512)
            for k in range(8):
                self.mm(psB[:, 0:512],
                        ar0[:, CL * k + 128 * i:CL * k + 128 * (i + 1)],
                        ar1[:, 512 * k:512 * (k + 1)], start=(k == 0), stop=False,
                        reads=keys("ar0", CL * k + 128 * i,
                                   CL * k + 128 * (i + 1)) +
                        keys("ar1", 512 * k, 512 * (k + 1)),
                        writes=kps, tag="mb_l1mm")
            self.mm(psB[:, 0:512], sb["c_o1x128b"][:], W1[0:1, 1536:1536 + DM],
                    start=False, stop=True,
                    reads=kb1b + keys("c_o1x128b"), writes=kps, tag="mb_l1bias")
            ke = keys("E", 512 * i, 512 * (i + 1))
            self.cp("scalar" if i % 2 else "vector", E[:, 512 * i:512 * (i + 1)],
                    psB[:, 0:512], reads=kps, writes=ke, tag="mb_l1ev")

        for l in range(cfg.NL):
            lt = f"mb{l}"
            kcw = keys("G", 2144, 2176)
            self.dma(G[:, 2144:2144 + cfg.CK], self.ein["convw"][l],
                     reads=(), writes=kcw, tag=lt + "cw")
            self.dma(G[:, 2148:2149], as_col(self.ein["convb"][l, :]),
                     reads=(), writes=kcw, tag=lt + "cb")
            self.dma(G[:, 2149:2150], as_col(self.ein["dtb"][l, :]),
                     reads=(), writes=kcw, tag=lt + "db")
            self.dma(G[:, 2150:2151], as_col(self.ein["dpv"][l, :]),
                     reads=(), writes=kcw, tag=lt + "dp")
            kA = keys("G", 2176, 2208)
            self.dma(G[:, 2176:2176 + cfg.DS], self.ein["alog"][l],
                     reads=(), writes=kA, tag=lt + "al")
            self.act(G[:, 2176:2176 + cfg.DS], G[:, 2176:2176 + cfg.DS],
                     AF.Exp, reads=kA, writes=kA, tag=lt + "aexp")
            self.ts(G[:, 2176:2176 + cfg.DS], G[:, 2176:2176 + cfg.DS],
                    -1.0, ALU.mult, reads=kA, writes=kA, tag=lt + "aneg")
            for i in range(NT):
                pp = i % 2
                gs = pp * 16
                kst = keys("G", gs, gs + 16)
                ke = keys("E", 512 * i, 512 * (i + 1))
                ksq = keys("F", 2048 * pp, 2048 * pp + 512)
                self.act(F[:, 2048 * pp:2048 * pp + 512],
                         E[:, 512 * i:512 * (i + 1)], AF.Square,
                         reads=ke, writes=ksq, tag=lt + "rsq")
                self.red(G[:, gs:gs + 1], F[:, 2048 * pp:2048 * pp + 512],
                         ALU.add, reads=ksq, writes=kst, tag=lt + "rs")
                self.act(G[:, gs + 1:gs + 2], G[:, gs:gs + 1], AF.Sqrt,
                         reads=kst + keys("c_cb"), writes=kst, scale=1.0 / DM,
                         bias=self.sb["c_cb"][:, 0:1], tag=lt + "rsd")
                self.recip(G[:, gs + 2:gs + 3], G[:, gs + 1:gs + 2],
                           reads=kst, writes=kst, tag=lt + "rrc")
                khr = keys("F", 2048 * pp + 512, 2048 * pp + 1024)
                self.ts(F[:, 2048 * pp + 512:2048 * pp + 1024],
                        E[:, 512 * i:512 * (i + 1)], G[:, gs + 2:gs + 3],
                        ALU.mult, reads=ke + kst, writes=khr, tag=lt + "rmul")
                kbf = keys("W1", 512 * pp, 512 * (pp + 1))
                self.cp("vector", W1[:, 512 * pp:512 * (pp + 1)],
                        F[:, 2048 * pp + 512:2048 * pp + 1024],
                        reads=khr, writes=kbf, tag=lt + "rcast")
                for j in range(DM // 128):
                    kT = keys("psT", 128 * j, 128 * (j + 1))
                    self.trans(psT[:, 128 * j:128 * (j + 1)],
                               W1[:, 512 * pp + 128 * j:512 * pp + 128 * (j + 1)],
                               reads=kbf, writes=kT, tag=lt + "rT")
                    self.cp("vector",
                            ar0[:, CL * j + 128 * i:CL * j + 128 * (i + 1)],
                            psT[:, 128 * j:128 * (j + 1)], reads=kT,
                            writes=keys("ar0", CL * j + 128 * i,
                                        CL * j + 128 * (i + 1)),
                            tag=lt + "rTe")
            for j in range(DM // 128):
                self.dma(d["ag_hr_in"][128 * j:128 * (j + 1), :],
                         ar0[:, CL * j:CL * j + CL],
                         reads=keys("ar0", CL * j, CL * j + CL),
                         writes=(("ag_hr_in", j),), tag=lt + "agi")
            self.coll("AllGather", ALU.bypass, d["ag_hr_in"], d["ag_hr_out"],
                      reads=tuple(("ag_hr_in", j) for j in range(DM // 128)),
                      writes=(("ag_hr_out", 0),), tag=lt + "AG")
            self.dma_wfm(W0[:, 0:1024], self.wbf[f"inw{l}"], 256,
                         reads=((f"wbf_inw{l}", 0),), writes=keys("W0", 0, 1024),
                         tag=lt + "iw")
            for r in range(cfg.R):
                kin = keys("ar0", 4 * CL, 8 * CL)
                self.dma(ar0[:, 4 * CL:8 * CL].rearrange(
                             "p (c m) -> p c m", m=CL),
                         d["ag_hr_out"][512 * r:512 * (r + 1), :]
                         .rearrange("(c p) m -> p c m", p=128),
                         reads=(("ag_hr_out", 0),), writes=kin, tag=lt + "xzi")
                for m2 in range(2):
                    kps = keys("psA", 0, 2048)
                    for k in range(4):
                        for c_ in range(CL // 512):
                            self.mm(psA[:, 512 * c_:512 * (c_ + 1)],
                                    W0[:, 256 * k + 128 * m2:
                                       256 * k + 128 * (m2 + 1)],
                                    ar0[:, 4 * CL + CL * k + 512 * c_:
                                        4 * CL + CL * k + 512 * (c_ + 1)],
                                    start=(k == 0), stop=(k == 3),
                                    reads=kin + keys("W0", 0, 1024),
                                    writes=kps, tag=lt + "xzmm")
                    if m2 == 0:
                        for b in range(2):
                            cc = cfg.mcol0(b) + cfg.pos * r
                            self.cp("vector", ar2[:, cc:cc + cfg.pos],
                                    psA[:, b * cfg.pos:(b + 1) * cfg.pos],
                                    reads=kps, writes=keys("ar2", cc, cc + cfg.pos),
                                    tag=lt + "xpev")
                    else:
                        kzs = keys("ar1", 4096, 4096 + 2048)
                        self.cp("vector", ar1[:, 4096:4096 + 2048],
                                psA[:, 0:2048], reads=kps, writes=kzs,
                                tag=lt + "zev")
                        for b in range(2):
                            self.dma(d["ZSP"][:, b * S + cfg.pos * r:
                                              b * S + cfg.pos * r + cfg.pos],
                                     ar1[:, 4096 + b * cfg.pos:
                                         4096 + (b + 1) * cfg.pos],
                                     reads=kzs, writes=(("ZSP", 0),),
                                     tag=lt + "zsp")
            for b in range(2):
                self.memset(ar2[:, b * MB:b * MB + 3], 0.0, reads=(),
                            writes=keys("ar2", b * MB, b * MB + 3),
                            tag=lt + "pad", eng="vector")
            for b in range(2):
                c0 = cfg.mcol0(b)
                for ch in range(nct):
                    kacc = keys("F", 0, 2048)
                    src0 = c0 + 2048 * ch - 3
                    self.ts(F[:, 0:2048], ar2[:, src0:src0 + 2048],
                            G[:, 2144:2145], ALU.mult,
                            reads=keys("ar2", src0, src0 + 2048) + kcw,
                            writes=kacc, tag=lt + "cv0")
                    for k in range(1, cfg.CK):
                        self.stt(F[:, 0:2048], ar2[:, src0 + k:src0 + k + 2048],
                                 G[:, 2144 + k:2145 + k], F[:, 0:2048],
                                 ALU.mult, ALU.add,
                                 reads=keys("ar2", src0 + k, src0 + k + 2048) +
                                 kcw + kacc, writes=kacc, tag=lt + "cvk")
                    ku = keys("ar1", 6144, 8192)
                    self.act(ar1[:, 6144:6144 + 2048], F[:, 0:2048], AF.Silu,
                             bias=G[:, 2148:2149], reads=kacc + kcw, writes=ku,
                             tag=lt + "cvs")
                    g = b * nct + ch
                    self.dma(d["USP"][:, scol(g):scol(g) + 2048],
                             ar1[:, 6144:6144 + 2048], reads=ku,
                             writes=(("USP", g),), tag=lt + "usp")
            self.dma(W0[:, 1024:1088], self.wbf[f"xprojw{l}"],
                     reads=((f"wbf_xprojw{l}", 0),),
                     writes=keys("W0", 1024, 1088), tag=lt + "xpw")
            for g in range(ngl):
                kuc = keys("ar1", 8192, 10240)
                self.dma(ar1[:, 8192:8192 + 2048],
                         d["USP"][:, scol(g):scol(g) + 2048],
                         reads=(("USP", g),), writes=kuc, tag=lt + "xui")
                for c_ in range(4):
                    kps = keys("psB", 512 * (c_ % 2), 512 * (c_ % 2) + 512)
                    self.mm(psB[0:64, 512 * (c_ % 2):512 * (c_ % 2) + 512],
                            W0[:, 1024:1088],
                            ar1[:, 8192 + 512 * c_:8192 + 512 * (c_ + 1)],
                            start=True, stop=True,
                            reads=kuc + keys("W0", 1024, 1088), writes=kps,
                            tag=lt + "xpm")
                    kf = keys("F", 2048 + 512 * (c_ % 2),
                              2048 + 512 * (c_ % 2) + 512)
                    self.cp("vector", F[0:64, 2048 + 512 * (c_ % 2):
                                        2048 + 512 * (c_ % 2) + 512],
                            psB[0:64, 512 * (c_ % 2):512 * (c_ % 2) + 512],
                            reads=kps, writes=kf, tag=lt + "xpe")
                    self.dma(d["dblpart"][0:64, 2048 * g + 512 * c_:
                                          2048 * g + 512 * (c_ + 1)],
                             F[0:64, 2048 + 512 * (c_ % 2):
                               2048 + 512 * (c_ % 2) + 512],
                             reads=kf, writes=(("dblpart", g),), tag=lt + "xpo")
            self.coll("AllReduce", ALU.add, d["dblpart"], d["dblfull"],
                      reads=tuple(("dblpart", g) for g in range(ngl)),
                      writes=(("dblfull", 0),), tag=lt + "AR")
            self.dma(W0[0:cfg.DTR, 1088:1216], self.wbf[f"dtw{l}"],
                     reads=((f"wbf_dtw{l}", 0),), writes=keys("W0", 1088, 1216),
                     tag=lt + "dtw")
            for g in range(ngl):
                pc = pcol(g)
                kdg = keys("G", 0, 2048)
                self.dma(G[0:cfg.DTR, 0:2048],
                         d["dblfull"][0:cfg.DTR, 2048 * g:2048 * (g + 1)],
                         reads=(("dblfull", 0),), writes=kdg, tag=lt + "dti")
                kdb = keys("W1", 0, 2048)
                self.cp("vector", W1[0:cfg.DTR, 0:2048], G[0:cfg.DTR, 0:2048],
                        reads=kdg, writes=kdb, tag=lt + "dtc")
                kps = keys("psA", 0, 2048)
                for c_ in range(4):
                    self.mm(psA[:, 512 * c_:512 * (c_ + 1)],
                            W0[0:cfg.DTR, 1088:1216],
                            W1[0:cfg.DTR, 512 * c_:512 * (c_ + 1)],
                            start=True, stop=True,
                            reads=kdb + keys("W0", 1088, 1216), writes=kps,
                            tag=lt + "dtm")
                kx = keys("F", 0, 2048)
                self.ts(F[:, 0:2048], psA[:, 0:2048], G[:, 2149:2150],
                        ALU.add, reads=kps + kcw, writes=kx, tag=lt + "dt0")
                ka = keys("F", 2048, 4096)
                self.act(F[:, 2048:4096], F[:, 0:2048], AF.Abs,
                         reads=kx, writes=ka, tag=lt + "dt1")
                kge = keys("G", 0, 2048)
                self.act(G[:, 0:2048], F[:, 2048:4096], AF.Exp, scale=-1.0,
                         reads=ka, writes=kge, tag=lt + "dt2")
                self.act(F[:, 2048:4096], G[:, 0:2048], AF.Ln,
                         bias=self.sb["c_cb"][:, 2:3],
                         reads=kge + keys("c_cb"), writes=ka, tag=lt + "dt3")
                self.stt(ar3[:, pc:pc + 2048], F[:, 0:2048],
                         self.sb["c_cb"][:, 3:4],
                         F[:, 2048:4096], ALU.max, ALU.add,
                         reads=kx + ka + keys("c_cb"),
                         writes=keys("ar3", pc, pc + 2048), tag=lt + "dt4")
            for g in range(ngl):
                pc = pcol(g)
                kuc = keys("ar2", S, S + 2048)
                self.dma(ar2[:, S:S + 2048],
                         d["USP"][:, scol(g):scol(g) + 2048],
                         reads=(("USP", g),), writes=kuc, tag=lt + "dui")
                self.tt(ALU.mult, ar1[:, pc:pc + 2048], ar3[:, pc:pc + 2048],
                        ar2[:, S:S + 2048],
                        reads=keys("ar3", pc, pc + 2048) + kuc,
                        writes=keys("ar1", pc, pc + 2048), tag=lt + "du")
            for b in range(2):
                c0 = cfg.mcol0(b)
                for n in range(cfg.DS):
                    kan = keys("ar2", 0, S)
                    self.act(ar2[:, 0:S], ar3[:, c0:c0 + S], AF.Exp,
                             scale=G[:, 2176 + n:2177 + n],
                             reads=keys("ar3", c0, c0 + S) + kA,
                             writes=kan, tag=lt + "sa")
                    kcy = keys("G", 2208, 2240)
                    self.memset(G[:, 2208:2209], 0.0, reads=(), writes=kcy,
                                tag=lt + "scy", eng="vector")
                    for c in range(nct):
                        kbr = keys("G", 0, 2048)
                        self.dma(G[0:1, 0:2048],
                                 as_row(d["dblfull"][cfg.DTR + n,
                                                     b * S + 2048 * c:
                                                     b * S + 2048 * (c + 1)]),
                                 reads=(("dblfull", 0),), writes=kbr,
                                 tag=lt + "sB")
                        kbc = keys("W0", 1280, 3328)
                        self.cp("vector", W0[0:1, 1280:3328], G[0:1, 0:2048],
                                reads=kbr, writes=kbc, tag=lt + "sBc")
                        kbrp = keys("psA", 0, 2048)
                        for c_ in range(4):
                            self.mm(psA[:, 512 * c_:512 * (c_ + 1)],
                                    sb["c_o1x128b"][:],
                                    W0[0:1, 1280 + 512 * c_:
                                       1280 + 512 * (c_ + 1)],
                                    start=True, stop=True,
                                    reads=kbc + keys("c_o1x128b"),
                                    writes=kbrp, tag=lt + "sBr")
                        kw = keys("ar2", S, S + 2048)
                        self.tt(ALU.mult, ar2[:, S:S + 2048],
                                ar1[:, c0 + 2048 * c:c0 + 2048 * (c + 1)],
                                psA[:, 0:2048],
                                reads=keys("ar1", c0 + 2048 * c,
                                           c0 + 2048 * (c + 1)) + kbrp,
                                writes=kw, tag=lt + "sw")
                        kh = keys("ar2", S + 2048, S + 4096)
                        self.scan_op(ar2[:, S + 2048:S + 4096],
                                     ar2[:, 2048 * c:2048 * (c + 1)],
                                     ar2[:, S:S + 2048], G[:, 2208:2209],
                                     reads=kan + kw + kcy, writes=kh,
                                     tag=lt + "scan")
                        self.cp("vector", G[:, 2208:2209],
                                ar2[:, S + 4095:S + 4096], reads=kh,
                                writes=kcy, tag=lt + "scar")
                        kcr = keys("F", 0, 2048)
                        self.dma(F[0:1, 0:2048],
                                 as_row(d["dblfull"][cfg.DTR + cfg.DS + n,
                                                     b * S + 2048 * c:
                                                     b * S + 2048 * (c + 1)]),
                                 reads=(("dblfull", 0),), writes=kcr,
                                 tag=lt + "sC")
                        kcc = keys("W1", 0, 2048)
                        self.cp("vector", W1[0:1, 0:2048], F[0:1, 0:2048],
                                reads=kcr, writes=kcc, tag=lt + "sCc")
                        kcrp = keys("psA", 0, 2048)
                        for c_ in range(4):
                            self.mm(psA[:, 512 * c_:512 * (c_ + 1)],
                                    sb["c_o1x128b"][:],
                                    W1[0:1, 512 * c_:512 * (c_ + 1)],
                                    start=True, stop=True,
                                    reads=kcc + keys("c_o1x128b"),
                                    writes=kcrp, tag=lt + "sCr")
                        kpr = keys("ar2", S + 4096, S + 6144)
                        self.tt(ALU.mult, ar2[:, S + 4096:S + 6144],
                                ar2[:, S + 2048:S + 4096], psA[:, 0:2048],
                                reads=kh + kcrp, writes=kpr, tag=lt + "sp")
                        ky = keys("ar0", c0 + 2048 * c, c0 + 2048 * (c + 1))
                        if n == 0:
                            self.cp("vector",
                                    ar0[:, c0 + 2048 * c:c0 + 2048 * (c + 1)],
                                    ar2[:, S + 4096:S + 6144], reads=kpr,
                                    writes=ky, tag=lt + "sy0")
                        else:
                            self.tt(ALU.add,
                                    ar0[:, c0 + 2048 * c:c0 + 2048 * (c + 1)],
                                    ar0[:, c0 + 2048 * c:c0 + 2048 * (c + 1)],
                                    ar2[:, S + 4096:S + 6144],
                                    reads=ky + kpr, writes=ky, tag=lt + "sy")
            for g in range(ngl):
                pc = pcol(g)
                kuc = keys("ar2", S, S + 2048)
                self.dma(ar2[:, S:S + 2048],
                         d["USP"][:, scol(g):scol(g) + 2048],
                         reads=(("USP", g),), writes=kuc, tag=lt + "yui")
                kzc = keys("ar2", S + 2048, S + 4096)
                self.dma(ar2[:, S + 2048:S + 4096],
                         d["ZSP"][:, scol(g):scol(g) + 2048],
                         reads=(("ZSP", 0),), writes=kzc, tag=lt + "yzi")
                k1 = keys("ar2", S + 4096, S + 6144)
                self.stt(ar2[:, S + 4096:S + 6144], ar2[:, S:S + 2048],
                         G[:, 2150:2151], ar0[:, pc:pc + 2048],
                         ALU.mult, ALU.add,
                         reads=kuc + keys("ar0", pc, pc + 2048) + kcw,
                         writes=k1, tag=lt + "yd")
                k2 = keys("ar2", S + 6144, S + 8192)
                self.act(ar2[:, S + 6144:S + 8192], ar2[:, S + 2048:S + 4096],
                         AF.Silu, reads=kzc, writes=k2, tag=lt + "ysz")
                self.tt(ALU.mult, ar0[:, pc:pc + 2048], ar2[:, S + 4096:S + 6144],
                        ar2[:, S + 6144:S + 8192], reads=k1 + k2,
                        writes=keys("ar0", pc, pc + 2048), tag=lt + "yg")
                gr = 2 * (g % nct)
                bb = g // nct
                self.dma(d["YX"][gr:gr + 2, :,
                                 bb * cfg.pos:(bb + 1) * cfg.pos]
                         .rearrange("r p t -> p r t"),
                         ar0[:, pc:pc + 2048]
                         .rearrange("p (r t) -> p r t", r=2),
                         reads=keys("ar0", pc, pc + 2048),
                         writes=(("YX", gr), ("YX", gr + 1)), tag=lt + "yxo")
            self.coll("AllGather", ALU.bypass, d["YX"], d[f"yxg{l}"],
                      reads=tuple(("YX", r) for r in range(cfg.R)),
                      writes=((f"yxg{l}", 0),), tag=lt + "AG")
            yrows = d[f"yxg{l}"].rearrange("s q t -> (s q) t")
            kyr = keys("ar0", 0, 16384)
            for src_ in range(cfg.R):
                self.p.op("gpsimd",
                          lambda e, o=ar0[:, CL * src_:CL * (src_ + 1)],
                          i=yrows, x=sb["IDXS"][:, 16 + src_:17 + src_]:
                          e.indirect_dma_start(
                              out=o, out_offset=None, in_=i,
                              in_offset=IndirectOffsetOnAxis(x, 0)),
                          ((f"yxg{l}", 0),) + keys("IDXS"),
                          keys("ar0", CL * src_, CL * (src_ + 1)),
                          kind="d", tag=lt + "ygth")
            kow = keys("ar1", 0, 4096)
            self.dma_wfm(ar1[:, 0:4096], self.wbf[f"outw{l}"], 512,
                         reads=((f"wbf_outw{l}", 0),), writes=kow,
                         tag=lt + "oww")
            for i in range(NT):
                pp = i % 2
                kps = keys("psA", 512 * pp, 512 * (pp + 1))
                for src in range(cfg.R):
                    self.mm(psA[:, 512 * pp:512 * (pp + 1)],
                            ar0[:, src * CL + 128 * i:src * CL + 128 * (i + 1)],
                            ar1[:, 512 * src:512 * (src + 1)],
                            start=(src == 0), stop=(src == cfg.R - 1),
                            reads=kyr + kow, writes=kps, tag=lt + "owm")
                self.tt(ALU.add, E[:, 512 * i:512 * (i + 1)],
                        E[:, 512 * i:512 * (i + 1)],
                        psA[:, 512 * pp:512 * (pp + 1)],
                        reads=keys("E", 512 * i, 512 * (i + 1)) + kps,
                        writes=keys("E", 512 * i, 512 * (i + 1)), tag=lt + "hra")
        self.dma_wfm(ar0[:, 0:4096], self.wbf["lin2_w"], 1024,
                     reads=(("wbf_lin2_w", 0),), writes=keys("ar0", 0, 4096),
                     tag="mb_l2w")
        kb2 = keys("G", 0, 1024)
        self.dma(G[0:1, 0:1024], self.ein["lin2_b"], reads=(), writes=kb2,
                 tag="mb_l2b")
        kb2b = keys("W1", 1024, 2048)
        self.cp("vector", W1[0:1, 1024:2048], G[0:1, 0:1024],
                reads=kb2, writes=kb2b, tag="mb_l2bc")
        for i in range(cfg.NT_LOC):
            khb = keys("W1", 0, 512)
            self.cp("vector", W1[:, 0:512], E[:, 512 * i:512 * (i + 1)],
                    reads=keys("E", 512 * i, 512 * (i + 1)), writes=khb,
                    tag="mb_hc")
            khf = keys("W1", 512, 1024)
            for j in range(4):
                kT = keys("psT", 128 * j, 128 * (j + 1))
                self.trans(psT[:, 128 * j:128 * (j + 1)],
                           W1[:, 128 * j:128 * (j + 1)], reads=khb, writes=kT,
                           tag="mb_hT")
                self.cp("vector", W1[:, 512 + 128 * j:512 + 128 * (j + 1)],
                        psT[:, 128 * j:128 * (j + 1)], reads=kT, writes=khf,
                        tag="mb_hTe")
            kps = keys("psA", 0, 1024)
            for c_ in range(2):
                for k in range(4):
                    self.mm(psA[:, 512 * c_:512 * (c_ + 1)],
                            W1[:, 512 + 128 * k:512 + 128 * (k + 1)],
                            ar0[:, 1024 * k + 512 * c_:1024 * k + 512 * (c_ + 1)],
                            start=(k == 0), stop=False,
                            reads=khf + keys("ar0", 1024 * k + 512 * c_,
                                             1024 * k + 512 * (c_ + 1)),
                            writes=kps, tag="mb_l2m")
                self.mm(psA[:, 512 * c_:512 * (c_ + 1)], sb["c_o1x128b"][:],
                        W1[0:1, 1024 + 512 * c_:1024 + 512 * (c_ + 1)],
                        start=False, stop=True,
                        reads=kb2b + keys("c_o1x128b"), writes=kps,
                        tag="mb_l2bias")
            p3 = psA[:, 0:1024].rearrange("p (g k) -> p g k", k=32)
            km = keys("G", 1024, 1056)
            self.red(G[:, 1024:1056], p3, ALU.max, reads=kps, writes=km,
                     tag="mb_smx")
            g3 = G[:, 1024:1056].rearrange("p (g o) -> p g o", o=1)
            f3a = F[:, 0:1024].rearrange("p (g k) -> p g k", k=32)
            f3b = F[:, 1024:2048].rearrange("p (g k) -> p g k", k=32)
            ksb = keys("F", 0, 1024)
            p3b, g3b = bass.broadcast_tensor_aps(p3, g3)
            self.p.op("vector", lambda e, d_=f3a, a=p3b, bb=g3b:
                      e.tensor_tensor(out=d_, in0=a, in1=bb, op=ALU.subtract),
                      kps + km, ksb, tag="mb_sms")
            kex = keys("F", 1024, 2048)
            self.act(F[:, 1024:2048], F[:, 0:1024], AF.Exp, reads=ksb,
                     writes=kex, tag="mb_sme")
            ks2 = keys("G", 1056, 1088)
            self.red(G[:, 1056:1088], f3b, ALU.add, reads=kex, writes=ks2,
                     tag="mb_smsum")
            kr2 = keys("G", 1088, 1120)
            self.recip(G[:, 1088:1120], G[:, 1056:1088], reads=ks2, writes=kr2,
                       tag="mb_smr")
            g3r = G[:, 1088:1120].rearrange("p (g o) -> p g o", o=1)
            f3c, g3rb = bass.broadcast_tensor_aps(f3b, g3r)
            self.p.op("vector", lambda e, d_=f3a, a=f3c, bb=g3rb:
                      e.tensor_tensor(out=d_, in0=a, in1=bb, op=ALU.mult),
                      kex + kr2, ksb, tag="mb_smm")
            kxf = keys("F", 2048, 3072)
            self.dma(F[:, 2048:3072], d["xcur"][128 * i:128 * (i + 1), :],
                     reads=keys("xcur"), writes=kxf, tag="mb_xi")
            self.tt(ALU.add, F[:, 2048:3072], F[:, 2048:3072], F[:, 0:1024],
                    reads=kxf + ksb, writes=kxf, tag="mb_xadd")
            self.dma(d["xcur"][128 * i:128 * (i + 1), :], F[:, 2048:3072],
                     reads=kxf, writes=keys("xcur"), tag="mb_xo")

    def phase_out(self):
        self.dma(self.d["out"], self.d["xcur"], reads=keys("xcur"),
                 writes=(("out", 0),), tag="final_out")
        self.p.op("sync", lambda e: e.nop(), reads=(("out", 0),), writes=(),
                  tag="final_wait")


def kernel(**inputs):
    cfg = Cfg()
    maps = shard_inputs(cfg, inputs)
    k = K(cfg)
    nc = k.build()
    res = run_bass_kernel_spmd(nc, maps, core_ids=list(range(cfg.R)))
    outs = [r["out"] for r in res.results]
    full = np.zeros((cfg.B, cfg.S, cfg.D), np.float32)
    for c in range(cfg.R):
        sl = slice(cfg.pos * c, cfg.pos * (c + 1))
        full[0, sl] = outs[c][:cfg.pos]
        full[1, sl] = outs[c][cfg.pos:]
    return full


# version 08
# speedup vs baseline: 1.4187x; 1.0242x over previous
import numpy as np
import ml_dtypes
import concourse.bass as bass
import concourse.mybir as mybir
from concourse.bass_utils import run_bass_kernel_spmd

F32 = mybir.dt.float32
BF16 = mybir.dt.bfloat16
I32 = mybir.dt.int32
from concourse.bass_core_rust import IndirectOffsetOnAxis
AF = mybir.ActivationFunctionType
ALU = mybir.AluOpType
AX = mybir.AxisListType
BF16NP = ml_dtypes.bfloat16


class Cfg:
    def __init__(self, S=8192, phases=None):
        self.R = 8
        self.B = 2
        self.S = S
        self.D = 1024
        self.H = 16
        self.dh = 64
        self.DM = 512
        self.DI = 1024
        self.DS = 16
        self.DTR = 32
        self.NL = 4
        self.CK = 4
        self.pos = S // self.R
        self.TL = self.B * self.pos
        self.T = self.B * S
        self.TCH = min(1024, self.pos)
        self.NT_LOC = self.TL // 128
        self.MB = 3 + S
        self.phases = phases if phases is not None else [
            "p0", "attn_a", "ffn1", "mamba", "ffn2", "out"]

    def mcol0(self, b):
        return b * self.MB + 3


def head_perm(cfg):
    p = []
    for half in (0, 1):
        for hp in (0, 1):
            for i in range(32):
                p.append(hp * 64 + half * 32 + i)
    return np.array(p)


def build_aux(cfg):
    S, dh = cfg.S, cfg.dh
    aux = {}
    aux["ident"] = np.eye(128, dtype=BF16NP)
    half = dh // 2
    theta = 1.0 / (1000000.0 ** (2 * np.arange(half) / dh))
    ang = np.arange(S)[None, :] * theta[:, None]
    cos, sin = np.cos(ang).astype(np.float32), np.sin(ang).astype(np.float32)
    ropeK = np.concatenate([np.tile(cos, (2, 1)), np.tile(sin, (2, 1))], 0)
    aux["ropeK"] = ropeK.astype(BF16NP)
    aux["ropeQ"] = (ropeK * 0.125).astype(BF16NP)
    hmp = np.zeros((128, 2), np.float32)
    perm = head_perm(cfg)
    for j in range(128):
        hmp[j, 0 if perm[j] < 64 else 1] = 1.0
    aux["headmask_pi"] = hmp.astype(BF16NP)
    hm = np.zeros((128, 2), np.float32)
    hm[:64, 0] = 1.0
    hm[64:, 1] = 1.0
    aux["headmask"] = hm.astype(BF16NP)
    TCH = cfg.TCH
    u = np.arange(-(TCH - 128), TCH)
    aux["band"] = np.where(u[None, :] >= np.arange(128)[:, None],
                           0.0, -1e30).astype(BF16NP)
    aux["ones_1x128_f32"] = np.ones((1, 128), np.float32)
    o4 = np.zeros((128, 128), np.float32)
    o4[0::32, :] = 1.0
    aux["ones4"] = o4
    aux["ones_1x64_f32"] = np.ones((1, 64), np.float32)
    aux["ones_col_f32"] = np.ones((128, 1), np.float32)
    aux["ones_1x128_bf"] = np.ones((1, 128), BF16NP)
    aux["ones_col_bf"] = np.ones((128, 1), BF16NP)
    aux["cbias"] = np.tile(np.array([[1e-5, 1e-6, 1.0, 0.0]], np.float32),
                           (128, 1))
    aux["ones_dram_bf"] = np.ones((S,), BF16NP)
    return aux


def shard_inputs(cfg, inp):
    aux = build_aux(cfg)
    perm = head_perm(cfg)
    x = np.asarray(inp["L0_attn_a_x"])
    maps = []
    for c in range(cfg.R):
        m = dict(aux)
        sl = slice(cfg.pos * c, cfg.pos * (c + 1))
        m["x_in"] = np.ascontiguousarray(
            np.concatenate([x[0, sl], x[1, sl]], 0)).astype(np.float32)
        lcols = np.arange(2 * c * 64, 2 * c * 64 + 128)
        g = np.asarray(inp["L0_attn_a_norm_scale"])
        m["wqa"] = np.ascontiguousarray(
            (np.asarray(inp["L0_attn_a_Wq"]) * g[:, None])[:, lcols[perm]])
        m["wka"] = np.ascontiguousarray(
            (np.asarray(inp["L0_attn_a_Wk"]) * g[:, None])[:, lcols[perm]])
        m["wva"] = np.ascontiguousarray(
            (np.asarray(inp["L0_attn_a_Wv"]) * g[:, None])[:, lcols])
        m["woa"] = np.ascontiguousarray(np.asarray(inp["L0_attn_a_Wo"]))
        m["wqf"] = np.ascontiguousarray(np.asarray(inp["L0_ffn_Wq"])[:, lcols])
        m["wkf"] = np.ascontiguousarray(np.asarray(inp["L0_ffn_Wk"])[:, lcols])
        m["wvf"] = np.ascontiguousarray(np.asarray(inp["L0_ffn_Wv"])[:, lcols])
        m["wof"] = np.ascontiguousarray(np.asarray(inp["L0_ffn_Wo"]))
        m["wff"] = np.ascontiguousarray(np.asarray(inp["L0_ffn_Wff"]))
        dsl = slice(128 * c, 128 * (c + 1))
        gidx = np.zeros((128, 24), np.int32)
        for src_ in range(8):
            for b_ in range(2):
                gidx[:, src_ * 2 + b_] = (src_ * 2048 + c * 256 + b_ * 128 +
                                          np.arange(128))
            gidx[:, 16 + src_] = src_ * 1024 + c * 128 + np.arange(128)
        m["gidx"] = gidx
        m["lin1_w"] = np.ascontiguousarray(np.asarray(inp["L0_attn_b_lin1_w"]))
        m["lin1_b"] = np.asarray(inp["L0_attn_b_lin1_b"]).reshape(1, -1)
        m["lin2_w"] = np.ascontiguousarray(np.asarray(inp["L0_attn_b_lin2_w"]))
        m["lin2_b"] = np.asarray(inp["L0_attn_b_lin2_b"]).reshape(1, -1)
        m["norm_w"] = np.ascontiguousarray(np.asarray(inp["L0_attn_b_norm_w"]))
        iw = np.asarray(inp["L0_attn_b_in_w"])
        m["inw"] = np.ascontiguousarray(
            np.concatenate([iw[:, :, dsl],
                            iw[:, :, cfg.DI + 128 * c: cfg.DI + 128 * (c + 1)]], -1))
        m["convw"] = np.ascontiguousarray(np.asarray(inp["L0_attn_b_conv_w"])[:, dsl, :])
        m["convb"] = np.ascontiguousarray(np.asarray(inp["L0_attn_b_conv_b"])[:, dsl])
        m["xprojw"] = np.ascontiguousarray(np.asarray(inp["L0_attn_b_xproj_w"])[:, dsl, :])
        m["dtw"] = np.ascontiguousarray(np.asarray(inp["L0_attn_b_dt_w"])[:, :, dsl])
        m["dtb"] = np.ascontiguousarray(np.asarray(inp["L0_attn_b_dt_b"])[:, dsl])
        m["alog"] = np.ascontiguousarray(np.asarray(inp["L0_attn_b_A_log"])[:, dsl, :])
        m["dpv"] = np.ascontiguousarray(np.asarray(inp["L0_attn_b_Dp"])[:, dsl])
        m["outw"] = np.ascontiguousarray(np.asarray(inp["L0_attn_b_out_w"]))
        maps.append(m)
    return maps



ENGS = ("tensor", "vector", "scalar", "gpsimd", "sync")

GRAN = {"ar0": 1024, "ar1": 1024, "ar2": 1024, "ar3": 1024,
        "E": 512, "F": 512, "G": 32, "W0": 128, "W1": 128,
        "psA": 512, "psB": 512, "psC": 512, "psT": 1024}


def keys(name, lo=None, hi=None):
    if lo is None:
        return ((name, 0),)
    gw = GRAN.get(name, 1 << 30)
    return tuple((name, g) for g in range(lo // gw, (hi + gw - 1) // gw))


class _Op:
    __slots__ = ("idx", "eng", "fn", "deps", "kind", "marked", "inc_no",
                 "waits", "clock", "ccsem", "tag")

    def __init__(self, idx, eng, fn, kind, tag):
        self.idx, self.eng, self.fn, self.kind, self.tag = idx, eng, fn, kind, tag
        self.deps = []
        self.marked = False
        self.inc_no = None
        self.waits = None
        self.clock = None
        self.ccsem = None


class Prog:
    def __init__(self):
        self.ops = []
        self.last_w = {}
        self.readers = {}
        self.n_cc = 0

    def op(self, eng, fn, reads=(), writes=(), kind="c", tag=""):
        o = _Op(len(self.ops), eng, fn, kind, tag)
        dep = {}
        for k in reads:
            w = self.last_w.get(k)
            if w is not None:
                dep[w.idx] = w
        for k in writes:
            w = self.last_w.get(k)
            if w is not None:
                dep[w.idx] = w
            for rd in self.readers.get(k, ()):
                dep[rd.idx] = rd
        o.deps = [d for d in dep.values()
                  if not (d.eng == "tensor" and eng == "tensor"
                          and d.kind == "c" and kind == "c")]
        for k in reads:
            self.readers.setdefault(k, []).append(o)
        for k in writes:
            self.last_w[k] = o
            self.readers[k] = []
        if kind == "cc":
            o.ccsem = self.n_cc
            self.n_cc += 1
        self.ops.append(o)
        return o

    def finalize_and_emit(self, block, sems):
        for o in self.ops:
            if o.kind == "d":
                o.marked = True
            for d in o.deps:
                d.marked = True
        counters = {}
        for o in self.ops:
            if not o.marked:
                continue
            if o.kind == "cc":
                o.inc_no = 1
            else:
                key = (o.eng, o.kind)
                counters[key] = counters.get(key, 0) + 1
                o.inc_no = counters[key]

        def semkey(o):
            return ("cc", o.ccsem) if o.kind == "cc" else (o.eng, o.kind)

        def val(o):
            return o.inc_no * (16 if o.kind == "d" else 1)

        last_on_eng = {}
        for o in self.ops:
            prev = last_on_eng.get(o.eng)
            base = dict(prev.clock) if prev is not None else {}
            need = {}
            for d in o.deps:
                sk, v = semkey(d), val(d)
                if need.get(sk, -1) < v:
                    need[sk] = v
                for sk2, v2 in d.clock.items():
                    if base.get(sk2, -1) < v2:
                        base[sk2] = v2
            o.waits = {sk: v for sk, v in need.items() if base.get(sk, 0) < v}
            clk = base
            for sk, v in need.items():
                if clk.get(sk, 0) < v:
                    clk[sk] = v
            o.clock = clk
            last_on_eng[o.eng] = o

        by_eng = {e: [] for e in ENGS}
        for o in self.ops:
            by_eng[o.eng].append(o)

        def emit_stream(eng_name):
            ops = by_eng[eng_name]

            def go(eng):
                for o in ops:
                    for sk in sorted(o.waits, key=str):
                        eng.wait_ge(sems[sk], o.waits[sk])
                    inst = o.fn(eng)
                    if o.kind == "cc":
                        inst.then_inc(sems[("cc", o.ccsem)])
                    elif o.marked:
                        if o.kind == "d":
                            inst.then_inc(sems[(o.eng, "d")], 16)
                        else:
                            inst.then_inc(sems[(o.eng, "c")], 1)
            return go

        for e in ENGS:
            if by_eng[e]:
                getattr(block, e)(emit_stream(e))


def ceil_div(a, b):
    return (a + b - 1) // b


def as_col(ap1d):
    return ap1d.rearrange("(a b) -> a b", b=1)


def as_row(ap1d):
    return ap1d.rearrange("(a b) -> a b", a=1)



class K:
    def __init__(self, cfg):
        self.cfg = cfg
        self.nc = bass.Bass("TRN2", target_bir_lowering=False, debug=False)
        self.sb = {}
        self.sbuf_bytes = 0

    def dtn(self, name, shape, dtype, kind="Internal", shared=False):
        if shared:
            return self.nc.dram_tensor(name, list(shape), dtype,
                                       addr_space="Shared")
        if kind == "Internal":
            return self.nc.dram_tensor(name, list(shape), dtype)
        return self.nc.dram_tensor(name, list(shape), dtype, kind=kind)

    def st(self, name, shape, dtype):
        t = self.ctx.enter_context(self.nc.sbuf_tensor(name, list(shape), dtype))
        self.sb[name] = t
        self.sbuf_bytes += int(np.prod(shape[1:])) * mybir.dt.size(dtype)
        return t

    def pt(self, name, shape, dtype):
        t = self.ctx.enter_context(self.nc.psum_tensor(name, list(shape), dtype))
        self.sb[name] = t
        return t

    def mm(self, out, lhsT, rhs, start, stop, reads, writes, tag="mm", tp=None):
        self.p.op("tensor",
                  lambda e, o=out, l=lhsT, r=rhs, s1=start, s2=stop, t=tp:
                  e.matmul(o, l, r, start=s1, stop=s2, tile_position=t),
                  reads, writes, tag=tag)

    def trans(self, out, in_, reads, writes, tag="tr"):
        ident = self.sb["c_ident"][:]
        self.p.op("tensor",
                  lambda e, o=out, i=in_, d=ident: e.transpose(o, i, d),
                  tuple(reads) + keys("c_ident"), writes, tag=tag)

    def cp(self, eng, dst, src, reads, writes, tag="cp"):
        if eng == "scalar":
            self.p.op("scalar", lambda e, d=dst, s=src: e.copy(d, s),
                      reads, writes, tag=tag)
        else:
            self.p.op("vector", lambda e, d=dst, s=src: e.tensor_copy(d, s),
                      reads, writes, tag=tag)

    def act(self, dst, src, func, reads, writes, bias=None, scale=None,
            accum=None, tag="act"):
        kw = {}
        if bias is not None:
            kw["bias"] = bias
        if scale is not None:
            kw["scale"] = scale
        if accum is not None:
            kw["accum_out"] = accum
        self.p.op("scalar",
                  lambda e, d=dst, s=src, f=func, kw=kw: e.activation(d, s, f, **kw),
                  reads, writes, tag=tag)

    def tt(self, op_alu, dst, a, b, reads, writes, tag="tt", eng="vector"):
        self.p.op(eng, lambda e, d=dst, x=a, y=b, o=op_alu:
                  e.tensor_tensor(out=d, in0=x, in1=y, op=o), reads, writes, tag=tag)

    def ts(self, dst, a, s1, op0, reads, writes, s2=None, op1=None, tag="ts"):
        def f(e, d=dst, x=a, v1=s1, v2=s2, o0=op0, o1=op1):
            if o1 is None:
                return e.tensor_scalar(out=d, in0=x, scalar1=v1, scalar2=None, op0=o0)
            return e.tensor_scalar(out=d, in0=x, scalar1=v1, scalar2=v2,
                                   op0=o0, op1=o1)
        self.p.op("vector", f, reads, writes, tag=tag)

    def stt(self, dst, in0, scalar, in1, op0, op1, reads, writes, tag="stt"):
        self.p.op("vector", lambda e, d=dst, a=in0, s=scalar, b=in1, o0=op0, o1=op1:
                  e.scalar_tensor_tensor(out=d, in0=a, scalar=s, in1=b, op0=o0, op1=o1),
                  reads, writes, tag=tag)

    def scan_op(self, dst, d0, d1, init, reads, writes, tag="scan"):
        self.p.op("vector", lambda e, d=dst, a=d0, b=d1, i=init:
                  e.tensor_tensor_scan(out=d, data0=a, data1=b, initial=i,
                                       op0=ALU.mult, op1=ALU.add),
                  reads, writes, tag=tag)

    def red(self, dst, src, op_alu, reads, writes, tag="red"):
        self.p.op("vector", lambda e, d=dst, s=src, o=op_alu:
                  e.tensor_reduce(out=d, in_=s, axis=AX.X, op=o), reads, writes, tag=tag)

    def recip(self, dst, src, reads, writes, tag="recip"):
        self.p.op("vector", lambda e, d=dst, s=src: e.reciprocal(d, s),
                  reads, writes, tag=tag)

    def dma(self, out_ap, in_ap, reads, writes, eng="sync", tag="dma",
            transpose=False):
        if transpose:
            self.p.op(eng, lambda e, o=out_ap, i=in_ap:
                      e.dma_start(out=o, in_=i, transpose=True),
                      reads, writes, kind="d", tag=tag)
        else:
            self.p.op(eng, lambda e, o=out_ap, i=in_ap: e.dma_start(out=o, in_=i),
                      reads, writes, kind="d", tag=tag)

    def coll(self, kind, op_alu, in_ap, out_ap, reads, writes, tag="cc"):
        groups = [list(range(self.cfg.R))]
        self.p.op("gpsimd", lambda e, k=kind, o=op_alu, i=in_ap, u=out_ap, g=groups:
                  e.collective_compute(k, o, replica_groups=g, ins=[i], outs=[u]),
                  reads, writes, kind="cc", tag=tag)

    def memset(self, ap, v, reads, writes, tag="memset", eng="gpsimd"):
        self.p.op(eng, lambda e, a=ap, vv=v: e.memset(a, vv), reads, writes, tag=tag)


    def dma_wfm(self, dst_slice, src_ap, mcols, reads, writes, tag):
        self.dma(dst_slice.rearrange("p (c m) -> p c m", m=mcols),
                 src_ap.rearrange("(c p) m -> p c m", p=128),
                 reads=reads, writes=writes, tag=tag)

    def build(self):
        import contextlib
        cfg = self.cfg
        nc = self.nc
        self.ctx = contextlib.ExitStack()
        S, B, D, TL, T = cfg.S, cfg.B, cfg.D, cfg.TL, cfg.T

        ein = {}

        def EIN(name, shape, dtype=F32):
            ein[name] = self.dtn(name, shape, dtype, kind="ExternalInput").ap()

        EIN("x_in", [TL, D])
        for nm, shp in [("wqa", [D, 128]), ("wka", [D, 128]), ("wva", [D, 128]),
                        ("woa", [D, D]), ("wqf", [D, 128]), ("wkf", [D, 128]),
                        ("wvf", [D, 128]), ("wof", [D, D]), ("wff", [D, D]),
                        ("lin1_w", [D, cfg.DM]), ("lin1_b", [1, cfg.DM]),
                        ("lin2_w", [cfg.DM, D]), ("lin2_b", [1, D]),
                        ("norm_w", [cfg.NL, cfg.DM]), ("inw", [cfg.NL, cfg.DM, 256]),
                        ("convw", [cfg.NL, 128, cfg.CK]), ("convb", [cfg.NL, 128]),
                        ("xprojw", [cfg.NL, 128, 64]), ("dtw", [cfg.NL, cfg.DTR, 128]),
                        ("dtb", [cfg.NL, 128]), ("alog", [cfg.NL, 128, cfg.DS]),
                        ("dpv", [cfg.NL, 128]), ("outw", [cfg.NL, cfg.DI, cfg.DM])]:
            EIN(nm, shp)
        for nm, shp, dt_ in [("ident", [128, 128], BF16), ("ropeK", [128, S], BF16),
                             ("ropeQ", [128, S], BF16), ("headmask_pi", [128, 2], BF16),
                             ("headmask", [128, 2], BF16),
                             ("band", [128, 2 * cfg.TCH - 128], BF16),
                             ("ones_1x128_f32", [1, 128], F32),
                             ("ones4", [128, 128], F32),
                             ("ones_1x64_f32", [1, 64], F32),
                             ("ones_col_f32", [128, 1], F32),
                             ("ones_1x128_bf", [1, 128], BF16),
                             ("ones_col_bf", [128, 1], BF16),
                             ("cbias", [128, 4], F32),
                             ("ones_dram_bf", [S], BF16),
                             ("gidx", [128, 24], I32)]:
            EIN(nm, shp, dt_)
        self.ein = ein
        out_t = self.dtn("out", [TL, D], F32, kind="ExternalOutput").ap()

        wbf = {}
        for nm, shp in [("wqa", [D, 128]), ("wka", [D, 128]), ("wva", [D, 128]),
                        ("woa", [D, D]), ("wqf", [D, 128]), ("wkf", [D, 128]),
                        ("wvf", [D, 128]), ("wof", [D, D]), ("wff", [D, D]),
                        ("lin1_w", [D, cfg.DM]), ("lin2_w", [cfg.DM, D])]:
            wbf[nm] = self.dtn("wbf_" + nm, shp, BF16).ap()
        for l in range(cfg.NL):
            wbf[f"inw{l}"] = self.dtn(f"wbf_inw{l}", [cfg.DM, 256], BF16).ap()
            wbf[f"xprojw{l}"] = self.dtn(f"wbf_xprojw{l}", [128, 64], BF16).ap()
            wbf[f"dtw{l}"] = self.dtn(f"wbf_dtw{l}", [cfg.DTR, 128], BF16).ap()
            wbf[f"outw{l}"] = self.dtn(f"wbf_outw{l}", [cfg.DI, cfg.DM], BF16).ap()
        self.wbf = wbf

        dd = {}
        dd["ag_xn_in"] = self.dtn("ag_xn_in", [D, TL], BF16).ap()
        dd["ag_xn_out"] = self.dtn("ag_xn_out", [cfg.R * D, TL], BF16, shared=True).ap()
        dd["ag_hr_in"] = self.dtn("ag_hr_in", [cfg.DM, TL], BF16).ap()
        dd["ag_hr_out"] = self.dtn("ag_hr_out", [cfg.R * cfg.DM, TL], BF16, shared=True).ap()
        dd["CONTRX"] = self.dtn("CONTRX", [cfg.R, 256, cfg.pos], BF16).ap()
        for u in range(3):
            dd[f"cxg{u}"] = self.dtn(f"cxg{u}", [8 * cfg.R, 256, cfg.pos],
                                     BF16, shared=True).ap()
        dd["YX"] = self.dtn("YX", [cfg.R, 128, TL], BF16).ap()
        for u in range(4):
            dd[f"yxg{u}"] = self.dtn(f"yxg{u}", [8 * cfg.R, 128, TL],
                                     BF16, shared=True).ap()
        dd["dblpart"] = self.dtn("dblpart", [64, T], F32).ap()
        dd["dblfull"] = self.dtn("dblfull", [64, T], F32, shared=True).ap()
        dd["QA"] = self.dtn("QA", [4, 65, S], BF16).ap()
        dd["KA"] = self.dtn("KA", [4, 65, S], BF16).ap()
        dd["VA"] = self.dtn("VA", [4, S, 64], BF16).ap()
        dd["norms_a"] = self.dtn("norms_a", [4, 2, S], F32).ap()
        dd["QF"] = self.dtn("QF", [B, 128, S], BF16).ap()
        dd["KF"] = self.dtn("KF", [B, 128, S], BF16).ap()
        dd["VF"] = self.dtn("VF", [B, S, 128], BF16).ap()
        dd["norms_f"] = self.dtn("norms_f", [B, 2, 2, S], F32).ap()
        dd["negMk"] = self.dtn("negMk", [B, 2, S], BF16).ap()
        dd["ZSP"] = self.dtn("ZSP", [128, T], BF16).ap()
        dd["USP"] = self.dtn("USP", [128, T], BF16).ap()
        dd["x_in"] = ein["x_in"]
        dd["xcur"] = self.dtn("xcur", [TL, D], F32).ap()
        dd["out"] = out_t
        self.d = dd

        AW = 16448
        for i in range(4):
            self.st(f"ar{i}", [128, AW], BF16)
        self.st("E", [128, 8192], F32)
        self.st("F", [128, 4096], F32)
        self.st("G", [128, 2304], F32)
        self.st("W0", [128, 4096], BF16)
        self.st("W1", [128, 2048], BF16)
        self.st("c_ident", [128, 128], BF16)
        self.st("c_hmpi", [128, 2], BF16)
        self.st("c_hm", [128, 2], BF16)
        self.st("c_band", [128, 2 * cfg.TCH - 128], BF16)
        self.st("c_o1x128f", [1, 128], F32)
        self.st("c_o4", [128, 128], F32)
        self.st("c_o1x64f", [1, 64], F32)
        self.st("c_ocolf", [128, 1], F32)
        self.st("c_o1x128b", [1, 128], BF16)
        self.st("c_ocolb", [128, 1], BF16)
        self.st("c_cb", [128, 4], F32)
        self.st("IDXS", [128, 24], I32)
        self.pt("psA", [128, 2048], F32)
        self.pt("psB", [128, 1024], F32)
        self.pt("psC", [128, 512], F32)
        self.pt("psT", [128, 1024], BF16)

        self.p = Prog()
        ph = cfg.phases
        if "p0" in ph:
            self.phase_p0()
        if "attn_a" in ph:
            self.phase_ln_ag(extra_rms=True, tag="ln1", do_ag=True, fm_ar="ar0")
            self.phase_attn_a_proj()
            self.phase_attn_a_attn()
            self.phase_wo_a2a("woa_out", "woa", "attna", "cxg0")
        for p_ in ph:
            if p_.startswith("ffn1"):
                stop = p_.split(":")[1] if ":" in p_ else None
                self.phase_ffn(0, stop=stop)
        if "mamba" in ph:
            self.phase_mamba()
        if "ffn2" in ph:
            self.phase_ffn(1)
        if "out" in ph:
            self.phase_out()

        sems = {}
        for e in ENGS:
            for knd in ("c", "d"):
                sems[(e, knd)] = self.ctx.enter_context(nc.semaphore(f"s_{e}_{knd}"))
        for i in range(self.p.n_cc):
            sems[("cc", i)] = self.ctx.enter_context(nc.semaphore(f"s_cc_{i}"))
        with nc.Block() as block:
            self.p.finalize_and_emit(block, sems)
        self.ctx.close()
        return nc

    def phase_p0(self):
        cfg, sb, ein = self.cfg, self.sb, self.ein
        for nm, dst in [("ident", "c_ident"), ("headmask_pi", "c_hmpi"),
                        ("headmask", "c_hm"), ("band", "c_band"),
                        ("ones_1x128_f32", "c_o1x128f"), ("ones4", "c_o4"),
                        ("ones_1x64_f32", "c_o1x64f"),
                        ("ones_col_f32", "c_ocolf"), ("ones_1x128_bf", "c_o1x128b"),
                        ("ones_col_bf", "c_ocolb"), ("cbias", "c_cb")]:
            self.dma(sb[dst][:], ein[nm], reads=(), writes=keys(dst), tag="cload")
        self.dma(self.d["xcur"], self.d["x_in"], reads=(), writes=keys("xcur"),
                 tag="xload")
        self.dma(sb["IDXS"][:], ein["gidx"], reads=(), writes=keys("IDXS"),
                 tag="gidxload")
        F, G, ar0 = sb["F"], sb["G"], sb["ar0"]

        def cast(ap_in, ap_out, rows, cols, okey, scale_rows=None):
            for i in range(ceil_div(rows, 128)):
                r0, rn = i * 128, min(128, rows - i * 128)
                for j in range(ceil_div(cols, 2048)):
                    c0, cn = j * 2048, min(2048, cols - j * 2048)
                    pp = (i + j) % 2
                    kf = keys("F", pp * 2048, pp * 2048 + cn)
                    ka = keys("ar0", pp * 2048, pp * 2048 + cn)
                    self.dma(F[0:rn, pp * 2048:pp * 2048 + cn],
                             ap_in[r0:r0 + rn, c0:c0 + cn],
                             reads=(), writes=kf, tag="wc_in")
                    if scale_rows is not None:
                        ks = keys("G", 2176 + pp, 2177 + pp)
                        self.dma(G[0:rn, 2176 + pp:2177 + pp],
                                 as_col(scale_rows[r0:r0 + rn]),
                                 reads=(), writes=ks, tag="wc_sc")
                        self.ts(F[0:rn, pp * 2048:pp * 2048 + cn],
                                F[0:rn, pp * 2048:pp * 2048 + cn],
                                G[0:rn, 2176 + pp:2177 + pp], ALU.mult,
                                reads=kf + ks, writes=kf, tag="wc_scale")
                    self.cp("vector", ar0[0:rn, pp * 2048:pp * 2048 + cn],
                            F[0:rn, pp * 2048:pp * 2048 + cn],
                            reads=kf, writes=ka, tag="wc_cast")
                    self.dma(ap_out[r0:r0 + rn, c0:c0 + cn],
                             ar0[0:rn, pp * 2048:pp * 2048 + cn],
                             reads=ka, writes=((okey, 0),), tag="wc_out")

        wb = self.wbf
        for nm, rows, cols in [("wqa", cfg.D, 128), ("wka", cfg.D, 128),
                               ("wva", cfg.D, 128), ("woa", cfg.D, cfg.D),
                               ("wqf", cfg.D, 128), ("wkf", cfg.D, 128),
                               ("wvf", cfg.D, 128), ("wof", cfg.D, cfg.D),
                               ("wff", cfg.D, cfg.D), ("lin1_w", cfg.D, cfg.DM),
                               ("lin2_w", cfg.DM, cfg.D)]:
            cast(ein[nm], wb[nm], rows, cols, "wbf_" + nm)
        for l in range(cfg.NL):
            cast(ein["inw"][l], wb[f"inw{l}"], cfg.DM, 256, f"wbf_inw{l}",
                 scale_rows=ein["norm_w"][l, :])
            cast(ein["xprojw"][l], wb[f"xprojw{l}"], 128, 64, f"wbf_xprojw{l}")
            cast(ein["dtw"][l], wb[f"dtw{l}"], cfg.DTR, 128, f"wbf_dtw{l}")
            cast(ein["outw"][l], wb[f"outw{l}"], cfg.DI, cfg.DM, f"wbf_outw{l}")

    def phase_ln_ag(self, extra_rms, tag, do_ag, fm_ar):
        cfg, sb = self.cfg, self.sb
        D, NT, TL = cfg.D, cfg.NT_LOC, cfg.TL
        F, G, W1, E = sb["F"], sb["G"], sb["W1"], sb["E"]
        arF = sb[fm_ar]
        psT = sb["psT"]
        nch = D // 128
        for i in range(NT):
            pp = i % 2
            xo = pp * 2048
            gs = pp * 16
            kx = keys("F", xo, xo + 1024)
            kxc = keys("F", xo + 1024, xo + 2048)
            kst = keys("G", gs, gs + 16)
            ke = keys("E", pp * 1024, pp * 1024 + 1024)
            self.dma(F[:, xo:xo + 1024], self.d["xcur"][128 * i:128 * (i + 1), :],
                     reads=keys("xcur"), writes=kx, tag=tag + "_in")
            self.red(G[:, gs:gs + 1], F[:, xo:xo + 1024], ALU.add,
                     reads=kx, writes=kst, tag=tag + "_sum")
            self.ts(G[:, gs + 1:gs + 2], G[:, gs:gs + 1], 1.0 / D, ALU.mult,
                    reads=kst, writes=kst, tag=tag + "_mean")
            self.ts(F[:, xo + 1024:xo + 2048], F[:, xo:xo + 1024],
                    G[:, gs + 1:gs + 2], ALU.subtract,
                    reads=kx + kst, writes=kxc, tag=tag + "_sub")
            self.act(E[:, pp * 1024:pp * 1024 + 1024], F[:, xo + 1024:xo + 2048],
                     AF.Square, reads=kxc, writes=ke, tag=tag + "_sq")
            self.red(G[:, gs + 2:gs + 3], E[:, pp * 1024:pp * 1024 + 1024],
                     ALU.add, reads=ke, writes=kst, tag=tag + "_var")
            self.act(G[:, gs + 3:gs + 4], G[:, gs + 2:gs + 3], AF.Sqrt,
                     reads=kst + keys("c_cb"), writes=kst, scale=1.0 / D,
                     bias=self.sb["c_cb"][:, 0:1], tag=tag + "_std")
            self.recip(G[:, gs + 4:gs + 5], G[:, gs + 3:gs + 4],
                       reads=kst, writes=kst, tag=tag + "_rstd")
            self.ts(F[:, xo:xo + 1024], F[:, xo + 1024:xo + 2048],
                    G[:, gs + 4:gs + 5], ALU.mult,
                    reads=kxc + kst, writes=kx, tag=tag + "_norm")
            if extra_rms:
                self.act(E[:, pp * 1024:pp * 1024 + 1024], F[:, xo:xo + 1024],
                         AF.Square, reads=kx, writes=ke, tag=tag + "_sq2")
                self.red(G[:, gs + 5:gs + 6], E[:, pp * 1024:pp * 1024 + 1024],
                         ALU.add, reads=ke, writes=kst, tag=tag + "_v2")
                self.act(G[:, gs + 6:gs + 7], G[:, gs + 5:gs + 6], AF.Sqrt,
                         reads=kst + keys("c_cb"), writes=kst, scale=1.0 / D,
                         bias=self.sb["c_cb"][:, 1:2], tag=tag + "_std2")
                self.recip(G[:, gs + 7:gs + 8], G[:, gs + 6:gs + 7],
                           reads=kst, writes=kst, tag=tag + "_rstd2")
                self.ts(F[:, xo:xo + 1024], F[:, xo:xo + 1024],
                        G[:, gs + 7:gs + 8], ALU.mult,
                        reads=kx + kst, writes=kx, tag=tag + "_norm2")
            kbf = keys("W1", pp * 1024, pp * 1024 + 1024)
            self.cp("vector", W1[:, pp * 1024:pp * 1024 + 1024], F[:, xo:xo + 1024],
                    reads=kx, writes=kbf, tag=tag + "_cast")
            for j in range(nch):
                kT = keys("psT", 128 * (j % 8), 128 * (j % 8) + 128)
                self.trans(psT[:, 128 * (j % 8):128 * (j % 8) + 128],
                           W1[:, pp * 1024 + 128 * j:pp * 1024 + 128 * (j + 1)],
                           reads=kbf, writes=kT, tag=tag + "_T")
                self.cp("vector",
                        arF[:, TL * j + 128 * i:TL * j + 128 * (i + 1)],
                        psT[:, 128 * (j % 8):128 * (j % 8) + 128],
                        reads=kT, writes=keys(fm_ar, TL * j + 128 * i,
                                              TL * j + 128 * (i + 1)),
                        tag=tag + "_Tev")
        if do_ag:
            for j in range(nch):
                self.dma(self.d["ag_xn_in"][128 * j:128 * (j + 1), :],
                         arF[:, TL * j:TL * j + TL],
                         reads=keys(fm_ar, TL * j, TL * j + TL),
                         writes=(("ag_xn_in", j),), tag=tag + "_agin")
            self.coll("AllGather", ALU.bypass, self.d["ag_xn_in"],
                      self.d["ag_xn_out"],
                      reads=tuple(("ag_xn_in", j) for j in range(nch)),
                      writes=(("ag_xn_out", 0),), tag=tag + "_AG")

    def phase_attn_a_proj(self):
        cfg, sb, d = self.cfg, self.sb, self.d
        S, D = cfg.S, cfg.D
        nch = D // 128
        W0, W1, G, F = sb["W0"], sb["W1"], sb["G"], sb["F"]
        psA, psB, psT = sb["psA"], sb["psB"], sb["psT"]
        CL = cfg.TL
        half = CL // 2
        for nm, off in [("wqa", 0), ("wka", 1024), ("wva", 2048)]:
            self.dma_wfm(W0[:, off:off + 1024], self.wbf[nm], 128,
                         reads=(("wbf_" + nm, 0),),
                         writes=keys("W0", off, off + 1024), tag="aw_load")
        for u in range(4):
            self.dma(d["KA"][u, 64, :], self.ein["ones_dram_bf"],
                     reads=(), writes=(("KA", u),), tag="ka_ones")

        for r in range(cfg.R):
            pp = r % 2
            arn = f"ar{pp}"
            arin = sb[arn]
            kin = keys(arn, 0, nch * CL)
            self.dma(arin[:, 0:nch * CL].rearrange("p (c m) -> p c m", m=CL),
                     d["ag_xn_out"][128 * nch * r:128 * nch * (r + 1), :]
                     .rearrange("(c p) m -> p c m", p=128),
                     reads=(("ag_xn_out", 0),), writes=kin, tag="aproj_in")
            kRT = keys("ar3", 0, 2048)
            self.dma(sb["ar3"][:, 0:1024],
                     self.ein["ropeQ"][:, cfg.pos * r:cfg.pos * r + cfg.pos],
                     reads=(), writes=kRT, tag="rtq")
            self.dma(sb["ar3"][:, 1024:2048],
                     self.ein["ropeK"][:, cfg.pos * r:cfg.pos * r + cfg.pos],
                     reads=(), writes=kRT, tag="rtk")

            def project(woff):
                kw = keys("W0", woff, woff + 1024)
                kps = keys("psA", 0, CL)
                for j in range(nch):
                    for c_ in range(CL // 512):
                        self.mm(psA[:, 512 * c_:512 * (c_ + 1)],
                                W0[:, woff + 128 * j:woff + 128 * (j + 1)],
                                arin[:, CL * j + 512 * c_:CL * j + 512 * (c_ + 1)],
                                start=(j == 0), stop=(j == nch - 1),
                                reads=kin + kw, writes=kps, tag="aproj_mm")
                return kps

            def rope(kps, toff):
                tb = sb["ar3"]
                kd = keys("W1", 0, CL)
                for b in range(2):
                    cs = b * half
                    kt = keys("F", 0, 2 * half)
                    self.tt(ALU.mult, F[0:64, 0:half], psA[0:64, cs:cs + half],
                            tb[0:64, toff:toff + half], reads=kps + kRT,
                            writes=kt, tag="rope1")
                    self.tt(ALU.mult, F[0:64, half:2 * half],
                            psA[64:128, cs:cs + half],
                            tb[64:128, toff:toff + half], reads=kps + kRT,
                            writes=kt, tag="rope2")
                    self.tt(ALU.subtract, W1[0:64, cs:cs + half], F[0:64, 0:half],
                            F[0:64, half:2 * half], reads=kt, writes=kd,
                            tag="rope3")
                    self.tt(ALU.mult, F[0:64, 0:half], psA[0:64, cs:cs + half],
                            tb[64:128, toff:toff + half], reads=kps + kRT,
                            writes=kt, tag="rope4")
                    self.tt(ALU.mult, F[0:64, half:2 * half],
                            psA[64:128, cs:cs + half],
                            tb[0:64, toff:toff + half], reads=kps + kRT,
                            writes=kt, tag="rope5")
                    self.tt(ALU.add, W1[64:128, cs:cs + half], F[0:64, 0:half],
                            F[0:64, half:2 * half], reads=kt, writes=kd,
                            tag="rope6")
                return kd

            def norms_store(kd, qk, dst_dram, dst_name):
                ksq = keys("ar2", 0, CL)
                self.act(sb["ar2"][:, 0:CL], W1[:, 0:CL], AF.Square,
                         reads=kd, writes=ksq, tag="an_sq")
                kn = keys("psA", 0, CL)
                for c_ in range(CL // 512):
                    self.mm(psA[0:2, 512 * c_:512 * (c_ + 1)], sb["c_hmpi"][:],
                            sb["ar2"][:, 512 * c_:512 * (c_ + 1)],
                            start=True, stop=True, reads=ksq + keys("c_hmpi"),
                            writes=kn, tag="an_mm")
                kev = keys("F", 2048, 2048 + CL)
                self.cp("vector", F[0:2, 2048:2048 + CL], psA[0:2, 0:CL],
                        reads=kn, writes=kev, tag="an_ev")
                for hp in range(2):
                    for b in range(2):
                        uu = b * 2 + hp
                        self.dma(d["norms_a"][uu, qk, cfg.pos * r:cfg.pos * r + half],
                                 F[hp:hp + 1, 2048 + b * half:2048 + (b + 1) * half],
                                 reads=kev, writes=(("norms_a", uu, qk),),
                                 tag="an_out")
                        for piece, r0 in ((0, 32 * hp), (1, 64 + 32 * hp)):
                            self.dma(dst_dram[uu, 32 * piece:32 * piece + 32,
                                              cfg.pos * r: cfg.pos * r + half],
                                     W1[r0:r0 + 32, b * half:(b + 1) * half],
                                     reads=kd, writes=((dst_name, uu),),
                                     tag="aqk_out")

            kq = project(0)
            kdq = rope(kq, 0)
            norms_store(kdq, 0, d["QA"], "QA")
            kk = project(1024)
            kdk = rope(kk, 1024)
            norms_store(kdk, 1, d["KA"], "KA")
            kv = project(2048)
            kvs = keys("ar2", 4096, 4096 + CL)
            self.cp("scalar", sb["ar2"][:, 4096:4096 + CL], psA[:, 0:CL],
                    reads=kv, writes=kvs, tag="av_ev")
            ntile = CL // 128
            for g in range(ceil_div(ntile, 8)):
                kT = keys("psT", 0, 1024)
                for tl in range(min(8, ntile - 8 * g)):
                    i = 8 * g + tl
                    self.trans(psT[:, 128 * tl:128 * (tl + 1)],
                               sb["ar2"][:, 4096 + 128 * i:4096 + 128 * (i + 1)],
                               reads=kvs, writes=kT, tag="av_T")
                eb = 8192 + 1024 * (g % 2)
                kTe = keys("ar2", eb, eb + 1024)
                self.cp("vector", sb["ar2"][:, eb:eb + 1024], psT[:, 0:1024],
                        reads=kT, writes=kTe, tag="av_Tev")
                for tl in range(min(8, ntile - 8 * g)):
                    i = 8 * g + tl
                    b = i // (ntile // 2)
                    row0 = cfg.pos * r + 128 * (i % (ntile // 2))
                    for hp in range(2):
                        uu = b * 2 + hp
                        self.dma(d["VA"][uu, row0:row0 + 128, :],
                                 sb["ar2"][:, eb + 128 * tl + 64 * hp:
                                           eb + 128 * tl + 64 * (hp + 1)],
                                 reads=kTe, writes=(("VA", uu),), tag="av_out")
        for u in range(4):
            ku = keys("G", 2048, 2080)
            ng = max(1, S // 2048)
            w = min(2048, S)
            for gg in range(ng):
                kg = keys("G", 0, w)
                self.dma(G[0:1, 0:w], d["norms_a"][u, 1, w * gg:w * (gg + 1)],
                         reads=(("norms_a", u, 1),), writes=kg, tag="mk_in")
                self.red(G[0:1, 2048 + gg:2049 + gg], G[0:1, 0:w], ALU.max,
                         reads=kg, writes=ku, tag="mk_red")
            self.red(G[0:1, 2080:2081], G[0:1, 2048:2048 + ng], ALU.max,
                     reads=ku, writes=keys("G", 2080, 2081), tag="mk_red2")
            self.act(G[0:1, 2081:2082], G[0:1, 2080:2081], AF.Sqrt,
                     reads=keys("G", 2080, 2081), writes=keys("G", 2081, 2082),
                     tag="mk_sqrt")
            for r in range(S // 1024):
                km = keys("G", 0, 1024)
                km2 = keys("G", 1024, 2048)
                self.dma(G[0:1, 0:1024],
                         d["norms_a"][u, 0, 1024 * r:1024 * (r + 1)],
                         reads=(("norms_a", u, 0),), writes=km, tag="mr_in")
                self.act(G[0:1, 1024:2048], G[0:1, 0:1024], AF.Sqrt,
                         reads=km, writes=km2, tag="mr_sqrt")
                self.ts(G[0:1, 0:1024], G[0:1, 1024:2048],
                        G[0:1, 2081:2082], ALU.mult,
                        reads=km2 + keys("G", 2081, 2082), writes=km,
                        s2=-1.0, op1=ALU.mult, tag="mr_mul")
                self.cp("vector", W1[0:1, 0:1024], G[0:1, 0:1024],
                        reads=km, writes=keys("W1", 0, 1024), tag="mr_cast")
                self.dma(d["QA"][u, 64, 1024 * r:1024 * (r + 1)],
                         W1[0:1, 0:1024], reads=keys("W1", 0, 1024),
                         writes=(("QA", u),), tag="mr_out")

    def phase_attn_a_attn(self):
        cfg, sb, d = self.cfg, self.sb, self.d
        S, TCH = cfg.S, cfg.TCH
        ar0, ar1, ar2 = sb["ar0"], sb["ar1"], sb["ar2"]
        psA, psB, G, E = sb["psA"], sb["psB"], sb["G"], sb["E"]
        nstile = S // 128
        nchunk = S // TCH
        spt = TCH // 128
        VOFF, POFF, COFF = 0, 4160, 10240
        for u in range(4):
            b, hp = u // 2, u % 2
            kq = keys("ar0", 0, S)
            kk = keys("ar1", 0, S)
            kv = keys("ar2", VOFF, VOFF + 65 * nstile)
            self.dma(ar0[0:65, 0:S], d["QA"][u], reads=(("QA", u),),
                     writes=kq, tag="aa_lq")
            self.dma(ar1[0:65, 0:S], d["KA"][u], reads=(("KA", u),),
                     writes=kk, tag="aa_lk")
            self.dma(ar2[:, VOFF:VOFF + 65 * nstile].rearrange(
                         "p (i c) -> p i c", c=65)[:, :, 0:64],
                     d["VA"][u].rearrange("(i p) c -> p i c", p=128),
                     reads=(("VA", u),), writes=kv, tag="aa_lv")
            self.memset(ar2[:, VOFF:VOFF + 65 * nstile].rearrange(
                            "p (i c) -> p i c", c=65)[:, :, 64:65], 1.0,
                        reads=(), writes=kv, tag="aa_vone", eng="vector")
            psC = sb["psC"]
            psTf = sb["psT"][:].bitcast(F32)
            kout = keys("psB", 0, TCH)

            def epilogue(r):
                kev = keys("E", 0, TCH)
                self.cp("scalar", E[0:64, 0:TCH], psB[0:64, 0:TCH],
                        reads=kout, writes=kev, tag="aa_ev")
                kz = keys("G", 0, TCH)
                self.recip(G[0:1, 0:TCH], psB[64:65, 0:TCH],
                           reads=kout, writes=kz, tag="aa_recz")
                for c_ in range(TCH // 512):
                    self.mm(psB[0:64, 512 * c_:512 * (c_ + 1)],
                            sb["c_o1x64f"][:], G[0:1, 512 * c_:512 * (c_ + 1)],
                            start=True, stop=True,
                            reads=kz + keys("c_o1x64f"), writes=kout,
                            tag="aa_zrep")
                kcs = keys("ar2", COFF, COFF + TCH)
                self.tt(ALU.mult, ar2[0:64, COFF:COFF + TCH],
                        E[0:64, 0:TCH], psB[0:64, 0:TCH],
                        reads=kev + kout, writes=kcs, tag="aa_nrm")
                self.dma(d["CONTRX"][r, 128 * b + 64 * hp:
                                     128 * b + 64 * hp + 64, :],
                         ar2[0:64, COFF:COFF + TCH],
                         reads=kcs, writes=(("CONTRX", r),), tag="aa_co")

            pend_ep = None
            for r in range(nchunk):
                nj = spt * (r + 1)

                def aa_pv(j, nj=nj):
                    po = 1024 * (j % 3)
                    kP = keys("ar2", POFF + po, POFF + po + 1024)
                    for c_ in range(TCH // 512):
                        self.mm(psB[0:65, 512 * c_:512 * (c_ + 1)],
                                ar2[:, VOFF + 65 * j:VOFF + 65 * (j + 1)],
                                ar2[:, POFF + po + 512 * c_:
                                    POFF + po + 512 * (c_ + 1)],
                                start=(j == 0), stop=(j == nj - 1),
                                reads=kP + kv, writes=kout, tag="aa_pv")

                for j in range(nj):
                    po = 1024 * (j % 3)
                    r3 = j % 3
                    if r3 < 2:
                        halves = [(psA[:, 1024 * r3:1024 * r3 + 512],
                                   keys("psA", 1024 * r3, 1024 * r3 + 512)),
                                  (psA[:, 1024 * r3 + 512:1024 * (r3 + 1)],
                                   keys("psA", 1024 * r3 + 512,
                                        1024 * (r3 + 1)))]
                    else:
                        halves = [(psC[:, 0:512], keys("psC", 0, 512)),
                                  (psTf[:, 0:512], keys("psT", 0, 1024))]
                    ksc = halves[0][1] + halves[1][1]
                    for c_ in range(TCH // 512):
                        self.mm(halves[c_][0],
                                ar1[0:65, 128 * j:128 * (j + 1)],
                                ar0[0:65, TCH * r + 512 * c_:
                                    TCH * r + 512 * (c_ + 1)],
                                start=True, stop=True,
                                reads=kq + kk, writes=halves[c_][1],
                                tag="aa_mm1")
                    o = j - spt * r
                    if r3 < 2:
                        if o >= 0:
                            stc = (TCH - 128) - 128 * o
                            self.tt(ALU.add, psA[:, 1024 * r3:1024 * (r3 + 1)],
                                    psA[:, 1024 * r3:1024 * (r3 + 1)],
                                    sb["c_band"][:, stc:stc + TCH],
                                    reads=ksc + keys("c_band"),
                                    writes=ksc, tag="aa_band")
                        kP = keys("ar2", POFF + po, POFF + po + 1024)
                        self.act(ar2[:, POFF + po:POFF + po + 1024],
                                 psA[:, 1024 * r3:1024 * (r3 + 1)], AF.Exp,
                                 reads=ksc, writes=kP, tag="aa_exp")
                    else:
                        for c_ in range(TCH // 512):
                            if o >= 0:
                                stc = (TCH - 128) - 128 * o + 512 * c_
                                self.tt(ALU.add, halves[c_][0], halves[c_][0],
                                        sb["c_band"][:, stc:stc + 512],
                                        reads=halves[c_][1] + keys("c_band"),
                                        writes=halves[c_][1], tag="aa_band")
                            kP = keys("ar2", POFF + po + 512 * c_,
                                      POFF + po + 512 * (c_ + 1))
                            self.act(ar2[:, POFF + po + 512 * c_:
                                         POFF + po + 512 * (c_ + 1)],
                                     halves[c_][0], AF.Exp,
                                     reads=halves[c_][1], writes=kP,
                                     tag="aa_exp")
                    if j == 1 and pend_ep is not None:
                        pend_ep()
                        pend_ep = None
                    if j >= 2:
                        aa_pv(j - 2)
                aa_pv(nj - 2)
                aa_pv(nj - 1)
                pend_ep = (lambda r=r: epilogue(r))
            pend_ep()

    def phase_wo_a2a(self, tag, wname, mode, gbuf):
        cfg, sb, d = self.cfg, self.sb, self.d
        psA, psT, F, W1 = sb["psA"], sb["psT"], sb["F"], sb["W1"]
        ar0, ar1, ar3 = sb["ar0"], sb["ar1"], sb["ar3"]
        CL = cfg.TL
        self.coll("AllGather", ALU.bypass, d["CONTRX"], d[gbuf],
                  reads=tuple(("CONTRX", r) for r in range(cfg.R)),
                  writes=((gbuf, 0),), tag=tag + "_AG")
        rows = d[gbuf].rearrange("s q t -> (s q) t")
        kcy = keys("ar3", 0, 16384)
        for j16 in range(16):
            self.p.op("gpsimd",
                      lambda e, o=ar3[:, 1024 * j16:1024 * (j16 + 1)],
                      i=rows, x=sb["IDXS"][:, j16:j16 + 1]:
                      e.indirect_dma_start(
                          out=o, out_offset=None, in_=i,
                          in_offset=IndirectOffsetOnAxis(x, 0)),
                      ((gbuf, 0),) + keys("IDXS"),
                      keys("ar3", 1024 * j16, 1024 * (j16 + 1)),
                      kind="d", tag=tag + "_gth")
        kw = keys("ar1", 0, 8192)
        self.dma_wfm(ar1[:, 0:8192], self.wbf[wname], 1024,
                     reads=(("wbf_" + wname, 0),), writes=kw, tag=tag + "_wl")
        for b in range(cfg.B):
            for i in range(cfg.pos // 128):
                pp = i % 2
                kps = keys("psA", 1024 * pp, 1024 * (pp + 1))
                for src in range(cfg.R):
                    sl = (src * 2 + b) * 1024
                    for c_ in range(2):
                        self.mm(psA[:, 1024 * pp + 512 * c_:
                                    1024 * pp + 512 * (c_ + 1)],
                                ar3[:, sl + 128 * i:sl + 128 * (i + 1)],
                                ar1[:, 1024 * src + 512 * c_:
                                    1024 * src + 512 * (c_ + 1)],
                                start=(src == 0), stop=(src == cfg.R - 1),
                                reads=kcy + kw, writes=kps, tag=tag + "_mm")
                it = b * (cfg.pos // 128) + i
                if mode == "attna":
                    ka = keys("F", 2048 * pp, 2048 * pp + 1024)
                    kb = keys("F", 2048 * pp + 1024, 2048 * pp + 2048)
                    self.dma(F[:, 2048 * pp + 1024:2048 * pp + 2048],
                             d["xcur"][128 * it:128 * (it + 1), :],
                             reads=keys("xcur"), writes=kb, tag=tag + "_xin")
                    self.tt(ALU.add, F[:, 2048 * pp:2048 * pp + 1024],
                            psA[:, 1024 * pp:1024 * (pp + 1)],
                            F[:, 2048 * pp + 1024:2048 * pp + 2048],
                            reads=kps + kb, writes=ka, tag=tag + "_add")
                    self.dma(d["xcur"][128 * it:128 * (it + 1), :],
                             F[:, 2048 * pp:2048 * pp + 1024],
                             reads=ka, writes=keys("xcur"), tag=tag + "_xout")
                else:
                    kw1 = keys("W1", 1024 * pp, 1024 * (pp + 1))
                    self.cp("vector", W1[:, 1024 * pp:1024 * (pp + 1)],
                            psA[:, 1024 * pp:1024 * (pp + 1)],
                            reads=kps, writes=kw1, tag=tag + "_zc")
                    for j in range(8):
                        kT = keys("psT", 128 * (j % 8), 128 * (j % 8) + 128)
                        self.trans(psT[:, 128 * (j % 8):128 * (j % 8) + 128],
                                   W1[:, 1024 * pp + 128 * j:
                                      1024 * pp + 128 * (j + 1)],
                                   reads=kw1, writes=kT, tag=tag + "_zT")
                        self.cp("vector",
                                ar0[:, CL * j + 128 * it:CL * j + 128 * (it + 1)],
                                psT[:, 128 * (j % 8):128 * (j % 8) + 128],
                                reads=kT,
                                writes=keys("ar0", CL * j + 128 * it,
                                            CL * j + 128 * (it + 1)),
                                tag=tag + "_zTe")

    def phase_ffn(self, inst, stop=None):
        cfg, sb, d = self.cfg, self.sb, self.d
        S, D = cfg.S, cfg.D
        tg = f"f{inst}"
        nch = D // 128
        CL = cfg.TL
        half = CL // 2
        W0, W1, G, F, E = sb["W0"], sb["W1"], sb["G"], sb["F"], sb["E"]
        psA, psB, psT = sb["psA"], sb["psB"], sb["psT"]
        self.phase_ln_ag(extra_rms=False, tag=tg + "ln", do_ag=True, fm_ar="ar0")
        for nm, off in [("wqf", 0), ("wkf", 1024), ("wvf", 2048)]:
            self.dma_wfm(W0[:, off:off + 1024], self.wbf[nm], 128,
                         reads=(("wbf_" + nm, 0),),
                         writes=keys("W0", off, off + 1024), tag=tg + "w")
        for r in range(cfg.R):
            pp = r % 2
            arn = f"ar{pp}"
            arin = sb[arn]
            kin = keys(arn, 0, nch * CL)
            self.dma(arin[:, 0:nch * CL].rearrange("p (c m) -> p c m", m=CL),
                     d["ag_xn_out"][128 * nch * r:128 * nch * (r + 1), :]
                     .rearrange("(c p) m -> p c m", p=128),
                     reads=(("ag_xn_out", 0),), writes=kin, tag=tg + "pin")

            def project(woff):
                kw = keys("W0", woff, woff + 1024)
                kps = keys("psA", 0, CL)
                for j in range(nch):
                    for c_ in range(CL // 512):
                        self.mm(psA[:, 512 * c_:512 * (c_ + 1)],
                                W0[:, woff + 128 * j:woff + 128 * (j + 1)],
                                arin[:, CL * j + 512 * c_:CL * j + 512 * (c_ + 1)],
                                start=(j == 0), stop=(j == nch - 1),
                                reads=kin + kw, writes=kps, tag=tg + "pmm")
                return kps

            def qk_store(kev, qk, dstn):
                for b in range(2):
                    self.dma(d[dstn][b][:, cfg.pos * r:cfg.pos * r + half],
                             W1[:, b * half:(b + 1) * half],
                             reads=kev, writes=((dstn, b),), tag=tg + "qko")
                ksq = keys("ar2", 0, CL)
                self.act(sb["ar2"][:, 0:CL], W1[:, 0:CL], AF.Square,
                         reads=kev, writes=ksq, tag=tg + "nsq")
                kn = keys("psA", 0, CL)
                for c_ in range(CL // 512):
                    self.mm(psA[0:2, 512 * c_:512 * (c_ + 1)], sb["c_hm"][:],
                            sb["ar2"][:, 512 * c_:512 * (c_ + 1)],
                            start=True, stop=True, reads=ksq + keys("c_hm"),
                            writes=kn, tag=tg + "nmm")
                kev2 = keys("F", 2048, 2048 + CL)
                self.cp("vector", F[0:2, 2048:2048 + CL], psA[0:2, 0:CL],
                        reads=kn, writes=kev2, tag=tg + "nev")
                for hp in range(2):
                    for b in range(2):
                        self.dma(d["norms_f"][b, qk, hp,
                                              cfg.pos * r:cfg.pos * r + half],
                                 F[hp:hp + 1, 2048 + b * half:2048 + (b + 1) * half],
                                 reads=kev2, writes=(("norms_f", qk),),
                                 tag=tg + "no")

            kq = project(0)
            kevq = keys("W1", 0, CL)
            self.act(W1[:, 0:CL], psA[:, 0:CL], AF.Copy, scale=1.0 / 32.0,
                     reads=kq, writes=kevq, tag=tg + "qev")
            qk_store(kevq, 0, "QF")
            kk = project(1024)
            kevk = keys("W1", 0, CL)
            self.cp("scalar", W1[:, 0:CL], psA[:, 0:CL], reads=kk,
                    writes=kevk, tag=tg + "kev")
            qk_store(kevk, 1, "KF")
            kv = project(2048)
            kvs = keys("ar2", 4096, 4096 + CL)
            self.cp("scalar", sb["ar2"][:, 4096:4096 + CL], psA[:, 0:CL],
                    reads=kv, writes=kvs, tag=tg + "vev")
            ntile = CL // 128
            for g in range(ceil_div(ntile, 8)):
                kT = keys("psT", 0, 1024)
                for tl in range(min(8, ntile - 8 * g)):
                    i = 8 * g + tl
                    self.trans(psT[:, 128 * tl:128 * (tl + 1)],
                               sb["ar2"][:, 4096 + 128 * i:4096 + 128 * (i + 1)],
                               reads=kvs, writes=kT, tag=tg + "vT")
                eb = 8192 + 1024 * (g % 2)
                kTe = keys("ar2", eb, eb + 1024)
                self.cp("vector", sb["ar2"][:, eb:eb + 1024], psT[:, 0:1024],
                        reads=kT, writes=kTe, tag=tg + "vTe")
                for tl in range(min(8, ntile - 8 * g)):
                    i = 8 * g + tl
                    b = i // (ntile // 2)
                    row0 = cfg.pos * r + 128 * (i % (ntile // 2))
                    self.dma(d["VF"][b, row0:row0 + 128, :],
                             sb["ar2"][:, eb + 128 * tl:eb + 128 * (tl + 1)],
                             reads=kTe, writes=(("VF", b),), tag=tg + "vo")
        if stop == "proj":
            return
        for b in range(2):
            for hp in range(2):
                ku = keys("G", 2080, 2082)
                ng = max(1, S // 2048)
                w = min(2048, S)
                for gg in range(ng):
                    kg = keys("G", 0, w)
                    self.dma(G[0:1, 0:w],
                             d["norms_f"][b, 0, hp, w * gg:w * (gg + 1)],
                             reads=(("norms_f", 0),), writes=kg, tag=tg + "mqi")
                    self.red(G[0:1, 2048 + gg:2049 + gg], G[0:1, 0:w], ALU.max,
                             reads=kg, writes=keys("G", 2048, 2048 + ng),
                             tag=tg + "mqr")
                self.red(G[0:1, 2080:2081], G[0:1, 2048:2048 + ng], ALU.max,
                         reads=keys("G", 2048, 2048 + ng), writes=ku,
                         tag=tg + "mqr2")
                self.act(G[0:1, 2081:2082], G[0:1, 2080:2081], AF.Sqrt,
                         reads=ku, writes=ku, tag=tg + "mqs")
                for r in range(S // 1024):
                    km = keys("G", 0, 1024)
                    km2 = keys("G", 1024, 2048)
                    self.dma(G[0:1, 0:1024],
                             d["norms_f"][b, 1, hp, 1024 * r:1024 * (r + 1)],
                             reads=(("norms_f", 1),), writes=km, tag=tg + "mki")
                    self.act(G[0:1, 1024:2048], G[0:1, 0:1024], AF.Sqrt,
                             reads=km, writes=km2, tag=tg + "mks")
                    self.ts(G[0:1, 0:1024], G[0:1, 1024:2048],
                            G[0:1, 2081:2082], ALU.mult, reads=km2 + ku,
                            writes=km, s2=-1.0, op1=ALU.mult, tag=tg + "mkm")
                    kmb = keys("W1", 0, 1024)
                    self.cp("vector", W1[0:1, 0:1024], G[0:1, 0:1024],
                            reads=km, writes=kmb, tag=tg + "mkc")
                    self.dma(d["negMk"][b, hp, 1024 * r:1024 * (r + 1)],
                             W1[0:1, 0:1024], reads=kmb,
                             writes=(("negMk", b, hp),), tag=tg + "mko")
        if stop == "negmk":
            return
        ar0, ar1, ar2, ar3 = sb["ar0"], sb["ar1"], sb["ar2"], sb["ar3"]
        nkt = S // 128
        psC = sb["psC"]
        psTf = sb["psT"][:].bitcast(F32)
        PREG = {(0, 0): ("ar0", 0), (1, 0): ("ar0", 8192),
                (2, 0): ("ar3", 0), (0, 1): ("ar1", 0),
                (1, 1): ("ar1", 8192), (2, 1): ("ar3", 8192)}
        for b in range(cfg.B):
            kQ = keys("ar2", 0, S)
            self.dma(ar2[:, 0:S], d["QF"][b], reads=(("QF", b),), writes=kQ,
                     tag=tg + "alq")
            kM = keys("G", 1024, 1024 + 2 * nkt)
            kMs = keys("W1", 1536, 1536 + 2 * nkt)
            for hp in range(2):
                self.dma(W1[:, 1536 + nkt * hp:1536 + nkt * (hp + 1)],
                         d["negMk"][b, hp].rearrange("(a q) -> a q", q=128),
                         reads=(("negMk", b, hp),), writes=kMs,
                         tag=tg + "aM", transpose=True)
            self.cp("vector", G[:, 1024:1024 + 2 * nkt],
                    W1[:, 1536:1536 + 2 * nkt], reads=kMs, writes=kM,
                    tag=tg + "aMc")
            kacc = keys("E", 0, S)
            self.memset(E[:, 0:S], 0.0, reads=(), writes=kacc, tag=tg + "az")

            def scores_block(j):
                pp = j % 2
                kK = keys("W1", 128 * pp, 128 * (pp + 1))
                self.dma(W1[:, 128 * pp:128 * (pp + 1)],
                         d["KF"][b][:, 128 * j:128 * (j + 1)],
                         reads=(("KF", b),), writes=kK, tag=tg + "aki")
                kVt = keys("W1", 1664 + 128 * pp, 1792 + 128 * pp)
                self.dma(W1[:, 1664 + 128 * pp:1792 + 128 * pp],
                         d["VF"][b][128 * j:128 * (j + 1), :],
                         reads=(("VF", b),), writes=kVt, tag=tg + "avt")
                pn0, pb0 = PREG[(j % 3, 0)]
                pn1, pb1 = PREG[(j % 3, 1)]
                for s in range(16):
                    hp = s % 2
                    q0 = 1024 * (s // 2)
                    r3 = s % 3
                    Pn, PB = (pn0, pb0) if hp == 0 else (pn1, pb1)
                    Par = sb[Pn]
                    if r3 < 2:
                        rps, ra = psA, 1024 * r3
                        ksa = keys("psA", ra, ra + 1024)
                    else:
                        rps, ra = psB, 0
                        ksa = keys("psB", 0, 1024)
                    for c_ in range(2):
                        self.mm(rps[:, ra + 512 * c_:ra + 512 * (c_ + 1)],
                                W1[64 * hp:64 * (hp + 1),
                                   128 * pp:128 * pp + 128],
                                ar2[64 * hp:64 * (hp + 1),
                                    q0 + 512 * c_:q0 + 512 * (c_ + 1)],
                                start=True, stop=True,
                                reads=kK + kQ, writes=ksa, tag=tg + "amm")
                    self.act(Par[:, PB + q0:PB + q0 + 1024],
                             rps[:, ra:ra + 1024], AF.Exp,
                             bias=G[:, 1024 + nkt * hp + j:
                                    1025 + nkt * hp + j],
                             reads=ksa + kM,
                             writes=keys(Pn, PB + q0, PB + q0 + 1024),
                             tag=tg + "aexp")
                vs = 256 + 128 * (j % 3)
                for hp in range(2):
                    Pn, PB = (pn0, pb0) if hp == 0 else (pn1, pb1)
                    Par = sb[Pn]
                    kP = keys(Pn, PB, PB + 8192)
                    kzz = keys("G", 2136 + 2 * hp, 2138 + 2 * hp)
                    self.red(G[:, 2136 + 2 * hp:2137 + 2 * hp],
                             Par[:, PB:PB + 8192], ALU.add,
                             reads=kP, writes=kzz, tag=tg + "azs")
                    self.recip(G[:, 2137 + 2 * hp:2138 + 2 * hp],
                               G[:, 2136 + 2 * hp:2137 + 2 * hp],
                               reads=kzz, writes=kzz, tag=tg + "azr")
                    self.ts(W1[:, vs + 64 * hp:vs + 64 * (hp + 1)],
                            W1[:, 1664 + 128 * pp + 64 * hp:
                               1664 + 128 * pp + 64 * (hp + 1)],
                            G[:, 2137 + 2 * hp:2138 + 2 * hp], ALU.mult,
                            reads=kVt + kzz,
                            writes=keys("W1", vs + 64 * hp, vs + 64 * (hp + 1)),
                            tag=tg + "avv")

            def pv_block2(ja, jb):
                for s in range(16):
                    q0 = 512 * s
                    if s % 2 == 0:
                        ps, kps = psC, keys("psC", 0, 512)
                    else:
                        ps, kps = psTf, keys("psT", 0, 1024)
                    for i_, j in enumerate((ja, jb)):
                        vs = 256 + 128 * (j % 3)
                        for hp in range(2):
                            Pn, PB = PREG[(j % 3, hp)]
                            Par = sb[Pn]
                            self.mm(ps[64 * hp:64 * (hp + 1), 0:512],
                                    W1[:, vs + 64 * hp:vs + 64 * (hp + 1)],
                                    Par[:, PB + q0:PB + q0 + 512],
                                    start=(i_ == 0), stop=(i_ == 1),
                                    reads=keys(Pn, PB + q0, PB + q0 + 512) +
                                    keys("W1", vs + 64 * hp,
                                         vs + 64 * (hp + 1)),
                                    writes=kps, tag=tg + "apv")
                    ke = keys("E", q0, q0 + 512)
                    self.tt(ALU.add, E[:, q0:q0 + 512], E[:, q0:q0 + 512],
                            ps[:, 0:512], reads=ke + kps, writes=ke,
                            tag=tg + "aac")

            pend = None
            for j in range(nkt):
                scores_block(j)
                if pend is not None:
                    pv_block2(*pend)
                    pend = None
                if j % 2 == 1:
                    pend = (j - 1, j)
            if pend is not None:
                pv_block2(*pend)
            for g in range(S // 2048):
                kcs = keys("ar2", 8192 + 2048 * (g % 2), 8192 + 2048 * (g % 2) + 2048)
                self.cp("vector", ar2[:, 8192 + 2048 * (g % 2):
                                      8192 + 2048 * (g % 2) + 2048],
                        E[:, 2048 * g:2048 * (g + 1)],
                        reads=kacc, writes=kcs, tag=tg + "acast")
                self.dma(d["CONTRX"][2 * g:2 * g + 2, 128 * b:128 * (b + 1), :]
                         .rearrange("r p t -> p r t"),
                         ar2[:, 8192 + 2048 * (g % 2):
                             8192 + 2048 * (g % 2) + 2048]
                         .rearrange("p (r t) -> p r t", r=2),
                         reads=kcs, writes=(("CONTRX", 2 * g), ("CONTRX", 2 * g + 1)),
                         tag=tg + "aco")
        if stop == "attn":
            return
        self.phase_wo_a2a(tg + "wo", "wof", "ffn", f"cxg{1 + inst}")
        if stop == "rs":
            return
        self.dma_wfm(sb["ar1"][:, 0:8 * 1024], self.wbf["wff"], 1024,
                     reads=(("wbf_wff", 0),), writes=keys("ar1", 0, 8192),
                     tag=tg + "wffl")
        kzfm = keys("ar0", 0, 8 * CL)
        kwf = keys("ar1", 0, 8192)
        if stop == "zfm":
            return
        nth = CL // 1024
        for th in range(nth):
            tc = 1024 * th
            for i in range(8):
                kps = keys("psA", 0, 1024)
                for kc in range(8):
                    for c_ in range(2):
                        self.mm(psA[:, 512 * c_:512 * (c_ + 1)],
                                sb["ar1"][:, 1024 * kc + 128 * i:
                                          1024 * kc + 128 * (i + 1)],
                                sb["ar0"][:, CL * kc + tc + 512 * c_:
                                          CL * kc + tc + 512 * (c_ + 1)],
                                start=(kc == 0), stop=(kc == 7),
                                reads=kwf + kzfm, writes=kps, tag=tg + "fmm")
                kw = keys("ar2", 2048 * i + 1024 * (th % 2),
                          2048 * i + 1024 * (th % 2) + 1024)
                if stop != "zn3":
                    self.cp("vector",
                            sb["ar2"][:, 2048 * i + 1024 * (th % 2):
                                      2048 * i + 1024 * (th % 2) + 1024],
                            psA[:, 0:1024], reads=kps, writes=kw, tag=tg + "fev")
                ksq = keys("F", 0, 1024)
                if stop not in ("zn3", "zn4"):
                    wsl = sb["ar2"][:, 2048 * i + 1024 * (th % 2):
                                    2048 * i + 1024 * (th % 2) + 1024]
                    self.tt(ALU.mult, F[:, 0:1024], wsl, wsl,
                            reads=kw, writes=ksq, tag=tg + "fsq")
                if stop not in ("zn1", "zn3", "zn4"):
                    for c_ in range(2):
                        self.mm(psB[0:1, 512 * c_:512 * (c_ + 1)],
                                sb["c_ocolf"][:], F[:, 512 * c_:512 * (c_ + 1)],
                                start=(i == 0) or stop == "zn2",
                                stop=(i == 7) or stop == "zn2",
                                reads=ksq + keys("c_ocolf"),
                                writes=keys("psB", 0, 1024), tag=tg + "fns")
            if stop in ("zmm", "zn1", "zn2", "zn3", "zn4"):
                continue
            krs0 = keys("G", 0, 1024)
            self.cp("vector", G[0:1, 0:1024], psB[0:1, 0:1024],
                    reads=keys("psB", 0, 1024), writes=krs0, tag=tg + "frse")
            krs1 = keys("G", 1024, 2048)
            self.act(G[0:1, 1024:2048], G[0:1, 0:1024], AF.Sqrt,
                     reads=krs0, writes=krs1, tag=tg + "frs")
            self.ts(G[0:1, 0:1024], G[0:1, 1024:2048], 1e-12, ALU.max,
                    reads=krs1, writes=krs0, tag=tg + "fmx")
            krr = keys("G", 1024, 2048)
            self.recip(G[0:1, 1024:2048], G[0:1, 0:1024], reads=krs0,
                       writes=krr, tag=tg + "frr")
            kbr = keys("psB", 0, 1024)
            for c_ in range(2):
                self.mm(psB[:, 512 * c_:512 * (c_ + 1)], sb["c_o1x128f"][:],
                        G[0:1, 1024 + 512 * c_:1024 + 512 * (c_ + 1)],
                        start=True, stop=True,
                        reads=krr + keys("c_o1x128f"), writes=kbr,
                        tag=tg + "fbc")
            if stop == "zbc":
                continue
            for i in range(8):
                co = 2048 * i + 1024 * (th % 2)
                kw = keys("ar2", co, co + 1024)
                kg1 = keys("ar3", 1024 * i, 1024 * (i + 1))
                self.tt(ALU.mult, sb["ar3"][:, 1024 * i:1024 * (i + 1)],
                        sb["ar2"][:, co:co + 1024], psB[:, 0:1024],
                        reads=kw + kbr, writes=kg1, tag=tg + "fnm")
                self.act(sb["ar2"][:, co:co + 1024],
                         sb["ar3"][:, 1024 * i:1024 * (i + 1)],
                         AF.Gelu, reads=kg1, writes=kw, tag=tg + "fgel")
            if stop == "zgel":
                continue
            for tt_ in range(8):
                kT = keys("psT", 0, 1024)
                for i in range(8):
                    co = 2048 * i + 1024 * (th % 2)
                    self.trans(psT[:, 128 * i:128 * (i + 1)],
                               sb["ar2"][:, co + 128 * tt_:co + 128 * (tt_ + 1)],
                               reads=keys("ar2", co, co + 1024),
                               writes=kT, tag=tg + "fT")
                it = 8 * th + tt_
                pp = tt_ % 2
                kxf = keys("F", 2048 * pp, 2048 * pp + 1024)
                self.dma(F[:, 2048 * pp:2048 * pp + 1024],
                         d["xcur"][128 * it:128 * (it + 1), :],
                         reads=keys("xcur"), writes=kxf, tag=tg + "fxi")
                self.tt(ALU.add, F[:, 2048 * pp:2048 * pp + 1024],
                        F[:, 2048 * pp:2048 * pp + 1024], psT[:, 0:1024],
                        reads=kxf + kT, writes=kxf, tag=tg + "fadd")
                self.dma(d["xcur"][128 * it:128 * (it + 1), :],
                         F[:, 2048 * pp:2048 * pp + 1024],
                         reads=kxf, writes=keys("xcur"), tag=tg + "fxo")

    def phase_mamba(self):
        cfg, sb, d = self.cfg, self.sb, self.d
        S, DM, T = cfg.S, cfg.DM, cfg.T
        MB = cfg.MB
        W0, W1, G, F, E = sb["W0"], sb["W1"], sb["G"], sb["F"], sb["E"]
        psA, psB, psT = sb["psA"], sb["psB"], sb["psT"]
        ar0, ar1, ar2, ar3 = sb["ar0"], sb["ar1"], sb["ar2"], sb["ar3"]
        NT = cfg.NT_LOC
        CL = cfg.TL
        nct = S // 2048
        ngl = T // 2048

        def pcol(g):
            b = g // nct
            return cfg.mcol0(b) + 2048 * (g % nct)

        def scol(g):
            return 2048 * g

        self.phase_ln_ag(extra_rms=False, tag="mln", do_ag=False, fm_ar="ar0")
        self.dma_wfm(ar1[:, 0:8 * 512], self.wbf["lin1_w"], 512,
                     reads=(("wbf_lin1_w", 0),), writes=keys("ar1", 0, 4096),
                     tag="mb_l1w")
        kb1 = keys("G", 0, 512)
        self.dma(G[0:1, 0:DM], self.ein["lin1_b"], reads=(), writes=kb1,
                 tag="mb_l1b")
        kb1b = keys("W1", 1536, 2048)
        self.cp("vector", W1[0:1, 1536:1536 + DM], G[0:1, 0:DM],
                reads=kb1, writes=kb1b, tag="mb_l1bc")
        for i in range(NT):
            kps = keys("psB", 0, 512)
            for k in range(8):
                self.mm(psB[:, 0:512],
                        ar0[:, CL * k + 128 * i:CL * k + 128 * (i + 1)],
                        ar1[:, 512 * k:512 * (k + 1)], start=(k == 0), stop=False,
                        reads=keys("ar0", CL * k + 128 * i,
                                   CL * k + 128 * (i + 1)) +
                        keys("ar1", 512 * k, 512 * (k + 1)),
                        writes=kps, tag="mb_l1mm")
            self.mm(psB[:, 0:512], sb["c_o1x128b"][:], W1[0:1, 1536:1536 + DM],
                    start=False, stop=True,
                    reads=kb1b + keys("c_o1x128b"), writes=kps, tag="mb_l1bias")
            ke = keys("E", 512 * i, 512 * (i + 1))
            self.cp("scalar" if i % 2 else "vector", E[:, 512 * i:512 * (i + 1)],
                    psB[:, 0:512], reads=kps, writes=ke, tag="mb_l1ev")

        for l in range(cfg.NL):
            lt = f"mb{l}"
            kcw = keys("G", 2144, 2176)
            self.dma(G[:, 2144:2144 + cfg.CK], self.ein["convw"][l],
                     reads=(), writes=kcw, tag=lt + "cw")
            self.dma(G[:, 2148:2149], as_col(self.ein["convb"][l, :]),
                     reads=(), writes=kcw, tag=lt + "cb")
            self.dma(G[:, 2149:2150], as_col(self.ein["dtb"][l, :]),
                     reads=(), writes=kcw, tag=lt + "db")
            self.dma(G[:, 2150:2151], as_col(self.ein["dpv"][l, :]),
                     reads=(), writes=kcw, tag=lt + "dp")
            kA = keys("G", 2176, 2208)
            self.dma(G[:, 2176:2176 + cfg.DS], self.ein["alog"][l],
                     reads=(), writes=kA, tag=lt + "al")
            self.act(G[:, 2176:2176 + cfg.DS], G[:, 2176:2176 + cfg.DS],
                     AF.Exp, reads=kA, writes=kA, tag=lt + "aexp")
            self.ts(G[:, 2176:2176 + cfg.DS], G[:, 2176:2176 + cfg.DS],
                    -1.0, ALU.mult, reads=kA, writes=kA, tag=lt + "aneg")
            for i in range(NT):
                pp = i % 2
                gs = pp * 16
                kst = keys("G", gs, gs + 16)
                ke = keys("E", 512 * i, 512 * (i + 1))
                ksq = keys("F", 2048 * pp, 2048 * pp + 512)
                self.act(F[:, 2048 * pp:2048 * pp + 512],
                         E[:, 512 * i:512 * (i + 1)], AF.Square,
                         reads=ke, writes=ksq, tag=lt + "rsq")
                self.red(G[:, gs:gs + 1], F[:, 2048 * pp:2048 * pp + 512],
                         ALU.add, reads=ksq, writes=kst, tag=lt + "rs")
                self.act(G[:, gs + 1:gs + 2], G[:, gs:gs + 1], AF.Sqrt,
                         reads=kst + keys("c_cb"), writes=kst, scale=1.0 / DM,
                         bias=self.sb["c_cb"][:, 0:1], tag=lt + "rsd")
                self.recip(G[:, gs + 2:gs + 3], G[:, gs + 1:gs + 2],
                           reads=kst, writes=kst, tag=lt + "rrc")
                khr = keys("F", 2048 * pp + 512, 2048 * pp + 1024)
                self.ts(F[:, 2048 * pp + 512:2048 * pp + 1024],
                        E[:, 512 * i:512 * (i + 1)], G[:, gs + 2:gs + 3],
                        ALU.mult, reads=ke + kst, writes=khr, tag=lt + "rmul")
                kbf = keys("W1", 512 * pp, 512 * (pp + 1))
                self.cp("vector", W1[:, 512 * pp:512 * (pp + 1)],
                        F[:, 2048 * pp + 512:2048 * pp + 1024],
                        reads=khr, writes=kbf, tag=lt + "rcast")
                for j in range(DM // 128):
                    kT = keys("psT", 128 * j, 128 * (j + 1))
                    self.trans(psT[:, 128 * j:128 * (j + 1)],
                               W1[:, 512 * pp + 128 * j:512 * pp + 128 * (j + 1)],
                               reads=kbf, writes=kT, tag=lt + "rT")
                    self.cp("vector",
                            ar0[:, CL * j + 128 * i:CL * j + 128 * (i + 1)],
                            psT[:, 128 * j:128 * (j + 1)], reads=kT,
                            writes=keys("ar0", CL * j + 128 * i,
                                        CL * j + 128 * (i + 1)),
                            tag=lt + "rTe")
            for j in range(DM // 128):
                self.dma(d["ag_hr_in"][128 * j:128 * (j + 1), :],
                         ar0[:, CL * j:CL * j + CL],
                         reads=keys("ar0", CL * j, CL * j + CL),
                         writes=(("ag_hr_in", j),), tag=lt + "agi")
            self.coll("AllGather", ALU.bypass, d["ag_hr_in"], d["ag_hr_out"],
                      reads=tuple(("ag_hr_in", j) for j in range(DM // 128)),
                      writes=(("ag_hr_out", 0),), tag=lt + "AG")
            self.dma_wfm(W0[:, 0:1024], self.wbf[f"inw{l}"], 256,
                         reads=((f"wbf_inw{l}", 0),), writes=keys("W0", 0, 1024),
                         tag=lt + "iw")
            for r in range(cfg.R):
                kin = keys("ar0", 4 * CL, 8 * CL)
                self.dma(ar0[:, 4 * CL:8 * CL].rearrange(
                             "p (c m) -> p c m", m=CL),
                         d["ag_hr_out"][512 * r:512 * (r + 1), :]
                         .rearrange("(c p) m -> p c m", p=128),
                         reads=(("ag_hr_out", 0),), writes=kin, tag=lt + "xzi")
                for m2 in range(2):
                    kps = keys("psA", 0, 2048)
                    for k in range(4):
                        for c_ in range(CL // 512):
                            self.mm(psA[:, 512 * c_:512 * (c_ + 1)],
                                    W0[:, 256 * k + 128 * m2:
                                       256 * k + 128 * (m2 + 1)],
                                    ar0[:, 4 * CL + CL * k + 512 * c_:
                                        4 * CL + CL * k + 512 * (c_ + 1)],
                                    start=(k == 0), stop=(k == 3),
                                    reads=kin + keys("W0", 0, 1024),
                                    writes=kps, tag=lt + "xzmm")
                    if m2 == 0:
                        for b in range(2):
                            cc = cfg.mcol0(b) + cfg.pos * r
                            self.cp("vector", ar2[:, cc:cc + cfg.pos],
                                    psA[:, b * cfg.pos:(b + 1) * cfg.pos],
                                    reads=kps, writes=keys("ar2", cc, cc + cfg.pos),
                                    tag=lt + "xpev")
                    else:
                        kzs = keys("ar1", 4096, 4096 + 2048)
                        self.cp("vector", ar1[:, 4096:4096 + 2048],
                                psA[:, 0:2048], reads=kps, writes=kzs,
                                tag=lt + "zev")
                        for b in range(2):
                            self.dma(d["ZSP"][:, b * S + cfg.pos * r:
                                              b * S + cfg.pos * r + cfg.pos],
                                     ar1[:, 4096 + b * cfg.pos:
                                         4096 + (b + 1) * cfg.pos],
                                     reads=kzs, writes=(("ZSP", 0),),
                                     tag=lt + "zsp")
            for b in range(2):
                self.memset(ar2[:, b * MB:b * MB + 3], 0.0, reads=(),
                            writes=keys("ar2", b * MB, b * MB + 3),
                            tag=lt + "pad", eng="vector")
            for b in range(2):
                c0 = cfg.mcol0(b)
                for ch in range(nct):
                    kacc = keys("F", 0, 2048)
                    src0 = c0 + 2048 * ch - 3
                    self.ts(F[:, 0:2048], ar2[:, src0:src0 + 2048],
                            G[:, 2144:2145], ALU.mult,
                            reads=keys("ar2", src0, src0 + 2048) + kcw,
                            writes=kacc, tag=lt + "cv0")
                    for k in range(1, cfg.CK):
                        self.stt(F[:, 0:2048], ar2[:, src0 + k:src0 + k + 2048],
                                 G[:, 2144 + k:2145 + k], F[:, 0:2048],
                                 ALU.mult, ALU.add,
                                 reads=keys("ar2", src0 + k, src0 + k + 2048) +
                                 kcw + kacc, writes=kacc, tag=lt + "cvk")
                    ku = keys("ar1", 6144, 8192)
                    self.act(ar1[:, 6144:6144 + 2048], F[:, 0:2048], AF.Silu,
                             bias=G[:, 2148:2149], reads=kacc + kcw, writes=ku,
                             tag=lt + "cvs")
                    g = b * nct + ch
                    self.dma(d["USP"][:, scol(g):scol(g) + 2048],
                             ar1[:, 6144:6144 + 2048], reads=ku,
                             writes=(("USP", g),), tag=lt + "usp")
            self.dma(W0[:, 1024:1088], self.wbf[f"xprojw{l}"],
                     reads=((f"wbf_xprojw{l}", 0),),
                     writes=keys("W0", 1024, 1088), tag=lt + "xpw")
            for g in range(ngl):
                kuc = keys("ar1", 8192, 10240)
                self.dma(ar1[:, 8192:8192 + 2048],
                         d["USP"][:, scol(g):scol(g) + 2048],
                         reads=(("USP", g),), writes=kuc, tag=lt + "xui")
                for c_ in range(4):
                    kps = keys("psB", 512 * (c_ % 2), 512 * (c_ % 2) + 512)
                    self.mm(psB[0:64, 512 * (c_ % 2):512 * (c_ % 2) + 512],
                            W0[:, 1024:1088],
                            ar1[:, 8192 + 512 * c_:8192 + 512 * (c_ + 1)],
                            start=True, stop=True,
                            reads=kuc + keys("W0", 1024, 1088), writes=kps,
                            tag=lt + "xpm")
                    kf = keys("F", 2048 + 512 * (c_ % 2),
                              2048 + 512 * (c_ % 2) + 512)
                    self.cp("vector", F[0:64, 2048 + 512 * (c_ % 2):
                                        2048 + 512 * (c_ % 2) + 512],
                            psB[0:64, 512 * (c_ % 2):512 * (c_ % 2) + 512],
                            reads=kps, writes=kf, tag=lt + "xpe")
                    self.dma(d["dblpart"][0:64, 2048 * g + 512 * c_:
                                          2048 * g + 512 * (c_ + 1)],
                             F[0:64, 2048 + 512 * (c_ % 2):
                               2048 + 512 * (c_ % 2) + 512],
                             reads=kf, writes=(("dblpart", g),), tag=lt + "xpo")
            self.coll("AllReduce", ALU.add, d["dblpart"], d["dblfull"],
                      reads=tuple(("dblpart", g) for g in range(ngl)),
                      writes=(("dblfull", 0),), tag=lt + "AR")
            self.dma(W0[0:cfg.DTR, 1088:1216], self.wbf[f"dtw{l}"],
                     reads=((f"wbf_dtw{l}", 0),), writes=keys("W0", 1088, 1216),
                     tag=lt + "dtw")
            for g in range(ngl):
                pc = pcol(g)
                kdg = keys("G", 0, 2048)
                self.dma(G[0:cfg.DTR, 0:2048],
                         d["dblfull"][0:cfg.DTR, 2048 * g:2048 * (g + 1)],
                         reads=(("dblfull", 0),), writes=kdg, tag=lt + "dti")
                kdb = keys("W1", 0, 2048)
                self.cp("vector", W1[0:cfg.DTR, 0:2048], G[0:cfg.DTR, 0:2048],
                        reads=kdg, writes=kdb, tag=lt + "dtc")
                kps = keys("psA", 0, 2048)
                for c_ in range(4):
                    self.mm(psA[:, 512 * c_:512 * (c_ + 1)],
                            W0[0:cfg.DTR, 1088:1216],
                            W1[0:cfg.DTR, 512 * c_:512 * (c_ + 1)],
                            start=True, stop=True,
                            reads=kdb + keys("W0", 1088, 1216), writes=kps,
                            tag=lt + "dtm")
                kx = keys("F", 0, 2048)
                self.ts(F[:, 0:2048], psA[:, 0:2048], G[:, 2149:2150],
                        ALU.add, reads=kps + kcw, writes=kx, tag=lt + "dt0")
                ka = keys("F", 2048, 4096)
                self.act(F[:, 2048:4096], F[:, 0:2048], AF.Abs,
                         reads=kx, writes=ka, tag=lt + "dt1")
                kge = keys("G", 0, 2048)
                self.act(G[:, 0:2048], F[:, 2048:4096], AF.Exp, scale=-1.0,
                         reads=ka, writes=kge, tag=lt + "dt2")
                self.act(F[:, 2048:4096], G[:, 0:2048], AF.Ln,
                         bias=self.sb["c_cb"][:, 2:3],
                         reads=kge + keys("c_cb"), writes=ka, tag=lt + "dt3")
                self.stt(ar3[:, pc:pc + 2048], F[:, 0:2048],
                         self.sb["c_cb"][:, 3:4],
                         F[:, 2048:4096], ALU.max, ALU.add,
                         reads=kx + ka + keys("c_cb"),
                         writes=keys("ar3", pc, pc + 2048), tag=lt + "dt4")
            for g in range(ngl):
                pc = pcol(g)
                kuc = keys("ar2", S, S + 2048)
                self.dma(ar2[:, S:S + 2048],
                         d["USP"][:, scol(g):scol(g) + 2048],
                         reads=(("USP", g),), writes=kuc, tag=lt + "dui")
                self.tt(ALU.mult, ar1[:, pc:pc + 2048], ar3[:, pc:pc + 2048],
                        ar2[:, S:S + 2048],
                        reads=keys("ar3", pc, pc + 2048) + kuc,
                        writes=keys("ar1", pc, pc + 2048), tag=lt + "du")
            for b in range(2):
                c0 = cfg.mcol0(b)
                for n in range(cfg.DS):
                    kan = keys("ar2", 0, S)
                    self.act(ar2[:, 0:S], ar3[:, c0:c0 + S], AF.Exp,
                             scale=G[:, 2176 + n:2177 + n],
                             reads=keys("ar3", c0, c0 + S) + kA,
                             writes=kan, tag=lt + "sa")
                    kcy = keys("G", 2208, 2240)
                    self.memset(G[:, 2208:2209], 0.0, reads=(), writes=kcy,
                                tag=lt + "scy", eng="vector")
                    for c in range(nct):
                        kbr = keys("G", 0, 2048)
                        self.dma(G[0:1, 0:2048],
                                 as_row(d["dblfull"][cfg.DTR + n,
                                                     b * S + 2048 * c:
                                                     b * S + 2048 * (c + 1)]),
                                 reads=(("dblfull", 0),), writes=kbr,
                                 tag=lt + "sB")
                        kbc = keys("W0", 1280, 3328)
                        self.cp("vector", W0[0:1, 1280:3328], G[0:1, 0:2048],
                                reads=kbr, writes=kbc, tag=lt + "sBc")
                        kbrp = keys("psA", 0, 2048)
                        for c_ in range(4):
                            self.mm(psA[:, 512 * c_:512 * (c_ + 1)],
                                    sb["c_o1x128b"][:],
                                    W0[0:1, 1280 + 512 * c_:
                                       1280 + 512 * (c_ + 1)],
                                    start=True, stop=True,
                                    reads=kbc + keys("c_o1x128b"),
                                    writes=kbrp, tag=lt + "sBr")
                        kw = keys("ar2", S, S + 2048)
                        self.tt(ALU.mult, ar2[:, S:S + 2048],
                                ar1[:, c0 + 2048 * c:c0 + 2048 * (c + 1)],
                                psA[:, 0:2048],
                                reads=keys("ar1", c0 + 2048 * c,
                                           c0 + 2048 * (c + 1)) + kbrp,
                                writes=kw, tag=lt + "sw")
                        kh = keys("ar2", S + 2048, S + 4096)
                        self.scan_op(ar2[:, S + 2048:S + 4096],
                                     ar2[:, 2048 * c:2048 * (c + 1)],
                                     ar2[:, S:S + 2048], G[:, 2208:2209],
                                     reads=kan + kw + kcy, writes=kh,
                                     tag=lt + "scan")
                        self.cp("vector", G[:, 2208:2209],
                                ar2[:, S + 4095:S + 4096], reads=kh,
                                writes=kcy, tag=lt + "scar")
                        kcr = keys("F", 0, 2048)
                        self.dma(F[0:1, 0:2048],
                                 as_row(d["dblfull"][cfg.DTR + cfg.DS + n,
                                                     b * S + 2048 * c:
                                                     b * S + 2048 * (c + 1)]),
                                 reads=(("dblfull", 0),), writes=kcr,
                                 tag=lt + "sC")
                        kcc = keys("W1", 0, 2048)
                        self.cp("vector", W1[0:1, 0:2048], F[0:1, 0:2048],
                                reads=kcr, writes=kcc, tag=lt + "sCc")
                        kcrp = keys("psA", 0, 2048)
                        for c_ in range(4):
                            self.mm(psA[:, 512 * c_:512 * (c_ + 1)],
                                    sb["c_o1x128b"][:],
                                    W1[0:1, 512 * c_:512 * (c_ + 1)],
                                    start=True, stop=True,
                                    reads=kcc + keys("c_o1x128b"),
                                    writes=kcrp, tag=lt + "sCr")
                        kpr = keys("ar2", S + 4096, S + 6144)
                        self.tt(ALU.mult, ar2[:, S + 4096:S + 6144],
                                ar2[:, S + 2048:S + 4096], psA[:, 0:2048],
                                reads=kh + kcrp, writes=kpr, tag=lt + "sp")
                        ky = keys("ar0", c0 + 2048 * c, c0 + 2048 * (c + 1))
                        if n == 0:
                            self.cp("vector",
                                    ar0[:, c0 + 2048 * c:c0 + 2048 * (c + 1)],
                                    ar2[:, S + 4096:S + 6144], reads=kpr,
                                    writes=ky, tag=lt + "sy0")
                        else:
                            self.tt(ALU.add,
                                    ar0[:, c0 + 2048 * c:c0 + 2048 * (c + 1)],
                                    ar0[:, c0 + 2048 * c:c0 + 2048 * (c + 1)],
                                    ar2[:, S + 4096:S + 6144],
                                    reads=ky + kpr, writes=ky, tag=lt + "sy")
            for g in range(ngl):
                pc = pcol(g)
                kuc = keys("ar2", S, S + 2048)
                self.dma(ar2[:, S:S + 2048],
                         d["USP"][:, scol(g):scol(g) + 2048],
                         reads=(("USP", g),), writes=kuc, tag=lt + "yui")
                kzc = keys("ar2", S + 2048, S + 4096)
                self.dma(ar2[:, S + 2048:S + 4096],
                         d["ZSP"][:, scol(g):scol(g) + 2048],
                         reads=(("ZSP", 0),), writes=kzc, tag=lt + "yzi")
                k1 = keys("ar2", S + 4096, S + 6144)
                self.stt(ar2[:, S + 4096:S + 6144], ar2[:, S:S + 2048],
                         G[:, 2150:2151], ar0[:, pc:pc + 2048],
                         ALU.mult, ALU.add,
                         reads=kuc + keys("ar0", pc, pc + 2048) + kcw,
                         writes=k1, tag=lt + "yd")
                k2 = keys("ar2", S + 6144, S + 8192)
                self.act(ar2[:, S + 6144:S + 8192], ar2[:, S + 2048:S + 4096],
                         AF.Silu, reads=kzc, writes=k2, tag=lt + "ysz")
                self.tt(ALU.mult, ar0[:, pc:pc + 2048], ar2[:, S + 4096:S + 6144],
                        ar2[:, S + 6144:S + 8192], reads=k1 + k2,
                        writes=keys("ar0", pc, pc + 2048), tag=lt + "yg")
                gr = 2 * (g % nct)
                bb = g // nct
                self.dma(d["YX"][gr:gr + 2, :,
                                 bb * cfg.pos:(bb + 1) * cfg.pos]
                         .rearrange("r p t -> p r t"),
                         ar0[:, pc:pc + 2048]
                         .rearrange("p (r t) -> p r t", r=2),
                         reads=keys("ar0", pc, pc + 2048),
                         writes=(("YX", gr), ("YX", gr + 1)), tag=lt + "yxo")
            self.coll("AllGather", ALU.bypass, d["YX"], d[f"yxg{l}"],
                      reads=tuple(("YX", r) for r in range(cfg.R)),
                      writes=((f"yxg{l}", 0),), tag=lt + "AG")
            yrows = d[f"yxg{l}"].rearrange("s q t -> (s q) t")
            kyr = keys("ar0", 0, 16384)
            for src_ in range(cfg.R):
                self.p.op("gpsimd",
                          lambda e, o=ar0[:, CL * src_:CL * (src_ + 1)],
                          i=yrows, x=sb["IDXS"][:, 16 + src_:17 + src_]:
                          e.indirect_dma_start(
                              out=o, out_offset=None, in_=i,
                              in_offset=IndirectOffsetOnAxis(x, 0)),
                          ((f"yxg{l}", 0),) + keys("IDXS"),
                          keys("ar0", CL * src_, CL * (src_ + 1)),
                          kind="d", tag=lt + "ygth")
            kow = keys("ar1", 0, 4096)
            self.dma_wfm(ar1[:, 0:4096], self.wbf[f"outw{l}"], 512,
                         reads=((f"wbf_outw{l}", 0),), writes=kow,
                         tag=lt + "oww")
            for i in range(NT):
                pp = i % 2
                kps = keys("psA", 512 * pp, 512 * (pp + 1))
                for src in range(cfg.R):
                    self.mm(psA[:, 512 * pp:512 * (pp + 1)],
                            ar0[:, src * CL + 128 * i:src * CL + 128 * (i + 1)],
                            ar1[:, 512 * src:512 * (src + 1)],
                            start=(src == 0), stop=(src == cfg.R - 1),
                            reads=kyr + kow, writes=kps, tag=lt + "owm")
                self.tt(ALU.add, E[:, 512 * i:512 * (i + 1)],
                        E[:, 512 * i:512 * (i + 1)],
                        psA[:, 512 * pp:512 * (pp + 1)],
                        reads=keys("E", 512 * i, 512 * (i + 1)) + kps,
                        writes=keys("E", 512 * i, 512 * (i + 1)), tag=lt + "hra")
        self.dma_wfm(ar0[:, 0:4096], self.wbf["lin2_w"], 1024,
                     reads=(("wbf_lin2_w", 0),), writes=keys("ar0", 0, 4096),
                     tag="mb_l2w")
        kb2 = keys("G", 0, 1024)
        self.dma(G[0:1, 0:1024], self.ein["lin2_b"], reads=(), writes=kb2,
                 tag="mb_l2b")
        kb2b = keys("W1", 1024, 2048)
        self.cp("vector", W1[0:1, 1024:2048], G[0:1, 0:1024],
                reads=kb2, writes=kb2b, tag="mb_l2bc")
        for i in range(cfg.NT_LOC):
            khb = keys("W1", 0, 512)
            self.cp("vector", W1[:, 0:512], E[:, 512 * i:512 * (i + 1)],
                    reads=keys("E", 512 * i, 512 * (i + 1)), writes=khb,
                    tag="mb_hc")
            khf = keys("W1", 512, 1024)
            for j in range(4):
                kT = keys("psT", 128 * j, 128 * (j + 1))
                self.trans(psT[:, 128 * j:128 * (j + 1)],
                           W1[:, 128 * j:128 * (j + 1)], reads=khb, writes=kT,
                           tag="mb_hT")
                self.cp("vector", W1[:, 512 + 128 * j:512 + 128 * (j + 1)],
                        psT[:, 128 * j:128 * (j + 1)], reads=kT, writes=khf,
                        tag="mb_hTe")
            kps = keys("psA", 0, 1024)
            for c_ in range(2):
                for k in range(4):
                    self.mm(psA[:, 512 * c_:512 * (c_ + 1)],
                            W1[:, 512 + 128 * k:512 + 128 * (k + 1)],
                            ar0[:, 1024 * k + 512 * c_:1024 * k + 512 * (c_ + 1)],
                            start=(k == 0), stop=False,
                            reads=khf + keys("ar0", 1024 * k + 512 * c_,
                                             1024 * k + 512 * (c_ + 1)),
                            writes=kps, tag="mb_l2m")
                self.mm(psA[:, 512 * c_:512 * (c_ + 1)], sb["c_o1x128b"][:],
                        W1[0:1, 1024 + 512 * c_:1024 + 512 * (c_ + 1)],
                        start=False, stop=True,
                        reads=kb2b + keys("c_o1x128b"), writes=kps,
                        tag="mb_l2bias")
            p3 = psA[:, 0:1024].rearrange("p (g k) -> p g k", k=32)
            km = keys("G", 1024, 1056)
            self.red(G[:, 1024:1056], p3, ALU.max, reads=kps, writes=km,
                     tag="mb_smx")
            g3 = G[:, 1024:1056].rearrange("p (g o) -> p g o", o=1)
            f3a = F[:, 0:1024].rearrange("p (g k) -> p g k", k=32)
            f3b = F[:, 1024:2048].rearrange("p (g k) -> p g k", k=32)
            ksb = keys("F", 0, 1024)
            p3b, g3b = bass.broadcast_tensor_aps(p3, g3)
            self.p.op("vector", lambda e, d_=f3a, a=p3b, bb=g3b:
                      e.tensor_tensor(out=d_, in0=a, in1=bb, op=ALU.subtract),
                      kps + km, ksb, tag="mb_sms")
            kex = keys("F", 1024, 2048)
            self.act(F[:, 1024:2048], F[:, 0:1024], AF.Exp, reads=ksb,
                     writes=kex, tag="mb_sme")
            ks2 = keys("G", 1056, 1088)
            self.red(G[:, 1056:1088], f3b, ALU.add, reads=kex, writes=ks2,
                     tag="mb_smsum")
            kr2 = keys("G", 1088, 1120)
            self.recip(G[:, 1088:1120], G[:, 1056:1088], reads=ks2, writes=kr2,
                       tag="mb_smr")
            g3r = G[:, 1088:1120].rearrange("p (g o) -> p g o", o=1)
            f3c, g3rb = bass.broadcast_tensor_aps(f3b, g3r)
            self.p.op("vector", lambda e, d_=f3a, a=f3c, bb=g3rb:
                      e.tensor_tensor(out=d_, in0=a, in1=bb, op=ALU.mult),
                      kex + kr2, ksb, tag="mb_smm")
            kxf = keys("F", 2048, 3072)
            self.dma(F[:, 2048:3072], d["xcur"][128 * i:128 * (i + 1), :],
                     reads=keys("xcur"), writes=kxf, tag="mb_xi")
            self.tt(ALU.add, F[:, 2048:3072], F[:, 2048:3072], F[:, 0:1024],
                    reads=kxf + ksb, writes=kxf, tag="mb_xadd")
            self.dma(d["xcur"][128 * i:128 * (i + 1), :], F[:, 2048:3072],
                     reads=kxf, writes=keys("xcur"), tag="mb_xo")

    def phase_out(self):
        self.dma(self.d["out"], self.d["xcur"], reads=keys("xcur"),
                 writes=(("out", 0),), tag="final_out")
        self.p.op("sync", lambda e: e.nop(), reads=(("out", 0),), writes=(),
                  tag="final_wait")


def kernel(**inputs):
    cfg = Cfg()
    maps = shard_inputs(cfg, inputs)
    k = K(cfg)
    nc = k.build()
    res = run_bass_kernel_spmd(nc, maps, core_ids=list(range(cfg.R)))
    outs = [r["out"] for r in res.results]
    full = np.zeros((cfg.B, cfg.S, cfg.D), np.float32)
    for c in range(cfg.R):
        sl = slice(cfg.pos * c, cfg.pos * (c + 1))
        full[0, sl] = outs[c][:cfg.pos]
        full[1, sl] = outs[c][cfg.pos:]
    return full
